# version 01
# baseline (speedup 1.0000x reference)
import math
import numpy as np
from contextlib import ExitStack

import concourse.bass as bass
import concourse.mybir as mybir
from concourse.bass_utils import run_bass_kernel_spmd
from concourse.bass_core_rust import IndirectOffsetOnAxis

F32 = mybir.dt.float32
F32R = mybir.dt.float32r
I32 = mybir.dt.int32
AF = mybir.ActivationFunctionType
ALU = mybir.AluOpType

B, T, C = 4, 4096, 1024
H = 16
HD = 64
NCORES = 8
E = 8
TOPK = 2
CAP = 4096
N_TOK = B * T
EPS = 1e-5
SC = 512
RC = 128
NSC = T // SC
NRC = SC // RC


def host_constants():
    c = {}
    c["eye128"] = np.eye(128, dtype=np.float32)
    jj, tt = np.meshgrid(np.arange(128), np.arange(128), indexing="ij")
    c["tri_incl"] = (jj <= tt).astype(np.float32)
    c["tri_excl"] = (jj < tt).astype(np.float32)
    inv_freq = 1.0 / (10000.0 ** (np.arange(0, HD, 2, dtype=np.float64) / HD))
    ang = np.arange(T, dtype=np.float64)[None, :] * inv_freq[:, None]
    c["rope_cos"] = np.cos(ang).astype(np.float32)
    c["rope_sin"] = np.sin(ang).astype(np.float32)
    lo = math.log(1.0 - 0.999)
    hi = math.log(1.0 - 0.9)
    gam = 1.0 - np.exp(np.linspace(lo, hi, H, dtype=np.float64))
    c["gammas"] = gam.astype(np.float32)
    j = np.arange(RC)[:, None]
    i = np.arange(RC)[None, :]
    dts = []
    for h in range(H):
        dt = np.where(i >= j, gam[h] ** np.maximum(i - j, 0), 0.0)
        dts.append(dt)
    c["ret_dt"] = np.concatenate(dts, axis=1).astype(np.float32)
    a = np.arange(RC)[None, :]
    gcross = np.stack([np.tile(gam[h] ** (a + 1), (64, 1)) for h in range(H)], axis=1)
    c["ret_gcross"] = gcross.astype(np.float32)
    a = np.arange(RC)[:, None]
    kd = np.stack([np.tile(gam[h] ** (RC - 1 - a), (1, 64)) for h in range(H)], axis=1)
    c["ret_kdec3"] = kd.astype(np.float32)
    c["ret_gs_f"] = [float(gam[h] ** RC) for h in range(H)]

    ind = np.zeros((128, 2), dtype=np.float32)
    ind[0:64, 0] = 1.0
    ind[64:128, 1] = 1.0
    c["headsel"] = ind
    c["headselT"] = np.ascontiguousarray(ind.T)
    c["onesrow"] = np.ones((1, 128), dtype=np.float32)
    kk, qq = np.meshgrid(np.arange(128), np.arange(128), indexing="ij")
    c["diagmask"] = (kk <= qq).astype(np.float32)
    c["iota8"] = np.tile(np.arange(8, dtype=np.float32)[None, :], (128, 1))
    c["iota128"] = np.arange(128, dtype=np.float32).reshape(128, 1)
    c["ones128"] = np.ones((128, 1), dtype=np.float32)
    c["onesall"] = np.ones((128, 128), dtype=np.float32)
    c["iota8p"] = c["iota8"] + 1000.0
    pp_, aa_ = np.meshgrid(np.arange(128), np.arange(32), indexing="ij")
    c["tok0"] = (pp_ + 128.0 * aa_).astype(np.float32)
    pp_, ff_ = np.meshgrid(np.arange(128), np.arange(512), indexing="ij")
    dm = np.stack([(pp_ <= ff_ - 128 * r).astype(np.float32) for r in range(4)], axis=1)
    c["diagm"] = dm.reshape(128, 4 * 512)
    return c


class Prog:

    def __init__(self, name="mk"):
        s = bass.Bass("TRN2", target_bir_lowering=False, enable_asserts=False,
                      detect_race_conditions=False)
        self.nc = s
        self.stack = ExitStack()
        self.sems = {}
        self.counts = {}

    def sem(self, name):
        if name not in self.sems:
            self.sems[name] = self.stack.enter_context(self.nc.semaphore(name))
            self.counts[name] = 0
        return self.sems[name]

    def inc(self, instr, name, dma=False):
        n = 16 if dma else 1
        instr.then_inc(self.sem(name), n)
        self.counts[name] += n
        return self.counts[name]

    def sb(self, name, shape, dtype):
        st = self.phase_stack if getattr(self, "phase_stack", None) is not None else self.stack
        return st.enter_context(self.nc.sbuf_tensor(name, shape, dtype))

    def open_phase(self):
        self.phase_stack = ExitStack()

    def close_phase(self):
        self.phase_stack.close()
        self.phase_stack = None

    def ps(self, name, shape, dtype=F32):
        return self.stack.enter_context(self.nc.psum_tensor(name, shape, dtype))

    def gdma(self, eng, out, in_, sname, **kw):
        sem = self.sem(sname)
        if self.counts[sname] > 0:
            eng.wait_ge(sem, self.counts[sname])
        ins = eng.dma_start(out=out, in_=in_, **kw)
        ins.then_inc(sem, 16)
        self.counts[sname] += 16
        return ins

    def close(self):
        self.stack.close()


def ceil_div(a, b):
    return (a + b - 1) // b






def emit_p1_retention(prog, io):
    sb = prog.sb
    NG, HPG = 4, 4
    GF = 256

    xin = sb("p1_xin", [128, 1, 8, 512], F32R)
    xsq = sb("p1_xsq", [128, 8, 512], F32R)
    xhat = xsq
    w_q = sb("p1_wq", [128, 8, GF], F32R)
    w_k = sb("p1_wk", [128, 8, GF], F32R)
    w_v = sb("p1_wv", [128, 8, GF], F32R)
    w_o = sb("p1_wo", [64, 4, 1024], F32R)
    qtm = sb("p1_qtm", [128, GF], F32R)
    ktm = sb("p1_ktm", [128, 4, GF], F32R)
    vtm = sb("p1_vtm", [128, 4, GF], F32R)
    qfm = sb("p1_qfm", [64, 4, 512], F32R)
    kfm = sb("p1_kfm", [64, 4, 512], F32R)
    ybuf = sb("p1_ybuf", [64, 4, T], F32R)
    tstate = sb("p1_tst", [64, 4, 64], F32R)
    abar = sb("p1_abar", [128, 512], F32R)
    qhat = sb("p1_qhat", [64, 4, 128], F32R)
    khat2 = sb("p1_khat2", [128, 256], F32R)
    ropet = sb("p1_ropet", [128, 2, 128], F32)
    stat_m = sb("p1_sm", [1, 512], F32R)
    stat_a = sb("p1_sa", [1, 512], F32R)
    mb_bc = sb("p1_mb", [128, 512], F32R)
    ab_bc = sb("p1_ab", [128, 512], F32R)
    stat_t = ab_bc[0:1, :].bitcast(F32)
    stat_t2 = mb_bc[0:1, :].bitcast(F32)
    gnred = sb("p1_gnred", [64, 8], F32)
    gnsml = sb("p1_gnsml", [1, 12], F32)
    gnstg = sb("p1_gnstg", [1, 512], F32R)
    gnsqt = sb("p1_gnsqt", [1, 4], F32)
    gn_s = sb("p1_gns", [64, 4], F32)
    gn_t = sb("p1_gnt", [64, 4], F32)
    gn_b = sb("p1_gnb", [64, 4], F32)
    dbgstage = sb("p1_dbgst", [64, 20], F32)
    wostg = sb("p1_wostg", [128, 4, 512], F32)

    cosb, sinb = io["cosb"], io["sinb"]
    onesb, onesrow, eyeb = io["onesb"], io["onesrow"], io["eyeb"]
    dtb, gcrossb, kdec3b = io["dtb"], io["gcrossb"], io["kdec3b"]
    gnwb, gnbb = io["gnwb"], io["gnbb"]
    xT, ret_hbm = io["xT"], io["ret_hbm"]
    pp, ppr = io["pp"], io["ppr"]
    gs_f = io["gs_f"]

    names = ["ld", "w", "sq", "stat", "sml", "sml2", "bc", "bce", "xh", "prj",
             "vev", "rope", "tp", "tpe", "rpe", "rdv", "gn", "gn2", "gn3",
             "gnA", "yp", "wo", "woe", "wod", "dbg"]
    s = {n: prog.sem("p1_" + n) for n in names}
    dbg = io.get("dbg")

    def prj_consumer(n):
        i, r = divmod(n, 12)
        tt, j = divmod(r, 3)
        if j < 2:
            return s["rope"], i * 8 + tt * 2 + j + 1
        return s["vev"], i * 4 + tt + 1

    def f_sync(sync):
        for g in range(NG):
            if g > 0:
                sync.wait_ge(s["woe"], g * 64)
            for (wt, dst) in ((io["wqT"], w_q), (io["wkT"], w_k), (io["wvT"], w_v)):
                src = wt[:, g * GF:(g + 1) * GF].rearrange("(k p) f -> p k f", p=128)
                prog.gdma(sync, dst[:], src, "p1_w")
            src = io["woT"][g * GF:(g + 1) * GF, :].rearrange("(h p) f -> p h f", p=64)
            prog.gdma(sync, w_o[:], src, "p1_w")
            prog.gdma(sync, dtb[:], io["c_dt"][:, g * 512:(g + 1) * 512], "p1_w")
            prog.gdma(sync, gcrossb[:].rearrange("p a b -> p (a b)"),
                      io["c_gcross"][:, g * 512:(g + 1) * 512], "p1_w")
            prog.gdma(sync, kdec3b[:].rearrange("p a b -> p (a b)"),
                      io["c_kdec3"][:, g * 256:(g + 1) * 256], "p1_w")
            for sc in range(NSC):
                i = g * NSC + sc
                if i >= 1:
                    sync.wait_ge(s["xh"], i)
                src = xT.rearrange("(k p) t -> p k t", p=128)[:, :, sc * 512:(sc + 1) * 512]
                prog.gdma(sync, xin[:, 0], src, "p1_ld")
            if g == 0:
                for tt in range(8):
                    for hf in range(2):
                        sync.wait_ge(s["woe"], tt * 8 + hf * 4 + 4)
                        dst = ret_hbm[hf * 512:(hf + 1) * 512, tt * 512:(tt + 1) * 512
                                      ].rearrange("(k p) t -> p k t", p=128).bitcast(F32)
                        prog.gdma(sync, dst, wostg[:], "p1_wod")

    def f_gps(gps):
        for g in range(1, NG):
            for tt in range(8):
                for hf in range(2):
                    gps.wait_ge(s["woe"], g * 64 + tt * 8 + hf * 4 + 4)
                    dst = ret_hbm[hf * 512:(hf + 1) * 512, tt * 512:(tt + 1) * 512
                                  ].rearrange("(k p) t -> p k t", p=128).bitcast(F32)
                    prog.gdma(gps, dst, wostg[:], "p1_wod", accum_op=ALU.add)

    def f_act(act):
        for g in range(NG):
            for sc in range(NSC):
                i = g * NSC + sc
                act.wait_ge(s["ld"], 16 * (i + 1))
                if i > 0:
                    act.wait_ge(s["prj"], 12 * i)
                if g > 0 and sc == 0:
                    act.wait_ge(s["gn"], 2 * g - 1)
                act.activation(xsq[:], xin[:, 0], AF.Square).then_inc(s["sq"], 1)
                act.wait_ge(s["sml"], i + 1)
                act.activation(stat_a[:], stat_t, AF.Sqrt).then_inc(s["sml2"], 1)
                if dbg and i == 0:
                    act.wait_ge(s["rdv"], 8)
                    prog.gdma(act, dbg["xhat"], xhat[:].rearrange("p a b -> p (a b)"), "p1_dbg")
                    prog.gdma(act, dbg["qfm"], qfm[:].rearrange("p a b -> p (a b)"), "p1_dbg")
                    prog.gdma(act, dbg["kfm"], kfm[:].rearrange("p a b -> p (a b)"), "p1_dbg")
                    prog.gdma(act, dbg["vtm"], vtm[:].rearrange("p a b -> p (a b)"), "p1_dbg")
                    act.wait_ge(prog.sems["p1_dbg"], prog.counts["p1_dbg"])
            if dbg and g == 0:
                act.wait_ge(s["gn"], 2 * g + 1)
                prog.gdma(act, dbg["y"], ybuf[:].rearrange("p a b -> p (a b)"), "p1_dbg")
                act.wait_ge(prog.sems["p1_dbg"], prog.counts["p1_dbg"])
            act.wait_ge(s["gn"], 2 * g + 2)
            act.activation(gnsqt[:], gnsml[:, 4:8], AF.Sqrt).then_inc(s["gnA"], 1)
            if dbg and g == 0:
                act.wait_ge(s["yp"], 4)
                prog.gdma(act, dbg["ypost"], ybuf[:].rearrange("p a b -> p (a b)"), "p1_dbg")
                prog.gdma(act, dbg["gn2"][:, 0:20], dbgstage[:], "p1_dbg")
                act.wait_ge(prog.sems["p1_dbg"], prog.counts["p1_dbg"])

    def f_pe(pe):
        for g in range(NG):
            pe.wait_ge(s["w"], 112 * (g + 1))
            if g > 0:
                pe.wait_ge(s["woe"], 64 * g)
                pe.wait_ge(s["gn"], 2 * g)
                pe.wait_ge(s["yp"], g * HPG)
            for sc in range(NSC):
                i = g * NSC + sc
                pe.wait_ge(s["ld"], 16 * (i + 1))
                pe.wait_ge(s["sq"], i + 1)
                if i > 0:
                    pe.wait_ge(s["rdv"], 8 * i)
                for k in range(8):
                    pe.matmul(pp[0:1, 0, :], onesb[:, 0:1], xin[:, 0, k],
                              start=(k == 0), stop=(k == 7))
                for k in range(8):
                    ins = pe.matmul(pp[0:1, 1, :], onesb[:, 0:1], xsq[:, k],
                                    start=(k == 0), stop=(k == 7))
                ins.then_inc(s["stat"], 1)
                pe.wait_ge(s["sml2"], i + 1)
                if i > 0:
                    pe.wait_ge(s["bce"], i)
                pe.matmul(pp[:, 2, :], onesrow[0:1, :], stat_m[:], start=True, stop=True)
                pe.matmul(pp[:, 3, :], onesrow[0:1, :], stat_a[:], start=True, stop=True
                          ).then_inc(s["bc"], 1)
                pe.wait_ge(s["xh"], i + 1)
                for tt in range(4):
                    for j, wbuf in enumerate((w_q, w_k, w_v)):
                        n = i * 12 + tt * 3 + j
                        if n >= 2:
                            cs, cv = prj_consumer(n - 2)
                            pe.wait_ge(cs, cv)
                        bank = 4 + n % 2
                        for k in range(8):
                            ins = pe.matmul(pp[:, bank, 0:GF], xhat[:, k, tt * 128:(tt + 1) * 128],
                                            wbuf[:, k], start=(k == 0), stop=(k == 7))
                        ins.then_inc(s["prj"], 1)
                    pe.wait_ge(s["rope"], i * 8 + tt * 2 + 2)
                    for jj, srcb in enumerate((qtm[:], ktm[:, tt])):
                        for h in range(4):
                            ntp = (i * 4 + tt) * 8 + jj * 4 + h + 1
                            if ntp >= 3:
                                pe.wait_ge(s["tpe"], ntp - 2)
                            pe.transpose(ppr[:, (ntp - 1) % 2, 0:128][0:64, :] if False else ppr[0:64, (ntp - 1) % 2, 0:128],
                                         srcb[:, h * 64:(h + 1) * 64], eyeb[:]
                                         ).then_inc(s["tp"], 1)
                for rc in range(4):
                    pe.wait_ge(s["tpe"], (i * 4 + rc + 1) * 8)
                    pe.wait_ge(s["vev"], i * 4 + rc + 1)
                    first = (sc == 0 and rc == 0)
                    idx = i * 4 + rc
                    rcs = slice(rc * 128, (rc + 1) * 128)
                    for h in range(4):
                        ins = pe.matmul(pp[:, 2, h * 128:(h + 1) * 128], kfm[0:64, h, rcs],
                                        qfm[0:64, h, rcs], start=True, stop=True)
                    ins.then_inc(s["rpe"], 1)
                    pe.wait_ge(s["rdv"], 2 * idx + 1)
                    for h in range(4):
                        hs = slice(h * 64, (h + 1) * 64)
                        pe.matmul(pp[0:64, 3, h * 128:(h + 1) * 128],
                                  vtm[:, rc, hs], abar[:, h * 128:(h + 1) * 128],
                                  start=True, stop=first)
                        if not first:
                            pe.matmul(pp[0:64, 3, h * 128:(h + 1) * 128],
                                      tstate[:, h, :], qhat[:, h, :],
                                      start=False, stop=True)
                        ins = pe.matmul(pp[0:64, 0, 256 + h * 64:256 + (h + 1) * 64],
                                        khat2[:, hs], vtm[:, rc, hs], start=True, stop=True)
                    ins.then_inc(s["rpe"], 1)
            pe.wait_ge(s["gn"], 2 * g + 1)
            pe.matmul(pp[0:1, 0, 0:8], onesb[0:64, 0:1].bitcast(F32), gnred[:], start=True, stop=True)
            pe.matmul(pp[:, 4, :], eyeb[:], xhat[:, 0], start=True, stop=True
                      ).then_inc(s["gn2"], 1)
            pe.wait_ge(s["gn3"], g + 1)
            pe.matmul(pp[0:64, 1, :], onesrow[0:1, 0:64], gnstg[:], start=True, stop=True)
            pe.matmul(pp[:, 4, :], eyeb[:], xhat[:, 0], start=True, stop=True
                      ).then_inc(s["gn2"], 1)
            pe.wait_ge(s["yp"], (g + 1) * HPG)
            for tt in range(8):
                for oc in range(8):
                    nwo = g * 64 + tt * 8 + oc
                    bank = 4 + nwo % 2
                    if nwo >= 2:
                        pe.wait_ge(s["woe"], nwo - 1)
                    for h in range(4):
                        ins = pe.matmul(pp[:, bank, :], w_o[:, h, oc * 128:(oc + 1) * 128],
                                        ybuf[:, h, tt * 512:(tt + 1) * 512],
                                        start=(h == 0), stop=(h == 3))
                    ins.then_inc(s["wo"], 1)

    def f_dve(dve):
        for g in range(NG):
            dve.wait_ge(s["w"], 112 * (g + 1))
            if dbg and g == 1:
                dve.wait_ge(prog.sems["p1_dbg"], prog.counts["p1_dbg"])
            for sc in range(NSC):
                i = g * NSC + sc
                dve.wait_ge(s["stat"], i + 1)
                dve.tensor_scalar_mul(stat_m[:], pp[0:1, 0, :], 1.0 / 1024.0)
                dve.tensor_scalar_mul(stat_t, pp[0:1, 1, :], 1.0 / 1024.0)
                dve.tensor_tensor(stat_t2, stat_m[:].bitcast(F32), stat_m[:].bitcast(F32), ALU.mult)
                dve.tensor_tensor(stat_t, stat_t, stat_t2, ALU.subtract)
                dve.tensor_scalar_add(stat_t, stat_t, EPS)
                dve.reciprocal(stat_t, stat_t).then_inc(s["sml"], 1)
                dve.wait_ge(s["bc"], i + 1)
                dve.tensor_copy(mb_bc[:], pp[:, 2, :])
                dve.tensor_copy(ab_bc[:], pp[:, 3, :]).then_inc(s["bce"], 1)
                for k in range(8):
                    dve.tensor_tensor(xsq[:, k], xin[:, 0, k], mb_bc[:], ALU.subtract)
                    ins = dve.tensor_tensor(xhat[:, k], xsq[:, k], ab_bc[:], ALU.mult)
                ins.then_inc(s["xh"], 1)
                for tt in range(4):
                    base_prj = i * 12 + tt * 3
                    for j, dstb in ((0, qtm[:]), (1, ktm[:, tt])):
                        bank = 4 + (base_prj + j) % 2
                        dve.wait_ge(s["prj"], base_prj + j + 1)
                        if j == 0 and (i * 4 + tt) >= 1:
                            dve.wait_ge(s["tp"], (i * 4 + tt - 1) * 8 + 4)
                        dve.tensor_copy(dstb, pp[:, bank, 0:GF]).then_inc(s["rope"], 1)
                    bank = 4 + (base_prj + 2) % 2
                    dve.wait_ge(s["prj"], base_prj + 3)
                    dve.tensor_copy(vtm[:, tt], pp[:, bank, 0:GF]).then_inc(s["vev"], 1)
                    for n in range(8):
                        ntp = (i * 4 + tt) * 8 + n + 1
                        dve.wait_ge(s["tp"], ntp)
                        dstb = qfm if n < 4 else kfm
                        h = n % 4
                        dve.tensor_copy(dstb[:, h, tt * 128:(tt + 1) * 128],
                                        ppr[0:64, (ntp - 1) % 2, 0:128]).then_inc(s["tpe"], 1)
                for rc in range(4):
                    first = (sc == 0 and rc == 0)
                    if g > 0 and sc == 0 and rc == 0:
                        dve.wait_ge(s["wo"], g * 64)
                    idx = i * 4 + rc
                    rcs = slice(rc * 128, (rc + 1) * 128)
                    hq = 4 * g
                    dve.wait_ge(s["rpe"], 2 * idx + 1)
                    dve.tensor_tensor(abar[:], pp[:, 2, 0:512],
                                      dtb[:], ALU.mult)
                    if not first:
                        dve.tensor_tensor(qhat[:], qfm[:, :, rcs], gcrossb[:], ALU.mult)
                    dve.tensor_tensor(khat2[:].rearrange("p (a b) -> p a b", b=64),
                                      ktm[:, rc].rearrange("p (a b) -> p a b", b=64),
                                      kdec3b[:], ALU.mult).then_inc(s["rdv"], 1)
                    dve.wait_ge(s["rpe"], 2 * idx + 2)
                    dve.tensor_copy(ybuf[:, :, (sc * 4 + rc) * 128:(sc * 4 + rc + 1) * 128],
                                    pp[0:64, 3, 0:512].rearrange("p (a b) -> p a b", b=128))
                    for h in range(4):
                        if first:
                            dve.tensor_copy(tstate[:, h, :], pp[0:64, 0, 256 + h * 64:256 + (h + 1) * 64])
                        else:
                            dve.scalar_tensor_tensor(tstate[:, h, :], tstate[:, h, :],
                                                     gs_f[hq + h], pp[0:64, 0, 256 + h * 64:256 + (h + 1) * 64],
                                                     op0=ALU.mult, op1=ALU.add)
                    dve.engine_nop().then_inc(s["rdv"], 1)
            for h in range(4):
                dve.reduce_sum(gnred[:, h:h + 1], ybuf[:, h], axis=mybir.AxisListType.X)
            for h in range(4):
                dve.tensor_tensor(xsq[0:64, 0:8].rearrange("p a t -> p (a t)"),
                                  ybuf[:, h], ybuf[:, h], ALU.mult)
                dve.reduce_sum(gnred[:, 4 + h:5 + h],
                               xsq[0:64, 0:8].rearrange("p a t -> p (a t)"),
                               axis=mybir.AxisListType.X)
            dve.engine_nop().then_inc(s["gn"], 1)
            dve.wait_ge(s["gn2"], 2 * g + 1)
            dve.tensor_scalar_mul(gnsml[:, 0:8], pp[0:1, 0, 0:8], 1.0 / (64.0 * 4096.0))
            dve.tensor_tensor(gnsml[:, 8:12], gnsml[:, 0:4], gnsml[:, 0:4], ALU.mult)
            dve.tensor_tensor(gnsml[:, 4:8], gnsml[:, 4:8], gnsml[:, 8:12], ALU.subtract)
            dve.tensor_scalar_add(gnsml[:, 4:8], gnsml[:, 4:8], EPS).then_inc(s["gn"], 1)
            dve.wait_ge(s["gnA"], g + 1)
            with prog.nc.allow_low_precision("f32r stage"):
                dve.reciprocal(gnstg[:, 0:4], gnsqt[:])
                dve.tensor_copy(gnstg[:, 4:8], gnsml[:, 0:4]).then_inc(s["gn3"], 1)
            dve.wait_ge(s["gn2"], 2 * g + 2)
            dve.tensor_copy(mb_bc[0:64, :], pp[0:64, 1, :])
            dve.tensor_copy(ab_bc[0:64, :], mb_bc[0:64, :])
            dve.tensor_tensor(gn_s[:], mb_bc[0:64, 0:4].bitcast(F32), gnwb[:, 4 * g:4 * g + 4], ALU.mult)
            dve.tensor_copy(ab_bc[0:64, :], mb_bc[0:64, :])
            dve.tensor_tensor(gn_t[:], mb_bc[0:64, 4:8].bitcast(F32), gn_s[:], ALU.mult)
            dve.tensor_copy(ab_bc[0:64, :], mb_bc[0:64, :])
            dve.tensor_tensor(gn_b[:], gnbb[:, 4 * g:4 * g + 4], gn_t[:], ALU.subtract)
            dve.tensor_copy(ab_bc[0:64, :], mb_bc[0:64, :])
            if dbg and g == 0:
                dve.tensor_copy(dbgstage[:, 0:4], gn_s[:])
                dve.tensor_copy(dbgstage[:, 4:8], gn_b[:])
                dve.tensor_copy(dbgstage[:, 8:16], pp[0:64, 1, 0:8])
                dve.tensor_copy(dbgstage[:, 16:20], gn_t[:])
            for h in range(4):
                dve.tensor_scalar_mul(ybuf[:, h], ybuf[:, h], gn_s[:, h:h + 1])
                dve.tensor_scalar_add(ybuf[:, h], ybuf[:, h], gn_b[:, h:h + 1]
                                      ).then_inc(s["yp"], 1)
            for tt in range(8):
                for oc in range(8):
                    nwo = g * 64 + tt * 8 + oc
                    bank = 4 + nwo % 2
                    dve.wait_ge(s["wo"], nwo + 1)
                    nhalf = (g * 8 + tt) * 2 + oc // 4
                    if nhalf >= 1 and oc % 4 == 0:
                        dve.wait_ge(s["wod"], 16 * nhalf)
                    dve.tensor_copy(wostg[:, oc % 4], pp[:, bank, :]).then_inc(s["woe"], 1)

    return dict(f_sync=f_sync, f_gps=f_gps, f_act=f_act, f_pe=f_pe, f_dve=f_dve)



def declare_common(prog):
    nc = prog.nc
    sb = prog.sb
    io = {}
    cdefs = [
        ("c_ones", [128, 1], F32R, "ones128", "onesb"),
        ("c_onesrow", [1, 128], F32R, "onesrow", "onesrow"),
        ("c_eye", [128, 128], F32R, "eye128", "eyeb"),
        ("c_dt", [128, 16 * 128], F32, "ret_dt", "dtb"),
        ("c_gcross", [64, 16 * 128], F32, "ret_gcross", "gcrossb"),
        ("c_kdec3", [128, 16 * 64], F32, "ret_kdec3", "kdec3b"),
        ("c_onesall", [128, 128], F32R, "onesall", "onesall"),
        ("c_diag", [128, 4 * 512], F32R, "diagm", "diagmb"),
        ("c_cos", [T, 32], F32, "rope_cos_tm", "cosb"),
        ("c_sin", [T, 32], F32, "rope_sin_tm", "sinb"),
    ]
    io["_cdefs"] = cdefs
    for nm, shape, dt_, ckey, sbkey in cdefs:
        io[nm] = nc.dram_tensor(nm, shape, dt_, kind="ExternalInput").ap()
    io["kdec3b"] = sb("cb_kdec3", [128, 4, 64], F32)
    io["cosb"] = sb("cb_cos", [128, 32, 32], F32)
    io["sinb"] = sb("cb_sin", [128, 32, 32], F32)
    io["onesb"] = sb("cb_ones", [128, 1], F32R)
    io["onesall"] = sb("cb_onesall", [128, 128], F32R)
    io["onesrow"] = sb("cb_onesrow", [1, 128], F32R)
    io["eyeb"] = sb("cb_eye", [128, 128], F32R)
    io["dtb"] = sb("cb_dt", [128, 4 * 128], F32)
    io["gcrossb"] = sb("cb_gcross", [64, 4, 128], F32)
    io["gnwb"] = sb("cb_gnw", [64, 16], F32)
    io["gnbb"] = sb("cb_gnb", [64, 16], F32)
    io["pp"] = prog.ps("pp", [128, 6, 512], F32)
    io["ppr"] = prog.ps("ppr", [128, 2, 512], F32R)
    return io


def emit_const_loads(prog, blk, io):
    s_c = prog.sem("c_ld")

    def f_sync(sync):
        n = 0
        pairs = [
            (io["onesb"][:], io["c_ones"]),
            (io["onesrow"][:], io["c_onesrow"]),
            (io["eyeb"][:], io["c_eye"]),
            (io["onesall"][:], io["c_onesall"]),
            (io["cosb"][:], io["c_cos"].rearrange("(n p) f -> p n f", p=128)),
            (io["sinb"][:], io["c_sin"].rearrange("(n p) f -> p n f", p=128)),
            (io["gnwb"][:], io["gnw_in"]),
            (io["gnbb"][:], io["gnb_in"]),
        ]
        for dst, src in pairs:
            prog.gdma(sync, dst, src, "c_ld")
            n += 16
        sync.wait_ge(prog.sem("c_ld"), prog.counts["c_ld"])

    blk.sync(f_sync)


def const_in_map(consts):
    return {
        "c_ones": consts["ones128"], "c_onesrow": consts["onesrow"],
        "c_eye": consts["eye128"], "c_dt": consts["ret_dt"],
        "c_onesall": consts["onesall"], "c_diag": consts["diagm"],
        "c_gcross": consts["ret_gcross"].reshape(64, 16 * 128),
        "c_kdec3": consts["ret_kdec3"].reshape(128, 16 * 64),
        "c_cos": np.ascontiguousarray(consts["rope_cos"].T),
        "c_sin": np.ascontiguousarray(consts["rope_sin"].T),
    }


def build_p1_test(consts):
    prog = Prog()
    nc = prog.nc
    io = declare_common(prog)
    io["xT"] = nc.dram_tensor("xT", [C, T], F32R, kind="ExternalInput").ap()
    io["wqT"] = nc.dram_tensor("wqT", [C, C], F32R, kind="ExternalInput").ap()
    io["wkT"] = nc.dram_tensor("wkT", [C, C], F32R, kind="ExternalInput").ap()
    io["wvT"] = nc.dram_tensor("wvT", [C, C], F32R, kind="ExternalInput").ap()
    io["woT"] = nc.dram_tensor("woT", [C, C], F32R, kind="ExternalInput").ap()
    io["gnw_in"] = nc.dram_tensor("gnw_in", [64, 16], F32, kind="ExternalInput").ap()
    io["gnb_in"] = nc.dram_tensor("gnb_in", [64, 16], F32, kind="ExternalInput").ap()
    io["ret_hbm"] = nc.dram_tensor("ret_hbm", [C, T], F32R, kind="ExternalOutput").ap()
    io["gs_f"] = consts["ret_gs_f"]
    io["dbg"] = {
        "xhat": nc.dram_tensor("dbg_xhat", [128, 8 * 512], F32R, kind="ExternalOutput").ap(),
        "qfm": nc.dram_tensor("dbg_qfm", [64, 4 * 512], F32R, kind="ExternalOutput").ap(),
        "kfm": nc.dram_tensor("dbg_kfm", [64, 4 * 512], F32R, kind="ExternalOutput").ap(),
        "vtm": nc.dram_tensor("dbg_vtm", [128, 4 * 256], F32R, kind="ExternalOutput").ap(),
        "y": nc.dram_tensor("dbg_y", [64, 4 * 4096], F32R, kind="ExternalOutput").ap(),
        "ypost": nc.dram_tensor("dbg_ypost", [64, 4 * 4096], F32R, kind="ExternalOutput").ap(),
        "gns": nc.dram_tensor("dbg_gns", [64, 12], F32, kind="ExternalOutput").ap(),
        "gn2": nc.dram_tensor("dbg_gn2", [64, 20], F32, kind="ExternalOutput").ap(),
    }

    with prog.nc.Block() as blk0:
        emit_const_loads(prog, blk0, io)
    fns = emit_p1_retention(prog, io)
    with prog.nc.Block() as blk1:
        blk1.sync(fns["f_sync"])
        blk1.gpsimd(fns["f_gps"])
        blk1.scalar(fns["f_act"])
        blk1.tensor(fns["f_pe"])
        blk1.vector(fns["f_dve"])
    s_wod = prog.sems["p1_wod"]
    with prog.nc.Block() as blk2:
        def f_end(gps):
            gps.wait_ge(s_wod, prog.counts["p1_wod"])
        blk2.gpsimd(f_end)
    prog.close()
    return prog


def kernel(**inputs):
    raise NotImplementedError("under construction")



def emit_p2_qkv(prog, io):
    sb = prog.sb
    NG, GF = 4, 256

    xin = sb("p2_xin", [128, 8, 512], F32R)
    ret = sb("p2_ret", [128, 8, 512], F32R)
    w_q = sb("p2_wq", [128, 8, GF], F32R)
    w_k = sb("p2_wk", [128, 8, GF], F32R)
    w_v = sb("p2_wv", [128, 8, GF], F32R)
    qtm = sb("p2_qtm", [128, GF], F32R)
    ktm = sb("p2_ktm", [128, GF], F32R)
    vtm = sb("p2_vtm", [128, 4, GF], F32R)
    qfm = sb("p2_qfm", [64, 4, 512], F32R)
    kfm = sb("p2_kfm", [64, 4, 512], F32R)
    mb_bc = sb("p2_mb", [128, 512], F32)
    ab_bc = sb("p2_ab", [128, 512], F32)
    wtmp = sb("p2_wt", [128, 512], F32)
    rs_out = sb("p2_rs", [128, 512], F32)
    ropet = sb("p2_rope", [128, 2, 128], F32)
    lngb = sb("p2_lngb", [128, 2, 8], F32R)
    brow = sb("p2_brow", [1, 3 * GF], F32R)

    cosb, sinb = io["cosb"], io["sinb"]
    onesrow, eyeb, onesall = io["onesrow"], io["eyeb"], io["onesall"]
    xT, ret_hbm = io["xT"], io["ret_hbm"]
    x1_hbm, xm_hbm = io["x1_hbm"], io["xm_hbm"]
    k_hbm, v_hbm, q_hbm = io["k_hbm"], io["v_hbm"], io["q_hbm"]
    wqkv = io["wqkv"]
    pp, ppr = io["pp"], io["ppr"]

    names = ["ld", "w", "x1", "x1w", "sq", "stat", "st_ev", "sml", "sml2",
             "xh", "xh2", "xmw", "prj", "vev", "rope", "tp", "tpe", "bias",
             "gsc", "kqw", "vw"]
    s = {n: prog.sem("p2_" + n) for n in names}

    def prj_consumer(n):
        i, r = divmod(n, 12)
        tt, j = divmod(r, 3)
        if j < 2:
            return s["rope"], i * 8 + tt * 2 + j + 1
        return s["vev"], i * 4 + tt + 1

    def f_sync(sync):
        prog.gdma(sync, lngb[:].rearrange("p a b -> p (a b)"), io["lngb_in"], "p2_w")
        for g in range(NG):
            if g > 0:
                sync.wait_ge(s["prj"], 96 * g)
            for j, dst in enumerate((w_q, w_k, w_v)):
                src = wqkv[:, j * 1024 + g * GF: j * 1024 + (g + 1) * GF]
                prog.gdma(sync, dst[:], src.rearrange("(k p) f -> p k f", p=128), "p2_w")
            for sc in range(NSC):
                i = g * NSC + sc
                ts = slice(sc * 512, (sc + 1) * 512)
                if i >= 1:
                    sync.wait_ge(s["prj"], 12 * i)
                    sync.wait_ge(s["x1w"], 16 * i)
                    sync.wait_ge(s["xmw"], 16 * min(i, 8))
                prog.gdma(sync, xin[:], xT.rearrange("(k p) t -> p k t", p=128)[:, :, ts], "p2_ld")
                prog.gdma(sync, ret[:],
                          ret_hbm.rearrange("(k p) t -> p k t", p=128)[:, :, ts], "p2_ld")

    def f_scalar(act):
        for g in range(NG):
            for sc in range(NSC):
                i = g * NSC + sc
                ts = slice(sc * 512, (sc + 1) * 512)
                act.wait_ge(s["x1"], i + 1)
                if i >= 1:
                    act.wait_ge(s["xh2"], i)
                    if i <= 8:
                        act.wait_ge(s["xmw"], 16 * i)
                act.activation(xin[:], ret[:], AF.Square).then_inc(s["sq"], 1)
                act.wait_ge(s["sml"], 2 * i + 1)
                act.activation(rs_out[:], ab_bc[:], AF.Sqrt).then_inc(s["sml2"], 1)
                act.wait_ge(s["sml"], 2 * i + 2)
                act.activation(ab_bc[:], wtmp[:], AF.Sqrt).then_inc(s["sml2"], 1)
                act.wait_ge(s["tpe"], 32 * (i + 1))
                prog.gdma(act, k_hbm[4 * g:4 * g + 4, :, ts].rearrange("h p t -> p h t"),
                          kfm[:], "p2_kqw")
                prog.gdma(act, q_hbm[4 * g:4 * g + 4, :, ts].rearrange("h p t -> p h t"),
                          qfm[:], "p2_kqw")

    def f_pe(pe):
        for g in range(NG):
            pe.wait_ge(s["w"], 16 + 48 * (g + 1))
            if g > 0:
                pe.wait_ge(s["tpe"], 32 * 8 * g)
            for j, wbuf in enumerate((w_q, w_k, w_v)):
                dstp = pp[0:1, 0, (j % 2) * GF:(j % 2 + 1) * GF] if j < 2 \
                    else pp[0:1, 1, 0:GF]
                for k in range(8):
                    pe.matmul(dstp, lngb[:, 1, k:k + 1], wbuf[:, k],
                              start=(k == 0), stop=(k == 7))
            pe.matmul(pp[:, 4, :], eyeb[:], w_q[:, 0:2].rearrange("p a b -> p (a b)"),
                      start=True, stop=True).then_inc(s["bias"], 1)
            pe.wait_ge(s["gsc"], g + 1)
            for sc in range(NSC):
                i = g * NSC + sc
                if i > 0:
                    pe.wait_ge(s["st_ev"], i)
                pe.wait_ge(s["x1"], i + 1)
                for k in range(8):
                    pe.matmul(pp[:, 2, :], onesall[:], ret[:, k],
                              start=(k == 0), stop=(k == 7))
                pe.wait_ge(s["sq"], i + 1)
                for k in range(8):
                    ins = pe.matmul(pp[:, 3, :], onesall[:], xin[:, k],
                                    start=(k == 0), stop=(k == 7))
                ins.then_inc(s["stat"], 1)
                pe.wait_ge(s["xh2"], i + 1)
                for tt in range(4):
                    for j, wbuf in enumerate((w_q, w_k, w_v)):
                        n = i * 12 + tt * 3 + j
                        if n >= 2:
                            cs, cv = prj_consumer(n - 2)
                            pe.wait_ge(cs, cv)
                        bank = 4 + n % 2
                        for k in range(8):
                            pe.matmul(pp[:, bank, 0:GF], ret[:, k, tt * 128:(tt + 1) * 128],
                                      wbuf[:, k], start=(k == 0), stop=False)
                        ins = pe.matmul(pp[:, bank, 0:GF], onesrow[0:1, :],
                                        brow[:, j * GF:(j + 1) * GF], start=False, stop=True)
                        ins.then_inc(s["prj"], 1)
                    pe.wait_ge(s["rope"], i * 8 + tt * 2 + 2)
                    for jj, srcb in enumerate((qtm[:], ktm[:])):
                        for h in range(4):
                            ntp = (i * 4 + tt) * 8 + jj * 4 + h + 1
                            if ntp >= 3:
                                pe.wait_ge(s["tpe"], ntp - 2)
                            pe.transpose(ppr[0:64, (ntp - 1) % 2, 0:128],
                                         srcb[:, h * 64:(h + 1) * 64], eyeb[:]
                                         ).then_inc(s["tp"], 1)

    def f_dve(dve):
        for g in range(NG):
            dve.wait_ge(s["bias"], g + 1)
            dve.tensor_copy(brow[:, 0:512], pp[0:1, 0, 0:512])
            dve.tensor_copy(brow[:, 512:768], pp[0:1, 1, 0:256])
            for wbuf in (w_q, w_k, w_v):
                for k in range(8):
                    gb = lngb[:, 0, k:k + 1].broadcast_to([128, GF])
                    dve.tensor_tensor(wbuf[:, k], wbuf[:, k], gb, ALU.mult)
            dve.engine_nop().then_inc(s["gsc"], 1)
            for sc in range(NSC):
                i = g * NSC + sc
                dve.wait_ge(s["ld"], 32 * (i + 1))
                for k in range(8):
                    ins = dve.tensor_tensor(ret[:, k], xin[:, k], ret[:, k], ALU.add)
                ins.then_inc(s["x1"], 1)
                dve.wait_ge(s["stat"], i + 1)
                dve.tensor_scalar_mul(mb_bc[:], pp[:, 2, :], 1.0 / 1024.0)
                dve.tensor_scalar_mul(ab_bc[:], pp[:, 3, :], 1.0 / 1024.0
                                      ).then_inc(s["st_ev"], 1)
                dve.tensor_tensor(wtmp[:], mb_bc[:], mb_bc[:], ALU.mult)
                dve.tensor_tensor(ab_bc[:], ab_bc[:], wtmp[:], ALU.subtract)
                dve.tensor_copy(wtmp[:], ab_bc[:])
                dve.tensor_scalar_add(ab_bc[:], ab_bc[:], EPS)
                dve.engine_nop().then_inc(s["sml"], 1)
                dve.wait_ge(s["sml2"], 2 * i + 1)
                dve.reciprocal(rs_out[:], rs_out[:])
                dve.tensor_tensor(wtmp[:], wtmp[:], rs_out[:], ALU.mult)
                dve.tensor_tensor(wtmp[:], wtmp[:], rs_out[:], ALU.mult)
                dve.tensor_scalar_add(wtmp[:], wtmp[:], EPS)
                dve.engine_nop().then_inc(s["sml"], 1)
                for k in range(8):
                    dve.tensor_tensor(xin[:, k], ret[:, k], mb_bc[:], ALU.subtract)
                    ins = dve.tensor_tensor(xin[:, k], xin[:, k], rs_out[:], ALU.mult)
                ins.then_inc(s["xh"], 1)
                dve.wait_ge(s["sml2"], 2 * i + 2)
                dve.reciprocal(ab_bc[:], ab_bc[:])
                dve.wait_ge(s["x1w"], 16 * (i + 1))
                for k in range(8):
                    ins = dve.tensor_tensor(ret[:, k], xin[:, k], ab_bc[:], ALU.mult)
                ins.then_inc(s["xh2"], 1)
                for tt in range(4):
                    base_prj = i * 12 + tt * 3
                    for j, dstb in ((0, qtm[:]), (1, ktm[:])):
                        bank = 4 + (base_prj + j) % 2
                        dve.wait_ge(s["prj"], base_prj + j + 1)
                        if (i * 4 + tt) >= 1:
                            dve.wait_ge(s["tp"], (i * 4 + tt - 1) * 8 + 4 + j * 4)
                        src = pp[:, bank, 0:GF].rearrange("p (h f) -> p h f", f=64)
                        t1, t2 = src[:, :, 0:32], src[:, :, 32:64]
                        gt = sc * 4 + tt
                        cosB = cosb[:, gt].unsqueeze(1).broadcast_to([128, 4, 32])
                        sinB = sinb[:, gt].unsqueeze(1).broadcast_to([128, 4, 32])
                        dst = dstb.rearrange("p (h f) -> p h f", f=64)
                        r0 = ropet[:, 0].rearrange("p (h f) -> p h f", f=32)
                        r1 = ropet[:, 1].rearrange("p (h f) -> p h f", f=32)
                        dve.tensor_tensor(r0, t1, cosB, ALU.mult)
                        dve.tensor_tensor(r1, t2, sinB, ALU.mult)
                        dve.tensor_tensor(dst[:, :, 0:32], r0, r1, ALU.subtract)
                        dve.tensor_tensor(r0, t1, sinB, ALU.mult)
                        dve.tensor_tensor(r1, t2, cosB, ALU.mult)
                        dve.tensor_tensor(dst[:, :, 32:64], r0, r1, ALU.add
                                          ).then_inc(s["rope"], 1)
                    bank = 4 + (base_prj + 2) % 2
                    dve.wait_ge(s["prj"], base_prj + 3)
                    if (i * 4 + tt) >= 4:
                        dve.wait_ge(s["vw"], 16 * (i * 4 + tt - 3))
                    dve.tensor_copy(vtm[:, tt], pp[:, bank, 0:GF]).then_inc(s["vev"], 1)
                    if i > 0 and tt == 0:
                        dve.wait_ge(s["kqw"], 32 * i)
                    for n in range(8):
                        ntp = (i * 4 + tt) * 8 + n + 1
                        dve.wait_ge(s["tp"], ntp)
                        dstb = qfm if n < 4 else kfm
                        h = n % 4
                        dve.tensor_copy(dstb[:, h, tt * 128:(tt + 1) * 128],
                                        ppr[0:64, (ntp - 1) % 2, 0:128]).then_inc(s["tpe"], 1)

    def f_gps(gps):
        for g in range(NG):
            for sc in range(NSC):
                i = g * NSC + sc
                ts = slice(sc * 512, (sc + 1) * 512)
                gps.wait_ge(s["x1"], i + 1)
                prog.gdma(gps, x1_hbm.rearrange("(k p) t -> p k t", p=128)[:, :, ts],
                          ret[:], "p2_x1w")
                if g == 0:
                    gps.wait_ge(s["xh"], i + 1)
                    prog.gdma(gps, xm_hbm.rearrange("(k p) t -> p k t", p=128)[:, :, ts],
                              xin[:], "p2_xmw")
                for tt in range(4):
                    gps.wait_ge(s["vev"], i * 4 + tt + 1)
                    t0 = sc * 512 + tt * 128
                    prog.gdma(gps, v_hbm[t0:t0 + 128, g * GF:(g + 1) * GF],
                              vtm[:, tt], "p2_vw")

    return dict(f_sync=f_sync, f_scalar=f_scalar, f_pe=f_pe, f_dve=f_dve, f_gps=f_gps)



def emit_p3_attention(prog, io):
    sb = prog.sb
    NG = 4

    kbuf = sb("p3_k", [64, 2, T], F32R)
    vbuf = sb("p3_v", [128, 2, 32, 65], F32R)
    qbuf = sb("p3_q", [64, 2, T], F32R)
    pbuf = sb("p3_p", [128, 2, 512], F32R)
    obuf = sb("p3_o", [64, 2, T], F32R)
    lrow = sb("p3_lr", [1, 512], F32R)
    rbuf = sb("p3_r", [64, 512], F32)
    pads = sb("p3_pad", [64, 512], F32)
    padA = sb("p3_padA", [128, 512], F32)
    w_p = sb("p3_wp", [64, 4, 1024], F32R)
    otile = sb("p3_ot", [64, 2, 4, 512], F32R)
    wostg = sb("p3_wostg", [128, 4, 512], F32)
    diagmb = sb("p3_diag", [128, 4, 512], F32R)

    onesrow, eyeb = io["onesrow"], io["eyeb"]
    diagm = diagmb
    k_hbm, v_hbm, q_hbm, o_hbm = io["k_hbm"], io["v_hbm"], io["q_hbm"], io["o_hbm"]
    xm_hbm, wproj = io["xm_hbm"], io["wproj"]
    pp, ppr = io["pp"], io["ppr"]

    names = ["kv", "vone", "sS", "sP", "sP2", "sAVc", "sAV", "lr", "rb",
             "oev", "ow", "otl", "wpw", "wo", "woe", "wod"]
    s = {n: prog.sem("p3_" + n) for n in names}

    NKC = [(qt + 1) * 4 for qt in range(8)]
    KCT = sum(NKC)

    def zidx(h, qt, kc):
        return h * KCT + sum(NKC[:qt]) + kc

    def is_diag(qt, kc):
        return kc >= NKC[qt] - 4

    diag_count = {}
    c = 0
    for h in range(16):
        for qt in range(8):
            for kc in range(NKC[qt]):
                if is_diag(qt, kc):
                    c += 1
                diag_count[zidx(h, qt, kc)] = c

    def f_sync(sync):
        prog.gdma(sync, diagmb[:].rearrange("p a b -> p (a b)"), io["c_diag"], "p3_dg")
        for h in range(16):
            if h >= 2:
                sync.wait_ge(s["sAV"], 8 * (h - 1))
            prog.gdma(sync, kbuf[:, h % 2], k_hbm[h], "p3_kv")
            prog.gdma(sync, vbuf[:, h % 2, :, 0:64],
                      v_hbm[:, h * 64:(h + 1) * 64].rearrange("(a p) f -> p a f", p=128),
                      "p3_kv")
            prog.gdma(sync, qbuf[:, h % 2], q_hbm[h], "p3_kv")
        for g in range(NG):
            if g > 0:
                sync.wait_ge(s["wo"], 64 * g)
            src = wproj[g * 256:(g + 1) * 256, :].rearrange("(h p) f -> p h f", p=64)
            prog.gdma(sync, w_p[:], src, "p3_wpw")
            for tt in range(8):
                n = g * 8 + tt
                if n >= 2:
                    sync.wait_ge(s["wo"], (n - 1) * 8)
                ts = slice(tt * 512, (tt + 1) * 512)
                sync.wait_ge(s["ow"], 16 * (4 * g + 4))
                prog.gdma(sync, otile[:, n % 2],
                          o_hbm[4 * g:4 * g + 4, :, ts].rearrange("h p t -> p h t"), "p3_otl")

    def f_gps(gps):
        gps.wait_ge(prog.sem("p2_xmw"), prog.counts["p2_xmw"])
        for h in range(16):
            gps.wait_ge(s["oev"], h * 8 + 8)
            prog.gdma(gps, o_hbm[h], obuf[:, h % 2], "p3_ow")
        for g in range(NG):
            for tt in range(8):
                for hf in range(2):
                    gps.wait_ge(s["woe"], g * 64 + tt * 8 + hf * 4 + 4)
                    ts = slice(tt * 512, (tt + 1) * 512)
                    dst = xm_hbm[hf * 512:(hf + 1) * 512, ts
                                 ].rearrange("(k p) t -> p k t", p=128).bitcast(F32)
                    prog.gdma(gps, dst, wostg[:], "p3_wod", accum_op=ALU.add)

    def f_scalar(act):
        for h in range(16):
            for qt in range(8):
                for kc in range(NKC[qt]):
                    z = zidx(h, qt, kc)
                    act.wait_ge(s["sS"], z + 1)
                    if z >= 2:
                        act.wait_ge(s["sAVc"], z - 1)
                    act.activation(pbuf[:, z % 2], pp[:, z % 2, :], AF.Exp,
                                   scale=0.125).then_inc(s["sP"], 1)

    def f_dve(dve):
        dve.wait_ge(prog.sem("p3_dg"), 16)
        oneB = io["onesb"][:, 0:1].unsqueeze(1).broadcast_to([128, 32, 1])
        dve.tensor_copy(vbuf[:, 0, :, 64:65], oneB)
        dve.tensor_copy(vbuf[:, 1, :, 64:65], oneB).then_inc(s["vone"], 1)
        for h in range(16):
            for qt in range(8):
                for kc in range(NKC[qt]):
                    if not is_diag(qt, kc):
                        continue
                    z = zidx(h, qt, kc)
                    r = kc - qt * 4
                    dve.wait_ge(s["sP"], z + 1)
                    dve.tensor_tensor(pbuf[:, z % 2], pbuf[:, z % 2], diagm[:, r],
                                      ALU.mult).then_inc(s["sP2"], 1)
                dve.wait_ge(s["sAV"], h * 8 + qt + 1)
                with prog.nc.allow_low_precision("recip row f32r"):
                    dve.reciprocal(lrow[:], pp[64:65, 2, :])
                dve.engine_nop().then_inc(s["lr"], 1)
                dve.wait_ge(s["rb"], h * 8 + qt + 1)
                if h >= 2 and qt == 0:
                    dve.wait_ge(s["ow"], 16 * (h - 1))
                dve.tensor_copy(rbuf[:], pp[0:64, 3, :])
                dve.tensor_copy(pads[:], pp[0:64, 2, :])
                dve.tensor_tensor(obuf[:, h % 2, qt * 512:(qt + 1) * 512],
                                  pp[0:64, 2, :], rbuf[:], ALU.mult
                                  ).then_inc(s["oev"], 1)
        for g in range(NG):
            for tt in range(8):
                for oc in range(8):
                    nwo = g * 64 + tt * 8 + oc
                    bank = 4 + nwo % 2
                    dve.wait_ge(s["wo"], nwo + 1)
                    nhalf = (g * 8 + tt) * 2 + oc // 4
                    if nhalf >= 1 and oc % 4 == 0:
                        dve.wait_ge(s["wod"], 16 * nhalf)
                    dve.tensor_copy(wostg[:, oc % 4], pp[:, bank, :]).then_inc(s["woe"], 1)

    def f_pe(pe):
        pe.wait_ge(s["vone"], 1)
        for h in range(16):
            pe.wait_ge(s["kv"], 48 * (h + 1))
            for qt in range(8):
                qs = slice(qt * 512, (qt + 1) * 512)
                if h * 8 + qt >= 1:
                    pe.wait_ge(s["oev"], h * 8 + qt)
                for kc in range(NKC[qt]):
                    z = zidx(h, qt, kc)
                    if z >= 2:
                        pe.wait_ge(s["sP"], z - 1)
                    pe.matmul(pp[:, z % 2, :], kbuf[:, h % 2, kc * 128:(kc + 1) * 128],
                              qbuf[:, h % 2, qs], start=True, stop=True).then_inc(s["sS"], 1)
                    if is_diag(qt, kc):
                        pe.wait_ge(s["sP2"], diag_count[z])
                    else:
                        pe.wait_ge(s["sP"], z + 1)
                    pe.matmul(pp[0:65, 2, :], vbuf[:, h % 2, kc], pbuf[:, z % 2],
                              start=(kc == 0), stop=(kc == NKC[qt] - 1)).then_inc(s["sAVc"], 1)
                pe.matmul(pp[0:64, 5, :], eyeb[0:64, 0:64], kbuf[:, h % 2, 0:512],
                          start=True, stop=True).then_inc(s["sAV"], 1)
                pe.wait_ge(s["lr"], h * 8 + qt + 1)
                pe.matmul(pp[0:64, 3, :], onesrow[0:1, 0:64], lrow[:], start=True, stop=True)
                pe.matmul(pp[0:64, 5, :], eyeb[0:64, 0:64], kbuf[:, h % 2, 0:512],
                          start=True, stop=True).then_inc(s["rb"], 1)
        for g in range(NG):
            pe.wait_ge(s["wpw"], 16 * (g + 1))
            for tt in range(8):
                n = g * 8 + tt
                pe.wait_ge(s["otl"], 16 * (n + 1))
                for oc in range(8):
                    nwo = n * 8 + oc
                    bank = 4 + nwo % 2
                    if nwo >= 2:
                        pe.wait_ge(s["woe"], nwo - 1)
                    for hl in range(4):
                        ins = pe.matmul(pp[:, bank, :], w_p[:, hl, oc * 128:(oc + 1) * 128],
                                        otile[:, n % 2, hl], start=(hl == 0), stop=(hl == 3))
                    ins.then_inc(s["wo"], 1)

    return dict(f_sync=f_sync, f_gps=f_gps, f_scalar=f_scalar, f_pe=f_pe, f_dve=f_dve)



def emit_p4_gate(prog, io):
    sb = prog.sb
    xr = sb("p4_xr", [128, 8, 512], F32R)
    xsq = sb("p4_sq", [128, 8, 512], F32R)
    h2 = sb("p4_h2", [128, 8, 512], F32R)
    mb_bc = sb("p4_mb", [128, 512], F32)
    ab_bc = sb("p4_ab", [128, 512], F32)
    rs_out = sb("p4_rs", [128, 512], F32)
    h2tm = sb("p4_h2tm", [128, 2, 8, 128], F32R)
    lgro = sb("p4_lg", [8, 512], F32)
    lngb2 = sb("p4_lngb", [128, 2, 8], F32)
    wg = sb("p4_wg", [128, 8, 8], F32R)

    onesall, eyeb = io["onesall"], io["eyeb"]
    xm_hbm, wgate = io["xm_hbm"], io["wgate"]
    lgT_hbm, h2tm_hbm = io["lgT_hbm"], io["h2tm_hbm"]
    pp, ppr = io["pp"], io["ppr"]

    names = ["ld", "w", "sq", "stat", "st_ev", "sml", "sml2", "h2",
             "lg", "lge", "lgw", "tp", "tpe", "h2w"]
    s = {n: prog.sem("p4_" + n) for n in names}

    def f_sync(sync):
        prog.gdma(sync, lngb2[:].rearrange("p a b -> p (a b)"), io["lngb2_in"], "p4_w")
        prog.gdma(sync, wg[:], wgate.rearrange("(k p) e -> p k e", p=128), "p4_w")
        for sc in range(NSC):
            ts = slice(sc * 512, (sc + 1) * 512)
            if sc >= 1:
                sync.wait_ge(s["lg"], sc)
                sync.wait_ge(s["tpe"], 32 * sc)
            prog.gdma(sync, xr[:],
                      xm_hbm.rearrange("(k p) t -> p k t", p=128)[:, :, ts], "p4_ld")

    def f_scalar(act):
        for sc in range(NSC):
            act.wait_ge(s["ld"], 16 * (sc + 1))
            act.activation(xsq[:], xr[:], AF.Square).then_inc(s["sq"], 1)
            act.wait_ge(s["sml"], sc + 1)
            act.activation(rs_out[:], ab_bc[:], AF.Sqrt).then_inc(s["sml2"], 1)

    def f_pe(pe):
        pe.wait_ge(s["w"], 32)
        for sc in range(NSC):
            if sc > 0:
                pe.wait_ge(s["st_ev"], sc)
            pe.wait_ge(s["ld"], 16 * (sc + 1))
            for k in range(8):
                pe.matmul(pp[:, 2, :], onesall[:], xr[:, k],
                          start=(k == 0), stop=(k == 7))
            pe.wait_ge(s["sq"], sc + 1)
            for k in range(8):
                ins = pe.matmul(pp[:, 3, :], onesall[:], xsq[:, k],
                                start=(k == 0), stop=(k == 7))
            ins.then_inc(s["stat"], 1)
            pe.wait_ge(s["h2"], sc + 1)
            if sc > 0:
                pe.wait_ge(s["lge"], sc)
            for k in range(8):
                ins = pe.matmul(pp[0:8, 0, :], wg[:, k], h2[:, k],
                                start=(k == 0), stop=(k == 7))
            ins.then_inc(s["lg"], 1)
            for tt in range(4):
                for k in range(8):
                    ntp = (sc * 4 + tt) * 8 + k + 1
                    if ntp >= 3:
                        pe.wait_ge(s["tpe"], ntp - 2)
                    pe.transpose(ppr[:, (ntp - 1) % 2, 0:128],
                                 h2[:, k, tt * 128:(tt + 1) * 128], eyeb[:]
                                 ).then_inc(s["tp"], 1)

    def f_dve(dve):
        for sc in range(NSC):
            dve.wait_ge(s["stat"], sc + 1)
            dve.tensor_scalar_mul(mb_bc[:], pp[:, 2, :], 1.0 / 1024.0)
            dve.tensor_scalar_mul(ab_bc[:], pp[:, 3, :], 1.0 / 1024.0
                                  ).then_inc(s["st_ev"], 1)
            dve.tensor_tensor(rs_out[:], mb_bc[:], mb_bc[:], ALU.mult)
            dve.tensor_tensor(ab_bc[:], ab_bc[:], rs_out[:], ALU.subtract)
            dve.tensor_scalar_add(ab_bc[:], ab_bc[:], EPS)
            dve.engine_nop().then_inc(s["sml"], 1)
            dve.wait_ge(s["sml2"], sc + 1)
            dve.reciprocal(rs_out[:], rs_out[:])
            if sc > 0:
                dve.wait_ge(s["tp"], 32 * sc)
                dve.wait_ge(s["lg"], sc)
            for k in range(8):
                g2 = lngb2[:, 0, k:k + 1].broadcast_to([128, 512])
                b2 = lngb2[:, 1, k:k + 1].broadcast_to([128, 512])
                dve.tensor_tensor(h2[:, k], xr[:, k], mb_bc[:], ALU.subtract)
                dve.tensor_tensor(h2[:, k], h2[:, k], rs_out[:], ALU.mult)
                dve.tensor_tensor(h2[:, k], h2[:, k], g2, ALU.mult)
                ins = dve.tensor_tensor(h2[:, k], h2[:, k], b2, ALU.add)
            ins.then_inc(s["h2"], 1)
            dve.wait_ge(s["lg"], sc + 1)
            if sc > 0:
                dve.wait_ge(s["lgw"], 16 * sc)
            dve.tensor_copy(lgro[:], pp[0:8, 0, :]).then_inc(s["lge"], 1)
            for tt in range(4):
                n = sc * 4 + tt
                for k in range(8):
                    ntp = n * 8 + k + 1
                    dve.wait_ge(s["tp"], ntp)
                    if n >= 2 and k == 0:
                        dve.wait_ge(s["h2w"], 16 * (n - 1))
                    dve.tensor_copy(h2tm[:, n % 2, k], ppr[:, (ntp - 1) % 2, 0:128]
                                    ).then_inc(s["tpe"], 1)

    def f_gps(gps):
        for sc in range(NSC):
            ts = slice(sc * 512, (sc + 1) * 512)
            gps.wait_ge(s["lge"], sc + 1)
            prog.gdma(gps, lgT_hbm[:, ts], lgro[:], "p4_lgw")
            for tt in range(4):
                n = sc * 4 + tt
                gps.wait_ge(s["tpe"], 32 * sc + (tt + 1) * 8)
                t0 = sc * 512 + tt * 128
                prog.gdma(gps, h2tm_hbm[t0:t0 + 128, :],
                          h2tm[:, n % 2].rearrange("p k f -> p (k f)"), "p4_h2w")

    return dict(f_sync=f_sync, f_scalar=f_scalar, f_pe=f_pe, f_dve=f_dve, f_gps=f_gps)



ENGSECS = [("sync", "f_sync"), ("gpsimd", "f_gps"), ("scalar", "f_scalar"),
           ("tensor", "f_pe"), ("vector", "f_dve")]


def emit_phase_seq(prog, io, phase_list):
    barrier = {}
    for prefix, emit_fn in phase_list:
        prog.open_phase()
        fns = emit_fn(prog, io)
        snapshot = dict(barrier)
        with prog.nc.Block() as blk:
            for eng_attr, key in ENGSECS:
                f = fns.get(key) or fns.get({"f_scalar": "f_act"}.get(key, key))
                def wrapped(eng, f=f, snap=snapshot):
                    for nm, cnt in snap.items():
                        if cnt > 0:
                            eng.wait_ge(prog.sems[nm], cnt)
                    if f is not None:
                        f(eng)
                getattr(blk, eng_attr)(wrapped)
        prog.close_phase()
        barrier = {nm: c for nm, c in prog.counts.items() if c > 0}
    return barrier


def build_p12_test(consts):
    prog = Prog()
    nc = prog.nc
    io = declare_common(prog)
    for nm in ("xT", "wqT", "wkT", "wvT", "woT"):
        io[nm] = nc.dram_tensor(nm, [C, C] if nm != "xT" else [C, T], F32R,
                                kind="ExternalInput").ap()
    io["gnw_in"] = nc.dram_tensor("gnw_in", [64, 16], F32, kind="ExternalInput").ap()
    io["gnb_in"] = nc.dram_tensor("gnb_in", [64, 16], F32, kind="ExternalInput").ap()
    io["wqkv"] = nc.dram_tensor("wqkv", [C, 3 * C], F32R, kind="ExternalInput").ap()
    io["lngb_in"] = nc.dram_tensor("lngb_in", [128, 16], F32R, kind="ExternalInput").ap()
    io["ret_hbm"] = nc.dram_tensor("ret_hbm", [C, T], F32R, kind="Internal").ap()
    io["x1_hbm"] = nc.dram_tensor("x1_hbm", [C, T], F32R, kind="ExternalOutput").ap()
    io["xm_hbm"] = nc.dram_tensor("xm_hbm", [C, T], F32R, kind="ExternalOutput").ap()
    io["k_hbm"] = nc.dram_tensor("k_hbm", [H, 64, T], F32R, kind="ExternalOutput").ap()
    io["q_hbm"] = nc.dram_tensor("q_hbm", [H, 64, T], F32R, kind="ExternalOutput").ap()
    io["v_hbm"] = nc.dram_tensor("v_hbm", [T, C], F32R, kind="ExternalOutput").ap()
    io["gs_f"] = consts["ret_gs_f"]

    with prog.nc.Block() as blk0:
        emit_const_loads(prog, blk0, io)
    emit_phase_seq(prog, io, [("p1", emit_p1_retention), ("p2", emit_p2_qkv)])
    with prog.nc.Block() as blk2:
        def f_end(gps):
            for nm in ("p2_x1w", "p2_xmw", "p2_kqw", "p2_vw"):
                gps.wait_ge(prog.sems[nm], prog.counts[nm])
        blk2.gpsimd(f_end)
    prog.close()
    return prog



def declare_l2(prog):
    nc = prog.nc
    sb = prog.sb
    io = {}
    cdefs = [
        ("c_eye", [128, 128], F32R, "eye128", "eyeb"),
        ("c_triin", [128, 128], F32R, "tri_incl", "triin"),
        ("c_triex", [128, 128], F32R, "tri_excl", "triex"),
        ("c_ones", [128, 1], F32R, "ones128", "onesb"),
        ("c_onesrow", [1, 128], F32R, "onesrow", "onesrow"),
        ("c_iota8", [128, 8], F32, "iota8", "iota8b"),
        ("c_iota8p", [128, 8], F32, "iota8p", "iota8pb"),
        ("c_tok0", [128, 32], F32, "tok0", "tok0b"),
        ("c_eyef", [128, 128], F32, "eye128", "eyef"),
    ]
    io["_cdefs"] = cdefs
    for nm, shape, dt_, ckey, sbkey in cdefs:
        io[nm] = nc.dram_tensor(nm, shape, dt_, kind="ExternalInput").ap()
        io[sbkey] = sb("cb2_" + sbkey, shape, dt_)
    io["pp"] = prog.ps("pp", [128, 6, 512], F32)
    io["ppr"] = prog.ps("ppr", [128, 2, 512], F32R)
    io["lgT_all"] = nc.dram_tensor("lgT_all", [4 * 8, T], F32R, kind="ExternalInput").ap()
    io["h2tm_all"] = nc.dram_tensor("h2tm_all", [N_TOK, C], F32R, kind="ExternalInput").ap()
    io["w1"] = nc.dram_tensor("w1", [C, 4 * C], F32R, kind="ExternalInput").ap()
    io["w2"] = nc.dram_tensor("w2", [4 * C, C], F32R, kind="ExternalInput").ap()
    io["b1_in"] = nc.dram_tensor("b1_in", [128, 32], F32, kind="ExternalInput").ap()
    io["b2row"] = nc.dram_tensor("b2row", [1, C], F32R, kind="ExternalInput").ap()
    io["emeta"] = nc.dram_tensor("emeta", [128, 1], F32, kind="ExternalInput").ap()
    io["exp_out"] = nc.dram_tensor("exp_out", [CAP, C], F32, kind="ExternalOutput").ap()
    io["trow_hbm"] = nc.dram_tensor("trow_hbm", [4, 128, 64], F32, kind="ExternalOutput").ap()
    io["tgate_hbm"] = nc.dram_tensor("tgate_hbm", [4, 128, 64], F32, kind="ExternalOutput").ap()
    io["gathtab"] = nc.dram_tensor("gathtab", [4097, 1], F32, kind="Internal").ap()
    io["tsd_d"] = nc.dram_tensor("tsd_d", [4, 256], F32R, kind="Internal").ap()
    io["toff_d"] = nc.dram_tensor("toff_d", [128, 8], F32R, kind="Internal").ap()
    io["ein_hbm"] = nc.dram_tensor("ein_hbm", [C, CAP], F32R, kind="Internal").ap()
    return io


def emit_l2_consts(prog, blk, io):
    def f_sync(sync):
        for nm, shape, dt_, ckey, sbkey in io["_cdefs"]:
            prog.gdma(sync, io[sbkey][:], io[nm], "l2_c")
        sync.wait_ge(prog.sem("l2_c"), prog.counts["l2_c"])
    blk.sync(f_sync)


def l2_const_in_map(consts):
    return {
        "c_eye": consts["eye128"], "c_triin": consts["tri_incl"],
        "c_triex": consts["tri_excl"], "c_ones": consts["ones128"],
        "c_onesrow": consts["onesrow"][:, 0:128],
        "c_iota8": consts["iota8"], "c_iota8p": consts["iota8p"],
        "c_tok0": consts["tok0"], "c_eyef": consts["eye128"],
    }


def emit_l2_routing(prog, io):
    sb = prog.sb
    lgt = sb("r_lgt", [8, 4, T], F32R)
    lgm = sb("r_lgm", [128, 4, 32, 8], F32)
    m_all = sb("r_m", [128, 4, 32, 8], F32R)
    m1 = sb("r_m1", [128, 4, 32, 8], F32)
    m2 = sb("r_m2", [128, 4, 32, 8], F32)
    prb = sb("r_p", [128, 4, 32, 8], F32)
    dbb = sb("r_d", [128, 4, 32, 8], F32)
    tmp8 = sb("r_t8", [128, 32, 8], F32)
    mx1 = sb("r_mx1", [128, 4, 32], F32)
    idx1 = sb("r_i1", [128, 4, 32], F32)
    idx2 = sb("r_i2", [128, 4, 32], F32)
    gate1 = sb("r_g1", [128, 4, 32], F32)
    gate2 = sb("r_g2", [128, 4, 32], F32)
    posge = sb("r_pos", [128, 4, 256], F32)
    sumB = sb("r_sum", [128, 4, 32], F32)
    g1B = sb("r_g1b", [128, 4, 32], F32)
    g2B = sb("r_g2b", [128, 4, 32], F32)
    tmpB = sb("r_tmpB", [128, 4, 32, 8], F32)
    kpB = sb("r_kpB", [128, 4, 256], F32)
    pos1B = sb("r_p1", [128, 4, 32], F32)
    pos2B = sb("r_p2", [128, 4, 32], F32)
    kp1B = sb("r_kp1", [128, 4, 32], F32)
    kp2B = sb("r_kp2", [128, 4, 32], F32)
    t32B = sb("r_t32", [128, 4, 32], F32)
    u32B = sb("r_u32", [128, 4, 32], F32)
    w32B = sb("r_w32", [128, 4, 32], F32)
    trow = sb("r_trow", [128, 4, 32, 2], F32)
    tgate = sb("r_tg", [128, 4, 32, 2], F32)
    offs = sb("r_off", [128, 4, 32], F32)
    offi = sb("r_offi", [128, 4, 32], I32)
    tokf = sb("r_tokf", [128, 4, 32], F32)
    tsall = sb("r_ts", [1, 4, 256], F32R)
    tspos = sb("r_tsp", [128, 256], F32R)
    toffx = sb("r_toffx", [128, 8], F32)
    toffrow = sb("r_toffr", [1, 4, 256], F32R)
    zbuf = sb("r_z", [128, 33], F32)
    emetab = sb("r_em", [128, 1], F32)

    eyeb, triin, triex = io["eyeb"], io["triin"], io["triex"]
    onesb, onesrow = io["onesb"], io["onesrow"]
    iota8b, iota8pb, tok0b = io["iota8b"], io["iota8pb"], io["tok0b"]
    pp, ppr = io["pp"], io["ppr"]

    names = ["ld", "tp", "tpe", "top", "ex", "ex2", "msum", "msf", "tse",
             "tsd", "tri", "trif", "toe", "tod", "pos", "posf", "pose",
             "sl", "zf", "sc", "tw", "em"]
    s = {n: prog.sem("l2r_" + n) for n in names}

    def f_sync(sync):
        prog.gdma(sync, emetab[:], io["emeta"], "l2r_em")
        prog.gdma(sync, lgt[:], io["lgT_all"].rearrange("(s e) t -> e s t", e=8), "l2r_ld")

    def f_pe(pe):
        pe.wait_ge(s["ld"], 16)
        for n in range(128):
            sq, tt = divmod(n, 32)
            if n >= 2:
                pe.wait_ge(s["tpe"], n - 1)
            pe.transpose(ppr[:, n % 2, 0:8], lgt[0:8, sq, tt * 128:(tt + 1) * 128],
                         eyeb[0:8, 0:8]).then_inc(s["tp"], 1)
        for sq in range(4):
            pe.wait_ge(s["top"], sq + 1)
            pe.matmul(pp[0:1, 0, sq * 128:sq * 128 + 256] if False else
                      pp[0:1, 0, 0:256] if sq == 0 else
                      pp[0:1, 1, 0:256] if sq == 1 else
                      pp[0:1, 2, 0:256] if sq == 2 else pp[0:1, 3, 0:256],
                      onesb[:, 0:1], m_all[:, sq], start=True, stop=True)
        pe.matmul(pp[:, 4, :], eyeb[:], m_all[:, 0:2].rearrange("p s a e -> p (s a e)"),
                  start=True, stop=True).then_inc(s["msf"], 1)
        pe.wait_ge(prog.sem("l2r_tsd"), 32)
        pe.matmul(pp[:, 0, 0:256], triex[:], tspos[:], start=True, stop=True)
        pe.matmul(pp[:, 4, :], eyeb[:], m_all[:, 0:2].rearrange("p s a e -> p (s a e)"),
                  start=True, stop=True).then_inc(s["trif"], 1)
        pe.wait_ge(prog.sem("l2r_tod"), 80)
        for sq in range(4):
            pe.matmul(pp[:, sq, 0:256], triin[:], m_all[:, sq], start=True, stop=False)
            pe.matmul(pp[:, sq, 0:256], onesrow[0:1, :], toffrow[:, sq],
                      start=False, stop=True)
        pe.matmul(pp[:, 4, :], eyeb[:], m_all[:, 0:2].rearrange("p s a e -> p (s a e)"),
                  start=True, stop=True).then_inc(s["posf"], 1)

    def f_dve(dve):
        for n in range(128):
            sq, tt = divmod(n, 32)
            dve.wait_ge(s["tp"], n + 1)
            dve.tensor_copy(lgm[:, sq, tt], ppr[:, n % 2, 0:8]).then_inc(s["tpe"], 1)
        for step in range(7):
            for sq in range(4):
                L = lgm[:, sq]
                if step == 0:
                    dve.tensor_reduce(mx1[:, sq, :], L, axis=mybir.AxisListType.X,
                                      op=ALU.max)
                elif step == 1:
                    mb = mx1[:, sq].unsqueeze(2).broadcast_to([128, 32, 8])
                    dve.tensor_tensor(m1[:, sq], L, mb, ALU.is_equal)
                elif step == 2:
                    ib = iota8pb[:].unsqueeze(1).broadcast_to([128, 32, 8])
                    dve.scalar_tensor_tensor(tmp8[:], m1[:, sq], -1000.0, ib,
                                             op0=ALU.mult, op1=ALU.add)
                    dve.tensor_reduce(idx1[:, sq, :], tmp8[:], axis=mybir.AxisListType.X,
                                      op=ALU.min)
                elif step == 3:
                    ixb = idx1[:, sq].unsqueeze(2).broadcast_to([128, 32, 8])
                    iob = iota8b[:].unsqueeze(1).broadcast_to([128, 32, 8])
                    dve.tensor_tensor(m1[:, sq], iob, ixb, ALU.is_equal)
                    dve.scalar_tensor_tensor(tmp8[:], m1[:, sq], -1e30, L,
                                             op0=ALU.mult, op1=ALU.add)
                    dve.tensor_reduce(mx1[:, sq, :], tmp8[:], axis=mybir.AxisListType.X,
                                      op=ALU.max)
                    mb = mx1[:, sq].unsqueeze(2).broadcast_to([128, 32, 8])
                    dve.tensor_tensor(m2[:, sq], tmp8[:], mb, ALU.is_equal)
                elif step == 4:
                    ib = iota8pb[:].unsqueeze(1).broadcast_to([128, 32, 8])
                    dve.scalar_tensor_tensor(tmp8[:], m2[:, sq], -1000.0, ib,
                                             op0=ALU.mult, op1=ALU.add)
                    dve.tensor_reduce(idx2[:, sq, :], tmp8[:], axis=mybir.AxisListType.X,
                                      op=ALU.min)
                elif step == 5:
                    ixb = idx2[:, sq].unsqueeze(2).broadcast_to([128, 32, 8])
                    iob = iota8b[:].unsqueeze(1).broadcast_to([128, 32, 8])
                    dve.tensor_tensor(m2[:, sq], iob, ixb, ALU.is_equal)
                    dve.tensor_tensor(m_all[:, sq], m1[:, sq], m2[:, sq], ALU.add)
                elif step == 6:
                    dve.tensor_reduce(mx1[:, sq, :], L, axis=mybir.AxisListType.X,
                                      op=ALU.max)
                    mb = mx1[:, sq].unsqueeze(2).broadcast_to([128, 32, 8])
                    ins = dve.tensor_tensor(dbb[:, sq], L, mb, ALU.subtract)
                    if sq == 3:
                        ins.then_inc(s["ex"], 1)
        dve.engine_nop().then_inc(s["top"], 4)
        padw = tmp8
        def pad():
            dve.tensor_copy(padw[:], prb[:, 0])
        dve.wait_ge(s["ex2"], 1)
        dve.tensor_reduce(sumB[:], prb[:], axis=mybir.AxisListType.X, op=ALU.add)
        pad()
        dve.reciprocal(sumB[:], sumB[:])
        dve.tensor_tensor(tmpB[:], prb[:], m1[:], ALU.mult)
        pad()
        dve.tensor_reduce(g1B[:], tmpB[:], axis=mybir.AxisListType.X, op=ALU.add)
        dve.tensor_tensor(tmpB[:], prb[:], m2[:], ALU.mult)
        dve.tensor_tensor(gate1[:], g1B[:], sumB[:], ALU.mult)
        dve.tensor_reduce(g2B[:], tmpB[:], axis=mybir.AxisListType.X, op=ALU.add)
        pad()
        dve.tensor_tensor(gate2[:], g2B[:], sumB[:], ALU.mult)
        dve.wait_ge(s["msf"], 1)
        for sq in range(4):
            src = (pp[0:1, 0, 0:256] if sq == 0 else pp[0:1, 1, 0:256]
                   if sq == 1 else pp[0:1, 2, 0:256] if sq == 2 else pp[0:1, 3, 0:256])
            dve.tensor_copy(tsall[:, sq], src)
        dve.memset(zbuf[:], 0.0)
        dve.tensor_scalar_mul(tspos[:], io["onesb"][:, 0:1].broadcast_to([128, 256]), 0.0)
        dve.engine_nop().then_inc(s["tse"], 1)
        dve.wait_ge(s["trif"], 1)
        dve.tensor_copy(toffx[:], pp[0:128, 0, 0:8]).then_inc(s["toe"], 1)
        dve.wait_ge(s["posf"], 1)
        for sq in range(4):
            dve.tensor_copy(posge[:, sq], pp[:, sq, 0:256])
        dve.engine_nop().then_inc(s["pose"], 4)
        PB = posge[:].rearrange("p s (a e) -> p s a e", e=8)
        dve.tensor_scalar_min(kpB[:], posge[:], float(CAP))
        pad()
        dve.tensor_tensor(kpB[:], kpB[:], posge[:], ALU.is_equal)
        pad()
        dve.tensor_tensor(kpB[:], kpB[:], m_all[:].bitcast(F32).rearrange(
            "p s a e -> p s (a e)"), ALU.mult)
        pad()
        KB = kpB[:].rearrange("p s (a e) -> p s a e", e=8)
        for j, (mj, posj, kpj, idxj, gatej) in enumerate(
                ((m1, pos1B, kp1B, idx1, gate1), (m2, pos2B, kp2B, idx2, gate2))):
            dve.tensor_tensor(tmpB[:], PB, mj[:], ALU.mult)
            pad()
            dve.tensor_reduce(posj[:], tmpB[:], axis=mybir.AxisListType.X, op=ALU.add)
            dve.tensor_tensor(tmpB[:], KB, mj[:], ALU.mult)
            pad()
            dve.tensor_reduce(kpj[:], tmpB[:], axis=mybir.AxisListType.X, op=ALU.add)
            pad()
            dve.tensor_tensor(t32B[:], kpj[:], posj[:], ALU.mult)
            pad()
            dve.scalar_tensor_tensor(t32B[:], kpj[:], -float(CAP), t32B[:],
                                     op0=ALU.mult, op1=ALU.add)
            pad()
            dve.tensor_scalar_add(t32B[:], t32B[:], float(CAP) - 1.0)
            pad()
            dve.scalar_tensor_tensor(trow[:, :, :, j], idxj[:], float(CAP),
                                     t32B[:], op0=ALU.mult, op1=ALU.add)
            dve.tensor_tensor(tgate[:, :, :, j], gatej[:], kpj[:], ALU.mult)
        emb = emetab[:].unsqueeze(2).broadcast_to([128, 4, 32])
        dve.tensor_tensor(t32B[:], idx1[:], emb, ALU.is_equal)
        pad()
        dve.tensor_tensor(t32B[:], t32B[:], kp1B[:], ALU.mult)
        dve.tensor_tensor(u32B[:], idx2[:], emb, ALU.is_equal)
        pad()
        dve.tensor_tensor(u32B[:], u32B[:], kp2B[:], ALU.mult)
        pad()
        dve.tensor_tensor(offs[:], t32B[:], pos1B[:], ALU.mult)
        dve.tensor_tensor(w32B[:], u32B[:], pos2B[:], ALU.mult)
        pad()
        dve.tensor_tensor(offs[:], offs[:], w32B[:], ALU.add)
        dve.tensor_tensor(t32B[:], t32B[:], u32B[:], ALU.add)
        pad()
        dve.tensor_tensor(offs[:], offs[:], t32B[:], ALU.subtract)
        pad()
        dve.scalar_tensor_tensor(offs[:], t32B[:], -float(CAP), offs[:],
                                 op0=ALU.mult, op1=ALU.add)
        pad()
        dve.tensor_scalar_add(offs[:], offs[:], float(CAP))
        pad()
        dve.tensor_copy(offi[:], offs[:])
        for sq in range(4):
            dve.tensor_scalar_add(tokf[:, sq], tok0b[:], float(sq * T))
        dve.engine_nop().then_inc(s["sl"], 4)

    def f_scalar(act):
        act.wait_ge(s["ex"], 1)
        act.activation(prb[:], dbb[:], AF.Exp).then_inc(s["ex2"], 1)

    def f_gps(gps):
        gps.wait_ge(s["tse"], 1)
        prog.gdma(gps, io["gathtab"][0:4096].rearrange("(p a) x -> p a x", p=128),
                  zbuf[:, 0:32].unsqueeze(2), "l2r_zf")
        prog.gdma(gps, io["gathtab"][4096:4097], zbuf[0:1, 32:33], "l2r_zf")
        prog.gdma(gps, io["tsd_d"], tsall[:, :, :], "l2r_tsd")
        prog.gdma(gps, tspos[:, 0:8],
                  io["tsd_d"].rearrange("s (q e) -> (s q) e", e=8), "l2r_tsd")
        gps.wait_ge(s["toe"], 1)
        prog.gdma(gps, io["toff_d"].bitcast(F32), toffx[:], "l2r_tod")
        for sq in range(4):
            src = io["toff_d"].rearrange("(s q) e -> s q e", s=4)[sq:sq + 1]
            prog.gdma(gps, toffrow[0:1, sq].rearrange("p (q e) -> p q e", e=8),
                      src, "l2r_tod")
        if "dbg_l2" in io:
            d = io["dbg_l2"]
            gps.wait_ge(s["sl"], 4)
            prog.gdma(gps, d["lgm"], lgm[:].rearrange("p a b c -> p (a b c)"), "l2r_dbg")
            prog.gdma(gps, d["mall"], m_all[:].rearrange("p a b c -> p (a b c)").bitcast(F32), "l2r_dbg")
            prog.gdma(gps, d["idx"], idx1[:].rearrange("p a b -> p (a b)"), "l2r_dbg")
            prog.gdma(gps, d["idx2"], idx2[:].rearrange("p a b -> p (a b)"), "l2r_dbg")
            prog.gdma(gps, d["posge"], posge[:].rearrange("p a b -> p (a b)"), "l2r_dbg")
            prog.gdma(gps, d["toffx"], toffx[:], "l2r_dbg")
            prog.gdma(gps, d["tsp"], tspos[:].bitcast(F32), "l2r_dbg")
            prog.gdma(gps, d["g1"], gate1[:].rearrange("p a b -> p (a b)"), "l2r_dbg")
            prog.gdma(gps, d["offs"], offs[:].rearrange("p a b -> p (a b)"), "l2r_dbg")
        gps.wait_ge(prog.sem("l2r_zf"), 32)
        for sq in range(4):
            gps.wait_ge(s["sl"], 4)
            for tt in range(32):
                ins = gps.indirect_dma_start(
                    out=io["gathtab"],
                    out_offset=IndirectOffsetOnAxis(offi[:, sq, tt:tt + 1], 0),
                    in_=tokf[:, sq, tt:tt + 1], in_offset=None)
                prog.inc(ins, "l2r_sc", dma=True)
            prog.gdma(gps, io["trow_hbm"][sq], trow[:, sq].rearrange("p a j -> p (a j)"),
                      "l2r_tw")
            prog.gdma(gps, io["tgate_hbm"][sq], tgate[:, sq].rearrange("p a j -> p (a j)"),
                      "l2r_tw")

    return dict(f_sync=f_sync, f_pe=f_pe, f_dve=f_dve, f_scalar=f_scalar, f_gps=f_gps)

def emit_l2_f0(prog, io):
    sb = prog.sb
    gtb2 = sb("f0_gtb2", [32, 128], F32)
    gtb = sb("f0_gtb", [128, 32], F32)
    gti = sb("f0_gti", [128, 32], I32)
    padf0 = sb("f0_pad", [128, 512], F32)
    etm = sb("f0_etm", [128, 2, 1024], F32R)
    efm = sb("f0_efm", [128, 8, 512], F32R)
    eyeb, eyef = io["eyeb"], io["eyef"]
    pp, ppr = io["pp"], io["ppr"]
    names = ["gt", "gt2", "gi", "g", "tp", "tpe", "ew"]
    s = {n: prog.sem("l2f0_" + n) for n in names}

    def f_sync(sync):
        sync.wait_ge(prog.sem("l2r_sc"), prog.counts["l2r_sc"])
        sync.wait_ge(prog.sem("l2r_zf"), prog.counts["l2r_zf"])
        prog.gdma(sync, gtb2[:].unsqueeze(2),
                  io["gathtab"][0:4096].rearrange("(a p) x -> a p x", p=128), "l2f0_gt")

    def f_gps(gps):
        gps.wait_ge(s["gi"], 1)
        for n in range(32):
            if n >= 2:
                gps.wait_ge(s["tp"], 8 * (n - 1))
            ins = gps.indirect_dma_start(
                out=etm[:, n % 2], out_offset=None,
                in_=io["h2tm_all"], in_offset=IndirectOffsetOnAxis(gti[:, n:n + 1], 0))
            prog.inc(ins, "l2f0_g", dma=True)

    def f_pe(pe):
        pe.wait_ge(prog.sem("l2f0_gt"), 16)
        pe.transpose(pp[:, 3, 0:32], gtb2[:], eyef[0:32, 0:32]).then_inc(s["gt2"], 1)
        for n in range(32):
            pe.wait_ge(s["g"], 16 * (n + 1))
            for k in range(8):
                ntp = n * 8 + k + 1
                if ntp >= 3:
                    pe.wait_ge(s["tpe"], ntp - 2)
                pe.transpose(ppr[:, (ntp - 1) % 2, 0:128],
                             etm[:, n % 2, k * 128:(k + 1) * 128], eyeb[:]
                             ).then_inc(s["tp"], 1)

    def f_dve(dve):
        dve.wait_ge(s["gt2"], 1)
        dve.tensor_copy(gtb[:], pp[:, 3, 0:32])
        dve.tensor_copy(padf0[:], pp[:, 3, 0:512])
        dve.tensor_copy(gti[:], gtb[:]).then_inc(s["gi"], 1)
        for n in range(32):
            TT, q = divmod(n, 4)
            if q == 0 and TT >= 1:
                dve.wait_ge(s["ew"], 16 * TT)
            for k in range(8):
                ntp = n * 8 + k + 1
                dve.wait_ge(s["tp"], ntp)
                dve.tensor_copy(efm[:, k, q * 128:(q + 1) * 128],
                                ppr[:, (ntp - 1) % 2, 0:128]).then_inc(s["tpe"], 1)

    def f_scalar(act):
        for TT in range(8):
            act.wait_ge(s["tpe"], (TT + 1) * 32)
            dst = io["ein_hbm"].rearrange("(k p) c -> p k c", p=128)[:, :, TT * 512:(TT + 1) * 512]
            prog.gdma(act, dst, efm[:], "l2f0_ew")

    return dict(f_sync=f_sync, f_gps=f_gps, f_pe=f_pe, f_dve=f_dve, f_scalar=f_scalar)


def emit_l2_f1(prog, io):
    sb = prog.sb
    w1q = sb("f1_w1", [128, 8, 1024], F32R)
    w2q = sb("f1_w2", [128, 8, 1024], F32R)
    einb = sb("f1_ein", [128, 2, 8, 512], F32R)
    hmq = sb("f1_hm", [128, 8, 512], F32R)
    tbuf = sb("f1_tb", [128, 2, 512], F32)
    ostg = sb("f1_os", [128, 4, 512], F32)
    b1b = sb("f1_b1", [128, 32], F32)
    b2bc = sb("f1_b2", [128, 2, 512], F32)
    b2r = sb("f1_b2r", [1, 1024], F32R)
    eyeb, onesrow = io["eyeb"], io["onesrow"]
    pp, ppr = io["pp"], io["ppr"]
    names = ["w", "eld", "h1", "b1a", "gel", "w2", "oev", "ow", "b2e", "b2f"]
    s = {n: prog.sem("l2f1_" + n) for n in names}

    def f_sync(sync):
        prog.gdma(sync, b1b[:], io["b1_in"], "l2f1_w")
        prog.gdma(sync, b2r[:], io["b2row"], "l2f1_w")
        for p in range(4):
            if p > 0:
                sync.wait_ge(s["w2"], p * 64)
            prog.gdma(sync, w1q[:],
                      io["w1"][:, p * 1024:(p + 1) * 1024].rearrange("(k p) f -> p k f", p=128),
                      "l2f1_w")
            prog.gdma(sync, w2q[:],
                      io["w2"][p * 1024:(p + 1) * 1024, :].rearrange("(k p) f -> p k f", p=128),
                      "l2f1_w")
            for TT in range(8):
                n = p * 8 + TT
                if n >= 2:
                    sync.wait_ge(s["h1"], (n - 1) * 8)
                src = io["ein_hbm"].rearrange("(k p) c -> p k c", p=128)[:, :, TT * 512:(TT + 1) * 512]
                prog.gdma(sync, einb[:, n % 2], src, "l2f1_eld")

    def f_pe(pe):
        pe.wait_ge(s["w"], 32)
        pe.matmul(pp[:, 5, :], onesrow[0:1, :], b2r[:, 0:512], start=True, stop=True)
        pe.matmul(pp[:, 4, :], onesrow[0:1, :], b2r[:, 512:1024], start=True, stop=True)
        pe.matmul(pp[0:64, 3, :], eyeb[0:64, 0:64], w1q[0:64, 0, 0:512],
                  start=True, stop=True).then_inc(s["b2f"], 1)
        for p in range(4):
            pe.wait_ge(s["w"], 32 + 32 * (p + 1))
            if p == 0:
                pe.wait_ge(s["b2e"], 1)
            for TT in range(8):
                n = p * 8 + TT
                pe.wait_ge(s["eld"], 16 * (n + 1))
                for fc in range(8):
                    nh = n * 8 + fc
                    if nh >= 2:
                        pe.wait_ge(s["b1a"], nh - 1)
                    for kc in range(8):
                        ins = pe.matmul(pp[:, 4 + fc % 2, :],
                                        w1q[:, kc, fc * 128:(fc + 1) * 128],
                                        einb[:, n % 2, kc], start=(kc == 0), stop=(kc == 7))
                    ins.then_inc(s["h1"], 1)
                for tq in range(4):
                    for dh in range(2):
                        no = (n * 4 + tq) * 2 + dh
                        bank = no % 4
                        if no >= 4:
                            pe.wait_ge(s["oev"], no - 3)
                        for kc in range(8):
                            pe.wait_ge(s["gel"], n * 8 + kc + 1)
                            ins = pe.matmul(pp[:, bank, :],
                                            hmq[:, kc, tq * 128:(tq + 1) * 128],
                                            w2q[:, kc, dh * 512:(dh + 1) * 512],
                                            start=(kc == 0), stop=(kc == 7))
                        ins.then_inc(s["w2"], 1)

    def f_dve(dve):
        dve.wait_ge(s["b2f"], 1)
        dve.tensor_copy(b2bc[:, 0], pp[:, 5, :])
        dve.tensor_copy(b2bc[:, 1], pp[:, 4, :]).then_inc(s["b2e"], 1)
        for p in range(4):
            for TT in range(8):
                n = p * 8 + TT
                for fc in range(8):
                    nh = n * 8 + fc
                    dve.wait_ge(s["h1"], nh + 1)
                    if nh >= 2:
                        dve.wait_ge(s["gel"], nh - 1)
                    bb = b1b[:, p * 8 + fc:p * 8 + fc + 1].broadcast_to([128, 512])
                    dve.tensor_tensor(tbuf[:, nh % 2], pp[:, 4 + fc % 2, :], bb, ALU.add
                                      ).then_inc(s["b1a"], 1)
                for tq in range(4):
                    for dh in range(2):
                        no = (n * 4 + tq) * 2 + dh
                        bank = no % 4
                        dve.wait_ge(s["w2"], no + 1)
                        if no >= 4:
                            dve.wait_ge(s["ow"], 16 * (no - 3))
                        if p == 0:
                            dve.tensor_tensor(ostg[:, no % 4], pp[:, bank, :],
                                              b2bc[:, dh], ALU.add).then_inc(s["oev"], 1)
                        else:
                            dve.tensor_copy(ostg[:, no % 4], pp[:, bank, :]
                                            ).then_inc(s["oev"], 1)

    def f_scalar(act):
        for p in range(4):
            for TT in range(8):
                n = p * 8 + TT
                for fc in range(8):
                    nh = n * 8 + fc
                    act.wait_ge(s["b1a"], nh + 1)
                    if n >= 1 and fc == 0:
                        act.wait_ge(s["w2"], n * 8)
                    act.activation(hmq[:, fc], tbuf[:, nh % 2], AF.Gelu
                                   ).then_inc(s["gel"], 1)

    def f_gps(gps):
        for p in range(4):
            for TT in range(8):
                n = p * 8 + TT
                for tq in range(4):
                    for dh in range(2):
                        no = (n * 4 + tq) * 2 + dh
                        gps.wait_ge(s["oev"], no + 1)
                        t0 = TT * 512 + tq * 128
                        dst = io["exp_out"][t0:t0 + 128, dh * 512:(dh + 1) * 512]
                        kw = {} if p == 0 else {"accum_op": ALU.add}
                        prog.gdma(gps, dst, ostg[:, no % 4], "l2f1_ow", **kw)

    return dict(f_sync=f_sync, f_pe=f_pe, f_dve=f_dve, f_scalar=f_scalar, f_gps=f_gps)


def build_l2_test(consts):
    prog = Prog()
    io = declare_l2(prog)
    with prog.nc.Block() as blk0:
        emit_l2_consts(prog, blk0, io)
    emit_phase_seq(prog, io, [("l2r", emit_l2_routing), ("l2f0", emit_l2_f0),
                              ("l2f1", emit_l2_f1)])
    with prog.nc.Block() as blk2:
        def f_end(gps):
            for nm in ("l2f1_ow", "l2r_tw"):
                gps.wait_ge(prog.sems[nm], prog.counts[nm])
        blk2.gpsimd(f_end)
    prog.close()
    return prog



def build_l3(consts):
    prog = Prog()
    nc = prog.nc
    sb = prog.sb
    eyef_d = nc.dram_tensor("c_eyef", [128, 128], F32, kind="ExternalInput").ap()
    x1_d = nc.dram_tensor("x1", [C, T], F32R, kind="ExternalInput").ap()
    xm_d = nc.dram_tensor("xm", [C, T], F32R, kind="ExternalInput").ap()
    trow_d = nc.dram_tensor("trow", [128, 64], F32, kind="ExternalInput").ap()
    tgate_d = nc.dram_tensor("tgate", [128, 64], F32, kind="ExternalInput").ap()
    eo_d = nc.dram_tensor("eo_all", [8 * CAP, C], F32, kind="ExternalInput").ap()
    out_d = nc.dram_tensor("out_hbm", [T, C], F32, kind="ExternalOutput").ap()

    eyef = sb("l3_eyef", [128, 128], F32)
    trowb = sb("l3_trow", [128, 64], F32)
    tgateb = sb("l3_tg", [128, 64], F32)
    trowi = sb("l3_ti", [128, 64], I32)
    gbuf = sb("l3_g", [128, 2, 2, 1024], F32)
    mbuf = sb("l3_m", [128, 2, 1024], F32)
    x1t = sb("l3_x1", [128, 2, 8, 128], F32R)
    xmt = sb("l3_xm", [128, 2, 8, 128], F32R)
    s12 = sb("l3_s12", [128, 2, 8, 128], F32)
    outt = sb("l3_o", [128, 2, 1024], F32)
    pp = prog.ps("pp", [128, 6, 512], F32)

    names = ["c", "t", "ti", "g", "mo", "ld", "sm", "tp", "ad", "ow"]
    s = {n: prog.sem("l3_" + n) for n in names}

    with prog.nc.Block() as blk:
        def f_sync(sync):
            prog.gdma(sync, eyef[:], eyef_d, "l3_c")
            prog.gdma(sync, trowb[:], trow_d, "l3_c")
            prog.gdma(sync, tgateb[:], tgate_d, "l3_c")
            for tt in range(32):
                ts = slice(tt * 128, (tt + 1) * 128)
                if tt >= 2:
                    sync.wait_ge(s["sm"], tt - 1)
                prog.gdma(sync, x1t[:, tt % 2],
                          x1_d.rearrange("(k p) t -> p k t", p=128)[:, :, ts], "l3_ld")
                prog.gdma(sync, xmt[:, tt % 2],
                          xm_d.rearrange("(k p) t -> p k t", p=128)[:, :, ts], "l3_ld")

        def f_gps(gps):
            gps.wait_ge(s["ti"], 1)
            for tt in range(32):
                for j in range(2):
                    if tt >= 2 and j == 0:
                        gps.wait_ge(s["mo"], tt - 1)
                    ins = gps.indirect_dma_start(
                        out=gbuf[:, tt % 2, j], out_offset=None,
                        in_=eo_d,
                        in_offset=IndirectOffsetOnAxis(trowi[:, tt * 2 + j:tt * 2 + j + 1], 0))
                    prog.inc(ins, "l3_g", dma=True)
                gps.wait_ge(s["ad"], tt + 1)
                prog.gdma(gps, out_d[tt * 128:(tt + 1) * 128, :], outt[:, tt % 2], "l3_ow")

        def f_pe(pe):
            for tt in range(32):
                pe.wait_ge(s["sm"], tt + 1)
                if tt >= 2:
                    pe.wait_ge(s["ad"], tt - 1)
                for k in range(8):
                    b = (tt % 2) * 2 + k // 4
                    pe.transpose(pp[:, b, (k % 4) * 128:(k % 4 + 1) * 128],
                                 s12[:, tt % 2, k], eyef[:]).then_inc(s["tp"], 1)

        def f_dve(dve):
            dve.wait_ge(prog.sem("l3_c"), 48)
            dve.tensor_copy(trowi[:], trowb[:]).then_inc(s["ti"], 1)
            for tt in range(32):
                dve.wait_ge(s["ld"], 32 * (tt + 1))
                dve.tensor_tensor(s12[:, tt % 2].bitcast(F32), x1t[:, tt % 2].bitcast(F32),
                                  xmt[:, tt % 2].bitcast(F32), ALU.add).then_inc(s["sm"], 1)
                dve.wait_ge(s["g"], 32 * (tt + 1))
                g0 = tgateb[:, tt * 2:tt * 2 + 1].broadcast_to([128, 1024])
                g1 = tgateb[:, tt * 2 + 1:tt * 2 + 2].broadcast_to([128, 1024])
                dve.tensor_tensor(mbuf[:, tt % 2], gbuf[:, tt % 2, 0], g0, ALU.mult)
                dve.tensor_tensor(gbuf[:, tt % 2, 1], gbuf[:, tt % 2, 1], g1, ALU.mult)
                dve.tensor_tensor(mbuf[:, tt % 2], mbuf[:, tt % 2], gbuf[:, tt % 2, 1],
                                  ALU.add).then_inc(s["mo"], 1)
                dve.wait_ge(s["tp"], 8 * (tt + 1))
                if tt >= 2:
                    dve.wait_ge(s["ow"], 16 * (tt - 1))
                pb = (tt % 2) * 2
                dve.tensor_tensor(outt[:, tt % 2],
                                  pp[:, pb:pb + 2, :].rearrange("p a b -> p (a b)"),
                                  mbuf[:, tt % 2], ALU.add).then_inc(s["ad"], 1)

        blk.sync(f_sync)
        blk.gpsimd(f_gps)
        blk.tensor(f_pe)
        blk.vector(f_dve)
    with prog.nc.Block() as blk2:
        def f_end(gps):
            gps.wait_ge(prog.sems["l3_ow"], prog.counts["l3_ow"])
        blk2.gpsimd(f_end)
    prog.close()
    return prog


def build_l1(consts):
    prog = Prog()
    nc = prog.nc
    io = declare_common(prog)
    for nm in ("wqT", "wkT", "wvT", "woT"):
        io[nm] = nc.dram_tensor(nm, [C, C], F32R, kind="ExternalInput").ap()
    io["xT"] = nc.dram_tensor("xT", [C, T], F32R, kind="ExternalInput").ap()
    io["gnw_in"] = nc.dram_tensor("gnw_in", [64, 16], F32, kind="ExternalInput").ap()
    io["gnb_in"] = nc.dram_tensor("gnb_in", [64, 16], F32, kind="ExternalInput").ap()
    io["wqkv"] = nc.dram_tensor("wqkv", [C, 3 * C], F32R, kind="ExternalInput").ap()
    io["lngb_in"] = nc.dram_tensor("lngb_in", [128, 16], F32R, kind="ExternalInput").ap()
    io["wproj"] = nc.dram_tensor("wproj", [C, C], F32R, kind="ExternalInput").ap()
    io["wgate"] = nc.dram_tensor("wgate", [C, 8], F32R, kind="ExternalInput").ap()
    io["lngb2_in"] = nc.dram_tensor("lngb2_in", [128, 16], F32, kind="ExternalInput").ap()
    io["ret_hbm"] = nc.dram_tensor("ret_hbm", [C, T], F32R, kind="Internal").ap()
    io["x1_hbm"] = nc.dram_tensor("x1_hbm", [C, T], F32R, kind="ExternalOutput").ap()
    io["xm_hbm"] = nc.dram_tensor("xm_hbm", [C, T], F32R, kind="ExternalOutput").ap()
    io["k_hbm"] = nc.dram_tensor("k_hbm", [H, 64, T], F32R, kind="Internal").ap()
    io["q_hbm"] = nc.dram_tensor("q_hbm", [H, 64, T], F32R, kind="Internal").ap()
    io["v_hbm"] = nc.dram_tensor("v_hbm", [T, C], F32R, kind="Internal").ap()
    io["o_hbm"] = nc.dram_tensor("o_hbm", [H, 64, T], F32R, kind="Internal").ap()
    io["lgT_hbm"] = nc.dram_tensor("lgT_hbm", [8, T], F32, kind="ExternalOutput").ap()
    io["h2tm_hbm"] = nc.dram_tensor("h2tm_hbm", [T, C], F32R, kind="ExternalOutput").ap()
    io["gs_f"] = consts["ret_gs_f"]

    with prog.nc.Block() as blk0:
        emit_const_loads(prog, blk0, io)
    emit_phase_seq(prog, io, [("p1", emit_p1_retention), ("p2", emit_p2_qkv),
                              ("p3", emit_p3_attention), ("p4", emit_p4_gate)])
    with prog.nc.Block() as blk2:
        def f_end(gps):
            for nm in ("p2_x1w", "p3_wod", "p4_lgw", "p4_h2w"):
                gps.wait_ge(prog.sems[nm], prog.counts[nm])
        blk2.gpsimd(f_end)
    prog.close()
    return prog


def _l1_in_map(consts, inputs, s):
    qc = [h * 192 + d for h in range(16) for d in range(64)]
    kc = [h * 192 + 64 + d for h in range(16) for d in range(64)]
    vc = [h * 192 + 128 + d for h in range(16) for d in range(64)]
    Wp = np.ascontiguousarray(inputs["L0_ffn_Wqkv"][:, qc + kc + vc])
    g1 = inputs["L0_ffn_ln1_g"].reshape(8, 128).T
    b1 = inputs["L0_ffn_ln1_b"].reshape(8, 128).T
    g2 = inputs["L0_ffn_ln2_g"].reshape(8, 128).T
    b2 = inputs["L0_ffn_ln2_b"].reshape(8, 128).T
    im = const_in_map(consts)
    im.update({
        "xT": np.ascontiguousarray(inputs["L0_attn_x"][s].T),
        "wqT": np.ascontiguousarray(inputs["L0_attn_wq"].T),
        "wkT": np.ascontiguousarray(inputs["L0_attn_wk"].T),
        "wvT": np.ascontiguousarray(inputs["L0_attn_wv"].T),
        "woT": np.ascontiguousarray(inputs["L0_attn_wo"].T),
        "gnw_in": np.ascontiguousarray(inputs["L0_attn_gn_w"].reshape(16, 64).T),
        "gnb_in": np.ascontiguousarray(inputs["L0_attn_gn_b"].reshape(16, 64).T),
        "wqkv": Wp,
        "lngb_in": np.ascontiguousarray(np.concatenate([g1, b1], axis=1)),
        "wproj": np.ascontiguousarray(inputs["L0_ffn_Wproj"]),
        "wgate": np.ascontiguousarray(inputs["L0_ffn_Wgate"]),
        "lngb2_in": np.ascontiguousarray(np.concatenate([g2, b2], axis=1)),
    })
    return im


def kernel(**inputs):
    inputs = {k: np.asarray(v) for k, v in inputs.items()}
    consts = host_constants()

    prog1 = build_l1(consts)
    in_maps1 = [_l1_in_map(consts, inputs, c >> 1) for c in range(NCORES)]
    res1 = run_bass_kernel_spmd(prog1.nc, in_maps1, core_ids=list(range(NCORES)))
    r1 = [res1.results[2 * s] for s in range(4)]

    prog2 = build_l2_test(consts)
    lgT_all = np.ascontiguousarray(np.stack([r1[s]["lgT_hbm"] for s in range(4)]
                                            ).reshape(32, T).astype(np.float32))
    h2tm_all = np.ascontiguousarray(np.concatenate(
        [r1[s]["h2tm_hbm"] for s in range(4)], axis=0).astype(np.float32))
    cim2 = l2_const_in_map(consts)
    in_maps2 = []
    for e in range(NCORES):
        im = dict(cim2)
        im.update({
            "lgT_all": lgT_all, "h2tm_all": h2tm_all,
            "w1": np.ascontiguousarray(inputs["L0_ffn_W1"][e]),
            "w2": np.ascontiguousarray(inputs["L0_ffn_W2"][e]),
            "b1_in": np.ascontiguousarray(inputs["L0_ffn_b1"][e].reshape(32, 128).T),
            "b2row": np.ascontiguousarray(inputs["L0_ffn_b2"][e].reshape(1, C)),
            "emeta": np.full((128, 1), float(e), dtype=np.float32),
        })
        in_maps2.append(im)
    res2 = run_bass_kernel_spmd(prog2.nc, in_maps2, core_ids=list(range(NCORES)))
    eo_all = np.ascontiguousarray(np.concatenate(
        [res2.results[e]["exp_out"] for e in range(NCORES)], axis=0).astype(np.float32))
    trow = res2.results[0]["trow_hbm"]
    tgate = res2.results[0]["tgate_hbm"]

    prog3 = build_l3(consts)
    in_maps3 = []
    for c in range(NCORES):
        s = c >> 1
        in_maps3.append({
            "c_eyef": consts["eye128"],
            "x1": np.ascontiguousarray(r1[s]["x1_hbm"].astype(np.float32)),
            "xm": np.ascontiguousarray(r1[s]["xm_hbm"].astype(np.float32)),
            "trow": np.ascontiguousarray(trow[s].astype(np.float32)),
            "tgate": np.ascontiguousarray(tgate[s].astype(np.float32)),
            "eo_all": eo_all,
        })
    res3 = run_bass_kernel_spmd(prog3.nc, in_maps3, core_ids=list(range(NCORES)))
    out = np.stack([res3.results[2 * s]["out_hbm"] for s in range(4)], axis=0)
    return np.ascontiguousarray(out.astype(np.float32))


# version 02
# speedup vs baseline: 1.0058x; 1.0058x over previous
import math
import numpy as np
from contextlib import ExitStack

import concourse.bass as bass
import concourse.mybir as mybir
from concourse.bass_utils import run_bass_kernel_spmd
from concourse.bass_core_rust import IndirectOffsetOnAxis

F32 = mybir.dt.float32
F32R = mybir.dt.float32r
I32 = mybir.dt.int32
AF = mybir.ActivationFunctionType
ALU = mybir.AluOpType

B, T, C = 4, 4096, 1024
H = 16
HD = 64
NCORES = 8
E = 8
TOPK = 2
CAP = 4096
N_TOK = B * T
EPS = 1e-5
SC = 512
RC = 128
NSC = T // SC
NRC = SC // RC


def host_constants():
    c = {}
    c["eye128"] = np.eye(128, dtype=np.float32)
    jj, tt = np.meshgrid(np.arange(128), np.arange(128), indexing="ij")
    c["tri_incl"] = (jj <= tt).astype(np.float32)
    c["tri_excl"] = (jj < tt).astype(np.float32)
    inv_freq = 1.0 / (10000.0 ** (np.arange(0, HD, 2, dtype=np.float64) / HD))
    ang = np.arange(T, dtype=np.float64)[None, :] * inv_freq[:, None]
    c["rope_cos"] = np.cos(ang).astype(np.float32)
    c["rope_sin"] = np.sin(ang).astype(np.float32)
    lo = math.log(1.0 - 0.999)
    hi = math.log(1.0 - 0.9)
    gam = 1.0 - np.exp(np.linspace(lo, hi, H, dtype=np.float64))
    c["gammas"] = gam.astype(np.float32)
    j = np.arange(RC)[:, None]
    i = np.arange(RC)[None, :]
    dts = []
    for h in range(H):
        dt = np.where(i >= j, gam[h] ** np.maximum(i - j, 0), 0.0)
        dts.append(dt)
    c["ret_dt"] = np.concatenate(dts, axis=1).astype(np.float32)
    a = np.arange(RC)[None, :]
    gcross = np.stack([np.tile(gam[h] ** (a + 1), (64, 1)) for h in range(H)], axis=1)
    c["ret_gcross"] = gcross.astype(np.float32)
    a = np.arange(RC)[:, None]
    kd = np.stack([np.tile(gam[h] ** (RC - 1 - a), (1, 64)) for h in range(H)], axis=1)
    c["ret_kdec3"] = kd.astype(np.float32)
    c["ret_gs_f"] = [float(gam[h] ** RC) for h in range(H)]

    ind = np.zeros((128, 2), dtype=np.float32)
    ind[0:64, 0] = 1.0
    ind[64:128, 1] = 1.0
    c["headsel"] = ind
    c["headselT"] = np.ascontiguousarray(ind.T)
    c["onesrow"] = np.ones((1, 128), dtype=np.float32)
    kk, qq = np.meshgrid(np.arange(128), np.arange(128), indexing="ij")
    c["diagmask"] = (kk <= qq).astype(np.float32)
    c["iota8"] = np.tile(np.arange(8, dtype=np.float32)[None, :], (128, 1))
    c["iota128"] = np.arange(128, dtype=np.float32).reshape(128, 1)
    c["ones128"] = np.ones((128, 1), dtype=np.float32)
    c["onesall"] = np.ones((128, 128), dtype=np.float32)
    c["iota8p"] = c["iota8"] + 1000.0
    pp_, aa_ = np.meshgrid(np.arange(128), np.arange(32), indexing="ij")
    c["tok0"] = (pp_ + 128.0 * aa_).astype(np.float32)
    pp_, ff_ = np.meshgrid(np.arange(128), np.arange(512), indexing="ij")
    dm = np.stack([(pp_ <= ff_ - 128 * r).astype(np.float32) for r in range(4)], axis=1)
    c["diagm"] = dm.reshape(128, 4 * 512)
    return c


class Prog:

    def __init__(self, name="mk"):
        s = bass.Bass("TRN2", target_bir_lowering=False, enable_asserts=False,
                      detect_race_conditions=False)
        self.nc = s
        self.stack = ExitStack()
        self.sems = {}
        self.counts = {}

    def sem(self, name):
        if name not in self.sems:
            self.sems[name] = self.stack.enter_context(self.nc.semaphore(name))
            self.counts[name] = 0
        return self.sems[name]

    def inc(self, instr, name, dma=False):
        n = 16 if dma else 1
        instr.then_inc(self.sem(name), n)
        self.counts[name] += n
        return self.counts[name]

    def sb(self, name, shape, dtype):
        st = self.phase_stack if getattr(self, "phase_stack", None) is not None else self.stack
        return st.enter_context(self.nc.sbuf_tensor(name, shape, dtype))

    def open_phase(self):
        self.phase_stack = ExitStack()

    def close_phase(self):
        self.phase_stack.close()
        self.phase_stack = None

    def ps(self, name, shape, dtype=F32):
        return self.stack.enter_context(self.nc.psum_tensor(name, shape, dtype))

    def gdma(self, eng, out, in_, sname, **kw):
        sem = self.sem(sname)
        if self.counts[sname] > 0:
            eng.wait_ge(sem, self.counts[sname])
        ins = eng.dma_start(out=out, in_=in_, **kw)
        ins.then_inc(sem, 16)
        self.counts[sname] += 16
        return ins

    def close(self):
        self.stack.close()


def ceil_div(a, b):
    return (a + b - 1) // b






def emit_p1_retention(prog, io):
    sb = prog.sb
    NG, HPG = 4, 4
    GF = 256

    xin = sb("p1_xin", [128, 1, 8, 512], F32R)
    xsq = sb("p1_xsq", [128, 8, 512], F32R)
    xhat = xsq
    w_q = sb("p1_wq", [128, 8, GF], F32R)
    w_k = sb("p1_wk", [128, 8, GF], F32R)
    w_v = sb("p1_wv", [128, 8, GF], F32R)
    w_o = sb("p1_wo", [64, 4, 1024], F32R)
    qtm = sb("p1_qtm", [128, GF], F32R)
    ktm = sb("p1_ktm", [128, 4, GF], F32R)
    vtm = sb("p1_vtm", [128, 4, GF], F32R)
    qfm = sb("p1_qfm", [64, 4, 512], F32R)
    kfm = sb("p1_kfm", [64, 4, 512], F32R)
    ybuf = sb("p1_ybuf", [64, 4, T], F32R)
    tstate = sb("p1_tst", [64, 4, 64], F32R)
    abar = sb("p1_abar", [128, 512], F32R)
    qhat = sb("p1_qhat", [64, 4, 128], F32R)
    khat2 = sb("p1_khat2", [128, 256], F32R)
    ropet = sb("p1_ropet", [128, 2, 128], F32)
    stat_m = sb("p1_sm", [1, 512], F32R)
    stat_a = sb("p1_sa", [1, 512], F32R)
    mb_bc = sb("p1_mb", [128, 512], F32R)
    ab_bc = sb("p1_ab", [128, 512], F32R)
    stat_t = ab_bc[0:1, :].bitcast(F32)
    stat_t2 = mb_bc[0:1, :].bitcast(F32)
    gnred = sb("p1_gnred", [64, 8], F32)
    gnsml = sb("p1_gnsml", [1, 12], F32)
    gnstg = sb("p1_gnstg", [1, 512], F32R)
    gnsqt = sb("p1_gnsqt", [1, 4], F32)
    gn_s = sb("p1_gns", [64, 4], F32)
    gn_t = sb("p1_gnt", [64, 4], F32)
    gn_b = sb("p1_gnb", [64, 4], F32)
    dbgstage = sb("p1_dbgst", [64, 20], F32)
    wostg = sb("p1_wostg", [128, 4, 512], F32)

    cosb, sinb = io["cosb"], io["sinb"]
    onesb, onesrow, eyeb = io["onesb"], io["onesrow"], io["eyeb"]
    dtb, gcrossb, kdec3b = io["dtb"], io["gcrossb"], io["kdec3b"]
    gnwb, gnbb = io["gnwb"], io["gnbb"]
    xT, ret_hbm = io["xT"], io["ret_hbm"]
    pp, ppr = io["pp"], io["ppr"]
    gs_f = io["gs_f"]

    names = ["ld", "w", "sq", "stat", "sml", "sml2", "bc", "bce", "xh", "prj",
             "vev", "rope", "tp", "tpe", "rpe", "rdv", "gn", "gn2", "gn3",
             "gnA", "yp", "wo", "woe", "wod", "dbg"]
    s = {n: prog.sem("p1_" + n) for n in names}
    dbg = io.get("dbg")

    def prj_consumer(n):
        i, r = divmod(n, 12)
        tt, j = divmod(r, 3)
        if j < 2:
            return s["rope"], i * 8 + tt * 2 + j + 1
        return s["vev"], i * 4 + tt + 1

    def f_sync(sync):
        for g in range(NG):
            if g > 0:
                sync.wait_ge(s["woe"], g * 64)
            for (wt, dst) in ((io["wqT"], w_q), (io["wkT"], w_k), (io["wvT"], w_v)):
                src = wt[:, g * GF:(g + 1) * GF].rearrange("(k p) f -> p k f", p=128)
                prog.gdma(sync, dst[:], src, "p1_w")
            src = io["woT"][g * GF:(g + 1) * GF, :].rearrange("(h p) f -> p h f", p=64)
            prog.gdma(sync, w_o[:], src, "p1_w")
            prog.gdma(sync, dtb[:], io["c_dt"][:, g * 512:(g + 1) * 512], "p1_w")
            prog.gdma(sync, gcrossb[:].rearrange("p a b -> p (a b)"),
                      io["c_gcross"][:, g * 512:(g + 1) * 512], "p1_w")
            prog.gdma(sync, kdec3b[:].rearrange("p a b -> p (a b)"),
                      io["c_kdec3"][:, g * 256:(g + 1) * 256], "p1_w")
            for sc in range(NSC):
                i = g * NSC + sc
                if i >= 1:
                    sync.wait_ge(s["xh"], i)
                src = xT.rearrange("(k p) t -> p k t", p=128)[:, :, sc * 512:(sc + 1) * 512]
                prog.gdma(sync, xin[:, 0], src, "p1_ld")
            if g == 0:
                for tt in range(8):
                    for hf in range(2):
                        sync.wait_ge(s["woe"], tt * 8 + hf * 4 + 4)
                        dst = ret_hbm[hf * 512:(hf + 1) * 512, tt * 512:(tt + 1) * 512
                                      ].rearrange("(k p) t -> p k t", p=128).bitcast(F32)
                        prog.gdma(sync, dst, wostg[:], "p1_wod")

    def f_gps(gps):
        for g in range(1, NG):
            for tt in range(8):
                for hf in range(2):
                    gps.wait_ge(s["woe"], g * 64 + tt * 8 + hf * 4 + 4)
                    dst = ret_hbm[hf * 512:(hf + 1) * 512, tt * 512:(tt + 1) * 512
                                  ].rearrange("(k p) t -> p k t", p=128).bitcast(F32)
                    prog.gdma(gps, dst, wostg[:], "p1_wod", accum_op=ALU.add)

    def f_act(act):
        for g in range(NG):
            for sc in range(NSC):
                i = g * NSC + sc
                act.wait_ge(s["ld"], 16 * (i + 1))
                if i > 0:
                    act.wait_ge(s["prj"], 12 * i)
                if g > 0 and sc == 0:
                    act.wait_ge(s["gn"], 2 * g - 1)
                act.activation(xsq[:], xin[:, 0], AF.Square).then_inc(s["sq"], 1)
                act.wait_ge(s["sml"], i + 1)
                act.activation(stat_a[:], stat_t, AF.Sqrt).then_inc(s["sml2"], 1)
                if dbg and i == 0:
                    act.wait_ge(s["rdv"], 8)
                    prog.gdma(act, dbg["xhat"], xhat[:].rearrange("p a b -> p (a b)"), "p1_dbg")
                    prog.gdma(act, dbg["qfm"], qfm[:].rearrange("p a b -> p (a b)"), "p1_dbg")
                    prog.gdma(act, dbg["kfm"], kfm[:].rearrange("p a b -> p (a b)"), "p1_dbg")
                    prog.gdma(act, dbg["vtm"], vtm[:].rearrange("p a b -> p (a b)"), "p1_dbg")
                    act.wait_ge(prog.sems["p1_dbg"], prog.counts["p1_dbg"])
            if dbg and g == 0:
                act.wait_ge(s["gn"], 2 * g + 1)
                prog.gdma(act, dbg["y"], ybuf[:].rearrange("p a b -> p (a b)"), "p1_dbg")
                act.wait_ge(prog.sems["p1_dbg"], prog.counts["p1_dbg"])
            act.wait_ge(s["gn"], 2 * g + 2)
            act.activation(gnsqt[:], gnsml[:, 4:8], AF.Sqrt).then_inc(s["gnA"], 1)
            if dbg and g == 0:
                act.wait_ge(s["yp"], 4)
                prog.gdma(act, dbg["ypost"], ybuf[:].rearrange("p a b -> p (a b)"), "p1_dbg")
                prog.gdma(act, dbg["gn2"][:, 0:20], dbgstage[:], "p1_dbg")
                act.wait_ge(prog.sems["p1_dbg"], prog.counts["p1_dbg"])

    def f_pe(pe):
        for g in range(NG):
            pe.wait_ge(s["w"], 112 * (g + 1))
            if g > 0:
                pe.wait_ge(s["woe"], 64 * g)
                pe.wait_ge(s["gn"], 2 * g)
                pe.wait_ge(s["yp"], g * HPG)
            for sc in range(NSC):
                i = g * NSC + sc
                pe.wait_ge(s["ld"], 16 * (i + 1))
                pe.wait_ge(s["sq"], i + 1)
                if i > 0:
                    pe.wait_ge(s["rdv"], 8 * i)
                for k in range(8):
                    pe.matmul(pp[0:1, 0, :], onesb[:, 0:1], xin[:, 0, k],
                              start=(k == 0), stop=(k == 7))
                for k in range(8):
                    ins = pe.matmul(pp[0:1, 1, :], onesb[:, 0:1], xsq[:, k],
                                    start=(k == 0), stop=(k == 7))
                ins.then_inc(s["stat"], 1)
                pe.wait_ge(s["sml2"], i + 1)
                if i > 0:
                    pe.wait_ge(s["bce"], i)
                pe.matmul(pp[:, 2, :], onesrow[0:1, :], stat_m[:], start=True, stop=True)
                pe.matmul(pp[:, 3, :], onesrow[0:1, :], stat_a[:], start=True, stop=True
                          ).then_inc(s["bc"], 1)
                pe.wait_ge(s["xh"], i + 1)
                for tt in range(4):
                    for j, wbuf in enumerate((w_q, w_k, w_v)):
                        n = i * 12 + tt * 3 + j
                        if n >= 2:
                            cs, cv = prj_consumer(n - 2)
                            pe.wait_ge(cs, cv)
                        bank = 4 + n % 2
                        for k in range(8):
                            ins = pe.matmul(pp[:, bank, 0:GF], xhat[:, k, tt * 128:(tt + 1) * 128],
                                            wbuf[:, k], start=(k == 0), stop=(k == 7))
                        ins.then_inc(s["prj"], 1)
                    pe.wait_ge(s["rope"], i * 8 + tt * 2 + 2)
                    for jj, srcb in enumerate((qtm[:], ktm[:, tt])):
                        for h in range(4):
                            ntp = (i * 4 + tt) * 8 + jj * 4 + h + 1
                            if ntp >= 3:
                                pe.wait_ge(s["tpe"], ntp - 2)
                            pe.transpose(ppr[:, (ntp - 1) % 2, 0:128][0:64, :] if False else ppr[0:64, (ntp - 1) % 2, 0:128],
                                         srcb[:, h * 64:(h + 1) * 64], eyeb[:]
                                         ).then_inc(s["tp"], 1)
                for rc in range(4):
                    pe.wait_ge(s["tpe"], (i * 4 + rc + 1) * 8)
                    pe.wait_ge(s["vev"], i * 4 + rc + 1)
                    first = (sc == 0 and rc == 0)
                    idx = i * 4 + rc
                    rcs = slice(rc * 128, (rc + 1) * 128)
                    for h in range(4):
                        ins = pe.matmul(pp[:, 2, h * 128:(h + 1) * 128], kfm[0:64, h, rcs],
                                        qfm[0:64, h, rcs], start=True, stop=True)
                    ins.then_inc(s["rpe"], 1)
                    pe.wait_ge(s["rdv"], 2 * idx + 1)
                    for h in range(4):
                        hs = slice(h * 64, (h + 1) * 64)
                        pe.matmul(pp[0:64, 3, h * 128:(h + 1) * 128],
                                  vtm[:, rc, hs], abar[:, h * 128:(h + 1) * 128],
                                  start=True, stop=first)
                        if not first:
                            pe.matmul(pp[0:64, 3, h * 128:(h + 1) * 128],
                                      tstate[:, h, :], qhat[:, h, :],
                                      start=False, stop=True)
                        ins = pe.matmul(pp[0:64, 0, 256 + h * 64:256 + (h + 1) * 64],
                                        khat2[:, hs], vtm[:, rc, hs], start=True, stop=True)
                    ins.then_inc(s["rpe"], 1)
            pe.wait_ge(s["gn"], 2 * g + 1)
            pe.matmul(pp[0:1, 0, 0:8], onesb[0:64, 0:1].bitcast(F32), gnred[:], start=True, stop=True)
            pe.matmul(pp[:, 4, :], eyeb[:], xhat[:, 0], start=True, stop=True
                      ).then_inc(s["gn2"], 1)
            pe.wait_ge(s["gn3"], g + 1)
            pe.matmul(pp[0:64, 1, :], onesrow[0:1, 0:64], gnstg[:], start=True, stop=True)
            pe.matmul(pp[:, 4, :], eyeb[:], xhat[:, 0], start=True, stop=True
                      ).then_inc(s["gn2"], 1)
            pe.wait_ge(s["yp"], (g + 1) * HPG)
            for tt in range(8):
                for oc in range(8):
                    nwo = g * 64 + tt * 8 + oc
                    bank = 4 + nwo % 2
                    if nwo >= 2:
                        pe.wait_ge(s["woe"], nwo - 1)
                    for h in range(4):
                        ins = pe.matmul(pp[:, bank, :], w_o[:, h, oc * 128:(oc + 1) * 128],
                                        ybuf[:, h, tt * 512:(tt + 1) * 512],
                                        start=(h == 0), stop=(h == 3))
                    ins.then_inc(s["wo"], 1)

    def f_dve(dve):
        for g in range(NG):
            dve.wait_ge(s["w"], 112 * (g + 1))
            if dbg and g == 1:
                dve.wait_ge(prog.sems["p1_dbg"], prog.counts["p1_dbg"])
            for sc in range(NSC):
                i = g * NSC + sc
                dve.wait_ge(s["stat"], i + 1)
                dve.tensor_scalar_mul(stat_m[:], pp[0:1, 0, :], 1.0 / 1024.0)
                dve.tensor_scalar_mul(stat_t, pp[0:1, 1, :], 1.0 / 1024.0)
                dve.tensor_tensor(stat_t2, stat_m[:].bitcast(F32), stat_m[:].bitcast(F32), ALU.mult)
                dve.tensor_tensor(stat_t, stat_t, stat_t2, ALU.subtract)
                dve.tensor_scalar_add(stat_t, stat_t, EPS)
                dve.reciprocal(stat_t, stat_t).then_inc(s["sml"], 1)
                dve.wait_ge(s["bc"], i + 1)
                dve.tensor_copy(mb_bc[:], pp[:, 2, :])
                dve.tensor_copy(ab_bc[:], pp[:, 3, :]).then_inc(s["bce"], 1)
                for k in range(8):
                    dve.tensor_tensor(xsq[:, k], xin[:, 0, k], mb_bc[:], ALU.subtract)
                    ins = dve.tensor_tensor(xhat[:, k], xsq[:, k], ab_bc[:], ALU.mult)
                ins.then_inc(s["xh"], 1)
                for tt in range(4):
                    base_prj = i * 12 + tt * 3
                    for j, dstb in ((0, qtm[:]), (1, ktm[:, tt])):
                        bank = 4 + (base_prj + j) % 2
                        dve.wait_ge(s["prj"], base_prj + j + 1)
                        if j == 0 and (i * 4 + tt) >= 1:
                            dve.wait_ge(s["tp"], (i * 4 + tt - 1) * 8 + 4)
                        dve.tensor_copy(dstb, pp[:, bank, 0:GF]).then_inc(s["rope"], 1)
                    bank = 4 + (base_prj + 2) % 2
                    dve.wait_ge(s["prj"], base_prj + 3)
                    dve.tensor_copy(vtm[:, tt], pp[:, bank, 0:GF]).then_inc(s["vev"], 1)
                    for n in range(8):
                        ntp = (i * 4 + tt) * 8 + n + 1
                        dve.wait_ge(s["tp"], ntp)
                        dstb = qfm if n < 4 else kfm
                        h = n % 4
                        dve.tensor_copy(dstb[:, h, tt * 128:(tt + 1) * 128],
                                        ppr[0:64, (ntp - 1) % 2, 0:128]).then_inc(s["tpe"], 1)
                for rc in range(4):
                    first = (sc == 0 and rc == 0)
                    if g > 0 and sc == 0 and rc == 0:
                        dve.wait_ge(s["wo"], g * 64)
                    idx = i * 4 + rc
                    rcs = slice(rc * 128, (rc + 1) * 128)
                    hq = 4 * g
                    dve.wait_ge(s["rpe"], 2 * idx + 1)
                    dve.tensor_tensor(abar[:], pp[:, 2, 0:512],
                                      dtb[:], ALU.mult)
                    if not first:
                        dve.tensor_tensor(qhat[:], qfm[:, :, rcs], gcrossb[:], ALU.mult)
                    dve.tensor_tensor(khat2[:].rearrange("p (a b) -> p a b", b=64),
                                      ktm[:, rc].rearrange("p (a b) -> p a b", b=64),
                                      kdec3b[:], ALU.mult).then_inc(s["rdv"], 1)
                    dve.wait_ge(s["rpe"], 2 * idx + 2)
                    dve.tensor_copy(ybuf[:, :, (sc * 4 + rc) * 128:(sc * 4 + rc + 1) * 128],
                                    pp[0:64, 3, 0:512].rearrange("p (a b) -> p a b", b=128))
                    for h in range(4):
                        if first:
                            dve.tensor_copy(tstate[:, h, :], pp[0:64, 0, 256 + h * 64:256 + (h + 1) * 64])
                        else:
                            dve.scalar_tensor_tensor(tstate[:, h, :], tstate[:, h, :],
                                                     gs_f[hq + h], pp[0:64, 0, 256 + h * 64:256 + (h + 1) * 64],
                                                     op0=ALU.mult, op1=ALU.add)
                    dve.engine_nop().then_inc(s["rdv"], 1)
            for h in range(4):
                dve.reduce_sum(gnred[:, h:h + 1], ybuf[:, h], axis=mybir.AxisListType.X)
            for h in range(4):
                dve.tensor_tensor(xsq[0:64, 0:8].rearrange("p a t -> p (a t)"),
                                  ybuf[:, h], ybuf[:, h], ALU.mult)
                dve.reduce_sum(gnred[:, 4 + h:5 + h],
                               xsq[0:64, 0:8].rearrange("p a t -> p (a t)"),
                               axis=mybir.AxisListType.X)
            dve.engine_nop().then_inc(s["gn"], 1)
            dve.wait_ge(s["gn2"], 2 * g + 1)
            dve.tensor_scalar_mul(gnsml[:, 0:8], pp[0:1, 0, 0:8], 1.0 / (64.0 * 4096.0))
            dve.tensor_tensor(gnsml[:, 8:12], gnsml[:, 0:4], gnsml[:, 0:4], ALU.mult)
            dve.tensor_tensor(gnsml[:, 4:8], gnsml[:, 4:8], gnsml[:, 8:12], ALU.subtract)
            dve.tensor_scalar_add(gnsml[:, 4:8], gnsml[:, 4:8], EPS).then_inc(s["gn"], 1)
            dve.wait_ge(s["gnA"], g + 1)
            with prog.nc.allow_low_precision("f32r stage"):
                dve.reciprocal(gnstg[:, 0:4], gnsqt[:])
                dve.tensor_copy(gnstg[:, 4:8], gnsml[:, 0:4]).then_inc(s["gn3"], 1)
            dve.wait_ge(s["gn2"], 2 * g + 2)
            dve.tensor_copy(mb_bc[0:64, :], pp[0:64, 1, :])
            dve.tensor_copy(ab_bc[0:64, :], mb_bc[0:64, :])
            dve.tensor_tensor(gn_s[:], mb_bc[0:64, 0:4].bitcast(F32), gnwb[:, 4 * g:4 * g + 4], ALU.mult)
            dve.tensor_copy(ab_bc[0:64, :], mb_bc[0:64, :])
            dve.tensor_tensor(gn_t[:], mb_bc[0:64, 4:8].bitcast(F32), gn_s[:], ALU.mult)
            dve.tensor_copy(ab_bc[0:64, :], mb_bc[0:64, :])
            dve.tensor_tensor(gn_b[:], gnbb[:, 4 * g:4 * g + 4], gn_t[:], ALU.subtract)
            dve.tensor_copy(ab_bc[0:64, :], mb_bc[0:64, :])
            if dbg and g == 0:
                dve.tensor_copy(dbgstage[:, 0:4], gn_s[:])
                dve.tensor_copy(dbgstage[:, 4:8], gn_b[:])
                dve.tensor_copy(dbgstage[:, 8:16], pp[0:64, 1, 0:8])
                dve.tensor_copy(dbgstage[:, 16:20], gn_t[:])
            for h in range(4):
                dve.tensor_scalar_mul(ybuf[:, h], ybuf[:, h], gn_s[:, h:h + 1])
                dve.tensor_scalar_add(ybuf[:, h], ybuf[:, h], gn_b[:, h:h + 1]
                                      ).then_inc(s["yp"], 1)
            for tt in range(8):
                for oc in range(8):
                    nwo = g * 64 + tt * 8 + oc
                    bank = 4 + nwo % 2
                    dve.wait_ge(s["wo"], nwo + 1)
                    nhalf = (g * 8 + tt) * 2 + oc // 4
                    if nhalf >= 1 and oc % 4 == 0:
                        dve.wait_ge(s["wod"], 16 * nhalf)
                    dve.tensor_copy(wostg[:, oc % 4], pp[:, bank, :]).then_inc(s["woe"], 1)

    return dict(f_sync=f_sync, f_gps=f_gps, f_act=f_act, f_pe=f_pe, f_dve=f_dve)



def declare_common(prog):
    nc = prog.nc
    sb = prog.sb
    io = {}
    cdefs = [
        ("c_ones", [128, 1], F32R, "ones128", "onesb"),
        ("c_onesrow", [1, 128], F32R, "onesrow", "onesrow"),
        ("c_eye", [128, 128], F32R, "eye128", "eyeb"),
        ("c_dt", [128, 16 * 128], F32, "ret_dt", "dtb"),
        ("c_gcross", [64, 16 * 128], F32, "ret_gcross", "gcrossb"),
        ("c_kdec3", [128, 16 * 64], F32, "ret_kdec3", "kdec3b"),
        ("c_onesall", [128, 128], F32R, "onesall", "onesall"),
        ("c_diag", [128, 4 * 512], F32R, "diagm", "diagmb"),
        ("c_cos", [T, 32], F32, "rope_cos_tm", "cosb"),
        ("c_sin", [T, 32], F32, "rope_sin_tm", "sinb"),
    ]
    io["_cdefs"] = cdefs
    for nm, shape, dt_, ckey, sbkey in cdefs:
        io[nm] = nc.dram_tensor(nm, shape, dt_, kind="ExternalInput").ap()
    io["kdec3b"] = sb("cb_kdec3", [128, 4, 64], F32)
    io["cosb"] = sb("cb_cos", [128, 32, 32], F32)
    io["sinb"] = sb("cb_sin", [128, 32, 32], F32)
    io["onesb"] = sb("cb_ones", [128, 1], F32R)
    io["onesall"] = sb("cb_onesall", [128, 128], F32R)
    io["onesrow"] = sb("cb_onesrow", [1, 128], F32R)
    io["eyeb"] = sb("cb_eye", [128, 128], F32R)
    io["dtb"] = sb("cb_dt", [128, 4 * 128], F32)
    io["gcrossb"] = sb("cb_gcross", [64, 4, 128], F32)
    io["gnwb"] = sb("cb_gnw", [64, 16], F32)
    io["gnbb"] = sb("cb_gnb", [64, 16], F32)
    io["pp"] = prog.ps("pp", [128, 6, 512], F32)
    io["ppr"] = prog.ps("ppr", [128, 2, 512], F32R)
    return io


def emit_const_loads(prog, blk, io):
    s_c = prog.sem("c_ld")

    def f_sync(sync):
        n = 0
        pairs = [
            (io["onesb"][:], io["c_ones"]),
            (io["onesrow"][:], io["c_onesrow"]),
            (io["eyeb"][:], io["c_eye"]),
            (io["onesall"][:], io["c_onesall"]),
            (io["cosb"][:], io["c_cos"].rearrange("(n p) f -> p n f", p=128)),
            (io["sinb"][:], io["c_sin"].rearrange("(n p) f -> p n f", p=128)),
            (io["gnwb"][:], io["gnw_in"]),
            (io["gnbb"][:], io["gnb_in"]),
        ]
        for dst, src in pairs:
            prog.gdma(sync, dst, src, "c_ld")
            n += 16
        sync.wait_ge(prog.sem("c_ld"), prog.counts["c_ld"])

    blk.sync(f_sync)


def const_in_map(consts):
    return {
        "c_ones": consts["ones128"], "c_onesrow": consts["onesrow"],
        "c_eye": consts["eye128"], "c_dt": consts["ret_dt"],
        "c_onesall": consts["onesall"], "c_diag": consts["diagm"],
        "c_gcross": consts["ret_gcross"].reshape(64, 16 * 128),
        "c_kdec3": consts["ret_kdec3"].reshape(128, 16 * 64),
        "c_cos": np.ascontiguousarray(consts["rope_cos"].T),
        "c_sin": np.ascontiguousarray(consts["rope_sin"].T),
    }


def build_p1_test(consts):
    prog = Prog()
    nc = prog.nc
    io = declare_common(prog)
    io["xT"] = nc.dram_tensor("xT", [C, T], F32R, kind="ExternalInput").ap()
    io["wqT"] = nc.dram_tensor("wqT", [C, C], F32R, kind="ExternalInput").ap()
    io["wkT"] = nc.dram_tensor("wkT", [C, C], F32R, kind="ExternalInput").ap()
    io["wvT"] = nc.dram_tensor("wvT", [C, C], F32R, kind="ExternalInput").ap()
    io["woT"] = nc.dram_tensor("woT", [C, C], F32R, kind="ExternalInput").ap()
    io["gnw_in"] = nc.dram_tensor("gnw_in", [64, 16], F32, kind="ExternalInput").ap()
    io["gnb_in"] = nc.dram_tensor("gnb_in", [64, 16], F32, kind="ExternalInput").ap()
    io["ret_hbm"] = nc.dram_tensor("ret_hbm", [C, T], F32R, kind="ExternalOutput").ap()
    io["gs_f"] = consts["ret_gs_f"]
    io["dbg"] = {
        "xhat": nc.dram_tensor("dbg_xhat", [128, 8 * 512], F32R, kind="ExternalOutput").ap(),
        "qfm": nc.dram_tensor("dbg_qfm", [64, 4 * 512], F32R, kind="ExternalOutput").ap(),
        "kfm": nc.dram_tensor("dbg_kfm", [64, 4 * 512], F32R, kind="ExternalOutput").ap(),
        "vtm": nc.dram_tensor("dbg_vtm", [128, 4 * 256], F32R, kind="ExternalOutput").ap(),
        "y": nc.dram_tensor("dbg_y", [64, 4 * 4096], F32R, kind="ExternalOutput").ap(),
        "ypost": nc.dram_tensor("dbg_ypost", [64, 4 * 4096], F32R, kind="ExternalOutput").ap(),
        "gns": nc.dram_tensor("dbg_gns", [64, 12], F32, kind="ExternalOutput").ap(),
        "gn2": nc.dram_tensor("dbg_gn2", [64, 20], F32, kind="ExternalOutput").ap(),
    }

    with prog.nc.Block() as blk0:
        emit_const_loads(prog, blk0, io)
    fns = emit_p1_retention(prog, io)
    with prog.nc.Block() as blk1:
        blk1.sync(fns["f_sync"])
        blk1.gpsimd(fns["f_gps"])
        blk1.scalar(fns["f_act"])
        blk1.tensor(fns["f_pe"])
        blk1.vector(fns["f_dve"])
    s_wod = prog.sems["p1_wod"]
    with prog.nc.Block() as blk2:
        def f_end(gps):
            gps.wait_ge(s_wod, prog.counts["p1_wod"])
        blk2.gpsimd(f_end)
    prog.close()
    return prog


def kernel(**inputs):
    raise NotImplementedError("under construction")



def emit_p2_qkv(prog, io):
    sb = prog.sb
    NG, GF = 4, 256

    xin = sb("p2_xin", [128, 8, 512], F32R)
    ret = sb("p2_ret", [128, 8, 512], F32R)
    w_q = sb("p2_wq", [128, 8, GF], F32R)
    w_k = sb("p2_wk", [128, 8, GF], F32R)
    w_v = sb("p2_wv", [128, 8, GF], F32R)
    qtm = sb("p2_qtm", [128, GF], F32R)
    ktm = sb("p2_ktm", [128, GF], F32R)
    vtm = sb("p2_vtm", [128, 4, GF], F32R)
    qfm = sb("p2_qfm", [64, 4, 512], F32R)
    kfm = sb("p2_kfm", [64, 4, 512], F32R)
    mb_bc = sb("p2_mb", [128, 512], F32)
    ab_bc = sb("p2_ab", [128, 512], F32)
    wtmp = sb("p2_wt", [128, 512], F32)
    rs_out = sb("p2_rs", [128, 512], F32)
    ropet = sb("p2_rope", [128, 2, 128], F32)
    lngb = sb("p2_lngb", [128, 2, 8], F32R)
    brow = sb("p2_brow", [1, 3 * GF], F32R)

    cosb, sinb = io["cosb"], io["sinb"]
    onesrow, eyeb, onesall = io["onesrow"], io["eyeb"], io["onesall"]
    xT, ret_hbm = io["xT"], io["ret_hbm"]
    x1_hbm, xm_hbm = io["x1_hbm"], io["xm_hbm"]
    k_hbm, v_hbm, q_hbm = io["k_hbm"], io["v_hbm"], io["q_hbm"]
    wqkv = io["wqkv"]
    pp, ppr = io["pp"], io["ppr"]

    names = ["ld", "w", "x1", "x1w", "sq", "stat", "st_ev", "sml", "sml2",
             "xh", "xh2", "xmw", "prj", "vev", "rope", "tp", "tpe", "bias",
             "gsc", "kqw", "vw"]
    s = {n: prog.sem("p2_" + n) for n in names}

    def prj_consumer(n):
        i, r = divmod(n, 12)
        tt, j = divmod(r, 3)
        if j < 2:
            return s["rope"], i * 8 + tt * 2 + j + 1
        return s["vev"], i * 4 + tt + 1

    def f_sync(sync):
        prog.gdma(sync, lngb[:].rearrange("p a b -> p (a b)"), io["lngb_in"], "p2_w")
        for g in range(NG):
            if g > 0:
                sync.wait_ge(s["prj"], 96 * g)
            for j, dst in enumerate((w_q, w_k, w_v)):
                src = wqkv[:, j * 1024 + g * GF: j * 1024 + (g + 1) * GF]
                prog.gdma(sync, dst[:], src.rearrange("(k p) f -> p k f", p=128), "p2_w")
            for sc in range(NSC):
                i = g * NSC + sc
                ts = slice(sc * 512, (sc + 1) * 512)
                if i >= 1:
                    sync.wait_ge(s["prj"], 12 * i)
                    sync.wait_ge(s["x1w"], 16 * i)
                    sync.wait_ge(s["xmw"], 16 * min(i, 8))
                prog.gdma(sync, xin[:], xT.rearrange("(k p) t -> p k t", p=128)[:, :, ts], "p2_ld")
                prog.gdma(sync, ret[:],
                          ret_hbm.rearrange("(k p) t -> p k t", p=128)[:, :, ts], "p2_ld")

    def f_scalar(act):
        for g in range(NG):
            for sc in range(NSC):
                i = g * NSC + sc
                ts = slice(sc * 512, (sc + 1) * 512)
                act.wait_ge(s["x1"], i + 1)
                if i >= 1:
                    act.wait_ge(s["xh2"], i)
                    if i <= 8:
                        act.wait_ge(s["xmw"], 16 * i)
                act.activation(xin[:], ret[:], AF.Square).then_inc(s["sq"], 1)
                act.wait_ge(s["sml"], 2 * i + 1)
                act.activation(rs_out[:], ab_bc[:], AF.Sqrt).then_inc(s["sml2"], 1)
                act.wait_ge(s["sml"], 2 * i + 2)
                act.activation(ab_bc[:], wtmp[:], AF.Sqrt).then_inc(s["sml2"], 1)
                act.wait_ge(s["tpe"], 32 * (i + 1))
                prog.gdma(act, k_hbm[4 * g:4 * g + 4, :, ts].rearrange("h p t -> p h t"),
                          kfm[:], "p2_kqw")
                prog.gdma(act, q_hbm[4 * g:4 * g + 4, :, ts].rearrange("h p t -> p h t"),
                          qfm[:], "p2_kqw")

    def f_pe(pe):
        for g in range(NG):
            pe.wait_ge(s["w"], 16 + 48 * (g + 1))
            if g > 0:
                pe.wait_ge(s["tpe"], 32 * 8 * g)
            for j, wbuf in enumerate((w_q, w_k, w_v)):
                dstp = pp[0:1, 0, (j % 2) * GF:(j % 2 + 1) * GF] if j < 2 \
                    else pp[0:1, 1, 0:GF]
                for k in range(8):
                    pe.matmul(dstp, lngb[:, 1, k:k + 1], wbuf[:, k],
                              start=(k == 0), stop=(k == 7))
            pe.matmul(pp[:, 4, :], eyeb[:], w_q[:, 0:2].rearrange("p a b -> p (a b)"),
                      start=True, stop=True).then_inc(s["bias"], 1)
            pe.wait_ge(s["gsc"], g + 1)
            for sc in range(NSC):
                i = g * NSC + sc
                if i > 0:
                    pe.wait_ge(s["st_ev"], i)
                pe.wait_ge(s["x1"], i + 1)
                for k in range(8):
                    pe.matmul(pp[:, 2, :], onesall[:], ret[:, k],
                              start=(k == 0), stop=(k == 7))
                pe.wait_ge(s["sq"], i + 1)
                for k in range(8):
                    ins = pe.matmul(pp[:, 3, :], onesall[:], xin[:, k],
                                    start=(k == 0), stop=(k == 7))
                ins.then_inc(s["stat"], 1)
                pe.wait_ge(s["xh2"], i + 1)
                for tt in range(4):
                    for j, wbuf in enumerate((w_q, w_k, w_v)):
                        n = i * 12 + tt * 3 + j
                        if n >= 2:
                            cs, cv = prj_consumer(n - 2)
                            pe.wait_ge(cs, cv)
                        bank = 4 + n % 2
                        for k in range(8):
                            pe.matmul(pp[:, bank, 0:GF], ret[:, k, tt * 128:(tt + 1) * 128],
                                      wbuf[:, k], start=(k == 0), stop=False)
                        ins = pe.matmul(pp[:, bank, 0:GF], onesrow[0:1, :],
                                        brow[:, j * GF:(j + 1) * GF], start=False, stop=True)
                        ins.then_inc(s["prj"], 1)
                    pe.wait_ge(s["rope"], i * 8 + tt * 2 + 2)
                    for jj, srcb in enumerate((qtm[:], ktm[:])):
                        for h in range(4):
                            ntp = (i * 4 + tt) * 8 + jj * 4 + h + 1
                            if ntp >= 3:
                                pe.wait_ge(s["tpe"], ntp - 2)
                            pe.transpose(ppr[0:64, (ntp - 1) % 2, 0:128],
                                         srcb[:, h * 64:(h + 1) * 64], eyeb[:]
                                         ).then_inc(s["tp"], 1)

    def f_dve(dve):
        for g in range(NG):
            dve.wait_ge(s["bias"], g + 1)
            dve.tensor_copy(brow[:, 0:512], pp[0:1, 0, 0:512])
            dve.tensor_copy(brow[:, 512:768], pp[0:1, 1, 0:256])
            for wbuf in (w_q, w_k, w_v):
                for k in range(8):
                    gb = lngb[:, 0, k:k + 1].broadcast_to([128, GF])
                    dve.tensor_tensor(wbuf[:, k], wbuf[:, k], gb, ALU.mult)
            dve.engine_nop().then_inc(s["gsc"], 1)
            for sc in range(NSC):
                i = g * NSC + sc
                dve.wait_ge(s["ld"], 32 * (i + 1))
                for k in range(8):
                    ins = dve.tensor_tensor(ret[:, k], xin[:, k], ret[:, k], ALU.add)
                ins.then_inc(s["x1"], 1)
                dve.wait_ge(s["stat"], i + 1)
                dve.tensor_scalar_mul(mb_bc[:], pp[:, 2, :], 1.0 / 1024.0)
                dve.tensor_scalar_mul(ab_bc[:], pp[:, 3, :], 1.0 / 1024.0
                                      ).then_inc(s["st_ev"], 1)
                dve.tensor_tensor(wtmp[:], mb_bc[:], mb_bc[:], ALU.mult)
                dve.tensor_tensor(ab_bc[:], ab_bc[:], wtmp[:], ALU.subtract)
                dve.tensor_copy(wtmp[:], ab_bc[:])
                dve.tensor_scalar_add(ab_bc[:], ab_bc[:], EPS)
                dve.engine_nop().then_inc(s["sml"], 1)
                dve.wait_ge(s["sml2"], 2 * i + 1)
                dve.reciprocal(rs_out[:], rs_out[:])
                dve.tensor_tensor(wtmp[:], wtmp[:], rs_out[:], ALU.mult)
                dve.tensor_tensor(wtmp[:], wtmp[:], rs_out[:], ALU.mult)
                dve.tensor_scalar_add(wtmp[:], wtmp[:], EPS)
                dve.engine_nop().then_inc(s["sml"], 1)
                for k in range(8):
                    dve.tensor_tensor(xin[:, k], ret[:, k], mb_bc[:], ALU.subtract)
                    ins = dve.tensor_tensor(xin[:, k], xin[:, k], rs_out[:], ALU.mult)
                ins.then_inc(s["xh"], 1)
                dve.wait_ge(s["sml2"], 2 * i + 2)
                dve.reciprocal(ab_bc[:], ab_bc[:])
                dve.wait_ge(s["x1w"], 16 * (i + 1))
                for k in range(8):
                    ins = dve.tensor_tensor(ret[:, k], xin[:, k], ab_bc[:], ALU.mult)
                ins.then_inc(s["xh2"], 1)
                for tt in range(4):
                    base_prj = i * 12 + tt * 3
                    for j, dstb in ((0, qtm[:]), (1, ktm[:])):
                        bank = 4 + (base_prj + j) % 2
                        dve.wait_ge(s["prj"], base_prj + j + 1)
                        if (i * 4 + tt) >= 1:
                            dve.wait_ge(s["tp"], (i * 4 + tt - 1) * 8 + 4 + j * 4)
                        src = pp[:, bank, 0:GF].rearrange("p (h f) -> p h f", f=64)
                        t1, t2 = src[:, :, 0:32], src[:, :, 32:64]
                        gt = sc * 4 + tt
                        cosB = cosb[:, gt].unsqueeze(1).broadcast_to([128, 4, 32])
                        sinB = sinb[:, gt].unsqueeze(1).broadcast_to([128, 4, 32])
                        dst = dstb.rearrange("p (h f) -> p h f", f=64)
                        r0 = ropet[:, 0].rearrange("p (h f) -> p h f", f=32)
                        r1 = ropet[:, 1].rearrange("p (h f) -> p h f", f=32)
                        dve.tensor_tensor(r0, t1, cosB, ALU.mult)
                        dve.tensor_tensor(r1, t2, sinB, ALU.mult)
                        dve.tensor_tensor(dst[:, :, 0:32], r0, r1, ALU.subtract)
                        dve.tensor_tensor(r0, t1, sinB, ALU.mult)
                        dve.tensor_tensor(r1, t2, cosB, ALU.mult)
                        dve.tensor_tensor(dst[:, :, 32:64], r0, r1, ALU.add
                                          ).then_inc(s["rope"], 1)
                    bank = 4 + (base_prj + 2) % 2
                    dve.wait_ge(s["prj"], base_prj + 3)
                    if (i * 4 + tt) >= 4:
                        dve.wait_ge(s["vw"], 16 * (i * 4 + tt - 3))
                    dve.tensor_copy(vtm[:, tt], pp[:, bank, 0:GF]).then_inc(s["vev"], 1)
                    if i > 0 and tt == 0:
                        dve.wait_ge(s["kqw"], 32 * i)
                    for n in range(8):
                        ntp = (i * 4 + tt) * 8 + n + 1
                        dve.wait_ge(s["tp"], ntp)
                        dstb = qfm if n < 4 else kfm
                        h = n % 4
                        dve.tensor_copy(dstb[:, h, tt * 128:(tt + 1) * 128],
                                        ppr[0:64, (ntp - 1) % 2, 0:128]).then_inc(s["tpe"], 1)

    def f_gps(gps):
        for g in range(NG):
            for sc in range(NSC):
                i = g * NSC + sc
                ts = slice(sc * 512, (sc + 1) * 512)
                gps.wait_ge(s["x1"], i + 1)
                prog.gdma(gps, x1_hbm.rearrange("(k p) t -> p k t", p=128)[:, :, ts],
                          ret[:], "p2_x1w")
                if g == 0:
                    gps.wait_ge(s["xh"], i + 1)
                    prog.gdma(gps, xm_hbm.rearrange("(k p) t -> p k t", p=128)[:, :, ts],
                              xin[:], "p2_xmw")
                for tt in range(4):
                    gps.wait_ge(s["vev"], i * 4 + tt + 1)
                    t0 = sc * 512 + tt * 128
                    prog.gdma(gps, v_hbm[t0:t0 + 128, g * GF:(g + 1) * GF],
                              vtm[:, tt], "p2_vw")

    return dict(f_sync=f_sync, f_scalar=f_scalar, f_pe=f_pe, f_dve=f_dve, f_gps=f_gps)



def emit_p3_attention(prog, io):
    sb = prog.sb
    NG = 4

    kbuf = sb("p3_k", [64, 2, T], F32R)
    vbuf = sb("p3_v", [128, 2, 32, 65], F32R)
    qbuf = sb("p3_q", [64, 2, T], F32R)
    pbuf = sb("p3_p", [128, 2, 512], F32R)
    obuf = sb("p3_o", [64, 2, T], F32R)
    lrow = sb("p3_lr", [1, 512], F32R)
    rbuf = sb("p3_r", [64, 512], F32)
    pads = sb("p3_pad", [64, 512], F32)
    padA = sb("p3_padA", [128, 512], F32)
    w_p = sb("p3_wp", [64, 4, 1024], F32R)
    otile = sb("p3_ot", [64, 2, 4, 512], F32R)
    wostg = sb("p3_wostg", [128, 4, 512], F32)
    diagmb = sb("p3_diag", [128, 4, 512], F32R)

    onesrow, eyeb = io["onesrow"], io["eyeb"]
    diagm = diagmb
    k_hbm, v_hbm, q_hbm, o_hbm = io["k_hbm"], io["v_hbm"], io["q_hbm"], io["o_hbm"]
    xm_hbm, wproj = io["xm_hbm"], io["wproj"]
    pp, ppr = io["pp"], io["ppr"]

    names = ["kv", "vone", "sS", "sP", "sP2", "sAVc", "sAV", "lr", "rb",
             "oev", "ow", "otl", "wpw", "wo", "woe", "wod"]
    s = {n: prog.sem("p3_" + n) for n in names}

    NKC = [(qt + 1) * 4 for qt in range(8)]
    KCT = sum(NKC)

    def zidx(h, qt, kc):
        return h * KCT + sum(NKC[:qt]) + kc

    def is_diag(qt, kc):
        return kc >= NKC[qt] - 4

    diag_count = {}
    c = 0
    for h in range(16):
        for qt in range(8):
            for kc in range(NKC[qt]):
                if is_diag(qt, kc):
                    c += 1
                diag_count[zidx(h, qt, kc)] = c

    def f_sync(sync):
        prog.gdma(sync, diagmb[:].rearrange("p a b -> p (a b)"), io["c_diag"], "p3_dg")
        for h in range(16):
            if h >= 2:
                sync.wait_ge(s["sAV"], 8 * (h - 1))
            prog.gdma(sync, kbuf[:, h % 2], k_hbm[h], "p3_kv")
            prog.gdma(sync, vbuf[:, h % 2, :, 0:64],
                      v_hbm[:, h * 64:(h + 1) * 64].rearrange("(a p) f -> p a f", p=128),
                      "p3_kv")
            prog.gdma(sync, qbuf[:, h % 2], q_hbm[h], "p3_kv")
        for g in range(NG):
            if g > 0:
                sync.wait_ge(s["wo"], 64 * g)
            src = wproj[g * 256:(g + 1) * 256, :].rearrange("(h p) f -> p h f", p=64)
            prog.gdma(sync, w_p[:], src, "p3_wpw")
            for tt in range(8):
                n = g * 8 + tt
                if n >= 2:
                    sync.wait_ge(s["wo"], (n - 1) * 8)
                ts = slice(tt * 512, (tt + 1) * 512)
                sync.wait_ge(s["ow"], 16 * (4 * g + 4))
                prog.gdma(sync, otile[:, n % 2],
                          o_hbm[4 * g:4 * g + 4, :, ts].rearrange("h p t -> p h t"), "p3_otl")

    def f_gps(gps):
        gps.wait_ge(prog.sem("p2_xmw"), prog.counts["p2_xmw"])
        for h in range(16):
            gps.wait_ge(s["oev"], h * 8 + 8)
            prog.gdma(gps, o_hbm[h], obuf[:, h % 2], "p3_ow")
        for g in range(NG):
            for tt in range(8):
                for hf in range(2):
                    gps.wait_ge(s["woe"], g * 64 + tt * 8 + hf * 4 + 4)
                    ts = slice(tt * 512, (tt + 1) * 512)
                    dst = xm_hbm[hf * 512:(hf + 1) * 512, ts
                                 ].rearrange("(k p) t -> p k t", p=128).bitcast(F32)
                    prog.gdma(gps, dst, wostg[:], "p3_wod", accum_op=ALU.add)

    def f_scalar(act):
        for h in range(16):
            for qt in range(8):
                for kc in range(NKC[qt]):
                    z = zidx(h, qt, kc)
                    act.wait_ge(s["sS"], z + 2 + h)
                    if z >= 2:
                        act.wait_ge(s["sAVc"], z - 1)
                    act.activation(pbuf[:, z % 2], pp[:, z % 2, :], AF.Exp,
                                   scale=0.125).then_inc(s["sP"], 1)

    def f_dve(dve):
        dve.wait_ge(prog.sem("p3_dg"), 16)
        oneB = io["onesb"][:, 0:1].unsqueeze(1).broadcast_to([128, 32, 1])
        dve.tensor_copy(vbuf[:, 0, :, 64:65], oneB)
        dve.tensor_copy(vbuf[:, 1, :, 64:65], oneB).then_inc(s["vone"], 1)
        for h in range(16):
            for qt in range(8):
                for kc in range(NKC[qt]):
                    if not is_diag(qt, kc):
                        continue
                    z = zidx(h, qt, kc)
                    r = kc - qt * 4
                    dve.wait_ge(s["sP"], z + 1)
                    dve.tensor_tensor(pbuf[:, z % 2], pbuf[:, z % 2], diagm[:, r],
                                      ALU.mult).then_inc(s["sP2"], 1)
                dve.wait_ge(s["sAV"], h * 8 + qt + 1)
                with prog.nc.allow_low_precision("recip row f32r"):
                    dve.reciprocal(lrow[:], pp[64:65, 2, :])
                dve.engine_nop().then_inc(s["lr"], 1)
                dve.wait_ge(s["rb"], h * 8 + qt + 1)
                if h >= 2 and qt == 0:
                    dve.wait_ge(s["ow"], 16 * (h - 1))
                dve.tensor_copy(rbuf[:], pp[0:64, 3, :])
                dve.tensor_copy(pads[:], pp[0:64, 2, :])
                dve.tensor_tensor(obuf[:, h % 2, qt * 512:(qt + 1) * 512],
                                  pp[0:64, 2, :], rbuf[:], ALU.mult
                                  ).then_inc(s["oev"], 1)
        for g in range(NG):
            for tt in range(8):
                for oc in range(8):
                    nwo = g * 64 + tt * 8 + oc
                    bank = 4 + nwo % 2
                    dve.wait_ge(s["wo"], nwo + 1)
                    nhalf = (g * 8 + tt) * 2 + oc // 4
                    if nhalf >= 1 and oc % 4 == 0:
                        dve.wait_ge(s["wod"], 16 * nhalf)
                    dve.tensor_copy(wostg[:, oc % 4], pp[:, bank, :]).then_inc(s["woe"], 1)

    def f_pe(pe):
        def emit_S(pe, h, z, qt, kc):
            if z >= 2:
                pe.wait_ge(s["sP"], z - 1)
            qs = slice(qt * 512, (qt + 1) * 512)
            pe.matmul(pp[:, z % 2, :], kbuf[:, h % 2, kc * 128:(kc + 1) * 128],
                      qbuf[:, h % 2, qs], start=True, stop=True).then_inc(s["sS"], 1)

        pe.wait_ge(s["vone"], 1)
        for h in range(16):
            pe.wait_ge(s["kv"], 48 * (h + 1))
            chunks = [(qt, kc) for qt in range(8) for kc in range(NKC[qt])]
            emit_S(pe, h, zidx(h, 0, 0), 0, 0)
            for ci, (qt, kc) in enumerate(chunks):
                z = zidx(h, qt, kc)
                if ci + 1 < len(chunks):
                    qt2, kc2 = chunks[ci + 1]
                    emit_S(pe, h, zidx(h, qt2, kc2), qt2, kc2)
                else:
                    pe.matmul(pp[0:64, 5, :], eyeb[0:64, 0:64], kbuf[:, h % 2, 0:512],
                              start=True, stop=True).then_inc(s["sS"], 1)
                if kc == 0 and h * 8 + qt >= 1:
                    pe.wait_ge(s["oev"], h * 8 + qt)
                if is_diag(qt, kc):
                    pe.wait_ge(s["sP2"], diag_count[z])
                else:
                    pe.wait_ge(s["sP"], z + 1)
                pe.matmul(pp[0:65, 2, :], vbuf[:, h % 2, kc], pbuf[:, z % 2],
                          start=(kc == 0), stop=(kc == NKC[qt] - 1)).then_inc(s["sAVc"], 1)
                if kc == NKC[qt] - 1:
                    pe.matmul(pp[0:64, 5, :], eyeb[0:64, 0:64], kbuf[:, h % 2, 0:512],
                              start=True, stop=True).then_inc(s["sAV"], 1)
                    pe.wait_ge(s["lr"], h * 8 + qt + 1)
                    pe.matmul(pp[0:64, 3, :], onesrow[0:1, 0:64], lrow[:], start=True, stop=True)
                    pe.matmul(pp[0:64, 5, :], eyeb[0:64, 0:64], kbuf[:, h % 2, 0:512],
                              start=True, stop=True).then_inc(s["rb"], 1)
        for g in range(NG):
            pe.wait_ge(s["wpw"], 16 * (g + 1))
            for tt in range(8):
                n = g * 8 + tt
                pe.wait_ge(s["otl"], 16 * (n + 1))
                for oc in range(8):
                    nwo = n * 8 + oc
                    bank = 4 + nwo % 2
                    if nwo >= 2:
                        pe.wait_ge(s["woe"], nwo - 1)
                    for hl in range(4):
                        ins = pe.matmul(pp[:, bank, :], w_p[:, hl, oc * 128:(oc + 1) * 128],
                                        otile[:, n % 2, hl], start=(hl == 0), stop=(hl == 3))
                    ins.then_inc(s["wo"], 1)

    return dict(f_sync=f_sync, f_gps=f_gps, f_scalar=f_scalar, f_pe=f_pe, f_dve=f_dve)



def emit_p4_gate(prog, io):
    sb = prog.sb
    xr = sb("p4_xr", [128, 8, 512], F32R)
    xsq = sb("p4_sq", [128, 8, 512], F32R)
    h2 = sb("p4_h2", [128, 8, 512], F32R)
    mb_bc = sb("p4_mb", [128, 512], F32)
    ab_bc = sb("p4_ab", [128, 512], F32)
    rs_out = sb("p4_rs", [128, 512], F32)
    h2tm = sb("p4_h2tm", [128, 2, 8, 128], F32R)
    lgro = sb("p4_lg", [8, 512], F32)
    lngb2 = sb("p4_lngb", [128, 2, 8], F32)
    wg = sb("p4_wg", [128, 8, 8], F32R)

    onesall, eyeb = io["onesall"], io["eyeb"]
    xm_hbm, wgate = io["xm_hbm"], io["wgate"]
    lgT_hbm, h2tm_hbm = io["lgT_hbm"], io["h2tm_hbm"]
    pp, ppr = io["pp"], io["ppr"]

    names = ["ld", "w", "sq", "stat", "st_ev", "sml", "sml2", "h2",
             "lg", "lge", "lgw", "tp", "tpe", "h2w"]
    s = {n: prog.sem("p4_" + n) for n in names}

    def f_sync(sync):
        prog.gdma(sync, lngb2[:].rearrange("p a b -> p (a b)"), io["lngb2_in"], "p4_w")
        prog.gdma(sync, wg[:], wgate.rearrange("(k p) e -> p k e", p=128), "p4_w")
        for sc in range(NSC):
            ts = slice(sc * 512, (sc + 1) * 512)
            if sc >= 1:
                sync.wait_ge(s["lg"], sc)
                sync.wait_ge(s["tpe"], 32 * sc)
            prog.gdma(sync, xr[:],
                      xm_hbm.rearrange("(k p) t -> p k t", p=128)[:, :, ts], "p4_ld")

    def f_scalar(act):
        for sc in range(NSC):
            act.wait_ge(s["ld"], 16 * (sc + 1))
            act.activation(xsq[:], xr[:], AF.Square).then_inc(s["sq"], 1)
            act.wait_ge(s["sml"], sc + 1)
            act.activation(rs_out[:], ab_bc[:], AF.Sqrt).then_inc(s["sml2"], 1)

    def f_pe(pe):
        pe.wait_ge(s["w"], 32)
        for sc in range(NSC):
            if sc > 0:
                pe.wait_ge(s["st_ev"], sc)
            pe.wait_ge(s["ld"], 16 * (sc + 1))
            for k in range(8):
                pe.matmul(pp[:, 2, :], onesall[:], xr[:, k],
                          start=(k == 0), stop=(k == 7))
            pe.wait_ge(s["sq"], sc + 1)
            for k in range(8):
                ins = pe.matmul(pp[:, 3, :], onesall[:], xsq[:, k],
                                start=(k == 0), stop=(k == 7))
            ins.then_inc(s["stat"], 1)
            pe.wait_ge(s["h2"], sc + 1)
            if sc > 0:
                pe.wait_ge(s["lge"], sc)
            for k in range(8):
                ins = pe.matmul(pp[0:8, 0, :], wg[:, k], h2[:, k],
                                start=(k == 0), stop=(k == 7))
            ins.then_inc(s["lg"], 1)
            for tt in range(4):
                for k in range(8):
                    ntp = (sc * 4 + tt) * 8 + k + 1
                    if ntp >= 3:
                        pe.wait_ge(s["tpe"], ntp - 2)
                    pe.transpose(ppr[:, (ntp - 1) % 2, 0:128],
                                 h2[:, k, tt * 128:(tt + 1) * 128], eyeb[:]
                                 ).then_inc(s["tp"], 1)

    def f_dve(dve):
        for sc in range(NSC):
            dve.wait_ge(s["stat"], sc + 1)
            dve.tensor_scalar_mul(mb_bc[:], pp[:, 2, :], 1.0 / 1024.0)
            dve.tensor_scalar_mul(ab_bc[:], pp[:, 3, :], 1.0 / 1024.0
                                  ).then_inc(s["st_ev"], 1)
            dve.tensor_tensor(rs_out[:], mb_bc[:], mb_bc[:], ALU.mult)
            dve.tensor_tensor(ab_bc[:], ab_bc[:], rs_out[:], ALU.subtract)
            dve.tensor_scalar_add(ab_bc[:], ab_bc[:], EPS)
            dve.engine_nop().then_inc(s["sml"], 1)
            dve.wait_ge(s["sml2"], sc + 1)
            dve.reciprocal(rs_out[:], rs_out[:])
            if sc > 0:
                dve.wait_ge(s["tp"], 32 * sc)
                dve.wait_ge(s["lg"], sc)
            for k in range(8):
                g2 = lngb2[:, 0, k:k + 1].broadcast_to([128, 512])
                b2 = lngb2[:, 1, k:k + 1].broadcast_to([128, 512])
                dve.tensor_tensor(h2[:, k], xr[:, k], mb_bc[:], ALU.subtract)
                dve.tensor_tensor(h2[:, k], h2[:, k], rs_out[:], ALU.mult)
                dve.tensor_tensor(h2[:, k], h2[:, k], g2, ALU.mult)
                ins = dve.tensor_tensor(h2[:, k], h2[:, k], b2, ALU.add)
            ins.then_inc(s["h2"], 1)
            dve.wait_ge(s["lg"], sc + 1)
            if sc > 0:
                dve.wait_ge(s["lgw"], 16 * sc)
            dve.tensor_copy(lgro[:], pp[0:8, 0, :]).then_inc(s["lge"], 1)
            for tt in range(4):
                n = sc * 4 + tt
                for k in range(8):
                    ntp = n * 8 + k + 1
                    dve.wait_ge(s["tp"], ntp)
                    if n >= 2 and k == 0:
                        dve.wait_ge(s["h2w"], 16 * (n - 1))
                    dve.tensor_copy(h2tm[:, n % 2, k], ppr[:, (ntp - 1) % 2, 0:128]
                                    ).then_inc(s["tpe"], 1)

    def f_gps(gps):
        for sc in range(NSC):
            ts = slice(sc * 512, (sc + 1) * 512)
            gps.wait_ge(s["lge"], sc + 1)
            prog.gdma(gps, lgT_hbm[:, ts], lgro[:], "p4_lgw")
            for tt in range(4):
                n = sc * 4 + tt
                gps.wait_ge(s["tpe"], 32 * sc + (tt + 1) * 8)
                t0 = sc * 512 + tt * 128
                prog.gdma(gps, h2tm_hbm[t0:t0 + 128, :],
                          h2tm[:, n % 2].rearrange("p k f -> p (k f)"), "p4_h2w")

    return dict(f_sync=f_sync, f_scalar=f_scalar, f_pe=f_pe, f_dve=f_dve, f_gps=f_gps)



ENGSECS = [("sync", "f_sync"), ("gpsimd", "f_gps"), ("scalar", "f_scalar"),
           ("tensor", "f_pe"), ("vector", "f_dve")]


def emit_phase_seq(prog, io, phase_list):
    barrier = {}
    for prefix, emit_fn in phase_list:
        prog.open_phase()
        fns = emit_fn(prog, io)
        snapshot = dict(barrier)
        with prog.nc.Block() as blk:
            for eng_attr, key in ENGSECS:
                f = fns.get(key) or fns.get({"f_scalar": "f_act"}.get(key, key))
                def wrapped(eng, f=f, snap=snapshot):
                    for nm, cnt in snap.items():
                        if cnt > 0:
                            eng.wait_ge(prog.sems[nm], cnt)
                    if f is not None:
                        f(eng)
                getattr(blk, eng_attr)(wrapped)
        prog.close_phase()
        barrier = {nm: c for nm, c in prog.counts.items() if c > 0}
    return barrier


def build_p12_test(consts):
    prog = Prog()
    nc = prog.nc
    io = declare_common(prog)
    for nm in ("xT", "wqT", "wkT", "wvT", "woT"):
        io[nm] = nc.dram_tensor(nm, [C, C] if nm != "xT" else [C, T], F32R,
                                kind="ExternalInput").ap()
    io["gnw_in"] = nc.dram_tensor("gnw_in", [64, 16], F32, kind="ExternalInput").ap()
    io["gnb_in"] = nc.dram_tensor("gnb_in", [64, 16], F32, kind="ExternalInput").ap()
    io["wqkv"] = nc.dram_tensor("wqkv", [C, 3 * C], F32R, kind="ExternalInput").ap()
    io["lngb_in"] = nc.dram_tensor("lngb_in", [128, 16], F32R, kind="ExternalInput").ap()
    io["ret_hbm"] = nc.dram_tensor("ret_hbm", [C, T], F32R, kind="Internal").ap()
    io["x1_hbm"] = nc.dram_tensor("x1_hbm", [C, T], F32R, kind="ExternalOutput").ap()
    io["xm_hbm"] = nc.dram_tensor("xm_hbm", [C, T], F32R, kind="ExternalOutput").ap()
    io["k_hbm"] = nc.dram_tensor("k_hbm", [H, 64, T], F32R, kind="ExternalOutput").ap()
    io["q_hbm"] = nc.dram_tensor("q_hbm", [H, 64, T], F32R, kind="ExternalOutput").ap()
    io["v_hbm"] = nc.dram_tensor("v_hbm", [T, C], F32R, kind="ExternalOutput").ap()
    io["gs_f"] = consts["ret_gs_f"]

    with prog.nc.Block() as blk0:
        emit_const_loads(prog, blk0, io)
    emit_phase_seq(prog, io, [("p1", emit_p1_retention), ("p2", emit_p2_qkv)])
    with prog.nc.Block() as blk2:
        def f_end(gps):
            for nm in ("p2_x1w", "p2_xmw", "p2_kqw", "p2_vw"):
                gps.wait_ge(prog.sems[nm], prog.counts[nm])
        blk2.gpsimd(f_end)
    prog.close()
    return prog



def declare_l2(prog):
    nc = prog.nc
    sb = prog.sb
    io = {}
    cdefs = [
        ("c_eye", [128, 128], F32R, "eye128", "eyeb"),
        ("c_triin", [128, 128], F32R, "tri_incl", "triin"),
        ("c_triex", [128, 128], F32R, "tri_excl", "triex"),
        ("c_ones", [128, 1], F32R, "ones128", "onesb"),
        ("c_onesrow", [1, 128], F32R, "onesrow", "onesrow"),
        ("c_iota8", [128, 8], F32, "iota8", "iota8b"),
        ("c_iota8p", [128, 8], F32, "iota8p", "iota8pb"),
        ("c_tok0", [128, 32], F32, "tok0", "tok0b"),
        ("c_eyef", [128, 128], F32, "eye128", "eyef"),
    ]
    io["_cdefs"] = cdefs
    for nm, shape, dt_, ckey, sbkey in cdefs:
        io[nm] = nc.dram_tensor(nm, shape, dt_, kind="ExternalInput").ap()
        io[sbkey] = sb("cb2_" + sbkey, shape, dt_)
    io["pp"] = prog.ps("pp", [128, 6, 512], F32)
    io["ppr"] = prog.ps("ppr", [128, 2, 512], F32R)
    io["lgT_all"] = nc.dram_tensor("lgT_all", [4 * 8, T], F32R, kind="ExternalInput").ap()
    io["h2tm_all"] = nc.dram_tensor("h2tm_all", [N_TOK, C], F32R, kind="ExternalInput").ap()
    io["w1"] = nc.dram_tensor("w1", [C, 4 * C], F32R, kind="ExternalInput").ap()
    io["w2"] = nc.dram_tensor("w2", [4 * C, C], F32R, kind="ExternalInput").ap()
    io["b1_in"] = nc.dram_tensor("b1_in", [128, 32], F32, kind="ExternalInput").ap()
    io["b2row"] = nc.dram_tensor("b2row", [1, C], F32R, kind="ExternalInput").ap()
    io["emeta"] = nc.dram_tensor("emeta", [128, 1], F32, kind="ExternalInput").ap()
    io["exp_out"] = nc.dram_tensor("exp_out", [CAP, C], F32, kind="ExternalOutput").ap()
    io["trow_hbm"] = nc.dram_tensor("trow_hbm", [4, 128, 64], F32, kind="ExternalOutput").ap()
    io["tgate_hbm"] = nc.dram_tensor("tgate_hbm", [4, 128, 64], F32, kind="ExternalOutput").ap()
    io["gathtab"] = nc.dram_tensor("gathtab", [4097, 1], F32, kind="Internal").ap()
    io["tsd_d"] = nc.dram_tensor("tsd_d", [4, 256], F32R, kind="Internal").ap()
    io["toff_d"] = nc.dram_tensor("toff_d", [128, 8], F32R, kind="Internal").ap()
    io["ein_hbm"] = nc.dram_tensor("ein_hbm", [C, CAP], F32R, kind="Internal").ap()
    return io


def emit_l2_consts(prog, blk, io):
    def f_sync(sync):
        for nm, shape, dt_, ckey, sbkey in io["_cdefs"]:
            prog.gdma(sync, io[sbkey][:], io[nm], "l2_c")
        sync.wait_ge(prog.sem("l2_c"), prog.counts["l2_c"])
    blk.sync(f_sync)


def l2_const_in_map(consts):
    return {
        "c_eye": consts["eye128"], "c_triin": consts["tri_incl"],
        "c_triex": consts["tri_excl"], "c_ones": consts["ones128"],
        "c_onesrow": consts["onesrow"][:, 0:128],
        "c_iota8": consts["iota8"], "c_iota8p": consts["iota8p"],
        "c_tok0": consts["tok0"], "c_eyef": consts["eye128"],
    }


def emit_l2_routing(prog, io):
    sb = prog.sb
    lgt = sb("r_lgt", [8, 4, T], F32R)
    lgm = sb("r_lgm", [128, 4, 32, 8], F32)
    m_all = sb("r_m", [128, 4, 32, 8], F32R)
    m1 = sb("r_m1", [128, 4, 32, 8], F32)
    m2 = sb("r_m2", [128, 4, 32, 8], F32)
    prb = sb("r_p", [128, 4, 32, 8], F32)
    dbb = sb("r_d", [128, 4, 32, 8], F32)
    tmp8 = sb("r_t8", [128, 32, 8], F32)
    mx1 = sb("r_mx1", [128, 4, 32], F32)
    idx1 = sb("r_i1", [128, 4, 32], F32)
    idx2 = sb("r_i2", [128, 4, 32], F32)
    gate1 = sb("r_g1", [128, 4, 32], F32)
    gate2 = sb("r_g2", [128, 4, 32], F32)
    posge = sb("r_pos", [128, 4, 256], F32)
    sumB = sb("r_sum", [128, 4, 32], F32)
    g1B = sb("r_g1b", [128, 4, 32], F32)
    g2B = sb("r_g2b", [128, 4, 32], F32)
    tmpB = sb("r_tmpB", [128, 4, 32, 8], F32)
    kpB = sb("r_kpB", [128, 4, 256], F32)
    pos1B = sb("r_p1", [128, 4, 32], F32)
    pos2B = sb("r_p2", [128, 4, 32], F32)
    kp1B = sb("r_kp1", [128, 4, 32], F32)
    kp2B = sb("r_kp2", [128, 4, 32], F32)
    t32B = sb("r_t32", [128, 4, 32], F32)
    u32B = sb("r_u32", [128, 4, 32], F32)
    w32B = sb("r_w32", [128, 4, 32], F32)
    trow = sb("r_trow", [128, 4, 32, 2], F32)
    tgate = sb("r_tg", [128, 4, 32, 2], F32)
    offs = sb("r_off", [128, 4, 32], F32)
    offi = sb("r_offi", [128, 4, 32], I32)
    tokf = sb("r_tokf", [128, 4, 32], F32)
    tsall = sb("r_ts", [1, 4, 256], F32R)
    tspos = sb("r_tsp", [128, 256], F32R)
    toffx = sb("r_toffx", [128, 8], F32)
    toffrow = sb("r_toffr", [1, 4, 256], F32R)
    zbuf = sb("r_z", [128, 33], F32)
    emetab = sb("r_em", [128, 1], F32)

    eyeb, triin, triex = io["eyeb"], io["triin"], io["triex"]
    onesb, onesrow = io["onesb"], io["onesrow"]
    iota8b, iota8pb, tok0b = io["iota8b"], io["iota8pb"], io["tok0b"]
    pp, ppr = io["pp"], io["ppr"]

    names = ["ld", "tp", "tpe", "top", "ex", "ex2", "msum", "msf", "tse",
             "tsd", "tri", "trif", "toe", "tod", "pos", "posf", "pose",
             "sl", "zf", "sc", "tw", "em"]
    s = {n: prog.sem("l2r_" + n) for n in names}

    def f_sync(sync):
        prog.gdma(sync, emetab[:], io["emeta"], "l2r_em")
        prog.gdma(sync, lgt[:], io["lgT_all"].rearrange("(s e) t -> e s t", e=8), "l2r_ld")

    def f_pe(pe):
        pe.wait_ge(s["ld"], 16)
        for n in range(128):
            sq, tt = divmod(n, 32)
            if n >= 2:
                pe.wait_ge(s["tpe"], n - 1)
            pe.transpose(ppr[:, n % 2, 0:8], lgt[0:8, sq, tt * 128:(tt + 1) * 128],
                         eyeb[0:8, 0:8]).then_inc(s["tp"], 1)
        for sq in range(4):
            pe.wait_ge(s["top"], sq + 1)
            pe.matmul(pp[0:1, 0, sq * 128:sq * 128 + 256] if False else
                      pp[0:1, 0, 0:256] if sq == 0 else
                      pp[0:1, 1, 0:256] if sq == 1 else
                      pp[0:1, 2, 0:256] if sq == 2 else pp[0:1, 3, 0:256],
                      onesb[:, 0:1], m_all[:, sq], start=True, stop=True)
        pe.matmul(pp[:, 4, :], eyeb[:], m_all[:, 0:2].rearrange("p s a e -> p (s a e)"),
                  start=True, stop=True).then_inc(s["msf"], 1)
        pe.wait_ge(prog.sem("l2r_tsd"), 32)
        pe.matmul(pp[:, 0, 0:256], triex[:], tspos[:], start=True, stop=True)
        pe.matmul(pp[:, 4, :], eyeb[:], m_all[:, 0:2].rearrange("p s a e -> p (s a e)"),
                  start=True, stop=True).then_inc(s["trif"], 1)
        pe.wait_ge(prog.sem("l2r_tod"), 80)
        for sq in range(4):
            pe.matmul(pp[:, sq, 0:256], triin[:], m_all[:, sq], start=True, stop=False)
            pe.matmul(pp[:, sq, 0:256], onesrow[0:1, :], toffrow[:, sq],
                      start=False, stop=True)
        pe.matmul(pp[:, 4, :], eyeb[:], m_all[:, 0:2].rearrange("p s a e -> p (s a e)"),
                  start=True, stop=True).then_inc(s["posf"], 1)

    def f_dve(dve):
        for n in range(128):
            sq, tt = divmod(n, 32)
            dve.wait_ge(s["tp"], n + 1)
            dve.tensor_copy(lgm[:, sq, tt], ppr[:, n % 2, 0:8]).then_inc(s["tpe"], 1)
        for step in range(7):
            for sq in range(4):
                L = lgm[:, sq]
                if step == 0:
                    dve.tensor_reduce(mx1[:, sq, :], L, axis=mybir.AxisListType.X,
                                      op=ALU.max)
                elif step == 1:
                    mb = mx1[:, sq].unsqueeze(2).broadcast_to([128, 32, 8])
                    dve.tensor_tensor(m1[:, sq], L, mb, ALU.is_equal)
                elif step == 2:
                    ib = iota8pb[:].unsqueeze(1).broadcast_to([128, 32, 8])
                    dve.scalar_tensor_tensor(tmp8[:], m1[:, sq], -1000.0, ib,
                                             op0=ALU.mult, op1=ALU.add)
                    dve.tensor_reduce(idx1[:, sq, :], tmp8[:], axis=mybir.AxisListType.X,
                                      op=ALU.min)
                elif step == 3:
                    ixb = idx1[:, sq].unsqueeze(2).broadcast_to([128, 32, 8])
                    iob = iota8b[:].unsqueeze(1).broadcast_to([128, 32, 8])
                    dve.tensor_tensor(m1[:, sq], iob, ixb, ALU.is_equal)
                    dve.scalar_tensor_tensor(tmp8[:], m1[:, sq], -1e30, L,
                                             op0=ALU.mult, op1=ALU.add)
                    dve.tensor_reduce(mx1[:, sq, :], tmp8[:], axis=mybir.AxisListType.X,
                                      op=ALU.max)
                    mb = mx1[:, sq].unsqueeze(2).broadcast_to([128, 32, 8])
                    dve.tensor_tensor(m2[:, sq], tmp8[:], mb, ALU.is_equal)
                elif step == 4:
                    ib = iota8pb[:].unsqueeze(1).broadcast_to([128, 32, 8])
                    dve.scalar_tensor_tensor(tmp8[:], m2[:, sq], -1000.0, ib,
                                             op0=ALU.mult, op1=ALU.add)
                    dve.tensor_reduce(idx2[:, sq, :], tmp8[:], axis=mybir.AxisListType.X,
                                      op=ALU.min)
                elif step == 5:
                    ixb = idx2[:, sq].unsqueeze(2).broadcast_to([128, 32, 8])
                    iob = iota8b[:].unsqueeze(1).broadcast_to([128, 32, 8])
                    dve.tensor_tensor(m2[:, sq], iob, ixb, ALU.is_equal)
                    dve.tensor_tensor(m_all[:, sq], m1[:, sq], m2[:, sq], ALU.add)
                elif step == 6:
                    dve.tensor_reduce(mx1[:, sq, :], L, axis=mybir.AxisListType.X,
                                      op=ALU.max)
                    mb = mx1[:, sq].unsqueeze(2).broadcast_to([128, 32, 8])
                    ins = dve.tensor_tensor(dbb[:, sq], L, mb, ALU.subtract)
                    if sq == 3:
                        ins.then_inc(s["ex"], 1)
        dve.engine_nop().then_inc(s["top"], 4)
        padw = tmp8
        def pad():
            dve.tensor_copy(padw[:], prb[:, 0])
        dve.wait_ge(s["ex2"], 1)
        dve.tensor_reduce(sumB[:], prb[:], axis=mybir.AxisListType.X, op=ALU.add)
        pad()
        dve.reciprocal(sumB[:], sumB[:])
        dve.tensor_tensor(tmpB[:], prb[:], m1[:], ALU.mult)
        pad()
        dve.tensor_reduce(g1B[:], tmpB[:], axis=mybir.AxisListType.X, op=ALU.add)
        dve.tensor_tensor(tmpB[:], prb[:], m2[:], ALU.mult)
        dve.tensor_tensor(gate1[:], g1B[:], sumB[:], ALU.mult)
        dve.tensor_reduce(g2B[:], tmpB[:], axis=mybir.AxisListType.X, op=ALU.add)
        pad()
        dve.tensor_tensor(gate2[:], g2B[:], sumB[:], ALU.mult)
        dve.wait_ge(s["msf"], 1)
        for sq in range(4):
            src = (pp[0:1, 0, 0:256] if sq == 0 else pp[0:1, 1, 0:256]
                   if sq == 1 else pp[0:1, 2, 0:256] if sq == 2 else pp[0:1, 3, 0:256])
            dve.tensor_copy(tsall[:, sq], src)
        dve.memset(zbuf[:], 0.0)
        dve.tensor_scalar_mul(tspos[:], io["onesb"][:, 0:1].broadcast_to([128, 256]), 0.0)
        dve.engine_nop().then_inc(s["tse"], 1)
        dve.wait_ge(s["trif"], 1)
        dve.tensor_copy(toffx[:], pp[0:128, 0, 0:8]).then_inc(s["toe"], 1)
        dve.wait_ge(s["posf"], 1)
        for sq in range(4):
            dve.tensor_copy(posge[:, sq], pp[:, sq, 0:256])
        dve.engine_nop().then_inc(s["pose"], 4)
        PB = posge[:].rearrange("p s (a e) -> p s a e", e=8)
        dve.tensor_scalar_min(kpB[:], posge[:], float(CAP))
        pad()
        dve.tensor_tensor(kpB[:], kpB[:], posge[:], ALU.is_equal)
        pad()
        dve.tensor_tensor(kpB[:], kpB[:], m_all[:].bitcast(F32).rearrange(
            "p s a e -> p s (a e)"), ALU.mult)
        pad()
        KB = kpB[:].rearrange("p s (a e) -> p s a e", e=8)
        for j, (mj, posj, kpj, idxj, gatej) in enumerate(
                ((m1, pos1B, kp1B, idx1, gate1), (m2, pos2B, kp2B, idx2, gate2))):
            dve.tensor_tensor(tmpB[:], PB, mj[:], ALU.mult)
            pad()
            dve.tensor_reduce(posj[:], tmpB[:], axis=mybir.AxisListType.X, op=ALU.add)
            dve.tensor_tensor(tmpB[:], KB, mj[:], ALU.mult)
            pad()
            dve.tensor_reduce(kpj[:], tmpB[:], axis=mybir.AxisListType.X, op=ALU.add)
            pad()
            dve.tensor_tensor(t32B[:], kpj[:], posj[:], ALU.mult)
            pad()
            dve.scalar_tensor_tensor(t32B[:], kpj[:], -float(CAP), t32B[:],
                                     op0=ALU.mult, op1=ALU.add)
            pad()
            dve.tensor_scalar_add(t32B[:], t32B[:], float(CAP) - 1.0)
            pad()
            dve.scalar_tensor_tensor(trow[:, :, :, j], idxj[:], float(CAP),
                                     t32B[:], op0=ALU.mult, op1=ALU.add)
            dve.tensor_tensor(tgate[:, :, :, j], gatej[:], kpj[:], ALU.mult)
        emb = emetab[:].unsqueeze(2).broadcast_to([128, 4, 32])
        dve.tensor_tensor(t32B[:], idx1[:], emb, ALU.is_equal)
        pad()
        dve.tensor_tensor(t32B[:], t32B[:], kp1B[:], ALU.mult)
        dve.tensor_tensor(u32B[:], idx2[:], emb, ALU.is_equal)
        pad()
        dve.tensor_tensor(u32B[:], u32B[:], kp2B[:], ALU.mult)
        pad()
        dve.tensor_tensor(offs[:], t32B[:], pos1B[:], ALU.mult)
        dve.tensor_tensor(w32B[:], u32B[:], pos2B[:], ALU.mult)
        pad()
        dve.tensor_tensor(offs[:], offs[:], w32B[:], ALU.add)
        dve.tensor_tensor(t32B[:], t32B[:], u32B[:], ALU.add)
        pad()
        dve.tensor_tensor(offs[:], offs[:], t32B[:], ALU.subtract)
        pad()
        dve.scalar_tensor_tensor(offs[:], t32B[:], -float(CAP), offs[:],
                                 op0=ALU.mult, op1=ALU.add)
        pad()
        dve.tensor_scalar_add(offs[:], offs[:], float(CAP))
        pad()
        dve.tensor_copy(offi[:], offs[:])
        for sq in range(4):
            dve.tensor_scalar_add(tokf[:, sq], tok0b[:], float(sq * T))
        dve.engine_nop().then_inc(s["sl"], 4)

    def f_scalar(act):
        act.wait_ge(s["ex"], 1)
        act.activation(prb[:], dbb[:], AF.Exp).then_inc(s["ex2"], 1)

    def f_gps(gps):
        gps.wait_ge(s["tse"], 1)
        prog.gdma(gps, io["gathtab"][0:4096].rearrange("(p a) x -> p a x", p=128),
                  zbuf[:, 0:32].unsqueeze(2), "l2r_zf")
        prog.gdma(gps, io["gathtab"][4096:4097], zbuf[0:1, 32:33], "l2r_zf")
        prog.gdma(gps, io["tsd_d"], tsall[:, :, :], "l2r_tsd")
        prog.gdma(gps, tspos[:, 0:8],
                  io["tsd_d"].rearrange("s (q e) -> (s q) e", e=8), "l2r_tsd")
        gps.wait_ge(s["toe"], 1)
        prog.gdma(gps, io["toff_d"].bitcast(F32), toffx[:], "l2r_tod")
        for sq in range(4):
            src = io["toff_d"].rearrange("(s q) e -> s q e", s=4)[sq:sq + 1]
            prog.gdma(gps, toffrow[0:1, sq].rearrange("p (q e) -> p q e", e=8),
                      src, "l2r_tod")
        if "dbg_l2" in io:
            d = io["dbg_l2"]
            gps.wait_ge(s["sl"], 4)
            prog.gdma(gps, d["lgm"], lgm[:].rearrange("p a b c -> p (a b c)"), "l2r_dbg")
            prog.gdma(gps, d["mall"], m_all[:].rearrange("p a b c -> p (a b c)").bitcast(F32), "l2r_dbg")
            prog.gdma(gps, d["idx"], idx1[:].rearrange("p a b -> p (a b)"), "l2r_dbg")
            prog.gdma(gps, d["idx2"], idx2[:].rearrange("p a b -> p (a b)"), "l2r_dbg")
            prog.gdma(gps, d["posge"], posge[:].rearrange("p a b -> p (a b)"), "l2r_dbg")
            prog.gdma(gps, d["toffx"], toffx[:], "l2r_dbg")
            prog.gdma(gps, d["tsp"], tspos[:].bitcast(F32), "l2r_dbg")
            prog.gdma(gps, d["g1"], gate1[:].rearrange("p a b -> p (a b)"), "l2r_dbg")
            prog.gdma(gps, d["offs"], offs[:].rearrange("p a b -> p (a b)"), "l2r_dbg")
        gps.wait_ge(prog.sem("l2r_zf"), 32)
        for sq in range(4):
            gps.wait_ge(s["sl"], 4)
            for tt in range(32):
                ins = gps.indirect_dma_start(
                    out=io["gathtab"],
                    out_offset=IndirectOffsetOnAxis(offi[:, sq, tt:tt + 1], 0),
                    in_=tokf[:, sq, tt:tt + 1], in_offset=None)
                prog.inc(ins, "l2r_sc", dma=True)
            prog.gdma(gps, io["trow_hbm"][sq], trow[:, sq].rearrange("p a j -> p (a j)"),
                      "l2r_tw")
            prog.gdma(gps, io["tgate_hbm"][sq], tgate[:, sq].rearrange("p a j -> p (a j)"),
                      "l2r_tw")

    return dict(f_sync=f_sync, f_pe=f_pe, f_dve=f_dve, f_scalar=f_scalar, f_gps=f_gps)

def emit_l2_f0(prog, io):
    sb = prog.sb
    gtb2 = sb("f0_gtb2", [32, 128], F32)
    gtb = sb("f0_gtb", [128, 32], F32)
    gti = sb("f0_gti", [128, 32], I32)
    padf0 = sb("f0_pad", [128, 512], F32)
    etm = sb("f0_etm", [128, 2, 1024], F32R)
    efm = sb("f0_efm", [128, 8, 512], F32R)
    eyeb, eyef = io["eyeb"], io["eyef"]
    pp, ppr = io["pp"], io["ppr"]
    names = ["gt", "gt2", "gi", "g", "tp", "tpe", "ew"]
    s = {n: prog.sem("l2f0_" + n) for n in names}

    def f_sync(sync):
        sync.wait_ge(prog.sem("l2r_sc"), prog.counts["l2r_sc"])
        sync.wait_ge(prog.sem("l2r_zf"), prog.counts["l2r_zf"])
        prog.gdma(sync, gtb2[:].unsqueeze(2),
                  io["gathtab"][0:4096].rearrange("(a p) x -> a p x", p=128), "l2f0_gt")

    def f_gps(gps):
        gps.wait_ge(s["gi"], 1)
        for n in range(32):
            if n >= 2:
                gps.wait_ge(s["tp"], 8 * (n - 1))
            ins = gps.indirect_dma_start(
                out=etm[:, n % 2], out_offset=None,
                in_=io["h2tm_all"], in_offset=IndirectOffsetOnAxis(gti[:, n:n + 1], 0))
            prog.inc(ins, "l2f0_g", dma=True)

    def f_pe(pe):
        pe.wait_ge(prog.sem("l2f0_gt"), 16)
        pe.transpose(pp[:, 3, 0:32], gtb2[:], eyef[0:32, 0:32]).then_inc(s["gt2"], 1)
        for n in range(32):
            pe.wait_ge(s["g"], 16 * (n + 1))
            for k in range(8):
                ntp = n * 8 + k + 1
                if ntp >= 3:
                    pe.wait_ge(s["tpe"], ntp - 2)
                pe.transpose(ppr[:, (ntp - 1) % 2, 0:128],
                             etm[:, n % 2, k * 128:(k + 1) * 128], eyeb[:]
                             ).then_inc(s["tp"], 1)

    def f_dve(dve):
        dve.wait_ge(s["gt2"], 1)
        dve.tensor_copy(gtb[:], pp[:, 3, 0:32])
        dve.tensor_copy(padf0[:], pp[:, 3, 0:512])
        dve.tensor_copy(gti[:], gtb[:]).then_inc(s["gi"], 1)
        for n in range(32):
            TT, q = divmod(n, 4)
            if q == 0 and TT >= 1:
                dve.wait_ge(s["ew"], 16 * TT)
            for k in range(8):
                ntp = n * 8 + k + 1
                dve.wait_ge(s["tp"], ntp)
                dve.tensor_copy(efm[:, k, q * 128:(q + 1) * 128],
                                ppr[:, (ntp - 1) % 2, 0:128]).then_inc(s["tpe"], 1)

    def f_scalar(act):
        for TT in range(8):
            act.wait_ge(s["tpe"], (TT + 1) * 32)
            dst = io["ein_hbm"].rearrange("(k p) c -> p k c", p=128)[:, :, TT * 512:(TT + 1) * 512]
            prog.gdma(act, dst, efm[:], "l2f0_ew")

    return dict(f_sync=f_sync, f_gps=f_gps, f_pe=f_pe, f_dve=f_dve, f_scalar=f_scalar)


def emit_l2_f1(prog, io):
    sb = prog.sb
    w1q = sb("f1_w1", [128, 8, 1024], F32R)
    w2q = sb("f1_w2", [128, 8, 1024], F32R)
    einb = sb("f1_ein", [128, 2, 8, 512], F32R)
    hmq = sb("f1_hm", [128, 8, 512], F32R)
    tbuf = sb("f1_tb", [128, 2, 512], F32)
    ostg = sb("f1_os", [128, 4, 512], F32)
    b1b = sb("f1_b1", [128, 32], F32)
    b2bc = sb("f1_b2", [128, 2, 512], F32)
    b2r = sb("f1_b2r", [1, 1024], F32R)
    eyeb, onesrow = io["eyeb"], io["onesrow"]
    pp, ppr = io["pp"], io["ppr"]
    names = ["w", "eld", "h1", "b1a", "gel", "w2", "oev", "ow", "b2e", "b2f"]
    s = {n: prog.sem("l2f1_" + n) for n in names}

    def f_sync(sync):
        prog.gdma(sync, b1b[:], io["b1_in"], "l2f1_w")
        prog.gdma(sync, b2r[:], io["b2row"], "l2f1_w")
        for p in range(4):
            if p > 0:
                sync.wait_ge(s["w2"], p * 64)
            prog.gdma(sync, w1q[:],
                      io["w1"][:, p * 1024:(p + 1) * 1024].rearrange("(k p) f -> p k f", p=128),
                      "l2f1_w")
            prog.gdma(sync, w2q[:],
                      io["w2"][p * 1024:(p + 1) * 1024, :].rearrange("(k p) f -> p k f", p=128),
                      "l2f1_w")
            for TT in range(8):
                n = p * 8 + TT
                if n >= 2:
                    sync.wait_ge(s["h1"], (n - 1) * 8)
                src = io["ein_hbm"].rearrange("(k p) c -> p k c", p=128)[:, :, TT * 512:(TT + 1) * 512]
                prog.gdma(sync, einb[:, n % 2], src, "l2f1_eld")

    def f_pe(pe):
        pe.wait_ge(s["w"], 32)
        pe.matmul(pp[:, 5, :], onesrow[0:1, :], b2r[:, 0:512], start=True, stop=True)
        pe.matmul(pp[:, 4, :], onesrow[0:1, :], b2r[:, 512:1024], start=True, stop=True)
        pe.matmul(pp[0:64, 3, :], eyeb[0:64, 0:64], w1q[0:64, 0, 0:512],
                  start=True, stop=True).then_inc(s["b2f"], 1)
        for p in range(4):
            pe.wait_ge(s["w"], 32 + 32 * (p + 1))
            if p == 0:
                pe.wait_ge(s["b2e"], 1)
            for TT in range(8):
                n = p * 8 + TT
                pe.wait_ge(s["eld"], 16 * (n + 1))
                for fc in range(8):
                    nh = n * 8 + fc
                    if nh >= 2:
                        pe.wait_ge(s["b1a"], nh - 1)
                    for kc in range(8):
                        ins = pe.matmul(pp[:, 4 + fc % 2, :],
                                        w1q[:, kc, fc * 128:(fc + 1) * 128],
                                        einb[:, n % 2, kc], start=(kc == 0), stop=(kc == 7))
                    ins.then_inc(s["h1"], 1)
                for tq in range(4):
                    for dh in range(2):
                        no = (n * 4 + tq) * 2 + dh
                        bank = no % 4
                        if no >= 4:
                            pe.wait_ge(s["oev"], no - 3)
                        for kc in range(8):
                            pe.wait_ge(s["gel"], n * 8 + kc + 1)
                            ins = pe.matmul(pp[:, bank, :],
                                            hmq[:, kc, tq * 128:(tq + 1) * 128],
                                            w2q[:, kc, dh * 512:(dh + 1) * 512],
                                            start=(kc == 0), stop=(kc == 7))
                        ins.then_inc(s["w2"], 1)

    def f_dve(dve):
        dve.wait_ge(s["b2f"], 1)
        dve.tensor_copy(b2bc[:, 0], pp[:, 5, :])
        dve.tensor_copy(b2bc[:, 1], pp[:, 4, :]).then_inc(s["b2e"], 1)
        for p in range(4):
            for TT in range(8):
                n = p * 8 + TT
                for fc in range(8):
                    nh = n * 8 + fc
                    dve.wait_ge(s["h1"], nh + 1)
                    if nh >= 2:
                        dve.wait_ge(s["gel"], nh - 1)
                    bb = b1b[:, p * 8 + fc:p * 8 + fc + 1].broadcast_to([128, 512])
                    dve.tensor_tensor(tbuf[:, nh % 2], pp[:, 4 + fc % 2, :], bb, ALU.add
                                      ).then_inc(s["b1a"], 1)
                for tq in range(4):
                    for dh in range(2):
                        no = (n * 4 + tq) * 2 + dh
                        bank = no % 4
                        dve.wait_ge(s["w2"], no + 1)
                        if no >= 4:
                            dve.wait_ge(s["ow"], 16 * (no - 3))
                        if p == 0:
                            dve.tensor_tensor(ostg[:, no % 4], pp[:, bank, :],
                                              b2bc[:, dh], ALU.add).then_inc(s["oev"], 1)
                        else:
                            dve.tensor_copy(ostg[:, no % 4], pp[:, bank, :]
                                            ).then_inc(s["oev"], 1)

    def f_scalar(act):
        for p in range(4):
            for TT in range(8):
                n = p * 8 + TT
                for fc in range(8):
                    nh = n * 8 + fc
                    act.wait_ge(s["b1a"], nh + 1)
                    if n >= 1 and fc == 0:
                        act.wait_ge(s["w2"], n * 8)
                    act.activation(hmq[:, fc], tbuf[:, nh % 2], AF.Gelu
                                   ).then_inc(s["gel"], 1)

    def f_gps(gps):
        for p in range(4):
            for TT in range(8):
                n = p * 8 + TT
                for tq in range(4):
                    for dh in range(2):
                        no = (n * 4 + tq) * 2 + dh
                        gps.wait_ge(s["oev"], no + 1)
                        t0 = TT * 512 + tq * 128
                        dst = io["exp_out"][t0:t0 + 128, dh * 512:(dh + 1) * 512]
                        kw = {} if p == 0 else {"accum_op": ALU.add}
                        prog.gdma(gps, dst, ostg[:, no % 4], "l2f1_ow", **kw)

    return dict(f_sync=f_sync, f_pe=f_pe, f_dve=f_dve, f_scalar=f_scalar, f_gps=f_gps)


def build_l2_test(consts):
    prog = Prog()
    io = declare_l2(prog)
    with prog.nc.Block() as blk0:
        emit_l2_consts(prog, blk0, io)
    emit_phase_seq(prog, io, [("l2r", emit_l2_routing), ("l2f0", emit_l2_f0),
                              ("l2f1", emit_l2_f1)])
    with prog.nc.Block() as blk2:
        def f_end(gps):
            for nm in ("l2f1_ow", "l2r_tw"):
                gps.wait_ge(prog.sems[nm], prog.counts[nm])
        blk2.gpsimd(f_end)
    prog.close()
    return prog



def build_l3(consts):
    prog = Prog()
    nc = prog.nc
    sb = prog.sb
    eyef_d = nc.dram_tensor("c_eyef", [128, 128], F32, kind="ExternalInput").ap()
    x1_d = nc.dram_tensor("x1", [C, T], F32R, kind="ExternalInput").ap()
    xm_d = nc.dram_tensor("xm", [C, T], F32R, kind="ExternalInput").ap()
    trow_d = nc.dram_tensor("trow", [128, 64], F32, kind="ExternalInput").ap()
    tgate_d = nc.dram_tensor("tgate", [128, 64], F32, kind="ExternalInput").ap()
    eo_d = nc.dram_tensor("eo_all", [8 * CAP, C], F32, kind="ExternalInput").ap()
    out_d = nc.dram_tensor("out_hbm", [T, C], F32, kind="ExternalOutput").ap()

    eyef = sb("l3_eyef", [128, 128], F32)
    trowb = sb("l3_trow", [128, 64], F32)
    tgateb = sb("l3_tg", [128, 64], F32)
    trowi = sb("l3_ti", [128, 64], I32)
    gbuf = sb("l3_g", [128, 2, 2, 1024], F32)
    mbuf = sb("l3_m", [128, 2, 1024], F32)
    x1t = sb("l3_x1", [128, 2, 8, 128], F32R)
    xmt = sb("l3_xm", [128, 2, 8, 128], F32R)
    s12 = sb("l3_s12", [128, 2, 8, 128], F32)
    outt = sb("l3_o", [128, 2, 1024], F32)
    pp = prog.ps("pp", [128, 6, 512], F32)

    names = ["c", "t", "ti", "g", "mo", "ld", "sm", "tp", "ad", "ow"]
    s = {n: prog.sem("l3_" + n) for n in names}

    with prog.nc.Block() as blk:
        def f_sync(sync):
            prog.gdma(sync, eyef[:], eyef_d, "l3_c")
            prog.gdma(sync, trowb[:], trow_d, "l3_c")
            prog.gdma(sync, tgateb[:], tgate_d, "l3_c")
            for tt in range(32):
                ts = slice(tt * 128, (tt + 1) * 128)
                if tt >= 2:
                    sync.wait_ge(s["sm"], tt - 1)
                prog.gdma(sync, x1t[:, tt % 2],
                          x1_d.rearrange("(k p) t -> p k t", p=128)[:, :, ts], "l3_ld")
                prog.gdma(sync, xmt[:, tt % 2],
                          xm_d.rearrange("(k p) t -> p k t", p=128)[:, :, ts], "l3_ld")

        def f_gps(gps):
            gps.wait_ge(s["ti"], 1)
            for tt in range(32):
                for j in range(2):
                    if tt >= 2 and j == 0:
                        gps.wait_ge(s["mo"], tt - 1)
                    ins = gps.indirect_dma_start(
                        out=gbuf[:, tt % 2, j], out_offset=None,
                        in_=eo_d,
                        in_offset=IndirectOffsetOnAxis(trowi[:, tt * 2 + j:tt * 2 + j + 1], 0))
                    prog.inc(ins, "l3_g", dma=True)
                gps.wait_ge(s["ad"], tt + 1)
                prog.gdma(gps, out_d[tt * 128:(tt + 1) * 128, :], outt[:, tt % 2], "l3_ow")

        def f_pe(pe):
            for tt in range(32):
                pe.wait_ge(s["sm"], tt + 1)
                if tt >= 2:
                    pe.wait_ge(s["ad"], tt - 1)
                for k in range(8):
                    b = (tt % 2) * 2 + k // 4
                    pe.transpose(pp[:, b, (k % 4) * 128:(k % 4 + 1) * 128],
                                 s12[:, tt % 2, k], eyef[:]).then_inc(s["tp"], 1)

        def f_dve(dve):
            dve.wait_ge(prog.sem("l3_c"), 48)
            dve.tensor_copy(trowi[:], trowb[:]).then_inc(s["ti"], 1)
            for tt in range(32):
                dve.wait_ge(s["ld"], 32 * (tt + 1))
                dve.tensor_tensor(s12[:, tt % 2].bitcast(F32), x1t[:, tt % 2].bitcast(F32),
                                  xmt[:, tt % 2].bitcast(F32), ALU.add).then_inc(s["sm"], 1)
                dve.wait_ge(s["g"], 32 * (tt + 1))
                g0 = tgateb[:, tt * 2:tt * 2 + 1].broadcast_to([128, 1024])
                g1 = tgateb[:, tt * 2 + 1:tt * 2 + 2].broadcast_to([128, 1024])
                dve.tensor_tensor(mbuf[:, tt % 2], gbuf[:, tt % 2, 0], g0, ALU.mult)
                dve.tensor_tensor(gbuf[:, tt % 2, 1], gbuf[:, tt % 2, 1], g1, ALU.mult)
                dve.tensor_tensor(mbuf[:, tt % 2], mbuf[:, tt % 2], gbuf[:, tt % 2, 1],
                                  ALU.add).then_inc(s["mo"], 1)
                dve.wait_ge(s["tp"], 8 * (tt + 1))
                if tt >= 2:
                    dve.wait_ge(s["ow"], 16 * (tt - 1))
                pb = (tt % 2) * 2
                dve.tensor_tensor(outt[:, tt % 2],
                                  pp[:, pb:pb + 2, :].rearrange("p a b -> p (a b)"),
                                  mbuf[:, tt % 2], ALU.add).then_inc(s["ad"], 1)

        blk.sync(f_sync)
        blk.gpsimd(f_gps)
        blk.tensor(f_pe)
        blk.vector(f_dve)
    with prog.nc.Block() as blk2:
        def f_end(gps):
            gps.wait_ge(prog.sems["l3_ow"], prog.counts["l3_ow"])
        blk2.gpsimd(f_end)
    prog.close()
    return prog


def build_l1(consts):
    prog = Prog()
    nc = prog.nc
    io = declare_common(prog)
    for nm in ("wqT", "wkT", "wvT", "woT"):
        io[nm] = nc.dram_tensor(nm, [C, C], F32R, kind="ExternalInput").ap()
    io["xT"] = nc.dram_tensor("xT", [C, T], F32R, kind="ExternalInput").ap()
    io["gnw_in"] = nc.dram_tensor("gnw_in", [64, 16], F32, kind="ExternalInput").ap()
    io["gnb_in"] = nc.dram_tensor("gnb_in", [64, 16], F32, kind="ExternalInput").ap()
    io["wqkv"] = nc.dram_tensor("wqkv", [C, 3 * C], F32R, kind="ExternalInput").ap()
    io["lngb_in"] = nc.dram_tensor("lngb_in", [128, 16], F32R, kind="ExternalInput").ap()
    io["wproj"] = nc.dram_tensor("wproj", [C, C], F32R, kind="ExternalInput").ap()
    io["wgate"] = nc.dram_tensor("wgate", [C, 8], F32R, kind="ExternalInput").ap()
    io["lngb2_in"] = nc.dram_tensor("lngb2_in", [128, 16], F32, kind="ExternalInput").ap()
    io["ret_hbm"] = nc.dram_tensor("ret_hbm", [C, T], F32R, kind="Internal").ap()
    io["x1_hbm"] = nc.dram_tensor("x1_hbm", [C, T], F32R, kind="ExternalOutput").ap()
    io["xm_hbm"] = nc.dram_tensor("xm_hbm", [C, T], F32R, kind="ExternalOutput").ap()
    io["k_hbm"] = nc.dram_tensor("k_hbm", [H, 64, T], F32R, kind="Internal").ap()
    io["q_hbm"] = nc.dram_tensor("q_hbm", [H, 64, T], F32R, kind="Internal").ap()
    io["v_hbm"] = nc.dram_tensor("v_hbm", [T, C], F32R, kind="Internal").ap()
    io["o_hbm"] = nc.dram_tensor("o_hbm", [H, 64, T], F32R, kind="Internal").ap()
    io["lgT_hbm"] = nc.dram_tensor("lgT_hbm", [8, T], F32, kind="ExternalOutput").ap()
    io["h2tm_hbm"] = nc.dram_tensor("h2tm_hbm", [T, C], F32R, kind="ExternalOutput").ap()
    io["gs_f"] = consts["ret_gs_f"]

    with prog.nc.Block() as blk0:
        emit_const_loads(prog, blk0, io)
    emit_phase_seq(prog, io, [("p1", emit_p1_retention), ("p2", emit_p2_qkv),
                              ("p3", emit_p3_attention), ("p4", emit_p4_gate)])
    with prog.nc.Block() as blk2:
        def f_end(gps):
            for nm in ("p2_x1w", "p3_wod", "p4_lgw", "p4_h2w"):
                gps.wait_ge(prog.sems[nm], prog.counts[nm])
        blk2.gpsimd(f_end)
    prog.close()
    return prog


def _l1_in_map(consts, inputs, s):
    qc = [h * 192 + d for h in range(16) for d in range(64)]
    kc = [h * 192 + 64 + d for h in range(16) for d in range(64)]
    vc = [h * 192 + 128 + d for h in range(16) for d in range(64)]
    Wp = np.ascontiguousarray(inputs["L0_ffn_Wqkv"][:, qc + kc + vc])
    g1 = inputs["L0_ffn_ln1_g"].reshape(8, 128).T
    b1 = inputs["L0_ffn_ln1_b"].reshape(8, 128).T
    g2 = inputs["L0_ffn_ln2_g"].reshape(8, 128).T
    b2 = inputs["L0_ffn_ln2_b"].reshape(8, 128).T
    im = const_in_map(consts)
    im.update({
        "xT": np.ascontiguousarray(inputs["L0_attn_x"][s].T),
        "wqT": np.ascontiguousarray(inputs["L0_attn_wq"].T),
        "wkT": np.ascontiguousarray(inputs["L0_attn_wk"].T),
        "wvT": np.ascontiguousarray(inputs["L0_attn_wv"].T),
        "woT": np.ascontiguousarray(inputs["L0_attn_wo"].T),
        "gnw_in": np.ascontiguousarray(inputs["L0_attn_gn_w"].reshape(16, 64).T),
        "gnb_in": np.ascontiguousarray(inputs["L0_attn_gn_b"].reshape(16, 64).T),
        "wqkv": Wp,
        "lngb_in": np.ascontiguousarray(np.concatenate([g1, b1], axis=1)),
        "wproj": np.ascontiguousarray(inputs["L0_ffn_Wproj"]),
        "wgate": np.ascontiguousarray(inputs["L0_ffn_Wgate"]),
        "lngb2_in": np.ascontiguousarray(np.concatenate([g2, b2], axis=1)),
    })
    return im


def kernel(**inputs):
    inputs = {k: np.asarray(v) for k, v in inputs.items()}
    consts = host_constants()

    prog1 = build_l1(consts)
    in_maps1 = [_l1_in_map(consts, inputs, c >> 1) for c in range(NCORES)]
    res1 = run_bass_kernel_spmd(prog1.nc, in_maps1, core_ids=list(range(NCORES)))
    r1 = [res1.results[2 * s] for s in range(4)]

    prog2 = build_l2_test(consts)
    lgT_all = np.ascontiguousarray(np.stack([r1[s]["lgT_hbm"] for s in range(4)]
                                            ).reshape(32, T).astype(np.float32))
    h2tm_all = np.ascontiguousarray(np.concatenate(
        [r1[s]["h2tm_hbm"] for s in range(4)], axis=0).astype(np.float32))
    cim2 = l2_const_in_map(consts)
    in_maps2 = []
    for e in range(NCORES):
        im = dict(cim2)
        im.update({
            "lgT_all": lgT_all, "h2tm_all": h2tm_all,
            "w1": np.ascontiguousarray(inputs["L0_ffn_W1"][e]),
            "w2": np.ascontiguousarray(inputs["L0_ffn_W2"][e]),
            "b1_in": np.ascontiguousarray(inputs["L0_ffn_b1"][e].reshape(32, 128).T),
            "b2row": np.ascontiguousarray(inputs["L0_ffn_b2"][e].reshape(1, C)),
            "emeta": np.full((128, 1), float(e), dtype=np.float32),
        })
        in_maps2.append(im)
    res2 = run_bass_kernel_spmd(prog2.nc, in_maps2, core_ids=list(range(NCORES)))
    eo_all = np.ascontiguousarray(np.concatenate(
        [res2.results[e]["exp_out"] for e in range(NCORES)], axis=0).astype(np.float32))
    trow = res2.results[0]["trow_hbm"]
    tgate = res2.results[0]["tgate_hbm"]

    prog3 = build_l3(consts)
    in_maps3 = []
    for c in range(NCORES):
        s = c >> 1
        in_maps3.append({
            "c_eyef": consts["eye128"],
            "x1": np.ascontiguousarray(r1[s]["x1_hbm"].astype(np.float32)),
            "xm": np.ascontiguousarray(r1[s]["xm_hbm"].astype(np.float32)),
            "trow": np.ascontiguousarray(trow[s].astype(np.float32)),
            "tgate": np.ascontiguousarray(tgate[s].astype(np.float32)),
            "eo_all": eo_all,
        })
    res3 = run_bass_kernel_spmd(prog3.nc, in_maps3, core_ids=list(range(NCORES)))
    out = np.stack([res3.results[2 * s]["out_hbm"] for s in range(4)], axis=0)
    return np.ascontiguousarray(out.astype(np.float32))


# version 03
# speedup vs baseline: 1.0448x; 1.0388x over previous
import math
import numpy as np
from contextlib import ExitStack

import concourse.bass as bass
import concourse.mybir as mybir
from concourse.bass_utils import run_bass_kernel_spmd
from concourse.bass_core_rust import IndirectOffsetOnAxis

F32 = mybir.dt.float32
F32R = mybir.dt.float32r
I32 = mybir.dt.int32
AF = mybir.ActivationFunctionType
ALU = mybir.AluOpType

B, T, C = 4, 4096, 1024
H = 16
HD = 64
NCORES = 8
E = 8
TOPK = 2
CAP = 4096
N_TOK = B * T
EPS = 1e-5
SC = 512
RC = 128
NSC = T // SC
NRC = SC // RC


def host_constants():
    c = {}
    c["eye128"] = np.eye(128, dtype=np.float32)
    jj, tt = np.meshgrid(np.arange(128), np.arange(128), indexing="ij")
    c["tri_incl"] = (jj <= tt).astype(np.float32)
    c["tri_excl"] = (jj < tt).astype(np.float32)
    inv_freq = 1.0 / (10000.0 ** (np.arange(0, HD, 2, dtype=np.float64) / HD))
    ang = np.arange(T, dtype=np.float64)[None, :] * inv_freq[:, None]
    c["rope_cos"] = np.cos(ang).astype(np.float32)
    c["rope_sin"] = np.sin(ang).astype(np.float32)
    lo = math.log(1.0 - 0.999)
    hi = math.log(1.0 - 0.9)
    gam = 1.0 - np.exp(np.linspace(lo, hi, H, dtype=np.float64))
    c["gammas"] = gam.astype(np.float32)
    j = np.arange(RC)[:, None]
    i = np.arange(RC)[None, :]
    dts = []
    for h in range(H):
        dt = np.where(i >= j, gam[h] ** np.maximum(i - j, 0), 0.0)
        dts.append(dt)
    c["ret_dt"] = np.concatenate(dts, axis=1).astype(np.float32)
    a = np.arange(RC)[None, :]
    gcross = np.stack([np.tile(gam[h] ** (a + 1), (64, 1)) for h in range(H)], axis=1)
    c["ret_gcross"] = gcross.astype(np.float32)
    a = np.arange(RC)[:, None]
    kd = np.stack([np.tile(gam[h] ** (RC - 1 - a), (1, 64)) for h in range(H)], axis=1)
    c["ret_kdec3"] = kd.astype(np.float32)
    c["ret_gs_f"] = [float(gam[h] ** RC) for h in range(H)]

    ind = np.zeros((128, 2), dtype=np.float32)
    ind[0:64, 0] = 1.0
    ind[64:128, 1] = 1.0
    c["headsel"] = ind
    c["headselT"] = np.ascontiguousarray(ind.T)
    c["onesrow"] = np.ones((1, 128), dtype=np.float32)
    kk, qq = np.meshgrid(np.arange(128), np.arange(128), indexing="ij")
    c["diagmask"] = (kk <= qq).astype(np.float32)
    c["iota8"] = np.tile(np.arange(8, dtype=np.float32)[None, :], (128, 1))
    c["iota128"] = np.arange(128, dtype=np.float32).reshape(128, 1)
    c["ones128"] = np.ones((128, 1), dtype=np.float32)
    c["onesall"] = np.ones((128, 128), dtype=np.float32)
    c["iota8p"] = c["iota8"] + 1000.0
    pp_, aa_ = np.meshgrid(np.arange(128), np.arange(32), indexing="ij")
    c["tok0"] = (pp_ + 128.0 * aa_).astype(np.float32)
    pp_, ff_ = np.meshgrid(np.arange(128), np.arange(512), indexing="ij")
    dm = np.stack([(pp_ <= ff_ - 128 * r).astype(np.float32) for r in range(4)], axis=1)
    c["diagm"] = dm.reshape(128, 4 * 512)
    return c


class Prog:

    def __init__(self, name="mk"):
        s = bass.Bass("TRN2", target_bir_lowering=False, enable_asserts=False,
                      detect_race_conditions=False)
        self.nc = s
        self.stack = ExitStack()
        self.sems = {}
        self.counts = {}

    def sem(self, name):
        if name not in self.sems:
            self.sems[name] = self.stack.enter_context(self.nc.semaphore(name))
            self.counts[name] = 0
        return self.sems[name]

    def inc(self, instr, name, dma=False):
        n = 16 if dma else 1
        instr.then_inc(self.sem(name), n)
        self.counts[name] += n
        return self.counts[name]

    def sb(self, name, shape, dtype):
        st = self.phase_stack if getattr(self, "phase_stack", None) is not None else self.stack
        return st.enter_context(self.nc.sbuf_tensor(name, shape, dtype))

    def open_phase(self):
        self.phase_stack = ExitStack()

    def close_phase(self):
        self.phase_stack.close()
        self.phase_stack = None

    def ps(self, name, shape, dtype=F32):
        return self.stack.enter_context(self.nc.psum_tensor(name, shape, dtype))

    def gdma(self, eng, out, in_, sname, **kw):
        sem = self.sem(sname)
        if self.counts[sname] > 0:
            eng.wait_ge(sem, self.counts[sname])
        ins = eng.dma_start(out=out, in_=in_, **kw)
        ins.then_inc(sem, 16)
        self.counts[sname] += 16
        return ins

    def close(self):
        self.stack.close()


def ceil_div(a, b):
    return (a + b - 1) // b






def emit_p1_retention(prog, io):
    sb = prog.sb
    NG, HPG = 4, 4
    GF = 256

    xin = sb("p1_xin", [128, 1, 8, 512], F32R)
    xsq = sb("p1_xsq", [128, 8, 512], F32R)
    xhat = xsq
    w_q = sb("p1_wq", [128, 8, GF], F32R)
    w_k = sb("p1_wk", [128, 8, GF], F32R)
    w_v = sb("p1_wv", [128, 8, GF], F32R)
    w_o = sb("p1_wo", [64, 4, 1024], F32R)
    qtm = sb("p1_qtm", [128, GF], F32R)
    ktm = sb("p1_ktm", [128, 4, GF], F32R)
    vtm = sb("p1_vtm", [128, 4, GF], F32R)
    qfm = sb("p1_qfm", [64, 4, 512], F32R)
    kfm = sb("p1_kfm", [64, 4, 512], F32R)
    ybuf = sb("p1_ybuf", [64, 4, T], F32R)
    tstate = sb("p1_tst", [64, 4, 64], F32R)
    abar = sb("p1_abar", [128, 512], F32R)
    qhat = sb("p1_qhat", [64, 4, 128], F32R)
    khat2 = sb("p1_khat2", [128, 256], F32R)
    ropet = sb("p1_ropet", [128, 2, 128], F32)
    stat_m = sb("p1_sm", [1, 512], F32R)
    stat_a = sb("p1_sa", [1, 512], F32R)
    mb_bc = sb("p1_mb", [128, 512], F32R)
    ab_bc = sb("p1_ab", [128, 512], F32R)
    stat_t = ab_bc[0:1, :].bitcast(F32)
    stat_t2 = mb_bc[0:1, :].bitcast(F32)
    gnred = sb("p1_gnred", [64, 8], F32)
    gnsml = sb("p1_gnsml", [1, 12], F32)
    gnstg = sb("p1_gnstg", [1, 512], F32R)
    gnsqt = sb("p1_gnsqt", [1, 4], F32)
    gn_s = sb("p1_gns", [64, 4], F32)
    gn_t = sb("p1_gnt", [64, 4], F32)
    gn_b = sb("p1_gnb", [64, 4], F32)
    dbgstage = sb("p1_dbgst", [64, 20], F32)
    wostg = sb("p1_wostg", [128, 4, 512], F32)

    cosb, sinb = io["cosb"], io["sinb"]
    onesb, onesrow, eyeb = io["onesb"], io["onesrow"], io["eyeb"]
    dtb, gcrossb, kdec3b = io["dtb"], io["gcrossb"], io["kdec3b"]
    gnwb, gnbb = io["gnwb"], io["gnbb"]
    xT, ret_hbm = io["xT"], io["ret_hbm"]
    pp, ppr = io["pp"], io["ppr"]
    gs_f = io["gs_f"]

    names = ["ld", "w", "sq", "stat", "sml", "sml2", "bc", "bce", "xh", "prj",
             "vev", "rope", "tp", "tpe", "rpe", "rdv", "gn", "gn2", "gn3",
             "gnA", "yp", "wo", "woe", "wod", "dbg"]
    s = {n: prog.sem("p1_" + n) for n in names}
    dbg = io.get("dbg")

    def prj_consumer(n):
        i, r = divmod(n, 12)
        tt, j = divmod(r, 3)
        if j < 2:
            return s["rope"], i * 8 + tt * 2 + j + 1
        return s["vev"], i * 4 + tt + 1

    def f_sync(sync):
        for g in range(NG):
            if g > 0:
                sync.wait_ge(s["woe"], g * 64)
            for (wt, dst) in ((io["wqT"], w_q), (io["wkT"], w_k), (io["wvT"], w_v)):
                src = wt[:, g * GF:(g + 1) * GF].rearrange("(k p) f -> p k f", p=128)
                prog.gdma(sync, dst[:], src, "p1_w")
            src = io["woT"][g * GF:(g + 1) * GF, :].rearrange("(h p) f -> p h f", p=64)
            prog.gdma(sync, w_o[:], src, "p1_w")
            prog.gdma(sync, dtb[:], io["c_dt"][:, g * 512:(g + 1) * 512], "p1_w")
            prog.gdma(sync, gcrossb[:].rearrange("p a b -> p (a b)"),
                      io["c_gcross"][:, g * 512:(g + 1) * 512], "p1_w")
            prog.gdma(sync, kdec3b[:].rearrange("p a b -> p (a b)"),
                      io["c_kdec3"][:, g * 256:(g + 1) * 256], "p1_w")
            for sc in range(NSC):
                i = g * NSC + sc
                if i >= 1:
                    sync.wait_ge(s["xh"], i)
                src = xT.rearrange("(k p) t -> p k t", p=128)[:, :, sc * 512:(sc + 1) * 512]
                prog.gdma(sync, xin[:, 0], src, "p1_ld")
            if g == 0:
                for tt in range(8):
                    for hf in range(2):
                        sync.wait_ge(s["woe"], tt * 8 + hf * 4 + 4)
                        dst = ret_hbm[hf * 512:(hf + 1) * 512, tt * 512:(tt + 1) * 512
                                      ].rearrange("(k p) t -> p k t", p=128).bitcast(F32)
                        prog.gdma(sync, dst, wostg[:], "p1_wod")

    def f_gps(gps):
        for g in range(1, NG):
            for tt in range(8):
                for hf in range(2):
                    gps.wait_ge(s["woe"], g * 64 + tt * 8 + hf * 4 + 4)
                    dst = ret_hbm[hf * 512:(hf + 1) * 512, tt * 512:(tt + 1) * 512
                                  ].rearrange("(k p) t -> p k t", p=128).bitcast(F32)
                    prog.gdma(gps, dst, wostg[:], "p1_wod", accum_op=ALU.add)

    def f_act(act):
        for g in range(NG):
            for sc in range(NSC):
                i = g * NSC + sc
                act.wait_ge(s["ld"], 16 * (i + 1))
                if i > 0:
                    act.wait_ge(s["prj"], 12 * i)
                if g > 0 and sc == 0:
                    act.wait_ge(s["gn"], 2 * g - 1)
                act.activation(xsq[:], xin[:, 0], AF.Square).then_inc(s["sq"], 1)
                act.wait_ge(s["sml"], i + 1)
                act.activation(stat_a[:], stat_t, AF.Sqrt).then_inc(s["sml2"], 1)
                if dbg and i == 0:
                    act.wait_ge(s["rdv"], 8)
                    prog.gdma(act, dbg["xhat"], xhat[:].rearrange("p a b -> p (a b)"), "p1_dbg")
                    prog.gdma(act, dbg["qfm"], qfm[:].rearrange("p a b -> p (a b)"), "p1_dbg")
                    prog.gdma(act, dbg["kfm"], kfm[:].rearrange("p a b -> p (a b)"), "p1_dbg")
                    prog.gdma(act, dbg["vtm"], vtm[:].rearrange("p a b -> p (a b)"), "p1_dbg")
                    act.wait_ge(prog.sems["p1_dbg"], prog.counts["p1_dbg"])
            if dbg and g == 0:
                act.wait_ge(s["gn"], 2 * g + 1)
                prog.gdma(act, dbg["y"], ybuf[:].rearrange("p a b -> p (a b)"), "p1_dbg")
                act.wait_ge(prog.sems["p1_dbg"], prog.counts["p1_dbg"])
            act.wait_ge(s["gn"], 2 * g + 2)
            act.activation(gnsqt[:], gnsml[:, 4:8], AF.Sqrt).then_inc(s["gnA"], 1)
            if dbg and g == 0:
                act.wait_ge(s["yp"], 4)
                prog.gdma(act, dbg["ypost"], ybuf[:].rearrange("p a b -> p (a b)"), "p1_dbg")
                prog.gdma(act, dbg["gn2"][:, 0:20], dbgstage[:], "p1_dbg")
                act.wait_ge(prog.sems["p1_dbg"], prog.counts["p1_dbg"])

    def f_pe(pe):
        for g in range(NG):
            pe.wait_ge(s["w"], 112 * (g + 1))
            if g > 0:
                pe.wait_ge(s["woe"], 64 * g)
                pe.wait_ge(s["gn"], 2 * g)
                pe.wait_ge(s["yp"], g * HPG)
            for sc in range(NSC):
                i = g * NSC + sc
                pe.wait_ge(s["ld"], 16 * (i + 1))
                pe.wait_ge(s["sq"], i + 1)
                if i > 0:
                    pe.wait_ge(s["rdv"], 8 * i)
                for k in range(8):
                    pe.matmul(pp[0:1, 0, :], onesb[:, 0:1], xin[:, 0, k],
                              start=(k == 0), stop=(k == 7))
                for k in range(8):
                    ins = pe.matmul(pp[0:1, 1, :], onesb[:, 0:1], xsq[:, k],
                                    start=(k == 0), stop=(k == 7))
                ins.then_inc(s["stat"], 1)
                pe.wait_ge(s["sml2"], i + 1)
                if i > 0:
                    pe.wait_ge(s["bce"], i)
                pe.matmul(pp[:, 2, :], onesrow[0:1, :], stat_m[:], start=True, stop=True)
                pe.matmul(pp[:, 3, :], onesrow[0:1, :], stat_a[:], start=True, stop=True
                          ).then_inc(s["bc"], 1)
                pe.wait_ge(s["xh"], i + 1)
                for tt in range(4):
                    for j, wbuf in enumerate((w_q, w_k, w_v)):
                        n = i * 12 + tt * 3 + j
                        if n >= 2:
                            cs, cv = prj_consumer(n - 2)
                            pe.wait_ge(cs, cv)
                        bank = 4 + n % 2
                        for k in range(8):
                            ins = pe.matmul(pp[:, bank, 0:GF], xhat[:, k, tt * 128:(tt + 1) * 128],
                                            wbuf[:, k], start=(k == 0), stop=(k == 7))
                        ins.then_inc(s["prj"], 1)
                    pe.wait_ge(s["rope"], i * 8 + tt * 2 + 2)
                    for jj, srcb in enumerate((qtm[:], ktm[:, tt])):
                        for h in range(4):
                            ntp = (i * 4 + tt) * 8 + jj * 4 + h + 1
                            if ntp >= 3:
                                pe.wait_ge(s["tpe"], ntp - 2)
                            pe.transpose(ppr[:, (ntp - 1) % 2, 0:128][0:64, :] if False else ppr[0:64, (ntp - 1) % 2, 0:128],
                                         srcb[:, h * 64:(h + 1) * 64], eyeb[:]
                                         ).then_inc(s["tp"], 1)
                for rc in range(4):
                    pe.wait_ge(s["tpe"], (i * 4 + rc + 1) * 8)
                    pe.wait_ge(s["vev"], i * 4 + rc + 1)
                    first = (sc == 0 and rc == 0)
                    idx = i * 4 + rc
                    rcs = slice(rc * 128, (rc + 1) * 128)
                    for h in range(4):
                        ins = pe.matmul(pp[:, 2, h * 128:(h + 1) * 128], kfm[0:64, h, rcs],
                                        qfm[0:64, h, rcs], start=True, stop=True)
                    ins.then_inc(s["rpe"], 1)
                    pe.wait_ge(s["rdv"], 2 * idx + 1)
                    for h in range(4):
                        hs = slice(h * 64, (h + 1) * 64)
                        pe.matmul(pp[0:64, 3, h * 128:(h + 1) * 128],
                                  vtm[:, rc, hs], abar[:, h * 128:(h + 1) * 128],
                                  start=True, stop=first)
                        if not first:
                            pe.matmul(pp[0:64, 3, h * 128:(h + 1) * 128],
                                      tstate[:, h, :], qhat[:, h, :],
                                      start=False, stop=True)
                        ins = pe.matmul(pp[0:64, 0, 256 + h * 64:256 + (h + 1) * 64],
                                        khat2[:, hs], vtm[:, rc, hs], start=True, stop=True)
                    ins.then_inc(s["rpe"], 1)
            pe.wait_ge(s["gn"], 2 * g + 1)
            pe.matmul(pp[0:1, 0, 0:8], onesb[0:64, 0:1].bitcast(F32), gnred[:], start=True, stop=True)
            pe.matmul(pp[:, 4, :], eyeb[:], xhat[:, 0], start=True, stop=True
                      ).then_inc(s["gn2"], 1)
            pe.wait_ge(s["gn3"], g + 1)
            pe.matmul(pp[0:64, 1, :], onesrow[0:1, 0:64], gnstg[:], start=True, stop=True)
            pe.matmul(pp[:, 4, :], eyeb[:], xhat[:, 0], start=True, stop=True
                      ).then_inc(s["gn2"], 1)
            pe.wait_ge(s["yp"], (g + 1) * HPG)
            for tt in range(8):
                for oc in range(8):
                    nwo = g * 64 + tt * 8 + oc
                    bank = 4 + nwo % 2
                    if nwo >= 2:
                        pe.wait_ge(s["woe"], nwo - 1)
                    for h in range(4):
                        ins = pe.matmul(pp[:, bank, :], w_o[:, h, oc * 128:(oc + 1) * 128],
                                        ybuf[:, h, tt * 512:(tt + 1) * 512],
                                        start=(h == 0), stop=(h == 3))
                    ins.then_inc(s["wo"], 1)

    def f_dve(dve):
        for g in range(NG):
            dve.wait_ge(s["w"], 112 * (g + 1))
            if dbg and g == 1:
                dve.wait_ge(prog.sems["p1_dbg"], prog.counts["p1_dbg"])
            for sc in range(NSC):
                i = g * NSC + sc
                dve.wait_ge(s["stat"], i + 1)
                dve.tensor_scalar_mul(stat_m[:], pp[0:1, 0, :], 1.0 / 1024.0)
                dve.tensor_scalar_mul(stat_t, pp[0:1, 1, :], 1.0 / 1024.0)
                dve.tensor_tensor(stat_t2, stat_m[:].bitcast(F32), stat_m[:].bitcast(F32), ALU.mult)
                dve.tensor_tensor(stat_t, stat_t, stat_t2, ALU.subtract)
                dve.tensor_scalar_add(stat_t, stat_t, EPS)
                dve.reciprocal(stat_t, stat_t).then_inc(s["sml"], 1)
                dve.wait_ge(s["bc"], i + 1)
                dve.tensor_copy(mb_bc[:], pp[:, 2, :])
                dve.tensor_copy(ab_bc[:], pp[:, 3, :]).then_inc(s["bce"], 1)
                for k in range(8):
                    dve.tensor_tensor(xsq[:, k], xin[:, 0, k], mb_bc[:], ALU.subtract)
                    ins = dve.tensor_tensor(xhat[:, k], xsq[:, k], ab_bc[:], ALU.mult)
                ins.then_inc(s["xh"], 1)
                for tt in range(4):
                    base_prj = i * 12 + tt * 3
                    for j, dstb in ((0, qtm[:]), (1, ktm[:, tt])):
                        bank = 4 + (base_prj + j) % 2
                        dve.wait_ge(s["prj"], base_prj + j + 1)
                        if j == 0 and (i * 4 + tt) >= 1:
                            dve.wait_ge(s["tp"], (i * 4 + tt - 1) * 8 + 4)
                        dve.tensor_copy(dstb, pp[:, bank, 0:GF]).then_inc(s["rope"], 1)
                    bank = 4 + (base_prj + 2) % 2
                    dve.wait_ge(s["prj"], base_prj + 3)
                    dve.tensor_copy(vtm[:, tt], pp[:, bank, 0:GF]).then_inc(s["vev"], 1)
                    for n in range(8):
                        ntp = (i * 4 + tt) * 8 + n + 1
                        dve.wait_ge(s["tp"], ntp)
                        dstb = qfm if n < 4 else kfm
                        h = n % 4
                        dve.tensor_copy(dstb[:, h, tt * 128:(tt + 1) * 128],
                                        ppr[0:64, (ntp - 1) % 2, 0:128]).then_inc(s["tpe"], 1)
                for rc in range(4):
                    first = (sc == 0 and rc == 0)
                    if g > 0 and sc == 0 and rc == 0:
                        dve.wait_ge(s["wo"], g * 64)
                    idx = i * 4 + rc
                    rcs = slice(rc * 128, (rc + 1) * 128)
                    hq = 4 * g
                    dve.wait_ge(s["rpe"], 2 * idx + 1)
                    dve.tensor_tensor(abar[:], pp[:, 2, 0:512],
                                      dtb[:], ALU.mult)
                    if not first:
                        dve.tensor_tensor(qhat[:], qfm[:, :, rcs], gcrossb[:], ALU.mult)
                    dve.tensor_tensor(khat2[:].rearrange("p (a b) -> p a b", b=64),
                                      ktm[:, rc].rearrange("p (a b) -> p a b", b=64),
                                      kdec3b[:], ALU.mult).then_inc(s["rdv"], 1)
                    dve.wait_ge(s["rpe"], 2 * idx + 2)
                    dve.tensor_copy(ybuf[:, :, (sc * 4 + rc) * 128:(sc * 4 + rc + 1) * 128],
                                    pp[0:64, 3, 0:512].rearrange("p (a b) -> p a b", b=128))
                    for h in range(4):
                        if first:
                            dve.tensor_copy(tstate[:, h, :], pp[0:64, 0, 256 + h * 64:256 + (h + 1) * 64])
                        else:
                            dve.scalar_tensor_tensor(tstate[:, h, :], tstate[:, h, :],
                                                     gs_f[hq + h], pp[0:64, 0, 256 + h * 64:256 + (h + 1) * 64],
                                                     op0=ALU.mult, op1=ALU.add)
                    dve.engine_nop().then_inc(s["rdv"], 1)
            for h in range(4):
                dve.reduce_sum(gnred[:, h:h + 1], ybuf[:, h], axis=mybir.AxisListType.X)
            for h in range(4):
                dve.tensor_tensor(xsq[0:64, 0:8].rearrange("p a t -> p (a t)"),
                                  ybuf[:, h], ybuf[:, h], ALU.mult)
                dve.reduce_sum(gnred[:, 4 + h:5 + h],
                               xsq[0:64, 0:8].rearrange("p a t -> p (a t)"),
                               axis=mybir.AxisListType.X)
            dve.engine_nop().then_inc(s["gn"], 1)
            dve.wait_ge(s["gn2"], 2 * g + 1)
            dve.tensor_scalar_mul(gnsml[:, 0:8], pp[0:1, 0, 0:8], 1.0 / (64.0 * 4096.0))
            dve.tensor_tensor(gnsml[:, 8:12], gnsml[:, 0:4], gnsml[:, 0:4], ALU.mult)
            dve.tensor_tensor(gnsml[:, 4:8], gnsml[:, 4:8], gnsml[:, 8:12], ALU.subtract)
            dve.tensor_scalar_add(gnsml[:, 4:8], gnsml[:, 4:8], EPS).then_inc(s["gn"], 1)
            dve.wait_ge(s["gnA"], g + 1)
            with prog.nc.allow_low_precision("f32r stage"):
                dve.reciprocal(gnstg[:, 0:4], gnsqt[:])
                dve.tensor_copy(gnstg[:, 4:8], gnsml[:, 0:4]).then_inc(s["gn3"], 1)
            dve.wait_ge(s["gn2"], 2 * g + 2)
            dve.tensor_copy(mb_bc[0:64, :], pp[0:64, 1, :])
            dve.tensor_copy(ab_bc[0:64, :], mb_bc[0:64, :])
            dve.tensor_tensor(gn_s[:], mb_bc[0:64, 0:4].bitcast(F32), gnwb[:, 4 * g:4 * g + 4], ALU.mult)
            dve.tensor_copy(ab_bc[0:64, :], mb_bc[0:64, :])
            dve.tensor_tensor(gn_t[:], mb_bc[0:64, 4:8].bitcast(F32), gn_s[:], ALU.mult)
            dve.tensor_copy(ab_bc[0:64, :], mb_bc[0:64, :])
            dve.tensor_tensor(gn_b[:], gnbb[:, 4 * g:4 * g + 4], gn_t[:], ALU.subtract)
            dve.tensor_copy(ab_bc[0:64, :], mb_bc[0:64, :])
            if dbg and g == 0:
                dve.tensor_copy(dbgstage[:, 0:4], gn_s[:])
                dve.tensor_copy(dbgstage[:, 4:8], gn_b[:])
                dve.tensor_copy(dbgstage[:, 8:16], pp[0:64, 1, 0:8])
                dve.tensor_copy(dbgstage[:, 16:20], gn_t[:])
            for h in range(4):
                dve.tensor_scalar_mul(ybuf[:, h], ybuf[:, h], gn_s[:, h:h + 1])
                dve.tensor_scalar_add(ybuf[:, h], ybuf[:, h], gn_b[:, h:h + 1]
                                      ).then_inc(s["yp"], 1)
            for tt in range(8):
                for oc in range(8):
                    nwo = g * 64 + tt * 8 + oc
                    bank = 4 + nwo % 2
                    dve.wait_ge(s["wo"], nwo + 1)
                    nhalf = (g * 8 + tt) * 2 + oc // 4
                    if nhalf >= 1 and oc % 4 == 0:
                        dve.wait_ge(s["wod"], 16 * nhalf)
                    dve.tensor_copy(wostg[:, oc % 4], pp[:, bank, :]).then_inc(s["woe"], 1)

    return dict(f_sync=f_sync, f_gps=f_gps, f_act=f_act, f_pe=f_pe, f_dve=f_dve)



def declare_common(prog):
    nc = prog.nc
    sb = prog.sb
    io = {}
    cdefs = [
        ("c_ones", [128, 1], F32R, "ones128", "onesb"),
        ("c_onesrow", [1, 128], F32R, "onesrow", "onesrow"),
        ("c_eye", [128, 128], F32R, "eye128", "eyeb"),
        ("c_dt", [128, 16 * 128], F32, "ret_dt", "dtb"),
        ("c_gcross", [64, 16 * 128], F32, "ret_gcross", "gcrossb"),
        ("c_kdec3", [128, 16 * 64], F32, "ret_kdec3", "kdec3b"),
        ("c_onesall", [128, 128], F32R, "onesall", "onesall"),
        ("c_diag", [128, 4 * 512], F32R, "diagm", "diagmb"),
        ("c_cos", [T, 32], F32, "rope_cos_tm", "cosb"),
        ("c_sin", [T, 32], F32, "rope_sin_tm", "sinb"),
    ]
    io["_cdefs"] = cdefs
    for nm, shape, dt_, ckey, sbkey in cdefs:
        io[nm] = nc.dram_tensor(nm, shape, dt_, kind="ExternalInput").ap()
    io["kdec3b"] = sb("cb_kdec3", [128, 4, 64], F32)
    io["cosb"] = sb("cb_cos", [128, 32, 32], F32)
    io["sinb"] = sb("cb_sin", [128, 32, 32], F32)
    io["onesb"] = sb("cb_ones", [128, 1], F32R)
    io["onesall"] = sb("cb_onesall", [128, 128], F32R)
    io["onesrow"] = sb("cb_onesrow", [1, 128], F32R)
    io["eyeb"] = sb("cb_eye", [128, 128], F32R)
    io["dtb"] = sb("cb_dt", [128, 4 * 128], F32)
    io["gcrossb"] = sb("cb_gcross", [64, 4, 128], F32)
    io["gnwb"] = sb("cb_gnw", [64, 16], F32)
    io["gnbb"] = sb("cb_gnb", [64, 16], F32)
    io["pp"] = prog.ps("pp", [128, 6, 512], F32)
    io["ppr"] = prog.ps("ppr", [128, 2, 512], F32R)
    return io


def emit_const_loads(prog, blk, io):
    s_c = prog.sem("c_ld")

    def f_sync(sync):
        n = 0
        pairs = [
            (io["onesb"][:], io["c_ones"]),
            (io["onesrow"][:], io["c_onesrow"]),
            (io["eyeb"][:], io["c_eye"]),
            (io["onesall"][:], io["c_onesall"]),
            (io["cosb"][:], io["c_cos"].rearrange("(n p) f -> p n f", p=128)),
            (io["sinb"][:], io["c_sin"].rearrange("(n p) f -> p n f", p=128)),
            (io["gnwb"][:], io["gnw_in"]),
            (io["gnbb"][:], io["gnb_in"]),
        ]
        for dst, src in pairs:
            prog.gdma(sync, dst, src, "c_ld")
            n += 16
        sync.wait_ge(prog.sem("c_ld"), prog.counts["c_ld"])

    blk.sync(f_sync)


def const_in_map(consts):
    return {
        "c_ones": consts["ones128"], "c_onesrow": consts["onesrow"],
        "c_eye": consts["eye128"], "c_dt": consts["ret_dt"],
        "c_onesall": consts["onesall"], "c_diag": consts["diagm"],
        "c_gcross": consts["ret_gcross"].reshape(64, 16 * 128),
        "c_kdec3": consts["ret_kdec3"].reshape(128, 16 * 64),
        "c_cos": np.ascontiguousarray(consts["rope_cos"].T),
        "c_sin": np.ascontiguousarray(consts["rope_sin"].T),
    }


def build_p1_test(consts):
    prog = Prog()
    nc = prog.nc
    io = declare_common(prog)
    io["xT"] = nc.dram_tensor("xT", [C, T], F32R, kind="ExternalInput").ap()
    io["wqT"] = nc.dram_tensor("wqT", [C, C], F32R, kind="ExternalInput").ap()
    io["wkT"] = nc.dram_tensor("wkT", [C, C], F32R, kind="ExternalInput").ap()
    io["wvT"] = nc.dram_tensor("wvT", [C, C], F32R, kind="ExternalInput").ap()
    io["woT"] = nc.dram_tensor("woT", [C, C], F32R, kind="ExternalInput").ap()
    io["gnw_in"] = nc.dram_tensor("gnw_in", [64, 16], F32, kind="ExternalInput").ap()
    io["gnb_in"] = nc.dram_tensor("gnb_in", [64, 16], F32, kind="ExternalInput").ap()
    io["ret_hbm"] = nc.dram_tensor("ret_hbm", [C, T], F32R, kind="ExternalOutput").ap()
    io["gs_f"] = consts["ret_gs_f"]
    io["dbg"] = {
        "xhat": nc.dram_tensor("dbg_xhat", [128, 8 * 512], F32R, kind="ExternalOutput").ap(),
        "qfm": nc.dram_tensor("dbg_qfm", [64, 4 * 512], F32R, kind="ExternalOutput").ap(),
        "kfm": nc.dram_tensor("dbg_kfm", [64, 4 * 512], F32R, kind="ExternalOutput").ap(),
        "vtm": nc.dram_tensor("dbg_vtm", [128, 4 * 256], F32R, kind="ExternalOutput").ap(),
        "y": nc.dram_tensor("dbg_y", [64, 4 * 4096], F32R, kind="ExternalOutput").ap(),
        "ypost": nc.dram_tensor("dbg_ypost", [64, 4 * 4096], F32R, kind="ExternalOutput").ap(),
        "gns": nc.dram_tensor("dbg_gns", [64, 12], F32, kind="ExternalOutput").ap(),
        "gn2": nc.dram_tensor("dbg_gn2", [64, 20], F32, kind="ExternalOutput").ap(),
    }

    with prog.nc.Block() as blk0:
        emit_const_loads(prog, blk0, io)
    fns = emit_p1_retention(prog, io)
    with prog.nc.Block() as blk1:
        blk1.sync(fns["f_sync"])
        blk1.gpsimd(fns["f_gps"])
        blk1.scalar(fns["f_act"])
        blk1.tensor(fns["f_pe"])
        blk1.vector(fns["f_dve"])
    s_wod = prog.sems["p1_wod"]
    with prog.nc.Block() as blk2:
        def f_end(gps):
            gps.wait_ge(s_wod, prog.counts["p1_wod"])
        blk2.gpsimd(f_end)
    prog.close()
    return prog


def kernel(**inputs):
    raise NotImplementedError("under construction")



def emit_p2_qkv(prog, io):
    sb = prog.sb
    NG, GF = 4, 256

    xin = sb("p2_xin", [128, 8, 512], F32R)
    ret = sb("p2_ret", [128, 8, 512], F32R)
    w_q = sb("p2_wq", [128, 8, GF], F32R)
    w_k = sb("p2_wk", [128, 8, GF], F32R)
    w_v = sb("p2_wv", [128, 8, GF], F32R)
    qtm = sb("p2_qtm", [128, GF], F32R)
    ktm = sb("p2_ktm", [128, GF], F32R)
    vtm = sb("p2_vtm", [128, 4, GF], F32R)
    qfm = sb("p2_qfm", [64, 4, 512], F32R)
    kfm = sb("p2_kfm", [64, 4, 512], F32R)
    xhb = sb("p2_xhb", [128, 8, 512], F32R)
    mb_bc = sb("p2_mb", [128, 512], F32)
    ab_bc = sb("p2_ab", [128, 512], F32)
    wtmp = sb("p2_wt", [128, 512], F32)
    rs_out = sb("p2_rs", [128, 512], F32)
    ropet = sb("p2_rope", [128, 2, 128], F32)
    lngb = sb("p2_lngb", [128, 2, 8], F32R)
    brow = sb("p2_brow", [1, 3 * GF], F32R)

    cosb, sinb = io["cosb"], io["sinb"]
    onesrow, eyeb, onesall = io["onesrow"], io["eyeb"], io["onesall"]
    xT, ret_hbm = io["xT"], io["ret_hbm"]
    x1_hbm, xm_hbm = io["x1_hbm"], io["xm_hbm"]
    k_hbm, v_hbm, q_hbm = io["k_hbm"], io["v_hbm"], io["q_hbm"]
    wqkv = io["wqkv"]
    pp, ppr = io["pp"], io["ppr"]

    names = ["ld", "w", "x1", "x1w", "sq", "stat", "st_ev", "sml", "sml2",
             "xh", "xh2", "xmw", "prj", "vev", "rope", "tp", "tpe", "bias",
             "gsc", "kqw", "vw"]
    s = {n: prog.sem("p2_" + n) for n in names}

    def prj_consumer(n):
        i, r = divmod(n, 12)
        tt, j = divmod(r, 3)
        if j < 2:
            return s["rope"], i * 8 + tt * 2 + j + 1
        return s["vev"], i * 4 + tt + 1

    def f_sync(sync):
        prog.gdma(sync, lngb[:].rearrange("p a b -> p (a b)"), io["lngb_in"], "p2_w")
        for g in range(NG):
            if g > 0:
                sync.wait_ge(s["prj"], 96 * g)
            for j, dst in enumerate((w_q, w_k, w_v)):
                src = wqkv[:, j * 1024 + g * GF: j * 1024 + (g + 1) * GF]
                prog.gdma(sync, dst[:], src.rearrange("(k p) f -> p k f", p=128), "p2_w")
            for sc in range(NSC):
                i = g * NSC + sc
                ts = slice(sc * 512, (sc + 1) * 512)
                if i >= 1:
                    sync.wait_ge(s["prj"], 12 * i)
                    sync.wait_ge(s["x1w"], 16 * i)
                    sync.wait_ge(s["xmw"], 16 * min(i, 8))
                prog.gdma(sync, xin[:], xT.rearrange("(k p) t -> p k t", p=128)[:, :, ts], "p2_ld")
                prog.gdma(sync, ret[:],
                          ret_hbm.rearrange("(k p) t -> p k t", p=128)[:, :, ts], "p2_ld")

    def f_scalar(act):
        for g in range(NG):
            for sc in range(NSC):
                i = g * NSC + sc
                ts = slice(sc * 512, (sc + 1) * 512)
                act.wait_ge(s["x1"], i + 1)
                if i >= 1:
                    act.wait_ge(s["xh2"], i)
                    if i <= 8:
                        act.wait_ge(s["xmw"], 16 * i)
                act.activation(xin[:], ret[:], AF.Square).then_inc(s["sq"], 1)
                act.wait_ge(s["sml"], 2 * i + 1)
                act.activation(rs_out[:], ab_bc[:], AF.Sqrt).then_inc(s["sml2"], 1)
                act.wait_ge(s["sml"], 2 * i + 2)
                act.activation(ab_bc[:], wtmp[:], AF.Sqrt).then_inc(s["sml2"], 1)
                act.wait_ge(s["tpe"], 32 * (i + 1))
                prog.gdma(act, k_hbm[4 * g:4 * g + 4, :, ts].rearrange("h p t -> p h t"),
                          kfm[:], "p2_kqw")
                prog.gdma(act, q_hbm[4 * g:4 * g + 4, :, ts].rearrange("h p t -> p h t"),
                          qfm[:], "p2_kqw")

    def f_pe(pe):
        for g in range(NG):
            pe.wait_ge(s["w"], 16 + 48 * (g + 1))
            if g > 0:
                pe.wait_ge(s["tpe"], 32 * 8 * g)
            for j, wbuf in enumerate((w_q, w_k, w_v)):
                dstp = pp[0:1, 0, (j % 2) * GF:(j % 2 + 1) * GF] if j < 2 \
                    else pp[0:1, 1, 0:GF]
                for k in range(8):
                    pe.matmul(dstp, lngb[:, 1, k:k + 1], wbuf[:, k],
                              start=(k == 0), stop=(k == 7))
            pe.matmul(pp[:, 4, :], eyeb[:], w_q[:, 0:2].rearrange("p a b -> p (a b)"),
                      start=True, stop=True).then_inc(s["bias"], 1)
            pe.wait_ge(s["gsc"], g + 1)
            for sc in range(NSC):
                i = g * NSC + sc
                if i > 0:
                    pe.wait_ge(s["st_ev"], i)
                pe.wait_ge(s["x1"], i + 1)
                for k in range(8):
                    pe.matmul(pp[:, 2, :], onesall[:], ret[:, k],
                              start=(k == 0), stop=(k == 7))
                pe.wait_ge(s["sq"], i + 1)
                for k in range(8):
                    ins = pe.matmul(pp[:, 3, :], onesall[:], xin[:, k],
                                    start=(k == 0), stop=(k == 7))
                ins.then_inc(s["stat"], 1)
                pe.wait_ge(s["xh2"], i + 1)
                for tt in range(4):
                    for j, wbuf in enumerate((w_q, w_k, w_v)):
                        n = i * 12 + tt * 3 + j
                        if n >= 2:
                            cs, cv = prj_consumer(n - 2)
                            pe.wait_ge(cs, cv)
                        bank = 4 + n % 2
                        for k in range(8):
                            pe.matmul(pp[:, bank, 0:GF], xhb[:, k, tt * 128:(tt + 1) * 128],
                                      wbuf[:, k], start=(k == 0), stop=False)
                        ins = pe.matmul(pp[:, bank, 0:GF], onesrow[0:1, :],
                                        brow[:, j * GF:(j + 1) * GF], start=False, stop=True)
                        ins.then_inc(s["prj"], 1)
                    pe.wait_ge(s["rope"], i * 8 + tt * 2 + 2)
                    for jj, srcb in enumerate((qtm[:], ktm[:])):
                        for h in range(4):
                            ntp = (i * 4 + tt) * 8 + jj * 4 + h + 1
                            if ntp >= 3:
                                pe.wait_ge(s["tpe"], ntp - 2)
                            pe.transpose(ppr[0:64, (ntp - 1) % 2, 0:128],
                                         srcb[:, h * 64:(h + 1) * 64], eyeb[:]
                                         ).then_inc(s["tp"], 1)

    def f_dve(dve):
        for g in range(NG):
            dve.wait_ge(s["bias"], g + 1)
            dve.tensor_copy(brow[:, 0:512], pp[0:1, 0, 0:512])
            dve.tensor_copy(brow[:, 512:768], pp[0:1, 1, 0:256])
            for wbuf in (w_q, w_k, w_v):
                for k in range(8):
                    gb = lngb[:, 0, k:k + 1].broadcast_to([128, GF])
                    dve.tensor_tensor(wbuf[:, k], wbuf[:, k], gb, ALU.mult)
            dve.engine_nop().then_inc(s["gsc"], 1)
            for sc in range(NSC):
                i = g * NSC + sc
                dve.wait_ge(s["ld"], 32 * (i + 1))
                for k in range(8):
                    ins = dve.tensor_tensor(ret[:, k], xin[:, k], ret[:, k], ALU.add)
                ins.then_inc(s["x1"], 1)
                dve.wait_ge(s["stat"], i + 1)
                dve.tensor_scalar_mul(mb_bc[:], pp[:, 2, :], 1.0 / 1024.0)
                dve.tensor_scalar_mul(ab_bc[:], pp[:, 3, :], 1.0 / 1024.0
                                      ).then_inc(s["st_ev"], 1)
                dve.tensor_tensor(wtmp[:], mb_bc[:], mb_bc[:], ALU.mult)
                dve.tensor_tensor(ab_bc[:], ab_bc[:], wtmp[:], ALU.subtract)
                dve.tensor_copy(wtmp[:], ab_bc[:])
                dve.tensor_scalar_add(ab_bc[:], ab_bc[:], EPS)
                dve.engine_nop().then_inc(s["sml"], 1)
                dve.wait_ge(s["sml2"], 2 * i + 1)
                dve.reciprocal(rs_out[:], rs_out[:])
                dve.tensor_tensor(wtmp[:], wtmp[:], rs_out[:], ALU.mult)
                dve.tensor_tensor(wtmp[:], wtmp[:], rs_out[:], ALU.mult)
                dve.tensor_scalar_add(wtmp[:], wtmp[:], EPS)
                dve.engine_nop().then_inc(s["sml"], 1)
                for k in range(8):
                    dve.tensor_tensor(xin[:, k], ret[:, k], mb_bc[:], ALU.subtract)
                    ins = dve.tensor_tensor(xin[:, k], xin[:, k], rs_out[:], ALU.mult)
                ins.then_inc(s["xh"], 1)
                dve.wait_ge(s["sml2"], 2 * i + 2)
                dve.reciprocal(ab_bc[:], ab_bc[:])
                if i > 0:
                    dve.wait_ge(s["prj"], 12 * i)
                for k in range(8):
                    ins = dve.tensor_tensor(xhb[:, k], xin[:, k], ab_bc[:], ALU.mult)
                ins.then_inc(s["xh2"], 1)
                for tt in range(4):
                    base_prj = i * 12 + tt * 3
                    for j, dstb in ((0, qtm[:]), (1, ktm[:])):
                        bank = 4 + (base_prj + j) % 2
                        dve.wait_ge(s["prj"], base_prj + j + 1)
                        if (i * 4 + tt) >= 1:
                            dve.wait_ge(s["tp"], (i * 4 + tt - 1) * 8 + 4 + j * 4)
                        src = pp[:, bank, 0:GF].rearrange("p (h f) -> p h f", f=64)
                        t1, t2 = src[:, :, 0:32], src[:, :, 32:64]
                        gt = sc * 4 + tt
                        cosB = cosb[:, gt].unsqueeze(1).broadcast_to([128, 4, 32])
                        sinB = sinb[:, gt].unsqueeze(1).broadcast_to([128, 4, 32])
                        dst = dstb.rearrange("p (h f) -> p h f", f=64)
                        r0 = ropet[:, 0].rearrange("p (h f) -> p h f", f=32)
                        r1 = ropet[:, 1].rearrange("p (h f) -> p h f", f=32)
                        dve.tensor_tensor(r0, t1, cosB, ALU.mult)
                        dve.tensor_tensor(r1, t2, sinB, ALU.mult)
                        dve.tensor_tensor(dst[:, :, 0:32], r0, r1, ALU.subtract)
                        dve.tensor_tensor(r0, t1, sinB, ALU.mult)
                        dve.tensor_tensor(r1, t2, cosB, ALU.mult)
                        dve.tensor_tensor(dst[:, :, 32:64], r0, r1, ALU.add
                                          ).then_inc(s["rope"], 1)
                    bank = 4 + (base_prj + 2) % 2
                    dve.wait_ge(s["prj"], base_prj + 3)
                    if (i * 4 + tt) >= 4:
                        dve.wait_ge(s["vw"], 16 * (i * 4 + tt - 3))
                    dve.tensor_copy(vtm[:, tt], pp[:, bank, 0:GF]).then_inc(s["vev"], 1)
                    if i > 0 and tt == 0:
                        dve.wait_ge(s["kqw"], 32 * i)
                    for n in range(8):
                        ntp = (i * 4 + tt) * 8 + n + 1
                        dve.wait_ge(s["tp"], ntp)
                        dstb = qfm if n < 4 else kfm
                        h = n % 4
                        dve.tensor_copy(dstb[:, h, tt * 128:(tt + 1) * 128],
                                        ppr[0:64, (ntp - 1) % 2, 0:128]).then_inc(s["tpe"], 1)

    def f_gps(gps):
        for g in range(NG):
            for sc in range(NSC):
                i = g * NSC + sc
                ts = slice(sc * 512, (sc + 1) * 512)
                gps.wait_ge(s["x1"], i + 1)
                prog.gdma(gps, x1_hbm.rearrange("(k p) t -> p k t", p=128)[:, :, ts],
                          ret[:], "p2_x1w")
                if g == 0:
                    gps.wait_ge(s["xh"], i + 1)
                    prog.gdma(gps, xm_hbm.rearrange("(k p) t -> p k t", p=128)[:, :, ts],
                              xin[:], "p2_xmw")
                for tt in range(4):
                    gps.wait_ge(s["vev"], i * 4 + tt + 1)
                    t0 = sc * 512 + tt * 128
                    prog.gdma(gps, v_hbm[t0:t0 + 128, g * GF:(g + 1) * GF],
                              vtm[:, tt], "p2_vw")

    return dict(f_sync=f_sync, f_scalar=f_scalar, f_pe=f_pe, f_dve=f_dve, f_gps=f_gps)



def emit_p3_attention(prog, io):
    sb = prog.sb
    NG = 4

    kbuf = sb("p3_k", [64, 2, T], F32R)
    vbuf = sb("p3_v", [128, 2, 32, 65], F32R)
    qbuf = sb("p3_q", [64, 2, T], F32R)
    pbuf = sb("p3_p", [128, 2, 512], F32R)
    obuf = sb("p3_o", [64, 2, T], F32R)
    lrow = sb("p3_lr", [1, 512], F32R)
    rbuf = sb("p3_r", [64, 512], F32)
    pads = sb("p3_pad", [64, 512], F32)
    padA = sb("p3_padA", [128, 512], F32)
    w_p = sb("p3_wp", [64, 4, 1024], F32R)
    otile = sb("p3_ot", [64, 2, 4, 512], F32R)
    wostg = sb("p3_wostg", [128, 4, 512], F32)
    diagmb = sb("p3_diag", [128, 4, 512], F32R)

    onesrow, eyeb = io["onesrow"], io["eyeb"]
    diagm = diagmb
    k_hbm, v_hbm, q_hbm, o_hbm = io["k_hbm"], io["v_hbm"], io["q_hbm"], io["o_hbm"]
    xm_hbm, wproj = io["xm_hbm"], io["wproj"]
    pp, ppr = io["pp"], io["ppr"]

    names = ["kv", "vone", "sS", "sP", "sP2", "sAVc", "sAV", "lr", "rb",
             "oev", "ow", "otl", "wpw", "wo", "woe", "wod"]
    s = {n: prog.sem("p3_" + n) for n in names}

    NKC = [(qt + 1) * 4 for qt in range(8)]
    KCT = sum(NKC)

    def zidx(h, qt, kc):
        return h * KCT + sum(NKC[:qt]) + kc

    def is_diag(qt, kc):
        return kc >= NKC[qt] - 4

    diag_count = {}
    c = 0
    for h in range(16):
        for qt in range(8):
            for kc in range(NKC[qt]):
                if is_diag(qt, kc):
                    c += 1
                diag_count[zidx(h, qt, kc)] = c

    def f_sync(sync):
        prog.gdma(sync, diagmb[:].rearrange("p a b -> p (a b)"), io["c_diag"], "p3_dg")
        for h in range(16):
            if h >= 2:
                sync.wait_ge(s["sAV"], 8 * (h - 1))
            prog.gdma(sync, kbuf[:, h % 2], k_hbm[h], "p3_kv")
            prog.gdma(sync, vbuf[:, h % 2, :, 0:64],
                      v_hbm[:, h * 64:(h + 1) * 64].rearrange("(a p) f -> p a f", p=128),
                      "p3_kv")
            prog.gdma(sync, qbuf[:, h % 2], q_hbm[h], "p3_kv")
        for g in range(NG):
            if g > 0:
                sync.wait_ge(s["wo"], 64 * g)
            src = wproj[g * 256:(g + 1) * 256, :].rearrange("(h p) f -> p h f", p=64)
            prog.gdma(sync, w_p[:], src, "p3_wpw")
            for tt in range(8):
                n = g * 8 + tt
                if n >= 2:
                    sync.wait_ge(s["wo"], (n - 1) * 8)
                ts = slice(tt * 512, (tt + 1) * 512)
                sync.wait_ge(s["ow"], 16 * (4 * g + 4))
                prog.gdma(sync, otile[:, n % 2],
                          o_hbm[4 * g:4 * g + 4, :, ts].rearrange("h p t -> p h t"), "p3_otl")

    def f_gps(gps):
        gps.wait_ge(prog.sem("p2_xmw"), prog.counts["p2_xmw"])
        for h in range(16):
            gps.wait_ge(s["oev"], h * 8 + 8)
            prog.gdma(gps, o_hbm[h], obuf[:, h % 2], "p3_ow")
        for g in range(NG):
            for tt in range(8):
                for hf in range(2):
                    gps.wait_ge(s["woe"], g * 64 + tt * 8 + hf * 4 + 4)
                    ts = slice(tt * 512, (tt + 1) * 512)
                    dst = xm_hbm[hf * 512:(hf + 1) * 512, ts
                                 ].rearrange("(k p) t -> p k t", p=128).bitcast(F32)
                    prog.gdma(gps, dst, wostg[:], "p3_wod", accum_op=ALU.add)

    def f_scalar(act):
        for h in range(16):
            for qt in range(8):
                for kc in range(NKC[qt]):
                    z = zidx(h, qt, kc)
                    act.wait_ge(s["sS"], z + 2 + h)
                    if z >= 2:
                        act.wait_ge(s["sAVc"], z - 1)
                    act.activation(pbuf[:, z % 2], pp[:, z % 2, :], AF.Exp,
                                   scale=0.125).then_inc(s["sP"], 1)

    def f_dve(dve):
        dve.wait_ge(prog.sem("p3_dg"), 16)
        oneB = io["onesb"][:, 0:1].unsqueeze(1).broadcast_to([128, 32, 1])
        dve.tensor_copy(vbuf[:, 0, :, 64:65], oneB)
        dve.tensor_copy(vbuf[:, 1, :, 64:65], oneB).then_inc(s["vone"], 1)
        for h in range(16):
            for qt in range(8):
                for kc in range(NKC[qt]):
                    if not is_diag(qt, kc):
                        continue
                    z = zidx(h, qt, kc)
                    r = kc - qt * 4
                    dve.wait_ge(s["sP"], z + 1)
                    dve.tensor_tensor(pbuf[:, z % 2], pbuf[:, z % 2], diagm[:, r],
                                      ALU.mult).then_inc(s["sP2"], 1)
                dve.wait_ge(s["sAV"], h * 8 + qt + 1)
                with prog.nc.allow_low_precision("recip row f32r"):
                    dve.reciprocal(lrow[:], pp[64:65, 2, :])
                dve.engine_nop().then_inc(s["lr"], 1)
                dve.wait_ge(s["rb"], h * 8 + qt + 1)
                if h >= 2 and qt == 0:
                    dve.wait_ge(s["ow"], 16 * (h - 1))
                dve.tensor_copy(rbuf[:], pp[0:64, 3, :])
                dve.tensor_copy(pads[:], pp[0:64, 2, :])
                dve.tensor_tensor(obuf[:, h % 2, qt * 512:(qt + 1) * 512],
                                  pp[0:64, 2, :], rbuf[:], ALU.mult
                                  ).then_inc(s["oev"], 1)
        for g in range(NG):
            for tt in range(8):
                for oc in range(8):
                    nwo = g * 64 + tt * 8 + oc
                    bank = 4 + nwo % 2
                    dve.wait_ge(s["wo"], nwo + 1)
                    nhalf = (g * 8 + tt) * 2 + oc // 4
                    if nhalf >= 1 and oc % 4 == 0:
                        dve.wait_ge(s["wod"], 16 * nhalf)
                    dve.tensor_copy(wostg[:, oc % 4], pp[:, bank, :]).then_inc(s["woe"], 1)

    def f_pe(pe):
        def emit_S(pe, h, z, qt, kc):
            if z >= 2:
                pe.wait_ge(s["sP"], z - 1)
            qs = slice(qt * 512, (qt + 1) * 512)
            pe.matmul(pp[:, z % 2, :], kbuf[:, h % 2, kc * 128:(kc + 1) * 128],
                      qbuf[:, h % 2, qs], start=True, stop=True).then_inc(s["sS"], 1)

        pe.wait_ge(s["vone"], 1)
        for h in range(16):
            pe.wait_ge(s["kv"], 48 * (h + 1))
            chunks = [(qt, kc) for qt in range(8) for kc in range(NKC[qt])]
            emit_S(pe, h, zidx(h, 0, 0), 0, 0)
            for ci, (qt, kc) in enumerate(chunks):
                z = zidx(h, qt, kc)
                if ci + 1 < len(chunks):
                    qt2, kc2 = chunks[ci + 1]
                    emit_S(pe, h, zidx(h, qt2, kc2), qt2, kc2)
                else:
                    pe.matmul(pp[0:64, 5, :], eyeb[0:64, 0:64], kbuf[:, h % 2, 0:512],
                              start=True, stop=True).then_inc(s["sS"], 1)
                if kc == 0 and h * 8 + qt >= 1:
                    pe.wait_ge(s["oev"], h * 8 + qt)
                if is_diag(qt, kc):
                    pe.wait_ge(s["sP2"], diag_count[z])
                else:
                    pe.wait_ge(s["sP"], z + 1)
                pe.matmul(pp[0:65, 2, :], vbuf[:, h % 2, kc], pbuf[:, z % 2],
                          start=(kc == 0), stop=(kc == NKC[qt] - 1)).then_inc(s["sAVc"], 1)
                if kc == NKC[qt] - 1:
                    pe.matmul(pp[0:64, 5, :], eyeb[0:64, 0:64], kbuf[:, h % 2, 0:512],
                              start=True, stop=True).then_inc(s["sAV"], 1)
                    pe.wait_ge(s["lr"], h * 8 + qt + 1)
                    pe.matmul(pp[0:64, 3, :], onesrow[0:1, 0:64], lrow[:], start=True, stop=True)
                    pe.matmul(pp[0:64, 5, :], eyeb[0:64, 0:64], kbuf[:, h % 2, 0:512],
                              start=True, stop=True).then_inc(s["rb"], 1)
        for g in range(NG):
            pe.wait_ge(s["wpw"], 16 * (g + 1))
            for tt in range(8):
                n = g * 8 + tt
                pe.wait_ge(s["otl"], 16 * (n + 1))
                for oc in range(8):
                    nwo = n * 8 + oc
                    bank = 4 + nwo % 2
                    if nwo >= 2:
                        pe.wait_ge(s["woe"], nwo - 1)
                    for hl in range(4):
                        ins = pe.matmul(pp[:, bank, :], w_p[:, hl, oc * 128:(oc + 1) * 128],
                                        otile[:, n % 2, hl], start=(hl == 0), stop=(hl == 3))
                    ins.then_inc(s["wo"], 1)

    return dict(f_sync=f_sync, f_gps=f_gps, f_scalar=f_scalar, f_pe=f_pe, f_dve=f_dve)



def emit_p4_gate(prog, io):
    sb = prog.sb
    xr = sb("p4_xr", [128, 8, 512], F32R)
    xsq = sb("p4_sq", [128, 8, 512], F32R)
    h2 = sb("p4_h2", [128, 8, 512], F32R)
    mb_bc = sb("p4_mb", [128, 512], F32)
    ab_bc = sb("p4_ab", [128, 512], F32)
    rs_out = sb("p4_rs", [128, 512], F32)
    h2tm = sb("p4_h2tm", [128, 2, 8, 128], F32R)
    lgro = sb("p4_lg", [8, 512], F32)
    lngb2 = sb("p4_lngb", [128, 2, 8], F32)
    wg = sb("p4_wg", [128, 8, 8], F32R)

    onesall, eyeb = io["onesall"], io["eyeb"]
    xm_hbm, wgate = io["xm_hbm"], io["wgate"]
    lgT_hbm, h2tm_hbm = io["lgT_hbm"], io["h2tm_hbm"]
    pp, ppr = io["pp"], io["ppr"]

    names = ["ld", "w", "sq", "stat", "st_ev", "sml", "sml2", "h2",
             "lg", "lge", "lgw", "tp", "tpe", "h2w"]
    s = {n: prog.sem("p4_" + n) for n in names}

    def f_sync(sync):
        prog.gdma(sync, lngb2[:].rearrange("p a b -> p (a b)"), io["lngb2_in"], "p4_w")
        prog.gdma(sync, wg[:], wgate.rearrange("(k p) e -> p k e", p=128), "p4_w")
        for sc in range(NSC):
            ts = slice(sc * 512, (sc + 1) * 512)
            if sc >= 1:
                sync.wait_ge(s["lg"], sc)
                sync.wait_ge(s["tpe"], 32 * sc)
            prog.gdma(sync, xr[:],
                      xm_hbm.rearrange("(k p) t -> p k t", p=128)[:, :, ts], "p4_ld")

    def f_scalar(act):
        for sc in range(NSC):
            act.wait_ge(s["ld"], 16 * (sc + 1))
            act.activation(xsq[:], xr[:], AF.Square).then_inc(s["sq"], 1)
            act.wait_ge(s["sml"], sc + 1)
            act.activation(rs_out[:], ab_bc[:], AF.Sqrt).then_inc(s["sml2"], 1)

    def f_pe(pe):
        pe.wait_ge(s["w"], 32)
        for sc in range(NSC):
            if sc > 0:
                pe.wait_ge(s["st_ev"], sc)
            pe.wait_ge(s["ld"], 16 * (sc + 1))
            for k in range(8):
                pe.matmul(pp[:, 2, :], onesall[:], xr[:, k],
                          start=(k == 0), stop=(k == 7))
            pe.wait_ge(s["sq"], sc + 1)
            for k in range(8):
                ins = pe.matmul(pp[:, 3, :], onesall[:], xsq[:, k],
                                start=(k == 0), stop=(k == 7))
            ins.then_inc(s["stat"], 1)
            pe.wait_ge(s["h2"], sc + 1)
            if sc > 0:
                pe.wait_ge(s["lge"], sc)
            for k in range(8):
                ins = pe.matmul(pp[0:8, 0, :], wg[:, k], h2[:, k],
                                start=(k == 0), stop=(k == 7))
            ins.then_inc(s["lg"], 1)
            for tt in range(4):
                for k in range(8):
                    ntp = (sc * 4 + tt) * 8 + k + 1
                    if ntp >= 3:
                        pe.wait_ge(s["tpe"], ntp - 2)
                    pe.transpose(ppr[:, (ntp - 1) % 2, 0:128],
                                 h2[:, k, tt * 128:(tt + 1) * 128], eyeb[:]
                                 ).then_inc(s["tp"], 1)

    def f_dve(dve):
        for sc in range(NSC):
            dve.wait_ge(s["stat"], sc + 1)
            dve.tensor_scalar_mul(mb_bc[:], pp[:, 2, :], 1.0 / 1024.0)
            dve.tensor_scalar_mul(ab_bc[:], pp[:, 3, :], 1.0 / 1024.0
                                  ).then_inc(s["st_ev"], 1)
            dve.tensor_tensor(rs_out[:], mb_bc[:], mb_bc[:], ALU.mult)
            dve.tensor_tensor(ab_bc[:], ab_bc[:], rs_out[:], ALU.subtract)
            dve.tensor_scalar_add(ab_bc[:], ab_bc[:], EPS)
            dve.engine_nop().then_inc(s["sml"], 1)
            dve.wait_ge(s["sml2"], sc + 1)
            dve.reciprocal(rs_out[:], rs_out[:])
            if sc > 0:
                dve.wait_ge(s["tp"], 32 * sc)
                dve.wait_ge(s["lg"], sc)
            for k in range(8):
                g2 = lngb2[:, 0, k:k + 1].broadcast_to([128, 512])
                b2 = lngb2[:, 1, k:k + 1].broadcast_to([128, 512])
                dve.tensor_tensor(h2[:, k], xr[:, k], mb_bc[:], ALU.subtract)
                dve.tensor_tensor(h2[:, k], h2[:, k], rs_out[:], ALU.mult)
                dve.tensor_tensor(h2[:, k], h2[:, k], g2, ALU.mult)
                ins = dve.tensor_tensor(h2[:, k], h2[:, k], b2, ALU.add)
            ins.then_inc(s["h2"], 1)
            dve.wait_ge(s["lg"], sc + 1)
            if sc > 0:
                dve.wait_ge(s["lgw"], 16 * sc)
            dve.tensor_copy(lgro[:], pp[0:8, 0, :]).then_inc(s["lge"], 1)
            for tt in range(4):
                n = sc * 4 + tt
                for k in range(8):
                    ntp = n * 8 + k + 1
                    dve.wait_ge(s["tp"], ntp)
                    if n >= 2 and k == 0:
                        dve.wait_ge(s["h2w"], 16 * (n - 1))
                    dve.tensor_copy(h2tm[:, n % 2, k], ppr[:, (ntp - 1) % 2, 0:128]
                                    ).then_inc(s["tpe"], 1)

    def f_gps(gps):
        for sc in range(NSC):
            ts = slice(sc * 512, (sc + 1) * 512)
            gps.wait_ge(s["lge"], sc + 1)
            prog.gdma(gps, lgT_hbm[:, ts], lgro[:], "p4_lgw")
            for tt in range(4):
                n = sc * 4 + tt
                gps.wait_ge(s["tpe"], 32 * sc + (tt + 1) * 8)
                t0 = sc * 512 + tt * 128
                prog.gdma(gps, h2tm_hbm[t0:t0 + 128, :],
                          h2tm[:, n % 2].rearrange("p k f -> p (k f)"), "p4_h2w")

    return dict(f_sync=f_sync, f_scalar=f_scalar, f_pe=f_pe, f_dve=f_dve, f_gps=f_gps)



ENGSECS = [("sync", "f_sync"), ("gpsimd", "f_gps"), ("scalar", "f_scalar"),
           ("tensor", "f_pe"), ("vector", "f_dve")]


def emit_phase_seq(prog, io, phase_list):
    barrier = {}
    for prefix, emit_fn in phase_list:
        prog.open_phase()
        fns = emit_fn(prog, io)
        snapshot = dict(barrier)
        with prog.nc.Block() as blk:
            for eng_attr, key in ENGSECS:
                f = fns.get(key) or fns.get({"f_scalar": "f_act"}.get(key, key))
                def wrapped(eng, f=f, snap=snapshot):
                    for nm, cnt in snap.items():
                        if cnt > 0:
                            eng.wait_ge(prog.sems[nm], cnt)
                    if f is not None:
                        f(eng)
                getattr(blk, eng_attr)(wrapped)
        prog.close_phase()
        barrier = {nm: c for nm, c in prog.counts.items() if c > 0}
    return barrier


def build_p12_test(consts):
    prog = Prog()
    nc = prog.nc
    io = declare_common(prog)
    for nm in ("xT", "wqT", "wkT", "wvT", "woT"):
        io[nm] = nc.dram_tensor(nm, [C, C] if nm != "xT" else [C, T], F32R,
                                kind="ExternalInput").ap()
    io["gnw_in"] = nc.dram_tensor("gnw_in", [64, 16], F32, kind="ExternalInput").ap()
    io["gnb_in"] = nc.dram_tensor("gnb_in", [64, 16], F32, kind="ExternalInput").ap()
    io["wqkv"] = nc.dram_tensor("wqkv", [C, 3 * C], F32R, kind="ExternalInput").ap()
    io["lngb_in"] = nc.dram_tensor("lngb_in", [128, 16], F32R, kind="ExternalInput").ap()
    io["ret_hbm"] = nc.dram_tensor("ret_hbm", [C, T], F32R, kind="Internal").ap()
    io["x1_hbm"] = nc.dram_tensor("x1_hbm", [C, T], F32R, kind="ExternalOutput").ap()
    io["xm_hbm"] = nc.dram_tensor("xm_hbm", [C, T], F32R, kind="ExternalOutput").ap()
    io["k_hbm"] = nc.dram_tensor("k_hbm", [H, 64, T], F32R, kind="ExternalOutput").ap()
    io["q_hbm"] = nc.dram_tensor("q_hbm", [H, 64, T], F32R, kind="ExternalOutput").ap()
    io["v_hbm"] = nc.dram_tensor("v_hbm", [T, C], F32R, kind="ExternalOutput").ap()
    io["gs_f"] = consts["ret_gs_f"]

    with prog.nc.Block() as blk0:
        emit_const_loads(prog, blk0, io)
    emit_phase_seq(prog, io, [("p1", emit_p1_retention), ("p2", emit_p2_qkv)])
    with prog.nc.Block() as blk2:
        def f_end(gps):
            for nm in ("p2_x1w", "p2_xmw", "p2_kqw", "p2_vw"):
                gps.wait_ge(prog.sems[nm], prog.counts[nm])
        blk2.gpsimd(f_end)
    prog.close()
    return prog



def declare_l2(prog):
    nc = prog.nc
    sb = prog.sb
    io = {}
    cdefs = [
        ("c_eye", [128, 128], F32R, "eye128", "eyeb"),
        ("c_triin", [128, 128], F32R, "tri_incl", "triin"),
        ("c_triex", [128, 128], F32R, "tri_excl", "triex"),
        ("c_ones", [128, 1], F32R, "ones128", "onesb"),
        ("c_onesrow", [1, 128], F32R, "onesrow", "onesrow"),
        ("c_iota8", [128, 8], F32, "iota8", "iota8b"),
        ("c_iota8p", [128, 8], F32, "iota8p", "iota8pb"),
        ("c_tok0", [128, 32], F32, "tok0", "tok0b"),
        ("c_eyef", [128, 128], F32, "eye128", "eyef"),
    ]
    io["_cdefs"] = cdefs
    for nm, shape, dt_, ckey, sbkey in cdefs:
        io[nm] = nc.dram_tensor(nm, shape, dt_, kind="ExternalInput").ap()
        io[sbkey] = sb("cb2_" + sbkey, shape, dt_)
    io["pp"] = prog.ps("pp", [128, 6, 512], F32)
    io["ppr"] = prog.ps("ppr", [128, 2, 512], F32R)
    io["lgT_all"] = nc.dram_tensor("lgT_all", [4 * 8, T], F32R, kind="ExternalInput").ap()
    io["h2tm_all"] = nc.dram_tensor("h2tm_all", [N_TOK, C], F32R, kind="ExternalInput").ap()
    io["w1"] = nc.dram_tensor("w1", [C, 4 * C], F32R, kind="ExternalInput").ap()
    io["w2"] = nc.dram_tensor("w2", [4 * C, C], F32R, kind="ExternalInput").ap()
    io["b1_in"] = nc.dram_tensor("b1_in", [128, 32], F32, kind="ExternalInput").ap()
    io["b2row"] = nc.dram_tensor("b2row", [1, C], F32R, kind="ExternalInput").ap()
    io["emeta"] = nc.dram_tensor("emeta", [128, 1], F32, kind="ExternalInput").ap()
    io["exp_out"] = nc.dram_tensor("exp_out", [CAP, C], F32, kind="ExternalOutput").ap()
    io["trow_hbm"] = nc.dram_tensor("trow_hbm", [4, 128, 64], F32, kind="ExternalOutput").ap()
    io["tgate_hbm"] = nc.dram_tensor("tgate_hbm", [4, 128, 64], F32, kind="ExternalOutput").ap()
    io["gathtab"] = nc.dram_tensor("gathtab", [4097, 1], F32, kind="Internal").ap()
    io["tsd_d"] = nc.dram_tensor("tsd_d", [4, 256], F32R, kind="Internal").ap()
    io["toff_d"] = nc.dram_tensor("toff_d", [128, 8], F32R, kind="Internal").ap()
    io["ein_hbm"] = nc.dram_tensor("ein_hbm", [C, CAP], F32R, kind="Internal").ap()
    return io


def emit_l2_consts(prog, blk, io):
    def f_sync(sync):
        for nm, shape, dt_, ckey, sbkey in io["_cdefs"]:
            prog.gdma(sync, io[sbkey][:], io[nm], "l2_c")
        sync.wait_ge(prog.sem("l2_c"), prog.counts["l2_c"])
    blk.sync(f_sync)


def l2_const_in_map(consts):
    return {
        "c_eye": consts["eye128"], "c_triin": consts["tri_incl"],
        "c_triex": consts["tri_excl"], "c_ones": consts["ones128"],
        "c_onesrow": consts["onesrow"][:, 0:128],
        "c_iota8": consts["iota8"], "c_iota8p": consts["iota8p"],
        "c_tok0": consts["tok0"], "c_eyef": consts["eye128"],
    }


def emit_l2_routing(prog, io):
    sb = prog.sb
    lgt = sb("r_lgt", [8, 4, T], F32R)
    lgm = sb("r_lgm", [128, 4, 32, 8], F32)
    m_all = sb("r_m", [128, 4, 32, 8], F32R)
    m1 = sb("r_m1", [128, 4, 32, 8], F32)
    m2 = sb("r_m2", [128, 4, 32, 8], F32)
    prb = sb("r_p", [128, 4, 32, 8], F32)
    dbb = sb("r_d", [128, 4, 32, 8], F32)
    tmp8 = sb("r_t8", [128, 32, 8], F32)
    mx1 = sb("r_mx1", [128, 4, 32], F32)
    idx1 = sb("r_i1", [128, 4, 32], F32)
    idx2 = sb("r_i2", [128, 4, 32], F32)
    gate1 = sb("r_g1", [128, 4, 32], F32)
    gate2 = sb("r_g2", [128, 4, 32], F32)
    posge = sb("r_pos", [128, 4, 256], F32)
    sumB = sb("r_sum", [128, 4, 32], F32)
    g1B = sb("r_g1b", [128, 4, 32], F32)
    g2B = sb("r_g2b", [128, 4, 32], F32)
    tmpB = sb("r_tmpB", [128, 4, 32, 8], F32)
    kpB = sb("r_kpB", [128, 4, 256], F32)
    pos1B = sb("r_p1", [128, 4, 32], F32)
    pos2B = sb("r_p2", [128, 4, 32], F32)
    kp1B = sb("r_kp1", [128, 4, 32], F32)
    kp2B = sb("r_kp2", [128, 4, 32], F32)
    t32B = sb("r_t32", [128, 4, 32], F32)
    u32B = sb("r_u32", [128, 4, 32], F32)
    w32B = sb("r_w32", [128, 4, 32], F32)
    trow = sb("r_trow", [128, 4, 32, 2], F32)
    tgate = sb("r_tg", [128, 4, 32, 2], F32)
    offs = sb("r_off", [128, 4, 32], F32)
    offi = sb("r_offi", [128, 4, 32], I32)
    tokf = sb("r_tokf", [128, 4, 32], F32)
    tsall = sb("r_ts", [1, 4, 256], F32R)
    tspos = sb("r_tsp", [128, 256], F32R)
    toffx = sb("r_toffx", [128, 8], F32)
    toffrow = sb("r_toffr", [1, 4, 256], F32R)
    zbuf = sb("r_z", [128, 33], F32)
    emetab = sb("r_em", [128, 1], F32)

    eyeb, triin, triex = io["eyeb"], io["triin"], io["triex"]
    onesb, onesrow = io["onesb"], io["onesrow"]
    iota8b, iota8pb, tok0b = io["iota8b"], io["iota8pb"], io["tok0b"]
    pp, ppr = io["pp"], io["ppr"]

    names = ["ld", "tp", "tpe", "top", "ex", "ex2", "msum", "msf", "tse",
             "tsd", "tri", "trif", "toe", "tod", "pos", "posf", "pose",
             "sl", "zf", "sc", "tw", "em"]
    s = {n: prog.sem("l2r_" + n) for n in names}

    def f_sync(sync):
        prog.gdma(sync, emetab[:], io["emeta"], "l2r_em")
        prog.gdma(sync, lgt[:], io["lgT_all"].rearrange("(s e) t -> e s t", e=8), "l2r_ld")

    def f_pe(pe):
        pe.wait_ge(s["ld"], 16)
        for n in range(128):
            sq, tt = divmod(n, 32)
            if n >= 2:
                pe.wait_ge(s["tpe"], n - 1)
            pe.transpose(ppr[:, n % 2, 0:8], lgt[0:8, sq, tt * 128:(tt + 1) * 128],
                         eyeb[0:8, 0:8]).then_inc(s["tp"], 1)
        for sq in range(4):
            pe.wait_ge(s["top"], sq + 1)
            pe.matmul(pp[0:1, 0, sq * 128:sq * 128 + 256] if False else
                      pp[0:1, 0, 0:256] if sq == 0 else
                      pp[0:1, 1, 0:256] if sq == 1 else
                      pp[0:1, 2, 0:256] if sq == 2 else pp[0:1, 3, 0:256],
                      onesb[:, 0:1], m_all[:, sq], start=True, stop=True)
        pe.matmul(pp[:, 4, :], eyeb[:], m_all[:, 0:2].rearrange("p s a e -> p (s a e)"),
                  start=True, stop=True).then_inc(s["msf"], 1)
        pe.wait_ge(prog.sem("l2r_tsd"), 32)
        pe.matmul(pp[:, 0, 0:256], triex[:], tspos[:], start=True, stop=True)
        pe.matmul(pp[:, 4, :], eyeb[:], m_all[:, 0:2].rearrange("p s a e -> p (s a e)"),
                  start=True, stop=True).then_inc(s["trif"], 1)
        pe.wait_ge(prog.sem("l2r_tod"), 80)
        for sq in range(4):
            pe.matmul(pp[:, sq, 0:256], triin[:], m_all[:, sq], start=True, stop=False)
            pe.matmul(pp[:, sq, 0:256], onesrow[0:1, :], toffrow[:, sq],
                      start=False, stop=True)
        pe.matmul(pp[:, 4, :], eyeb[:], m_all[:, 0:2].rearrange("p s a e -> p (s a e)"),
                  start=True, stop=True).then_inc(s["posf"], 1)

    def f_dve(dve):
        for n in range(128):
            sq, tt = divmod(n, 32)
            dve.wait_ge(s["tp"], n + 1)
            dve.tensor_copy(lgm[:, sq, tt], ppr[:, n % 2, 0:8]).then_inc(s["tpe"], 1)
        for step in range(7):
            for sq in range(4):
                L = lgm[:, sq]
                if step == 0:
                    dve.tensor_reduce(mx1[:, sq, :], L, axis=mybir.AxisListType.X,
                                      op=ALU.max)
                elif step == 1:
                    mb = mx1[:, sq].unsqueeze(2).broadcast_to([128, 32, 8])
                    dve.tensor_tensor(m1[:, sq], L, mb, ALU.is_equal)
                elif step == 2:
                    ib = iota8pb[:].unsqueeze(1).broadcast_to([128, 32, 8])
                    dve.scalar_tensor_tensor(tmp8[:], m1[:, sq], -1000.0, ib,
                                             op0=ALU.mult, op1=ALU.add)
                    dve.tensor_reduce(idx1[:, sq, :], tmp8[:], axis=mybir.AxisListType.X,
                                      op=ALU.min)
                elif step == 3:
                    ixb = idx1[:, sq].unsqueeze(2).broadcast_to([128, 32, 8])
                    iob = iota8b[:].unsqueeze(1).broadcast_to([128, 32, 8])
                    dve.tensor_tensor(m1[:, sq], iob, ixb, ALU.is_equal)
                    dve.scalar_tensor_tensor(tmp8[:], m1[:, sq], -1e30, L,
                                             op0=ALU.mult, op1=ALU.add)
                    dve.tensor_reduce(mx1[:, sq, :], tmp8[:], axis=mybir.AxisListType.X,
                                      op=ALU.max)
                    mb = mx1[:, sq].unsqueeze(2).broadcast_to([128, 32, 8])
                    dve.tensor_tensor(m2[:, sq], tmp8[:], mb, ALU.is_equal)
                elif step == 4:
                    ib = iota8pb[:].unsqueeze(1).broadcast_to([128, 32, 8])
                    dve.scalar_tensor_tensor(tmp8[:], m2[:, sq], -1000.0, ib,
                                             op0=ALU.mult, op1=ALU.add)
                    dve.tensor_reduce(idx2[:, sq, :], tmp8[:], axis=mybir.AxisListType.X,
                                      op=ALU.min)
                elif step == 5:
                    ixb = idx2[:, sq].unsqueeze(2).broadcast_to([128, 32, 8])
                    iob = iota8b[:].unsqueeze(1).broadcast_to([128, 32, 8])
                    dve.tensor_tensor(m2[:, sq], iob, ixb, ALU.is_equal)
                    dve.tensor_tensor(m_all[:, sq], m1[:, sq], m2[:, sq], ALU.add)
                elif step == 6:
                    dve.tensor_reduce(mx1[:, sq, :], L, axis=mybir.AxisListType.X,
                                      op=ALU.max)
                    mb = mx1[:, sq].unsqueeze(2).broadcast_to([128, 32, 8])
                    ins = dve.tensor_tensor(dbb[:, sq], L, mb, ALU.subtract)
                    if sq == 3:
                        ins.then_inc(s["ex"], 1)
        dve.engine_nop().then_inc(s["top"], 4)
        padw = tmp8
        def pad():
            dve.tensor_copy(padw[:], prb[:, 0])
        dve.wait_ge(s["ex2"], 1)
        dve.tensor_reduce(sumB[:], prb[:], axis=mybir.AxisListType.X, op=ALU.add)
        pad()
        dve.reciprocal(sumB[:], sumB[:])
        dve.tensor_tensor(tmpB[:], prb[:], m1[:], ALU.mult)
        pad()
        dve.tensor_reduce(g1B[:], tmpB[:], axis=mybir.AxisListType.X, op=ALU.add)
        dve.tensor_tensor(tmpB[:], prb[:], m2[:], ALU.mult)
        dve.tensor_tensor(gate1[:], g1B[:], sumB[:], ALU.mult)
        dve.tensor_reduce(g2B[:], tmpB[:], axis=mybir.AxisListType.X, op=ALU.add)
        pad()
        dve.tensor_tensor(gate2[:], g2B[:], sumB[:], ALU.mult)
        dve.wait_ge(s["msf"], 1)
        for sq in range(4):
            src = (pp[0:1, 0, 0:256] if sq == 0 else pp[0:1, 1, 0:256]
                   if sq == 1 else pp[0:1, 2, 0:256] if sq == 2 else pp[0:1, 3, 0:256])
            dve.tensor_copy(tsall[:, sq], src)
        dve.memset(zbuf[:], 0.0)
        dve.tensor_scalar_mul(tspos[:], io["onesb"][:, 0:1].broadcast_to([128, 256]), 0.0)
        dve.engine_nop().then_inc(s["tse"], 1)
        dve.wait_ge(s["trif"], 1)
        dve.tensor_copy(toffx[:], pp[0:128, 0, 0:8]).then_inc(s["toe"], 1)
        dve.wait_ge(s["posf"], 1)
        for sq in range(4):
            dve.tensor_copy(posge[:, sq], pp[:, sq, 0:256])
        dve.engine_nop().then_inc(s["pose"], 4)
        PB = posge[:].rearrange("p s (a e) -> p s a e", e=8)
        dve.tensor_scalar_min(kpB[:], posge[:], float(CAP))
        pad()
        dve.tensor_tensor(kpB[:], kpB[:], posge[:], ALU.is_equal)
        pad()
        dve.tensor_tensor(kpB[:], kpB[:], m_all[:].bitcast(F32).rearrange(
            "p s a e -> p s (a e)"), ALU.mult)
        pad()
        KB = kpB[:].rearrange("p s (a e) -> p s a e", e=8)
        for j, (mj, posj, kpj, idxj, gatej) in enumerate(
                ((m1, pos1B, kp1B, idx1, gate1), (m2, pos2B, kp2B, idx2, gate2))):
            dve.tensor_tensor(tmpB[:], PB, mj[:], ALU.mult)
            pad()
            dve.tensor_reduce(posj[:], tmpB[:], axis=mybir.AxisListType.X, op=ALU.add)
            dve.tensor_tensor(tmpB[:], KB, mj[:], ALU.mult)
            pad()
            dve.tensor_reduce(kpj[:], tmpB[:], axis=mybir.AxisListType.X, op=ALU.add)
            pad()
            dve.tensor_tensor(t32B[:], kpj[:], posj[:], ALU.mult)
            pad()
            dve.scalar_tensor_tensor(t32B[:], kpj[:], -float(CAP), t32B[:],
                                     op0=ALU.mult, op1=ALU.add)
            pad()
            dve.tensor_scalar_add(t32B[:], t32B[:], float(CAP) - 1.0)
            pad()
            dve.scalar_tensor_tensor(trow[:, :, :, j], idxj[:], float(CAP),
                                     t32B[:], op0=ALU.mult, op1=ALU.add)
            dve.tensor_tensor(tgate[:, :, :, j], gatej[:], kpj[:], ALU.mult)
        emb = emetab[:].unsqueeze(2).broadcast_to([128, 4, 32])
        dve.tensor_tensor(t32B[:], idx1[:], emb, ALU.is_equal)
        pad()
        dve.tensor_tensor(t32B[:], t32B[:], kp1B[:], ALU.mult)
        dve.tensor_tensor(u32B[:], idx2[:], emb, ALU.is_equal)
        pad()
        dve.tensor_tensor(u32B[:], u32B[:], kp2B[:], ALU.mult)
        pad()
        dve.tensor_tensor(offs[:], t32B[:], pos1B[:], ALU.mult)
        dve.tensor_tensor(w32B[:], u32B[:], pos2B[:], ALU.mult)
        pad()
        dve.tensor_tensor(offs[:], offs[:], w32B[:], ALU.add)
        dve.tensor_tensor(t32B[:], t32B[:], u32B[:], ALU.add)
        pad()
        dve.tensor_tensor(offs[:], offs[:], t32B[:], ALU.subtract)
        pad()
        dve.scalar_tensor_tensor(offs[:], t32B[:], -float(CAP), offs[:],
                                 op0=ALU.mult, op1=ALU.add)
        pad()
        dve.tensor_scalar_add(offs[:], offs[:], float(CAP))
        pad()
        dve.tensor_copy(offi[:], offs[:])
        for sq in range(4):
            dve.tensor_scalar_add(tokf[:, sq], tok0b[:], float(sq * T))
        dve.engine_nop().then_inc(s["sl"], 4)

    def f_scalar(act):
        act.wait_ge(s["ex"], 1)
        act.activation(prb[:], dbb[:], AF.Exp).then_inc(s["ex2"], 1)

    def f_gps(gps):
        gps.wait_ge(s["tse"], 1)
        prog.gdma(gps, io["gathtab"][0:4096].rearrange("(p a) x -> p a x", p=128),
                  zbuf[:, 0:32].unsqueeze(2), "l2r_zf")
        prog.gdma(gps, io["gathtab"][4096:4097], zbuf[0:1, 32:33], "l2r_zf")
        prog.gdma(gps, io["tsd_d"], tsall[:, :, :], "l2r_tsd")
        prog.gdma(gps, tspos[:, 0:8],
                  io["tsd_d"].rearrange("s (q e) -> (s q) e", e=8), "l2r_tsd")
        gps.wait_ge(s["toe"], 1)
        prog.gdma(gps, io["toff_d"].bitcast(F32), toffx[:], "l2r_tod")
        for sq in range(4):
            src = io["toff_d"].rearrange("(s q) e -> s q e", s=4)[sq:sq + 1]
            prog.gdma(gps, toffrow[0:1, sq].rearrange("p (q e) -> p q e", e=8),
                      src, "l2r_tod")
        if "dbg_l2" in io:
            d = io["dbg_l2"]
            gps.wait_ge(s["sl"], 4)
            prog.gdma(gps, d["lgm"], lgm[:].rearrange("p a b c -> p (a b c)"), "l2r_dbg")
            prog.gdma(gps, d["mall"], m_all[:].rearrange("p a b c -> p (a b c)").bitcast(F32), "l2r_dbg")
            prog.gdma(gps, d["idx"], idx1[:].rearrange("p a b -> p (a b)"), "l2r_dbg")
            prog.gdma(gps, d["idx2"], idx2[:].rearrange("p a b -> p (a b)"), "l2r_dbg")
            prog.gdma(gps, d["posge"], posge[:].rearrange("p a b -> p (a b)"), "l2r_dbg")
            prog.gdma(gps, d["toffx"], toffx[:], "l2r_dbg")
            prog.gdma(gps, d["tsp"], tspos[:].bitcast(F32), "l2r_dbg")
            prog.gdma(gps, d["g1"], gate1[:].rearrange("p a b -> p (a b)"), "l2r_dbg")
            prog.gdma(gps, d["offs"], offs[:].rearrange("p a b -> p (a b)"), "l2r_dbg")
        gps.wait_ge(prog.sem("l2r_zf"), 32)
        for sq in range(4):
            gps.wait_ge(s["sl"], 4)
            for tt in range(32):
                ins = gps.indirect_dma_start(
                    out=io["gathtab"],
                    out_offset=IndirectOffsetOnAxis(offi[:, sq, tt:tt + 1], 0),
                    in_=tokf[:, sq, tt:tt + 1], in_offset=None)
                prog.inc(ins, "l2r_sc", dma=True)
            prog.gdma(gps, io["trow_hbm"][sq], trow[:, sq].rearrange("p a j -> p (a j)"),
                      "l2r_tw")
            prog.gdma(gps, io["tgate_hbm"][sq], tgate[:, sq].rearrange("p a j -> p (a j)"),
                      "l2r_tw")

    return dict(f_sync=f_sync, f_pe=f_pe, f_dve=f_dve, f_scalar=f_scalar, f_gps=f_gps)

def emit_l2_f0(prog, io):
    sb = prog.sb
    gtb2 = sb("f0_gtb2", [32, 128], F32)
    gtb = sb("f0_gtb", [128, 32], F32)
    gti = sb("f0_gti", [128, 32], I32)
    padf0 = sb("f0_pad", [128, 512], F32)
    etm = sb("f0_etm", [128, 2, 1024], F32R)
    efm = sb("f0_efm", [128, 8, 512], F32R)
    eyeb, eyef = io["eyeb"], io["eyef"]
    pp, ppr = io["pp"], io["ppr"]
    names = ["gt", "gt2", "gi", "g", "tp", "tpe", "ew"]
    s = {n: prog.sem("l2f0_" + n) for n in names}

    def f_sync(sync):
        sync.wait_ge(prog.sem("l2r_sc"), prog.counts["l2r_sc"])
        sync.wait_ge(prog.sem("l2r_zf"), prog.counts["l2r_zf"])
        prog.gdma(sync, gtb2[:].unsqueeze(2),
                  io["gathtab"][0:4096].rearrange("(a p) x -> a p x", p=128), "l2f0_gt")

    def f_gps(gps):
        gps.wait_ge(s["gi"], 1)
        for n in range(32):
            if n >= 2:
                gps.wait_ge(s["tp"], 8 * (n - 1))
            ins = gps.indirect_dma_start(
                out=etm[:, n % 2], out_offset=None,
                in_=io["h2tm_all"], in_offset=IndirectOffsetOnAxis(gti[:, n:n + 1], 0))
            prog.inc(ins, "l2f0_g", dma=True)

    def f_pe(pe):
        pe.wait_ge(prog.sem("l2f0_gt"), 16)
        pe.transpose(pp[:, 3, 0:32], gtb2[:], eyef[0:32, 0:32]).then_inc(s["gt2"], 1)
        for n in range(32):
            pe.wait_ge(s["g"], 16 * (n + 1))
            for k in range(8):
                ntp = n * 8 + k + 1
                if ntp >= 3:
                    pe.wait_ge(s["tpe"], ntp - 2)
                pe.transpose(ppr[:, (ntp - 1) % 2, 0:128],
                             etm[:, n % 2, k * 128:(k + 1) * 128], eyeb[:]
                             ).then_inc(s["tp"], 1)

    def f_dve(dve):
        dve.wait_ge(s["gt2"], 1)
        dve.tensor_copy(gtb[:], pp[:, 3, 0:32])
        dve.tensor_copy(padf0[:], pp[:, 3, 0:512])
        dve.tensor_copy(gti[:], gtb[:]).then_inc(s["gi"], 1)
        for n in range(32):
            TT, q = divmod(n, 4)
            if q == 0 and TT >= 1:
                dve.wait_ge(s["ew"], 16 * TT)
            for k in range(8):
                ntp = n * 8 + k + 1
                dve.wait_ge(s["tp"], ntp)
                dve.tensor_copy(efm[:, k, q * 128:(q + 1) * 128],
                                ppr[:, (ntp - 1) % 2, 0:128]).then_inc(s["tpe"], 1)

    def f_scalar(act):
        for TT in range(8):
            act.wait_ge(s["tpe"], (TT + 1) * 32)
            dst = io["ein_hbm"].rearrange("(k p) c -> p k c", p=128)[:, :, TT * 512:(TT + 1) * 512]
            prog.gdma(act, dst, efm[:], "l2f0_ew")

    return dict(f_sync=f_sync, f_gps=f_gps, f_pe=f_pe, f_dve=f_dve, f_scalar=f_scalar)


def emit_l2_f1(prog, io):
    sb = prog.sb
    w1q = sb("f1_w1", [128, 8, 1024], F32R)
    w2q = sb("f1_w2", [128, 8, 1024], F32R)
    einb = sb("f1_ein", [128, 2, 8, 512], F32R)
    hmq = sb("f1_hm", [128, 8, 512], F32R)
    tbuf = sb("f1_tb", [128, 2, 512], F32)
    ostg = sb("f1_os", [128, 4, 512], F32)
    b1b = sb("f1_b1", [128, 32], F32)
    b2bc = sb("f1_b2", [128, 2, 512], F32)
    b2r = sb("f1_b2r", [1, 1024], F32R)
    eyeb, onesrow = io["eyeb"], io["onesrow"]
    pp, ppr = io["pp"], io["ppr"]
    names = ["w", "eld", "h1", "b1a", "gel", "w2", "oev", "ow", "b2e", "b2f"]
    s = {n: prog.sem("l2f1_" + n) for n in names}

    def f_sync(sync):
        for p in range(4):
            for TT in range(8):
                n = p * 8 + TT
                if n >= 2:
                    sync.wait_ge(s["h1"], (n - 1) * 8)
                src = io["ein_hbm"].rearrange("(k p) c -> p k c", p=128)[:, :, TT * 512:(TT + 1) * 512]
                prog.gdma(sync, einb[:, n % 2], src, "l2f1_eld")

    def f_pe(pe):
        pe.wait_ge(s["w"], 32)
        pe.matmul(pp[:, 5, :], onesrow[0:1, :], b2r[:, 0:512], start=True, stop=True)
        pe.matmul(pp[:, 4, :], onesrow[0:1, :], b2r[:, 512:1024], start=True, stop=True)
        pe.matmul(pp[0:64, 3, :], eyeb[0:64, 0:64], w1q[0:64, 0, 0:512],
                  start=True, stop=True).then_inc(s["b2f"], 1)
        for p in range(4):
            pe.wait_ge(s["w"], 32 + 32 * (p + 1))
            if p == 0:
                pe.wait_ge(s["b2e"], 1)
            for TT in range(8):
                n = p * 8 + TT
                pe.wait_ge(s["eld"], 16 * (n + 1))
                for fc in range(8):
                    nh = n * 8 + fc
                    if nh >= 2:
                        pe.wait_ge(s["b1a"], nh - 1)
                    for kc in range(8):
                        ins = pe.matmul(pp[:, 4 + fc % 2, :],
                                        w1q[:, kc, fc * 128:(fc + 1) * 128],
                                        einb[:, n % 2, kc], start=(kc == 0), stop=(kc == 7))
                    ins.then_inc(s["h1"], 1)
                for tq in range(4):
                    for dh in range(2):
                        no = (n * 4 + tq) * 2 + dh
                        bank = no % 4
                        if no >= 4:
                            pe.wait_ge(s["oev"], no - 3)
                        for kc in range(8):
                            pe.wait_ge(s["gel"], n * 8 + kc + 1)
                            ins = pe.matmul(pp[:, bank, :],
                                            hmq[:, kc, tq * 128:(tq + 1) * 128],
                                            w2q[:, kc, dh * 512:(dh + 1) * 512],
                                            start=(kc == 0), stop=(kc == 7))
                        ins.then_inc(s["w2"], 1)

    def f_dve(dve):
        dve.wait_ge(s["b2f"], 1)
        dve.tensor_copy(b2bc[:, 0], pp[:, 5, :])
        dve.tensor_copy(b2bc[:, 1], pp[:, 4, :]).then_inc(s["b2e"], 1)
        for p in range(4):
            for TT in range(8):
                n = p * 8 + TT
                for fc in range(8):
                    nh = n * 8 + fc
                    dve.wait_ge(s["h1"], nh + 1)
                    if nh >= 2:
                        dve.wait_ge(s["gel"], nh - 1)
                    bb = b1b[:, p * 8 + fc:p * 8 + fc + 1].broadcast_to([128, 512])
                    dve.tensor_tensor(tbuf[:, nh % 2], pp[:, 4 + fc % 2, :], bb, ALU.add
                                      ).then_inc(s["b1a"], 1)
                for tq in range(4):
                    for dh in range(2):
                        no = (n * 4 + tq) * 2 + dh
                        bank = no % 4
                        dve.wait_ge(s["w2"], no + 1)
                        if no >= 4:
                            dve.wait_ge(s["ow"], 16 * (no - 3))
                        if p == 0:
                            dve.tensor_tensor(ostg[:, no % 4], pp[:, bank, :],
                                              b2bc[:, dh], ALU.add).then_inc(s["oev"], 1)
                        else:
                            dve.tensor_copy(ostg[:, no % 4], pp[:, bank, :]
                                            ).then_inc(s["oev"], 1)

    def f_scalar(act):
        prog.gdma(act, b1b[:], io["b1_in"], "l2f1_w")
        prog.gdma(act, b2r[:], io["b2row"], "l2f1_w")
        for p in range(4):
            if p > 0:
                act.wait_ge(s["w2"], p * 64)
            prog.gdma(act, w1q[:],
                      io["w1"][:, p * 1024:(p + 1) * 1024].rearrange("(k p) f -> p k f", p=128),
                      "l2f1_w")
            prog.gdma(act, w2q[:],
                      io["w2"][p * 1024:(p + 1) * 1024, :].rearrange("(k p) f -> p k f", p=128),
                      "l2f1_w")
            for TT in range(8):
                n = p * 8 + TT
                for fc in range(8):
                    nh = n * 8 + fc
                    act.wait_ge(s["b1a"], nh + 1)
                    if n >= 1 and fc == 0:
                        act.wait_ge(s["w2"], n * 8)
                    act.activation(hmq[:, fc], tbuf[:, nh % 2], AF.Gelu
                                   ).then_inc(s["gel"], 1)

    def f_gps(gps):
        for p in range(4):
            for TT in range(8):
                n = p * 8 + TT
                for tq in range(4):
                    for dh in range(2):
                        no = (n * 4 + tq) * 2 + dh
                        gps.wait_ge(s["oev"], no + 1)
                        t0 = TT * 512 + tq * 128
                        dst = io["exp_out"][t0:t0 + 128, dh * 512:(dh + 1) * 512]
                        kw = {} if p == 0 else {"accum_op": ALU.add}
                        prog.gdma(gps, dst, ostg[:, no % 4], "l2f1_ow", **kw)

    return dict(f_sync=f_sync, f_pe=f_pe, f_dve=f_dve, f_scalar=f_scalar, f_gps=f_gps)


def build_l2_test(consts):
    prog = Prog()
    io = declare_l2(prog)
    with prog.nc.Block() as blk0:
        emit_l2_consts(prog, blk0, io)
    emit_phase_seq(prog, io, [("l2r", emit_l2_routing), ("l2f0", emit_l2_f0),
                              ("l2f1", emit_l2_f1)])
    with prog.nc.Block() as blk2:
        def f_end(gps):
            for nm in ("l2f1_ow", "l2r_tw"):
                gps.wait_ge(prog.sems[nm], prog.counts[nm])
        blk2.gpsimd(f_end)
    prog.close()
    return prog



def build_l3(consts):
    prog = Prog()
    nc = prog.nc
    sb = prog.sb
    eyef_d = nc.dram_tensor("c_eyef", [128, 128], F32, kind="ExternalInput").ap()
    x1_d = nc.dram_tensor("x1", [C, T], F32R, kind="ExternalInput").ap()
    xm_d = nc.dram_tensor("xm", [C, T], F32R, kind="ExternalInput").ap()
    trow_d = nc.dram_tensor("trow", [128, 64], F32, kind="ExternalInput").ap()
    tgate_d = nc.dram_tensor("tgate", [128, 64], F32, kind="ExternalInput").ap()
    eo_d = nc.dram_tensor("eo_all", [8 * CAP, C], F32, kind="ExternalInput").ap()
    out_d = nc.dram_tensor("out_hbm", [T, C], F32, kind="ExternalOutput").ap()

    eyef = sb("l3_eyef", [128, 128], F32)
    trowb = sb("l3_trow", [128, 64], F32)
    tgateb = sb("l3_tg", [128, 64], F32)
    trowi = sb("l3_ti", [128, 64], I32)
    gbuf = sb("l3_g", [128, 2, 2, 1024], F32)
    mbuf = sb("l3_m", [128, 2, 1024], F32)
    x1t = sb("l3_x1", [128, 2, 8, 128], F32R)
    xmt = sb("l3_xm", [128, 2, 8, 128], F32R)
    s12 = sb("l3_s12", [128, 2, 8, 128], F32)
    outt = sb("l3_o", [128, 2, 1024], F32)
    pp = prog.ps("pp", [128, 6, 512], F32)

    names = ["c", "t", "ti", "g", "mo", "ld", "sm", "tp", "ad", "ow"]
    s = {n: prog.sem("l3_" + n) for n in names}

    with prog.nc.Block() as blk:
        def f_sync(sync):
            prog.gdma(sync, eyef[:], eyef_d, "l3_c")
            prog.gdma(sync, trowb[:], trow_d, "l3_c")
            prog.gdma(sync, tgateb[:], tgate_d, "l3_c")
            for tt in range(32):
                ts = slice(tt * 128, (tt + 1) * 128)
                if tt >= 2:
                    sync.wait_ge(s["sm"], tt - 1)
                prog.gdma(sync, x1t[:, tt % 2],
                          x1_d.rearrange("(k p) t -> p k t", p=128)[:, :, ts], "l3_ld")
                prog.gdma(sync, xmt[:, tt % 2],
                          xm_d.rearrange("(k p) t -> p k t", p=128)[:, :, ts], "l3_ld")

        def f_gps(gps):
            gps.wait_ge(s["ti"], 1)
            for tt in range(32):
                for j in range(2):
                    if tt >= 2 and j == 0:
                        gps.wait_ge(s["mo"], tt - 1)
                    ins = gps.indirect_dma_start(
                        out=gbuf[:, tt % 2, j], out_offset=None,
                        in_=eo_d,
                        in_offset=IndirectOffsetOnAxis(trowi[:, tt * 2 + j:tt * 2 + j + 1], 0))
                    prog.inc(ins, "l3_g", dma=True)
                gps.wait_ge(s["ad"], tt + 1)
                prog.gdma(gps, out_d[tt * 128:(tt + 1) * 128, :], outt[:, tt % 2], "l3_ow")

        def f_pe(pe):
            for tt in range(32):
                pe.wait_ge(s["sm"], tt + 1)
                if tt >= 2:
                    pe.wait_ge(s["ad"], tt - 1)
                for k in range(8):
                    b = (tt % 2) * 2 + k // 4
                    pe.transpose(pp[:, b, (k % 4) * 128:(k % 4 + 1) * 128],
                                 s12[:, tt % 2, k], eyef[:]).then_inc(s["tp"], 1)

        def f_dve(dve):
            dve.wait_ge(prog.sem("l3_c"), 48)
            dve.tensor_copy(trowi[:], trowb[:]).then_inc(s["ti"], 1)
            for tt in range(32):
                dve.wait_ge(s["ld"], 32 * (tt + 1))
                dve.tensor_tensor(s12[:, tt % 2].bitcast(F32), x1t[:, tt % 2].bitcast(F32),
                                  xmt[:, tt % 2].bitcast(F32), ALU.add).then_inc(s["sm"], 1)
                dve.wait_ge(s["g"], 32 * (tt + 1))
                g0 = tgateb[:, tt * 2:tt * 2 + 1].broadcast_to([128, 1024])
                g1 = tgateb[:, tt * 2 + 1:tt * 2 + 2].broadcast_to([128, 1024])
                dve.tensor_tensor(mbuf[:, tt % 2], gbuf[:, tt % 2, 0], g0, ALU.mult)
                dve.tensor_tensor(gbuf[:, tt % 2, 1], gbuf[:, tt % 2, 1], g1, ALU.mult)
                dve.tensor_tensor(mbuf[:, tt % 2], mbuf[:, tt % 2], gbuf[:, tt % 2, 1],
                                  ALU.add).then_inc(s["mo"], 1)
                dve.wait_ge(s["tp"], 8 * (tt + 1))
                if tt >= 2:
                    dve.wait_ge(s["ow"], 16 * (tt - 1))
                pb = (tt % 2) * 2
                dve.tensor_tensor(outt[:, tt % 2],
                                  pp[:, pb:pb + 2, :].rearrange("p a b -> p (a b)"),
                                  mbuf[:, tt % 2], ALU.add).then_inc(s["ad"], 1)

        blk.sync(f_sync)
        blk.gpsimd(f_gps)
        blk.tensor(f_pe)
        blk.vector(f_dve)
    with prog.nc.Block() as blk2:
        def f_end(gps):
            gps.wait_ge(prog.sems["l3_ow"], prog.counts["l3_ow"])
        blk2.gpsimd(f_end)
    prog.close()
    return prog


def build_l1(consts):
    prog = Prog()
    nc = prog.nc
    io = declare_common(prog)
    for nm in ("wqT", "wkT", "wvT", "woT"):
        io[nm] = nc.dram_tensor(nm, [C, C], F32R, kind="ExternalInput").ap()
    io["xT"] = nc.dram_tensor("xT", [C, T], F32R, kind="ExternalInput").ap()
    io["gnw_in"] = nc.dram_tensor("gnw_in", [64, 16], F32, kind="ExternalInput").ap()
    io["gnb_in"] = nc.dram_tensor("gnb_in", [64, 16], F32, kind="ExternalInput").ap()
    io["wqkv"] = nc.dram_tensor("wqkv", [C, 3 * C], F32R, kind="ExternalInput").ap()
    io["lngb_in"] = nc.dram_tensor("lngb_in", [128, 16], F32R, kind="ExternalInput").ap()
    io["wproj"] = nc.dram_tensor("wproj", [C, C], F32R, kind="ExternalInput").ap()
    io["wgate"] = nc.dram_tensor("wgate", [C, 8], F32R, kind="ExternalInput").ap()
    io["lngb2_in"] = nc.dram_tensor("lngb2_in", [128, 16], F32, kind="ExternalInput").ap()
    io["ret_hbm"] = nc.dram_tensor("ret_hbm", [C, T], F32R, kind="Internal").ap()
    io["x1_hbm"] = nc.dram_tensor("x1_hbm", [C, T], F32R, kind="ExternalOutput").ap()
    io["xm_hbm"] = nc.dram_tensor("xm_hbm", [C, T], F32R, kind="ExternalOutput").ap()
    io["k_hbm"] = nc.dram_tensor("k_hbm", [H, 64, T], F32R, kind="Internal").ap()
    io["q_hbm"] = nc.dram_tensor("q_hbm", [H, 64, T], F32R, kind="Internal").ap()
    io["v_hbm"] = nc.dram_tensor("v_hbm", [T, C], F32R, kind="Internal").ap()
    io["o_hbm"] = nc.dram_tensor("o_hbm", [H, 64, T], F32R, kind="Internal").ap()
    io["lgT_hbm"] = nc.dram_tensor("lgT_hbm", [8, T], F32, kind="ExternalOutput").ap()
    io["h2tm_hbm"] = nc.dram_tensor("h2tm_hbm", [T, C], F32R, kind="ExternalOutput").ap()
    io["gs_f"] = consts["ret_gs_f"]

    with prog.nc.Block() as blk0:
        emit_const_loads(prog, blk0, io)
    emit_phase_seq(prog, io, [("p1", emit_p1_retention), ("p2", emit_p2_qkv),
                              ("p3", emit_p3_attention), ("p4", emit_p4_gate)])
    with prog.nc.Block() as blk2:
        def f_end(gps):
            for nm in ("p2_x1w", "p3_wod", "p4_lgw", "p4_h2w"):
                gps.wait_ge(prog.sems[nm], prog.counts[nm])
        blk2.gpsimd(f_end)
    prog.close()
    return prog


def _l1_in_map(consts, inputs, s):
    qc = [h * 192 + d for h in range(16) for d in range(64)]
    kc = [h * 192 + 64 + d for h in range(16) for d in range(64)]
    vc = [h * 192 + 128 + d for h in range(16) for d in range(64)]
    Wp = np.ascontiguousarray(inputs["L0_ffn_Wqkv"][:, qc + kc + vc])
    g1 = inputs["L0_ffn_ln1_g"].reshape(8, 128).T
    b1 = inputs["L0_ffn_ln1_b"].reshape(8, 128).T
    g2 = inputs["L0_ffn_ln2_g"].reshape(8, 128).T
    b2 = inputs["L0_ffn_ln2_b"].reshape(8, 128).T
    im = const_in_map(consts)
    im.update({
        "xT": np.ascontiguousarray(inputs["L0_attn_x"][s].T),
        "wqT": np.ascontiguousarray(inputs["L0_attn_wq"].T),
        "wkT": np.ascontiguousarray(inputs["L0_attn_wk"].T),
        "wvT": np.ascontiguousarray(inputs["L0_attn_wv"].T),
        "woT": np.ascontiguousarray(inputs["L0_attn_wo"].T),
        "gnw_in": np.ascontiguousarray(inputs["L0_attn_gn_w"].reshape(16, 64).T),
        "gnb_in": np.ascontiguousarray(inputs["L0_attn_gn_b"].reshape(16, 64).T),
        "wqkv": Wp,
        "lngb_in": np.ascontiguousarray(np.concatenate([g1, b1], axis=1)),
        "wproj": np.ascontiguousarray(inputs["L0_ffn_Wproj"]),
        "wgate": np.ascontiguousarray(inputs["L0_ffn_Wgate"]),
        "lngb2_in": np.ascontiguousarray(np.concatenate([g2, b2], axis=1)),
    })
    return im


def kernel(**inputs):
    inputs = {k: np.asarray(v) for k, v in inputs.items()}
    consts = host_constants()

    prog1 = build_l1(consts)
    in_maps1 = [_l1_in_map(consts, inputs, c >> 1) for c in range(NCORES)]
    res1 = run_bass_kernel_spmd(prog1.nc, in_maps1, core_ids=list(range(NCORES)))
    r1 = [res1.results[2 * s] for s in range(4)]

    prog2 = build_l2_test(consts)
    lgT_all = np.ascontiguousarray(np.stack([r1[s]["lgT_hbm"] for s in range(4)]
                                            ).reshape(32, T).astype(np.float32))
    h2tm_all = np.ascontiguousarray(np.concatenate(
        [r1[s]["h2tm_hbm"] for s in range(4)], axis=0).astype(np.float32))
    cim2 = l2_const_in_map(consts)
    in_maps2 = []
    for e in range(NCORES):
        im = dict(cim2)
        im.update({
            "lgT_all": lgT_all, "h2tm_all": h2tm_all,
            "w1": np.ascontiguousarray(inputs["L0_ffn_W1"][e]),
            "w2": np.ascontiguousarray(inputs["L0_ffn_W2"][e]),
            "b1_in": np.ascontiguousarray(inputs["L0_ffn_b1"][e].reshape(32, 128).T),
            "b2row": np.ascontiguousarray(inputs["L0_ffn_b2"][e].reshape(1, C)),
            "emeta": np.full((128, 1), float(e), dtype=np.float32),
        })
        in_maps2.append(im)
    res2 = run_bass_kernel_spmd(prog2.nc, in_maps2, core_ids=list(range(NCORES)))
    eo_all = np.ascontiguousarray(np.concatenate(
        [res2.results[e]["exp_out"] for e in range(NCORES)], axis=0).astype(np.float32))
    trow = res2.results[0]["trow_hbm"]
    tgate = res2.results[0]["tgate_hbm"]

    prog3 = build_l3(consts)
    in_maps3 = []
    for c in range(NCORES):
        s = c >> 1
        in_maps3.append({
            "c_eyef": consts["eye128"],
            "x1": np.ascontiguousarray(r1[s]["x1_hbm"].astype(np.float32)),
            "xm": np.ascontiguousarray(r1[s]["xm_hbm"].astype(np.float32)),
            "trow": np.ascontiguousarray(trow[s].astype(np.float32)),
            "tgate": np.ascontiguousarray(tgate[s].astype(np.float32)),
            "eo_all": eo_all,
        })
    res3 = run_bass_kernel_spmd(prog3.nc, in_maps3, core_ids=list(range(NCORES)))
    out = np.stack([res3.results[2 * s]["out_hbm"] for s in range(4)], axis=0)
    return np.ascontiguousarray(out.astype(np.float32))


# version 04
# speedup vs baseline: 1.1523x; 1.1029x over previous
import math
import numpy as np
from contextlib import ExitStack

import concourse.bass as bass
import concourse.mybir as mybir
from concourse.bass_utils import run_bass_kernel_spmd
from concourse.bass_core_rust import IndirectOffsetOnAxis

F32 = mybir.dt.float32
F32R = mybir.dt.float32r
I32 = mybir.dt.int32
AF = mybir.ActivationFunctionType
ALU = mybir.AluOpType

B, T, C = 4, 4096, 1024
H = 16
HD = 64
NCORES = 8
E = 8
TOPK = 2
CAP = 4096
N_TOK = B * T
EPS = 1e-5
SC = 512
RC = 128
NSC = T // SC
NRC = SC // RC


def host_constants():
    c = {}
    c["eye128"] = np.eye(128, dtype=np.float32)
    jj, tt = np.meshgrid(np.arange(128), np.arange(128), indexing="ij")
    c["tri_incl"] = (jj <= tt).astype(np.float32)
    c["tri_excl"] = (jj < tt).astype(np.float32)
    inv_freq = 1.0 / (10000.0 ** (np.arange(0, HD, 2, dtype=np.float64) / HD))
    ang = np.arange(T, dtype=np.float64)[None, :] * inv_freq[:, None]
    c["rope_cos"] = np.cos(ang).astype(np.float32)
    c["rope_sin"] = np.sin(ang).astype(np.float32)
    lo = math.log(1.0 - 0.999)
    hi = math.log(1.0 - 0.9)
    gam = 1.0 - np.exp(np.linspace(lo, hi, H, dtype=np.float64))
    c["gammas"] = gam.astype(np.float32)
    j = np.arange(RC)[:, None]
    i = np.arange(RC)[None, :]
    dts = []
    for h in range(H):
        dt = np.where(i >= j, gam[h] ** np.maximum(i - j, 0), 0.0)
        dts.append(dt)
    c["ret_dt"] = np.concatenate(dts, axis=1).astype(np.float32)
    a = np.arange(RC)[None, :]
    gcross = np.stack([np.tile(gam[h] ** (a + 1), (64, 1)) for h in range(H)], axis=1)
    c["ret_gcross"] = gcross.astype(np.float32)
    a = np.arange(RC)[:, None]
    kd = np.stack([np.tile(gam[h] ** (RC - 1 - a), (1, 64)) for h in range(H)], axis=1)
    c["ret_kdec3"] = kd.astype(np.float32)
    c["ret_gs_f"] = [float(gam[h] ** RC) for h in range(H)]

    ind = np.zeros((128, 2), dtype=np.float32)
    ind[0:64, 0] = 1.0
    ind[64:128, 1] = 1.0
    c["headsel"] = ind
    c["headselT"] = np.ascontiguousarray(ind.T)
    c["onesrow"] = np.ones((1, 128), dtype=np.float32)
    kk, qq = np.meshgrid(np.arange(128), np.arange(128), indexing="ij")
    c["diagmask"] = (kk <= qq).astype(np.float32)
    c["iota8"] = np.tile(np.arange(8, dtype=np.float32)[None, :], (128, 1))
    c["iota128"] = np.arange(128, dtype=np.float32).reshape(128, 1)
    c["ones128"] = np.ones((128, 1), dtype=np.float32)
    c["onesall"] = np.ones((128, 128), dtype=np.float32)
    c["iota8p"] = c["iota8"] + 1000.0
    pp_, aa_ = np.meshgrid(np.arange(128), np.arange(32), indexing="ij")
    c["tok0"] = (pp_ + 128.0 * aa_).astype(np.float32)
    pp_, ff_ = np.meshgrid(np.arange(128), np.arange(512), indexing="ij")
    dm = np.stack([(pp_ <= ff_ - 128 * r).astype(np.float32) for r in range(4)], axis=1)
    c["diagm"] = dm.reshape(128, 4 * 512)
    return c


class Prog:

    def __init__(self, name="mk"):
        s = bass.Bass("TRN2", target_bir_lowering=False, enable_asserts=False,
                      detect_race_conditions=False)
        self.nc = s
        self.stack = ExitStack()
        self.sems = {}
        self.counts = {}

    def sem(self, name):
        if name not in self.sems:
            self.sems[name] = self.stack.enter_context(self.nc.semaphore(name))
            self.counts[name] = 0
        return self.sems[name]

    def inc(self, instr, name, dma=False):
        n = 16 if dma else 1
        instr.then_inc(self.sem(name), n)
        self.counts[name] += n
        return self.counts[name]

    def sb(self, name, shape, dtype):
        st = self.phase_stack if getattr(self, "phase_stack", None) is not None else self.stack
        return st.enter_context(self.nc.sbuf_tensor(name, shape, dtype))

    def open_phase(self):
        self.phase_stack = ExitStack()

    def close_phase(self):
        self.phase_stack.close()
        self.phase_stack = None

    def ps(self, name, shape, dtype=F32):
        return self.stack.enter_context(self.nc.psum_tensor(name, shape, dtype))

    DEPTH2 = {"p1_ld", "p1_w", "p2_ld", "p2_w", "p2_kqw", "p2_vw", "p2_x1w",
              "p2_xmw", "p3_kv", "p3_otl", "p3_wpw", "p3_ow", "p3_wod",
              "p4_ld", "p4_h2w", "p4_lgw", "l2f0_gt", "l2f0_ew", "l2f1_eld",
              "l2f1_ow", "l2f1_w", "l3_ld", "l3_ow"}

    def gdma(self, eng, out, in_, sname, **kw):
        sem = self.sem(sname)
        lag = 16 if sname in self.DEPTH2 else 0
        if self.counts[sname] > lag:
            eng.wait_ge(sem, self.counts[sname] - lag)
        ins = eng.dma_start(out=out, in_=in_, **kw)
        ins.then_inc(sem, 16)
        self.counts[sname] += 16
        return ins

    def close(self):
        self.stack.close()


def ceil_div(a, b):
    return (a + b - 1) // b






def emit_p1_retention(prog, io):
    sb = prog.sb
    NG, HPG = 4, 4
    GF = 256

    xin = sb("p1_xin", [128, 1, 8, 512], F32R)
    xsq = sb("p1_xsq", [128, 8, 512], F32R)
    xhat = xsq
    w_q = sb("p1_wq", [128, 8, GF], F32R)
    w_k = sb("p1_wk", [128, 8, GF], F32R)
    w_v = sb("p1_wv", [128, 8, GF], F32R)
    w_o = sb("p1_wo", [64, 4, 1024], F32R)
    qtm = sb("p1_qtm", [128, GF], F32R)
    ktm = sb("p1_ktm", [128, 4, GF], F32R)
    vtm = sb("p1_vtm", [128, 4, GF], F32R)
    qfm = sb("p1_qfm", [64, 4, 512], F32R)
    kfm = sb("p1_kfm", [64, 4, 512], F32R)
    ybuf = sb("p1_ybuf", [64, 4, T], F32R)
    tstate = sb("p1_tst", [64, 4, 64], F32R)
    abar = sb("p1_abar", [128, 512], F32R)
    qhat = sb("p1_qhat", [64, 4, 128], F32R)
    khat2 = sb("p1_khat2", [128, 256], F32R)
    ropet = sb("p1_ropet", [128, 2, 128], F32)
    stat_m = sb("p1_sm", [1, 512], F32R)
    stat_a = sb("p1_sa", [1, 512], F32R)
    mb_bc = sb("p1_mb", [128, 512], F32R)
    ab_bc = sb("p1_ab", [128, 512], F32R)
    stat_t = ab_bc[0:1, :].bitcast(F32)
    stat_t2 = mb_bc[0:1, :].bitcast(F32)
    gnred = sb("p1_gnred", [64, 8], F32)
    gnsml = sb("p1_gnsml", [1, 12], F32)
    gnstg = sb("p1_gnstg", [1, 512], F32R)
    gnsqt = sb("p1_gnsqt", [1, 4], F32)
    gn_s = sb("p1_gns", [64, 4], F32)
    gn_t = sb("p1_gnt", [64, 4], F32)
    gn_b = sb("p1_gnb", [64, 4], F32)
    dbgstage = sb("p1_dbgst", [64, 20], F32)
    wostg = sb("p1_wostg", [128, 4, 512], F32)

    cosb, sinb = io["cosb"], io["sinb"]
    onesb, onesrow, eyeb = io["onesb"], io["onesrow"], io["eyeb"]
    dtb, gcrossb, kdec3b = io["dtb"], io["gcrossb"], io["kdec3b"]
    gnwb, gnbb = io["gnwb"], io["gnbb"]
    xT, ret_hbm = io["xT"], io["ret_hbm"]
    pp, ppr = io["pp"], io["ppr"]
    gs_f = io["gs_f"]

    names = ["ld", "w", "sq", "stat", "sml", "sml2", "bc", "bce", "xh", "prj",
             "vev", "rope", "tp", "tpe", "rpe", "rdv", "gn", "gn2", "gn3",
             "gnA", "yp", "wo", "woe", "wod", "dbg"]
    s = {n: prog.sem("p1_" + n) for n in names}
    dbg = io.get("dbg")

    def prj_consumer(n):
        i, r = divmod(n, 12)
        tt, j = divmod(r, 3)
        if j < 2:
            return s["rope"], i * 8 + tt * 2 + j + 1
        return s["vev"], i * 4 + tt + 1

    def f_sync(sync):
        for g in range(NG):
            if g > 0:
                sync.wait_ge(s["woe"], g * 64)
            for (wt, dst) in ((io["wqT"], w_q), (io["wkT"], w_k), (io["wvT"], w_v)):
                src = wt[:, g * GF:(g + 1) * GF].rearrange("(k p) f -> p k f", p=128)
                prog.gdma(sync, dst[:], src, "p1_w")
            src = io["woT"][g * GF:(g + 1) * GF, :].rearrange("(h p) f -> p h f", p=64)
            prog.gdma(sync, w_o[:], src, "p1_w")
            prog.gdma(sync, dtb[:], io["c_dt"][:, g * 512:(g + 1) * 512], "p1_w")
            prog.gdma(sync, gcrossb[:].rearrange("p a b -> p (a b)"),
                      io["c_gcross"][:, g * 512:(g + 1) * 512], "p1_w")
            prog.gdma(sync, kdec3b[:].rearrange("p a b -> p (a b)"),
                      io["c_kdec3"][:, g * 256:(g + 1) * 256], "p1_w")
            for sc in range(NSC):
                i = g * NSC + sc
                if i >= 1:
                    sync.wait_ge(s["xh"], i)
                src = xT.rearrange("(k p) t -> p k t", p=128)[:, :, sc * 512:(sc + 1) * 512]
                prog.gdma(sync, xin[:, 0], src, "p1_ld")
            if g == 0:
                for tt in range(8):
                    for hf in range(2):
                        sync.wait_ge(s["woe"], tt * 8 + hf * 4 + 4)
                        dst = ret_hbm[hf * 512:(hf + 1) * 512, tt * 512:(tt + 1) * 512
                                      ].rearrange("(k p) t -> p k t", p=128).bitcast(F32)
                        prog.gdma(sync, dst, wostg[:], "p1_wod")

    def f_gps(gps):
        for g in range(1, NG):
            for tt in range(8):
                for hf in range(2):
                    gps.wait_ge(s["woe"], g * 64 + tt * 8 + hf * 4 + 4)
                    dst = ret_hbm[hf * 512:(hf + 1) * 512, tt * 512:(tt + 1) * 512
                                  ].rearrange("(k p) t -> p k t", p=128).bitcast(F32)
                    prog.gdma(gps, dst, wostg[:], "p1_wod", accum_op=ALU.add)

    def f_act(act):
        for g in range(NG):
            for sc in range(NSC):
                i = g * NSC + sc
                act.wait_ge(s["ld"], 16 * (i + 1))
                if i > 0:
                    act.wait_ge(s["prj"], 12 * i)
                if g > 0 and sc == 0:
                    act.wait_ge(s["gn"], 2 * g - 1)
                act.activation(xsq[:], xin[:, 0], AF.Square).then_inc(s["sq"], 1)
                act.wait_ge(s["sml"], i + 1)
                act.activation(stat_a[:], stat_t, AF.Sqrt).then_inc(s["sml2"], 1)
                if dbg and i == 0:
                    act.wait_ge(s["rdv"], 8)
                    prog.gdma(act, dbg["xhat"], xhat[:].rearrange("p a b -> p (a b)"), "p1_dbg")
                    prog.gdma(act, dbg["qfm"], qfm[:].rearrange("p a b -> p (a b)"), "p1_dbg")
                    prog.gdma(act, dbg["kfm"], kfm[:].rearrange("p a b -> p (a b)"), "p1_dbg")
                    prog.gdma(act, dbg["vtm"], vtm[:].rearrange("p a b -> p (a b)"), "p1_dbg")
                    act.wait_ge(prog.sems["p1_dbg"], prog.counts["p1_dbg"])
            if dbg and g == 0:
                act.wait_ge(s["gn"], 2 * g + 1)
                prog.gdma(act, dbg["y"], ybuf[:].rearrange("p a b -> p (a b)"), "p1_dbg")
                act.wait_ge(prog.sems["p1_dbg"], prog.counts["p1_dbg"])
            act.wait_ge(s["gn"], 2 * g + 2)
            act.activation(gnsqt[:], gnsml[:, 4:8], AF.Sqrt).then_inc(s["gnA"], 1)
            if dbg and g == 0:
                act.wait_ge(s["yp"], 4)
                prog.gdma(act, dbg["ypost"], ybuf[:].rearrange("p a b -> p (a b)"), "p1_dbg")
                prog.gdma(act, dbg["gn2"][:, 0:20], dbgstage[:], "p1_dbg")
                act.wait_ge(prog.sems["p1_dbg"], prog.counts["p1_dbg"])

    def f_pe(pe):
        for g in range(NG):
            pe.wait_ge(s["w"], 112 * (g + 1))
            if g > 0:
                pe.wait_ge(s["woe"], 64 * g)
                pe.wait_ge(s["gn"], 2 * g)
                pe.wait_ge(s["yp"], g * HPG)
            for sc in range(NSC):
                i = g * NSC + sc
                pe.wait_ge(s["ld"], 16 * (i + 1))
                pe.wait_ge(s["sq"], i + 1)
                if i > 0:
                    pe.wait_ge(s["rdv"], 8 * i)
                for k in range(8):
                    pe.matmul(pp[0:1, 0, :], onesb[:, 0:1], xin[:, 0, k],
                              start=(k == 0), stop=(k == 7))
                for k in range(8):
                    ins = pe.matmul(pp[0:1, 1, :], onesb[:, 0:1], xsq[:, k],
                                    start=(k == 0), stop=(k == 7))
                ins.then_inc(s["stat"], 1)
                pe.wait_ge(s["sml2"], i + 1)
                if i > 0:
                    pe.wait_ge(s["bce"], i)
                pe.matmul(pp[:, 2, :], onesrow[0:1, :], stat_m[:], start=True, stop=True)
                pe.matmul(pp[:, 3, :], onesrow[0:1, :], stat_a[:], start=True, stop=True
                          ).then_inc(s["bc"], 1)
                pe.wait_ge(s["xh"], i + 1)
                for tt in range(4):
                    for j, wbuf in enumerate((w_q, w_k, w_v)):
                        n = i * 12 + tt * 3 + j
                        if n >= 2:
                            cs, cv = prj_consumer(n - 2)
                            pe.wait_ge(cs, cv)
                        bank = 4 + n % 2
                        for k in range(8):
                            ins = pe.matmul(pp[:, bank, 0:GF], xhat[:, k, tt * 128:(tt + 1) * 128],
                                            wbuf[:, k], start=(k == 0), stop=(k == 7))
                        ins.then_inc(s["prj"], 1)
                    pe.wait_ge(s["rope"], i * 8 + tt * 2 + 2)
                    for jj, srcb in enumerate((qtm[:], ktm[:, tt])):
                        for h in range(4):
                            ntp = (i * 4 + tt) * 8 + jj * 4 + h + 1
                            if ntp >= 3:
                                pe.wait_ge(s["tpe"], ntp - 2)
                            pe.transpose(ppr[:, (ntp - 1) % 2, 0:128][0:64, :] if False else ppr[0:64, (ntp - 1) % 2, 0:128],
                                         srcb[:, h * 64:(h + 1) * 64], eyeb[:]
                                         ).then_inc(s["tp"], 1)
                for rc in range(4):
                    pe.wait_ge(s["tpe"], (i * 4 + rc + 1) * 8)
                    pe.wait_ge(s["vev"], i * 4 + rc + 1)
                    first = (sc == 0 and rc == 0)
                    idx = i * 4 + rc
                    rcs = slice(rc * 128, (rc + 1) * 128)
                    for h in range(4):
                        ins = pe.matmul(pp[:, 2, h * 128:(h + 1) * 128], kfm[0:64, h, rcs],
                                        qfm[0:64, h, rcs], start=True, stop=True)
                    ins.then_inc(s["rpe"], 1)
                    pe.wait_ge(s["rdv"], 2 * idx + 1)
                    for h in range(4):
                        hs = slice(h * 64, (h + 1) * 64)
                        pe.matmul(pp[0:64, 3, h * 128:(h + 1) * 128],
                                  vtm[:, rc, hs], abar[:, h * 128:(h + 1) * 128],
                                  start=True, stop=first)
                        if not first:
                            pe.matmul(pp[0:64, 3, h * 128:(h + 1) * 128],
                                      tstate[:, h, :], qhat[:, h, :],
                                      start=False, stop=True)
                        ins = pe.matmul(pp[0:64, 0, 256 + h * 64:256 + (h + 1) * 64],
                                        khat2[:, hs], vtm[:, rc, hs], start=True, stop=True)
                    ins.then_inc(s["rpe"], 1)
            pe.wait_ge(s["gn"], 2 * g + 1)
            pe.matmul(pp[0:1, 0, 0:8], onesb[0:64, 0:1].bitcast(F32), gnred[:], start=True, stop=True)
            pe.matmul(pp[:, 4, :], eyeb[:], xhat[:, 0], start=True, stop=True
                      ).then_inc(s["gn2"], 1)
            pe.wait_ge(s["gn3"], g + 1)
            pe.matmul(pp[0:64, 1, :], onesrow[0:1, 0:64], gnstg[:], start=True, stop=True)
            pe.matmul(pp[:, 4, :], eyeb[:], xhat[:, 0], start=True, stop=True
                      ).then_inc(s["gn2"], 1)
            pe.wait_ge(s["yp"], (g + 1) * HPG)
            for tt in range(8):
                for oc in range(8):
                    nwo = g * 64 + tt * 8 + oc
                    bank = 4 + nwo % 2
                    if nwo >= 2:
                        pe.wait_ge(s["woe"], nwo - 1)
                    for h in range(4):
                        ins = pe.matmul(pp[:, bank, :], w_o[:, h, oc * 128:(oc + 1) * 128],
                                        ybuf[:, h, tt * 512:(tt + 1) * 512],
                                        start=(h == 0), stop=(h == 3))
                    ins.then_inc(s["wo"], 1)

    def f_dve(dve):
        for g in range(NG):
            dve.wait_ge(s["w"], 112 * (g + 1))
            if dbg and g == 1:
                dve.wait_ge(prog.sems["p1_dbg"], prog.counts["p1_dbg"])
            for sc in range(NSC):
                i = g * NSC + sc
                dve.wait_ge(s["stat"], i + 1)
                dve.tensor_scalar_mul(stat_m[:], pp[0:1, 0, :], 1.0 / 1024.0)
                dve.tensor_scalar_mul(stat_t, pp[0:1, 1, :], 1.0 / 1024.0)
                dve.tensor_tensor(stat_t2, stat_m[:].bitcast(F32), stat_m[:].bitcast(F32), ALU.mult)
                dve.tensor_tensor(stat_t, stat_t, stat_t2, ALU.subtract)
                dve.tensor_scalar_add(stat_t, stat_t, EPS)
                dve.reciprocal(stat_t, stat_t).then_inc(s["sml"], 1)
                dve.wait_ge(s["bc"], i + 1)
                dve.tensor_copy(mb_bc[:], pp[:, 2, :])
                dve.tensor_copy(ab_bc[:], pp[:, 3, :]).then_inc(s["bce"], 1)
                for k in range(8):
                    dve.tensor_tensor(xsq[:, k], xin[:, 0, k], mb_bc[:], ALU.subtract)
                    ins = dve.tensor_tensor(xhat[:, k], xsq[:, k], ab_bc[:], ALU.mult)
                ins.then_inc(s["xh"], 1)
                for tt in range(4):
                    base_prj = i * 12 + tt * 3
                    for j, dstb in ((0, qtm[:]), (1, ktm[:, tt])):
                        bank = 4 + (base_prj + j) % 2
                        dve.wait_ge(s["prj"], base_prj + j + 1)
                        if j == 0 and (i * 4 + tt) >= 1:
                            dve.wait_ge(s["tp"], (i * 4 + tt - 1) * 8 + 4)
                        dve.tensor_copy(dstb, pp[:, bank, 0:GF]).then_inc(s["rope"], 1)
                    bank = 4 + (base_prj + 2) % 2
                    dve.wait_ge(s["prj"], base_prj + 3)
                    dve.tensor_copy(vtm[:, tt], pp[:, bank, 0:GF]).then_inc(s["vev"], 1)
                    for n in range(8):
                        ntp = (i * 4 + tt) * 8 + n + 1
                        dve.wait_ge(s["tp"], ntp)
                        dstb = qfm if n < 4 else kfm
                        h = n % 4
                        dve.tensor_copy(dstb[:, h, tt * 128:(tt + 1) * 128],
                                        ppr[0:64, (ntp - 1) % 2, 0:128]).then_inc(s["tpe"], 1)
                for rc in range(4):
                    first = (sc == 0 and rc == 0)
                    if g > 0 and sc == 0 and rc == 0:
                        dve.wait_ge(s["wo"], g * 64)
                    idx = i * 4 + rc
                    rcs = slice(rc * 128, (rc + 1) * 128)
                    hq = 4 * g
                    dve.wait_ge(s["rpe"], 2 * idx + 1)
                    dve.tensor_tensor(abar[:], pp[:, 2, 0:512],
                                      dtb[:], ALU.mult)
                    if not first:
                        dve.tensor_tensor(qhat[:], qfm[:, :, rcs], gcrossb[:], ALU.mult)
                    dve.tensor_tensor(khat2[:].rearrange("p (a b) -> p a b", b=64),
                                      ktm[:, rc].rearrange("p (a b) -> p a b", b=64),
                                      kdec3b[:], ALU.mult).then_inc(s["rdv"], 1)
                    dve.wait_ge(s["rpe"], 2 * idx + 2)
                    dve.tensor_copy(ybuf[:, :, (sc * 4 + rc) * 128:(sc * 4 + rc + 1) * 128],
                                    pp[0:64, 3, 0:512].rearrange("p (a b) -> p a b", b=128))
                    for h in range(4):
                        if first:
                            dve.tensor_copy(tstate[:, h, :], pp[0:64, 0, 256 + h * 64:256 + (h + 1) * 64])
                        else:
                            dve.scalar_tensor_tensor(tstate[:, h, :], tstate[:, h, :],
                                                     gs_f[hq + h], pp[0:64, 0, 256 + h * 64:256 + (h + 1) * 64],
                                                     op0=ALU.mult, op1=ALU.add)
                    dve.engine_nop().then_inc(s["rdv"], 1)
            for h in range(4):
                dve.reduce_sum(gnred[:, h:h + 1], ybuf[:, h], axis=mybir.AxisListType.X)
            for h in range(4):
                dve.tensor_tensor(xsq[0:64, 0:8].rearrange("p a t -> p (a t)"),
                                  ybuf[:, h], ybuf[:, h], ALU.mult)
                dve.reduce_sum(gnred[:, 4 + h:5 + h],
                               xsq[0:64, 0:8].rearrange("p a t -> p (a t)"),
                               axis=mybir.AxisListType.X)
            dve.engine_nop().then_inc(s["gn"], 1)
            dve.wait_ge(s["gn2"], 2 * g + 1)
            dve.tensor_scalar_mul(gnsml[:, 0:8], pp[0:1, 0, 0:8], 1.0 / (64.0 * 4096.0))
            dve.tensor_tensor(gnsml[:, 8:12], gnsml[:, 0:4], gnsml[:, 0:4], ALU.mult)
            dve.tensor_tensor(gnsml[:, 4:8], gnsml[:, 4:8], gnsml[:, 8:12], ALU.subtract)
            dve.tensor_scalar_add(gnsml[:, 4:8], gnsml[:, 4:8], EPS).then_inc(s["gn"], 1)
            dve.wait_ge(s["gnA"], g + 1)
            with prog.nc.allow_low_precision("f32r stage"):
                dve.reciprocal(gnstg[:, 0:4], gnsqt[:])
                dve.tensor_copy(gnstg[:, 4:8], gnsml[:, 0:4]).then_inc(s["gn3"], 1)
            dve.wait_ge(s["gn2"], 2 * g + 2)
            dve.tensor_copy(mb_bc[0:64, :], pp[0:64, 1, :])
            dve.tensor_copy(ab_bc[0:64, :], mb_bc[0:64, :])
            dve.tensor_tensor(gn_s[:], mb_bc[0:64, 0:4].bitcast(F32), gnwb[:, 4 * g:4 * g + 4], ALU.mult)
            dve.tensor_copy(ab_bc[0:64, :], mb_bc[0:64, :])
            dve.tensor_tensor(gn_t[:], mb_bc[0:64, 4:8].bitcast(F32), gn_s[:], ALU.mult)
            dve.tensor_copy(ab_bc[0:64, :], mb_bc[0:64, :])
            dve.tensor_tensor(gn_b[:], gnbb[:, 4 * g:4 * g + 4], gn_t[:], ALU.subtract)
            dve.tensor_copy(ab_bc[0:64, :], mb_bc[0:64, :])
            if dbg and g == 0:
                dve.tensor_copy(dbgstage[:, 0:4], gn_s[:])
                dve.tensor_copy(dbgstage[:, 4:8], gn_b[:])
                dve.tensor_copy(dbgstage[:, 8:16], pp[0:64, 1, 0:8])
                dve.tensor_copy(dbgstage[:, 16:20], gn_t[:])
            for h in range(4):
                dve.tensor_scalar_mul(ybuf[:, h], ybuf[:, h], gn_s[:, h:h + 1])
                dve.tensor_scalar_add(ybuf[:, h], ybuf[:, h], gn_b[:, h:h + 1]
                                      ).then_inc(s["yp"], 1)
            for tt in range(8):
                for oc in range(8):
                    nwo = g * 64 + tt * 8 + oc
                    bank = 4 + nwo % 2
                    dve.wait_ge(s["wo"], nwo + 1)
                    nhalf = (g * 8 + tt) * 2 + oc // 4
                    if nhalf >= 1 and oc % 4 == 0:
                        dve.wait_ge(s["wod"], 16 * nhalf)
                    dve.tensor_copy(wostg[:, oc % 4], pp[:, bank, :]).then_inc(s["woe"], 1)

    return dict(f_sync=f_sync, f_gps=f_gps, f_act=f_act, f_pe=f_pe, f_dve=f_dve)



def declare_common(prog):
    nc = prog.nc
    sb = prog.sb
    io = {}
    cdefs = [
        ("c_ones", [128, 1], F32R, "ones128", "onesb"),
        ("c_onesrow", [1, 128], F32R, "onesrow", "onesrow"),
        ("c_eye", [128, 128], F32R, "eye128", "eyeb"),
        ("c_dt", [128, 16 * 128], F32, "ret_dt", "dtb"),
        ("c_gcross", [64, 16 * 128], F32, "ret_gcross", "gcrossb"),
        ("c_kdec3", [128, 16 * 64], F32, "ret_kdec3", "kdec3b"),
        ("c_onesall", [128, 128], F32R, "onesall", "onesall"),
        ("c_diag", [128, 4 * 512], F32R, "diagm", "diagmb"),
        ("c_cos", [T, 32], F32, "rope_cos_tm", "cosb"),
        ("c_sin", [T, 32], F32, "rope_sin_tm", "sinb"),
    ]
    io["_cdefs"] = cdefs
    for nm, shape, dt_, ckey, sbkey in cdefs:
        io[nm] = nc.dram_tensor(nm, shape, dt_, kind="ExternalInput").ap()
    io["kdec3b"] = sb("cb_kdec3", [128, 4, 64], F32)
    io["cosb"] = sb("cb_cos", [128, 32, 32], F32)
    io["sinb"] = sb("cb_sin", [128, 32, 32], F32)
    io["onesb"] = sb("cb_ones", [128, 1], F32R)
    io["onesall"] = sb("cb_onesall", [128, 128], F32R)
    io["onesrow"] = sb("cb_onesrow", [1, 128], F32R)
    io["eyeb"] = sb("cb_eye", [128, 128], F32R)
    io["dtb"] = sb("cb_dt", [128, 4 * 128], F32)
    io["gcrossb"] = sb("cb_gcross", [64, 4, 128], F32)
    io["gnwb"] = sb("cb_gnw", [64, 16], F32)
    io["gnbb"] = sb("cb_gnb", [64, 16], F32)
    io["pp"] = prog.ps("pp", [128, 6, 512], F32)
    io["ppr"] = prog.ps("ppr", [128, 2, 512], F32R)
    return io


def emit_const_loads(prog, blk, io):
    s_c = prog.sem("c_ld")

    def f_sync(sync):
        n = 0
        pairs = [
            (io["onesb"][:], io["c_ones"]),
            (io["onesrow"][:], io["c_onesrow"]),
            (io["eyeb"][:], io["c_eye"]),
            (io["onesall"][:], io["c_onesall"]),
            (io["cosb"][:], io["c_cos"].rearrange("(n p) f -> p n f", p=128)),
            (io["sinb"][:], io["c_sin"].rearrange("(n p) f -> p n f", p=128)),
            (io["gnwb"][:], io["gnw_in"]),
            (io["gnbb"][:], io["gnb_in"]),
        ]
        for dst, src in pairs:
            prog.gdma(sync, dst, src, "c_ld")
            n += 16
        sync.wait_ge(prog.sem("c_ld"), prog.counts["c_ld"])

    blk.sync(f_sync)


def const_in_map(consts):
    return {
        "c_ones": consts["ones128"], "c_onesrow": consts["onesrow"],
        "c_eye": consts["eye128"], "c_dt": consts["ret_dt"],
        "c_onesall": consts["onesall"], "c_diag": consts["diagm"],
        "c_gcross": consts["ret_gcross"].reshape(64, 16 * 128),
        "c_kdec3": consts["ret_kdec3"].reshape(128, 16 * 64),
        "c_cos": np.ascontiguousarray(consts["rope_cos"].T),
        "c_sin": np.ascontiguousarray(consts["rope_sin"].T),
    }


def build_p1_test(consts):
    prog = Prog()
    nc = prog.nc
    io = declare_common(prog)
    io["xT"] = nc.dram_tensor("xT", [C, T], F32R, kind="ExternalInput").ap()
    io["wqT"] = nc.dram_tensor("wqT", [C, C], F32R, kind="ExternalInput").ap()
    io["wkT"] = nc.dram_tensor("wkT", [C, C], F32R, kind="ExternalInput").ap()
    io["wvT"] = nc.dram_tensor("wvT", [C, C], F32R, kind="ExternalInput").ap()
    io["woT"] = nc.dram_tensor("woT", [C, C], F32R, kind="ExternalInput").ap()
    io["gnw_in"] = nc.dram_tensor("gnw_in", [64, 16], F32, kind="ExternalInput").ap()
    io["gnb_in"] = nc.dram_tensor("gnb_in", [64, 16], F32, kind="ExternalInput").ap()
    io["ret_hbm"] = nc.dram_tensor("ret_hbm", [C, T], F32R, kind="ExternalOutput").ap()
    io["gs_f"] = consts["ret_gs_f"]
    io["dbg"] = {
        "xhat": nc.dram_tensor("dbg_xhat", [128, 8 * 512], F32R, kind="ExternalOutput").ap(),
        "qfm": nc.dram_tensor("dbg_qfm", [64, 4 * 512], F32R, kind="ExternalOutput").ap(),
        "kfm": nc.dram_tensor("dbg_kfm", [64, 4 * 512], F32R, kind="ExternalOutput").ap(),
        "vtm": nc.dram_tensor("dbg_vtm", [128, 4 * 256], F32R, kind="ExternalOutput").ap(),
        "y": nc.dram_tensor("dbg_y", [64, 4 * 4096], F32R, kind="ExternalOutput").ap(),
        "ypost": nc.dram_tensor("dbg_ypost", [64, 4 * 4096], F32R, kind="ExternalOutput").ap(),
        "gns": nc.dram_tensor("dbg_gns", [64, 12], F32, kind="ExternalOutput").ap(),
        "gn2": nc.dram_tensor("dbg_gn2", [64, 20], F32, kind="ExternalOutput").ap(),
    }

    with prog.nc.Block() as blk0:
        emit_const_loads(prog, blk0, io)
    fns = emit_p1_retention(prog, io)
    with prog.nc.Block() as blk1:
        blk1.sync(fns["f_sync"])
        blk1.gpsimd(fns["f_gps"])
        blk1.scalar(fns["f_act"])
        blk1.tensor(fns["f_pe"])
        blk1.vector(fns["f_dve"])
    s_wod = prog.sems["p1_wod"]
    with prog.nc.Block() as blk2:
        def f_end(gps):
            gps.wait_ge(s_wod, prog.counts["p1_wod"])
        blk2.gpsimd(f_end)
    prog.close()
    return prog


def kernel(**inputs):
    raise NotImplementedError("under construction")



def emit_p2_qkv(prog, io):
    sb = prog.sb
    NG, GF = 4, 256

    xin = sb("p2_xin", [128, 8, 512], F32R)
    ret = sb("p2_ret", [128, 8, 512], F32R)
    w_q = sb("p2_wq", [128, 8, GF], F32R)
    w_k = sb("p2_wk", [128, 8, GF], F32R)
    w_v = sb("p2_wv", [128, 8, GF], F32R)
    qtm = sb("p2_qtm", [128, GF], F32R)
    ktm = sb("p2_ktm", [128, GF], F32R)
    vtm = sb("p2_vtm", [128, 4, GF], F32R)
    qfm = sb("p2_qfm", [64, 4, 512], F32R)
    kfm = sb("p2_kfm", [64, 4, 512], F32R)
    xhb = sb("p2_xhb", [128, 8, 512], F32R)
    mb_bc = sb("p2_mb", [128, 512], F32)
    ab_bc = sb("p2_ab", [128, 512], F32)
    wtmp = sb("p2_wt", [128, 512], F32)
    rs_out = sb("p2_rs", [128, 512], F32)
    ropet = sb("p2_rope", [128, 2, 128], F32)
    lngb = sb("p2_lngb", [128, 2, 8], F32R)
    brow = sb("p2_brow", [1, 3 * GF], F32R)

    cosb, sinb = io["cosb"], io["sinb"]
    onesrow, eyeb, onesall = io["onesrow"], io["eyeb"], io["onesall"]
    xT, ret_hbm = io["xT"], io["ret_hbm"]
    x1_hbm, xm_hbm = io["x1_hbm"], io["xm_hbm"]
    k_hbm, v_hbm, q_hbm = io["k_hbm"], io["v_hbm"], io["q_hbm"]
    wqkv = io["wqkv"]
    pp, ppr = io["pp"], io["ppr"]

    names = ["ld", "w", "x1", "x1w", "sq", "stat", "st_ev", "sml", "sml2",
             "xh", "xh2", "xmw", "prj", "vev", "rope", "tp", "tpe", "bias",
             "gsc", "kqw", "vw"]
    s = {n: prog.sem("p2_" + n) for n in names}

    def prj_consumer(n):
        i, r = divmod(n, 12)
        tt, j = divmod(r, 3)
        if j < 2:
            return s["rope"], i * 8 + tt * 2 + j + 1
        return s["vev"], i * 4 + tt + 1

    def f_sync(sync):
        prog.gdma(sync, lngb[:].rearrange("p a b -> p (a b)"), io["lngb_in"], "p2_w")
        for g in range(NG):
            if g > 0:
                sync.wait_ge(s["prj"], 96 * g)
            for j, dst in enumerate((w_q, w_k, w_v)):
                src = wqkv[:, j * 1024 + g * GF: j * 1024 + (g + 1) * GF]
                prog.gdma(sync, dst[:], src.rearrange("(k p) f -> p k f", p=128), "p2_w")
            for sc in range(NSC):
                i = g * NSC + sc
                ts = slice(sc * 512, (sc + 1) * 512)
                if i >= 1:
                    sync.wait_ge(s["prj"], 12 * i)
                    sync.wait_ge(s["x1w"], 16 * i)
                    sync.wait_ge(s["xmw"], 16 * min(i, 8))
                prog.gdma(sync, xin[:], xT.rearrange("(k p) t -> p k t", p=128)[:, :, ts], "p2_ld")
                prog.gdma(sync, ret[:],
                          ret_hbm.rearrange("(k p) t -> p k t", p=128)[:, :, ts], "p2_ld")

    def f_scalar(act):
        for g in range(NG):
            for sc in range(NSC):
                i = g * NSC + sc
                ts = slice(sc * 512, (sc + 1) * 512)
                act.wait_ge(s["x1"], i + 1)
                if i >= 1:
                    act.wait_ge(s["xh2"], i)
                    if i <= 8:
                        act.wait_ge(s["xmw"], 16 * i)
                act.activation(xin[:], ret[:], AF.Square).then_inc(s["sq"], 1)
                act.wait_ge(s["sml"], 2 * i + 1)
                act.activation(rs_out[:], ab_bc[:], AF.Sqrt).then_inc(s["sml2"], 1)
                act.wait_ge(s["sml"], 2 * i + 2)
                act.activation(ab_bc[:], wtmp[:], AF.Sqrt).then_inc(s["sml2"], 1)
                act.wait_ge(s["tpe"], 32 * (i + 1))
                prog.gdma(act, k_hbm[4 * g:4 * g + 4, :, ts].rearrange("h p t -> p h t"),
                          kfm[:], "p2_kqw")
                prog.gdma(act, q_hbm[4 * g:4 * g + 4, :, ts].rearrange("h p t -> p h t"),
                          qfm[:], "p2_kqw")

    def f_pe(pe):
        for g in range(NG):
            pe.wait_ge(s["w"], 16 + 48 * (g + 1))
            if g > 0:
                pe.wait_ge(s["tpe"], 32 * 8 * g)
            for j, wbuf in enumerate((w_q, w_k, w_v)):
                dstp = pp[0:1, 0, (j % 2) * GF:(j % 2 + 1) * GF] if j < 2 \
                    else pp[0:1, 1, 0:GF]
                for k in range(8):
                    pe.matmul(dstp, lngb[:, 1, k:k + 1], wbuf[:, k],
                              start=(k == 0), stop=(k == 7))
            pe.matmul(pp[:, 4, :], eyeb[:], w_q[:, 0:2].rearrange("p a b -> p (a b)"),
                      start=True, stop=True).then_inc(s["bias"], 1)
            pe.wait_ge(s["gsc"], g + 1)
            for sc in range(NSC):
                i = g * NSC + sc
                if i > 0:
                    pe.wait_ge(s["st_ev"], i)
                pe.wait_ge(s["x1"], i + 1)
                for k in range(8):
                    pe.matmul(pp[:, 2, :], onesall[:], ret[:, k],
                              start=(k == 0), stop=(k == 7))
                pe.wait_ge(s["sq"], i + 1)
                for k in range(8):
                    ins = pe.matmul(pp[:, 3, :], onesall[:], xin[:, k],
                                    start=(k == 0), stop=(k == 7))
                ins.then_inc(s["stat"], 1)
                pe.wait_ge(s["xh2"], i + 1)
                for tt in range(4):
                    for j, wbuf in enumerate((w_q, w_k, w_v)):
                        n = i * 12 + tt * 3 + j
                        if n >= 2:
                            cs, cv = prj_consumer(n - 2)
                            pe.wait_ge(cs, cv)
                        bank = 4 + n % 2
                        for k in range(8):
                            pe.matmul(pp[:, bank, 0:GF], xhb[:, k, tt * 128:(tt + 1) * 128],
                                      wbuf[:, k], start=(k == 0), stop=False)
                        ins = pe.matmul(pp[:, bank, 0:GF], onesrow[0:1, :],
                                        brow[:, j * GF:(j + 1) * GF], start=False, stop=True)
                        ins.then_inc(s["prj"], 1)
                    pe.wait_ge(s["rope"], i * 8 + tt * 2 + 2)
                    for jj, srcb in enumerate((qtm[:], ktm[:])):
                        for h in range(4):
                            ntp = (i * 4 + tt) * 8 + jj * 4 + h + 1
                            if ntp >= 3:
                                pe.wait_ge(s["tpe"], ntp - 2)
                            pe.transpose(ppr[0:64, (ntp - 1) % 2, 0:128],
                                         srcb[:, h * 64:(h + 1) * 64], eyeb[:]
                                         ).then_inc(s["tp"], 1)

    def f_dve(dve):
        for g in range(NG):
            dve.wait_ge(s["bias"], g + 1)
            dve.tensor_copy(brow[:, 0:512], pp[0:1, 0, 0:512])
            dve.tensor_copy(brow[:, 512:768], pp[0:1, 1, 0:256])
            for wbuf in (w_q, w_k, w_v):
                for k in range(8):
                    gb = lngb[:, 0, k:k + 1].broadcast_to([128, GF])
                    dve.tensor_tensor(wbuf[:, k], wbuf[:, k], gb, ALU.mult)
            dve.engine_nop().then_inc(s["gsc"], 1)
            for sc in range(NSC):
                i = g * NSC + sc
                dve.wait_ge(s["ld"], 32 * (i + 1))
                for k in range(8):
                    ins = dve.tensor_tensor(ret[:, k], xin[:, k], ret[:, k], ALU.add)
                ins.then_inc(s["x1"], 1)
                dve.wait_ge(s["stat"], i + 1)
                dve.tensor_scalar_mul(mb_bc[:], pp[:, 2, :], 1.0 / 1024.0)
                dve.tensor_scalar_mul(ab_bc[:], pp[:, 3, :], 1.0 / 1024.0
                                      ).then_inc(s["st_ev"], 1)
                dve.tensor_tensor(wtmp[:], mb_bc[:], mb_bc[:], ALU.mult)
                dve.tensor_tensor(ab_bc[:], ab_bc[:], wtmp[:], ALU.subtract)
                dve.tensor_copy(wtmp[:], ab_bc[:])
                dve.tensor_scalar_add(ab_bc[:], ab_bc[:], EPS)
                dve.engine_nop().then_inc(s["sml"], 1)
                dve.wait_ge(s["sml2"], 2 * i + 1)
                dve.reciprocal(rs_out[:], rs_out[:])
                dve.tensor_tensor(wtmp[:], wtmp[:], rs_out[:], ALU.mult)
                dve.tensor_tensor(wtmp[:], wtmp[:], rs_out[:], ALU.mult)
                dve.tensor_scalar_add(wtmp[:], wtmp[:], EPS)
                dve.engine_nop().then_inc(s["sml"], 1)
                for k in range(8):
                    dve.tensor_tensor(xin[:, k], ret[:, k], mb_bc[:], ALU.subtract)
                    ins = dve.tensor_tensor(xin[:, k], xin[:, k], rs_out[:], ALU.mult)
                ins.then_inc(s["xh"], 1)
                dve.wait_ge(s["sml2"], 2 * i + 2)
                dve.reciprocal(ab_bc[:], ab_bc[:])
                if i > 0:
                    dve.wait_ge(s["prj"], 12 * i)
                for k in range(8):
                    ins = dve.tensor_tensor(xhb[:, k], xin[:, k], ab_bc[:], ALU.mult)
                ins.then_inc(s["xh2"], 1)
                for tt in range(4):
                    base_prj = i * 12 + tt * 3
                    for j, dstb in ((0, qtm[:]), (1, ktm[:])):
                        bank = 4 + (base_prj + j) % 2
                        dve.wait_ge(s["prj"], base_prj + j + 1)
                        if (i * 4 + tt) >= 1:
                            dve.wait_ge(s["tp"], (i * 4 + tt - 1) * 8 + 4 + j * 4)
                        src = pp[:, bank, 0:GF].rearrange("p (h f) -> p h f", f=64)
                        t1, t2 = src[:, :, 0:32], src[:, :, 32:64]
                        gt = sc * 4 + tt
                        cosB = cosb[:, gt].unsqueeze(1).broadcast_to([128, 4, 32])
                        sinB = sinb[:, gt].unsqueeze(1).broadcast_to([128, 4, 32])
                        dst = dstb.rearrange("p (h f) -> p h f", f=64)
                        r0 = ropet[:, 0].rearrange("p (h f) -> p h f", f=32)
                        r1 = ropet[:, 1].rearrange("p (h f) -> p h f", f=32)
                        dve.tensor_tensor(r0, t1, cosB, ALU.mult)
                        dve.tensor_tensor(r1, t2, sinB, ALU.mult)
                        dve.tensor_tensor(dst[:, :, 0:32], r0, r1, ALU.subtract)
                        dve.tensor_tensor(r0, t1, sinB, ALU.mult)
                        dve.tensor_tensor(r1, t2, cosB, ALU.mult)
                        dve.tensor_tensor(dst[:, :, 32:64], r0, r1, ALU.add
                                          ).then_inc(s["rope"], 1)
                    bank = 4 + (base_prj + 2) % 2
                    dve.wait_ge(s["prj"], base_prj + 3)
                    if (i * 4 + tt) >= 4:
                        dve.wait_ge(s["vw"], 16 * (i * 4 + tt - 3))
                    dve.tensor_copy(vtm[:, tt], pp[:, bank, 0:GF]).then_inc(s["vev"], 1)
                    if i > 0 and tt == 0:
                        dve.wait_ge(s["kqw"], 32 * i)
                    for n in range(8):
                        ntp = (i * 4 + tt) * 8 + n + 1
                        dve.wait_ge(s["tp"], ntp)
                        dstb = qfm if n < 4 else kfm
                        h = n % 4
                        dve.tensor_copy(dstb[:, h, tt * 128:(tt + 1) * 128],
                                        ppr[0:64, (ntp - 1) % 2, 0:128]).then_inc(s["tpe"], 1)

    def f_gps(gps):
        for g in range(NG):
            for sc in range(NSC):
                i = g * NSC + sc
                ts = slice(sc * 512, (sc + 1) * 512)
                gps.wait_ge(s["x1"], i + 1)
                prog.gdma(gps, x1_hbm.rearrange("(k p) t -> p k t", p=128)[:, :, ts],
                          ret[:], "p2_x1w")
                if g == 0:
                    gps.wait_ge(s["xh"], i + 1)
                    prog.gdma(gps, xm_hbm.rearrange("(k p) t -> p k t", p=128)[:, :, ts],
                              xin[:], "p2_xmw")
                for tt in range(4):
                    gps.wait_ge(s["vev"], i * 4 + tt + 1)
                    t0 = sc * 512 + tt * 128
                    prog.gdma(gps, v_hbm[t0:t0 + 128, g * GF:(g + 1) * GF],
                              vtm[:, tt], "p2_vw")

    return dict(f_sync=f_sync, f_scalar=f_scalar, f_pe=f_pe, f_dve=f_dve, f_gps=f_gps)



def emit_p3_attention(prog, io):
    sb = prog.sb
    NG = 4

    kbuf = sb("p3_k", [64, 2, T], F32R)
    vbuf = sb("p3_v", [128, 2, 32, 65], F32R)
    qbuf = sb("p3_q", [64, 2, T], F32R)
    pbuf = sb("p3_p", [128, 2, 512], F32R)
    obuf = sb("p3_o", [64, 2, T], F32R)
    lrow = sb("p3_lr", [1, 512], F32R)
    rbuf = sb("p3_r", [64, 512], F32)
    pads = sb("p3_pad", [64, 512], F32)
    padA = sb("p3_padA", [128, 512], F32)
    w_p = sb("p3_wp", [64, 4, 1024], F32R)
    otile = sb("p3_ot", [64, 2, 4, 512], F32R)
    wostg = sb("p3_wostg", [128, 4, 512], F32)
    diagmb = sb("p3_diag", [128, 4, 512], F32R)

    onesrow, eyeb = io["onesrow"], io["eyeb"]
    diagm = diagmb
    k_hbm, v_hbm, q_hbm, o_hbm = io["k_hbm"], io["v_hbm"], io["q_hbm"], io["o_hbm"]
    xm_hbm, wproj = io["xm_hbm"], io["wproj"]
    pp, ppr = io["pp"], io["ppr"]

    names = ["kv", "vone", "sS", "sP", "sP2", "sAVc", "sAV", "lr", "rb",
             "oev", "ow", "otl", "wpw", "wo", "woe", "wod"]
    s = {n: prog.sem("p3_" + n) for n in names}

    NKC = [(qt + 1) * 4 for qt in range(8)]
    KCT = sum(NKC)

    def zidx(h, qt, kc):
        return h * KCT + sum(NKC[:qt]) + kc

    def is_diag(qt, kc):
        return kc >= NKC[qt] - 4

    diag_count = {}
    c = 0
    for h in range(16):
        for qt in range(8):
            for kc in range(NKC[qt]):
                if is_diag(qt, kc):
                    c += 1
                diag_count[zidx(h, qt, kc)] = c

    def f_sync(sync):
        prog.gdma(sync, diagmb[:].rearrange("p a b -> p (a b)"), io["c_diag"], "p3_dg")
        for h in range(16):
            if h >= 2:
                sync.wait_ge(s["sAV"], 8 * (h - 1))
            prog.gdma(sync, kbuf[:, h % 2], k_hbm[h], "p3_kv")
            prog.gdma(sync, vbuf[:, h % 2, :, 0:64],
                      v_hbm[:, h * 64:(h + 1) * 64].rearrange("(a p) f -> p a f", p=128),
                      "p3_kv")
            prog.gdma(sync, qbuf[:, h % 2], q_hbm[h], "p3_kv")
        for g in range(NG):
            if g > 0:
                sync.wait_ge(s["wo"], 64 * g)
            src = wproj[g * 256:(g + 1) * 256, :].rearrange("(h p) f -> p h f", p=64)
            prog.gdma(sync, w_p[:], src, "p3_wpw")
            for tt in range(8):
                n = g * 8 + tt
                if n >= 2:
                    sync.wait_ge(s["wo"], (n - 1) * 8)
                ts = slice(tt * 512, (tt + 1) * 512)
                sync.wait_ge(s["ow"], 16 * (4 * g + 4))
                prog.gdma(sync, otile[:, n % 2],
                          o_hbm[4 * g:4 * g + 4, :, ts].rearrange("h p t -> p h t"), "p3_otl")

    def f_gps(gps):
        gps.wait_ge(prog.sem("p2_xmw"), prog.counts["p2_xmw"])
        for h in range(16):
            gps.wait_ge(s["oev"], h * 8 + 8)
            prog.gdma(gps, o_hbm[h], obuf[:, h % 2], "p3_ow")
        for g in range(NG):
            for tt in range(8):
                for hf in range(2):
                    gps.wait_ge(s["woe"], g * 64 + tt * 8 + hf * 4 + 4)
                    ts = slice(tt * 512, (tt + 1) * 512)
                    dst = xm_hbm[hf * 512:(hf + 1) * 512, ts
                                 ].rearrange("(k p) t -> p k t", p=128).bitcast(F32)
                    prog.gdma(gps, dst, wostg[:], "p3_wod", accum_op=ALU.add)

    def f_scalar(act):
        for h in range(16):
            for qt in range(8):
                for kc in range(NKC[qt]):
                    z = zidx(h, qt, kc)
                    act.wait_ge(s["sS"], z + 2 + h)
                    if z >= 2:
                        act.wait_ge(s["sAVc"], z - 1)
                    act.activation(pbuf[:, z % 2], pp[:, z % 2, :], AF.Exp,
                                   scale=0.125).then_inc(s["sP"], 1)

    def f_dve(dve):
        dve.wait_ge(prog.sem("p3_dg"), 16)
        oneB = io["onesb"][:, 0:1].unsqueeze(1).broadcast_to([128, 32, 1])
        dve.tensor_copy(vbuf[:, 0, :, 64:65], oneB)
        dve.tensor_copy(vbuf[:, 1, :, 64:65], oneB).then_inc(s["vone"], 1)
        for h in range(16):
            for qt in range(8):
                for kc in range(NKC[qt]):
                    if not is_diag(qt, kc):
                        continue
                    z = zidx(h, qt, kc)
                    r = kc - qt * 4
                    dve.wait_ge(s["sP"], z + 1)
                    dve.tensor_tensor(pbuf[:, z % 2], pbuf[:, z % 2], diagm[:, r],
                                      ALU.mult).then_inc(s["sP2"], 1)
                dve.wait_ge(s["sAV"], h * 8 + qt + 1)
                with prog.nc.allow_low_precision("recip row f32r"):
                    dve.reciprocal(lrow[:], pp[64:65, 2, :])
                dve.engine_nop().then_inc(s["lr"], 1)
                dve.wait_ge(s["rb"], h * 8 + qt + 1)
                if h >= 2 and qt == 0:
                    dve.wait_ge(s["ow"], 16 * (h - 1))
                dve.tensor_copy(rbuf[:], pp[0:64, 3, :])
                dve.tensor_copy(pads[:], pp[0:64, 2, :])
                dve.tensor_tensor(obuf[:, h % 2, qt * 512:(qt + 1) * 512],
                                  pp[0:64, 2, :], rbuf[:], ALU.mult
                                  ).then_inc(s["oev"], 1)
        for g in range(NG):
            for tt in range(8):
                for oc in range(8):
                    nwo = g * 64 + tt * 8 + oc
                    bank = 4 + nwo % 2
                    dve.wait_ge(s["wo"], nwo + 1)
                    nhalf = (g * 8 + tt) * 2 + oc // 4
                    if nhalf >= 1 and oc % 4 == 0:
                        dve.wait_ge(s["wod"], 16 * nhalf)
                    dve.tensor_copy(wostg[:, oc % 4], pp[:, bank, :]).then_inc(s["woe"], 1)

    def f_pe(pe):
        def emit_S(pe, h, z, qt, kc):
            if z >= 2:
                pe.wait_ge(s["sP"], z - 1)
            qs = slice(qt * 512, (qt + 1) * 512)
            pe.matmul(pp[:, z % 2, :], kbuf[:, h % 2, kc * 128:(kc + 1) * 128],
                      qbuf[:, h % 2, qs], start=True, stop=True).then_inc(s["sS"], 1)

        pe.wait_ge(s["vone"], 1)
        for h in range(16):
            pe.wait_ge(s["kv"], 48 * (h + 1))
            chunks = [(qt, kc) for qt in range(8) for kc in range(NKC[qt])]
            emit_S(pe, h, zidx(h, 0, 0), 0, 0)
            for ci, (qt, kc) in enumerate(chunks):
                z = zidx(h, qt, kc)
                if ci + 1 < len(chunks):
                    qt2, kc2 = chunks[ci + 1]
                    emit_S(pe, h, zidx(h, qt2, kc2), qt2, kc2)
                else:
                    pe.matmul(pp[0:64, 5, :], eyeb[0:64, 0:64], kbuf[:, h % 2, 0:512],
                              start=True, stop=True).then_inc(s["sS"], 1)
                if kc == 0 and h * 8 + qt >= 1:
                    pe.wait_ge(s["oev"], h * 8 + qt)
                if is_diag(qt, kc):
                    pe.wait_ge(s["sP2"], diag_count[z])
                else:
                    pe.wait_ge(s["sP"], z + 1)
                pe.matmul(pp[0:65, 2, :], vbuf[:, h % 2, kc], pbuf[:, z % 2],
                          start=(kc == 0), stop=(kc == NKC[qt] - 1)).then_inc(s["sAVc"], 1)
                if kc == NKC[qt] - 1:
                    pe.matmul(pp[0:64, 5, :], eyeb[0:64, 0:64], kbuf[:, h % 2, 0:512],
                              start=True, stop=True).then_inc(s["sAV"], 1)
                    pe.wait_ge(s["lr"], h * 8 + qt + 1)
                    pe.matmul(pp[0:64, 3, :], onesrow[0:1, 0:64], lrow[:], start=True, stop=True)
                    pe.matmul(pp[0:64, 5, :], eyeb[0:64, 0:64], kbuf[:, h % 2, 0:512],
                              start=True, stop=True).then_inc(s["rb"], 1)
        for g in range(NG):
            pe.wait_ge(s["wpw"], 16 * (g + 1))
            for tt in range(8):
                n = g * 8 + tt
                pe.wait_ge(s["otl"], 16 * (n + 1))
                for oc in range(8):
                    nwo = n * 8 + oc
                    bank = 4 + nwo % 2
                    if nwo >= 2:
                        pe.wait_ge(s["woe"], nwo - 1)
                    for hl in range(4):
                        ins = pe.matmul(pp[:, bank, :], w_p[:, hl, oc * 128:(oc + 1) * 128],
                                        otile[:, n % 2, hl], start=(hl == 0), stop=(hl == 3))
                    ins.then_inc(s["wo"], 1)

    return dict(f_sync=f_sync, f_gps=f_gps, f_scalar=f_scalar, f_pe=f_pe, f_dve=f_dve)



def emit_p4_gate(prog, io):
    sb = prog.sb
    xr = sb("p4_xr", [128, 8, 512], F32R)
    xsq = sb("p4_sq", [128, 8, 512], F32R)
    h2 = sb("p4_h2", [128, 8, 512], F32R)
    mb_bc = sb("p4_mb", [128, 512], F32)
    ab_bc = sb("p4_ab", [128, 512], F32)
    rs_out = sb("p4_rs", [128, 512], F32)
    h2tm = sb("p4_h2tm", [128, 2, 8, 128], F32R)
    lgro = sb("p4_lg", [8, 512], F32)
    lngb2 = sb("p4_lngb", [128, 2, 8], F32)
    wg = sb("p4_wg", [128, 8, 8], F32R)

    onesall, eyeb = io["onesall"], io["eyeb"]
    xm_hbm, wgate = io["xm_hbm"], io["wgate"]
    lgT_hbm, h2tm_hbm = io["lgT_hbm"], io["h2tm_hbm"]
    pp, ppr = io["pp"], io["ppr"]

    names = ["ld", "w", "sq", "stat", "st_ev", "sml", "sml2", "h2",
             "lg", "lge", "lgw", "tp", "tpe", "h2w"]
    s = {n: prog.sem("p4_" + n) for n in names}

    def f_sync(sync):
        prog.gdma(sync, lngb2[:].rearrange("p a b -> p (a b)"), io["lngb2_in"], "p4_w")
        prog.gdma(sync, wg[:], wgate.rearrange("(k p) e -> p k e", p=128), "p4_w")
        for sc in range(NSC):
            ts = slice(sc * 512, (sc + 1) * 512)
            if sc >= 1:
                sync.wait_ge(s["lg"], sc)
                sync.wait_ge(s["tpe"], 32 * sc)
            prog.gdma(sync, xr[:],
                      xm_hbm.rearrange("(k p) t -> p k t", p=128)[:, :, ts], "p4_ld")

    def f_scalar(act):
        for sc in range(NSC):
            act.wait_ge(s["ld"], 16 * (sc + 1))
            act.activation(xsq[:], xr[:], AF.Square).then_inc(s["sq"], 1)
            act.wait_ge(s["sml"], sc + 1)
            act.activation(rs_out[:], ab_bc[:], AF.Sqrt).then_inc(s["sml2"], 1)

    def f_pe(pe):
        pe.wait_ge(s["w"], 32)
        for sc in range(NSC):
            if sc > 0:
                pe.wait_ge(s["st_ev"], sc)
            pe.wait_ge(s["ld"], 16 * (sc + 1))
            for k in range(8):
                pe.matmul(pp[:, 2, :], onesall[:], xr[:, k],
                          start=(k == 0), stop=(k == 7))
            pe.wait_ge(s["sq"], sc + 1)
            for k in range(8):
                ins = pe.matmul(pp[:, 3, :], onesall[:], xsq[:, k],
                                start=(k == 0), stop=(k == 7))
            ins.then_inc(s["stat"], 1)
            pe.wait_ge(s["h2"], sc + 1)
            if sc > 0:
                pe.wait_ge(s["lge"], sc)
            for k in range(8):
                ins = pe.matmul(pp[0:8, 0, :], wg[:, k], h2[:, k],
                                start=(k == 0), stop=(k == 7))
            ins.then_inc(s["lg"], 1)
            for tt in range(4):
                for k in range(8):
                    ntp = (sc * 4 + tt) * 8 + k + 1
                    if ntp >= 3:
                        pe.wait_ge(s["tpe"], ntp - 2)
                    pe.transpose(ppr[:, (ntp - 1) % 2, 0:128],
                                 h2[:, k, tt * 128:(tt + 1) * 128], eyeb[:]
                                 ).then_inc(s["tp"], 1)

    def f_dve(dve):
        for sc in range(NSC):
            dve.wait_ge(s["stat"], sc + 1)
            dve.tensor_scalar_mul(mb_bc[:], pp[:, 2, :], 1.0 / 1024.0)
            dve.tensor_scalar_mul(ab_bc[:], pp[:, 3, :], 1.0 / 1024.0
                                  ).then_inc(s["st_ev"], 1)
            dve.tensor_tensor(rs_out[:], mb_bc[:], mb_bc[:], ALU.mult)
            dve.tensor_tensor(ab_bc[:], ab_bc[:], rs_out[:], ALU.subtract)
            dve.tensor_scalar_add(ab_bc[:], ab_bc[:], EPS)
            dve.engine_nop().then_inc(s["sml"], 1)
            dve.wait_ge(s["sml2"], sc + 1)
            dve.reciprocal(rs_out[:], rs_out[:])
            if sc > 0:
                dve.wait_ge(s["tp"], 32 * sc)
                dve.wait_ge(s["lg"], sc)
            for k in range(8):
                g2 = lngb2[:, 0, k:k + 1].broadcast_to([128, 512])
                b2 = lngb2[:, 1, k:k + 1].broadcast_to([128, 512])
                dve.tensor_tensor(h2[:, k], xr[:, k], mb_bc[:], ALU.subtract)
                dve.tensor_tensor(h2[:, k], h2[:, k], rs_out[:], ALU.mult)
                dve.tensor_tensor(h2[:, k], h2[:, k], g2, ALU.mult)
                ins = dve.tensor_tensor(h2[:, k], h2[:, k], b2, ALU.add)
            ins.then_inc(s["h2"], 1)
            dve.wait_ge(s["lg"], sc + 1)
            if sc > 0:
                dve.wait_ge(s["lgw"], 16 * sc)
            dve.tensor_copy(lgro[:], pp[0:8, 0, :]).then_inc(s["lge"], 1)
            for tt in range(4):
                n = sc * 4 + tt
                for k in range(8):
                    ntp = n * 8 + k + 1
                    dve.wait_ge(s["tp"], ntp)
                    if n >= 2 and k == 0:
                        dve.wait_ge(s["h2w"], 16 * (n - 1))
                    dve.tensor_copy(h2tm[:, n % 2, k], ppr[:, (ntp - 1) % 2, 0:128]
                                    ).then_inc(s["tpe"], 1)

    def f_gps(gps):
        for sc in range(NSC):
            ts = slice(sc * 512, (sc + 1) * 512)
            gps.wait_ge(s["lge"], sc + 1)
            prog.gdma(gps, lgT_hbm[:, ts], lgro[:], "p4_lgw")
            for tt in range(4):
                n = sc * 4 + tt
                gps.wait_ge(s["tpe"], 32 * sc + (tt + 1) * 8)
                t0 = sc * 512 + tt * 128
                prog.gdma(gps, h2tm_hbm[t0:t0 + 128, :],
                          h2tm[:, n % 2].rearrange("p k f -> p (k f)"), "p4_h2w")

    return dict(f_sync=f_sync, f_scalar=f_scalar, f_pe=f_pe, f_dve=f_dve, f_gps=f_gps)



ENGSECS = [("sync", "f_sync"), ("gpsimd", "f_gps"), ("scalar", "f_scalar"),
           ("tensor", "f_pe"), ("vector", "f_dve")]


def emit_phase_seq(prog, io, phase_list):
    barrier = {}
    for prefix, emit_fn in phase_list:
        prog.open_phase()
        fns = emit_fn(prog, io)
        snapshot = dict(barrier)
        with prog.nc.Block() as blk:
            for eng_attr, key in ENGSECS:
                f = fns.get(key) or fns.get({"f_scalar": "f_act"}.get(key, key))
                def wrapped(eng, f=f, snap=snapshot):
                    for nm, cnt in snap.items():
                        if cnt > 0:
                            eng.wait_ge(prog.sems[nm], cnt)
                    if f is not None:
                        f(eng)
                getattr(blk, eng_attr)(wrapped)
        prog.close_phase()
        barrier = {nm: c for nm, c in prog.counts.items() if c > 0}
    return barrier


def build_p12_test(consts):
    prog = Prog()
    nc = prog.nc
    io = declare_common(prog)
    for nm in ("xT", "wqT", "wkT", "wvT", "woT"):
        io[nm] = nc.dram_tensor(nm, [C, C] if nm != "xT" else [C, T], F32R,
                                kind="ExternalInput").ap()
    io["gnw_in"] = nc.dram_tensor("gnw_in", [64, 16], F32, kind="ExternalInput").ap()
    io["gnb_in"] = nc.dram_tensor("gnb_in", [64, 16], F32, kind="ExternalInput").ap()
    io["wqkv"] = nc.dram_tensor("wqkv", [C, 3 * C], F32R, kind="ExternalInput").ap()
    io["lngb_in"] = nc.dram_tensor("lngb_in", [128, 16], F32R, kind="ExternalInput").ap()
    io["ret_hbm"] = nc.dram_tensor("ret_hbm", [C, T], F32R, kind="Internal").ap()
    io["x1_hbm"] = nc.dram_tensor("x1_hbm", [C, T], F32R, kind="ExternalOutput").ap()
    io["xm_hbm"] = nc.dram_tensor("xm_hbm", [C, T], F32R, kind="ExternalOutput").ap()
    io["k_hbm"] = nc.dram_tensor("k_hbm", [H, 64, T], F32R, kind="ExternalOutput").ap()
    io["q_hbm"] = nc.dram_tensor("q_hbm", [H, 64, T], F32R, kind="ExternalOutput").ap()
    io["v_hbm"] = nc.dram_tensor("v_hbm", [T, C], F32R, kind="ExternalOutput").ap()
    io["gs_f"] = consts["ret_gs_f"]

    with prog.nc.Block() as blk0:
        emit_const_loads(prog, blk0, io)
    emit_phase_seq(prog, io, [("p1", emit_p1_retention), ("p2", emit_p2_qkv)])
    with prog.nc.Block() as blk2:
        def f_end(gps):
            for nm in ("p2_x1w", "p2_xmw", "p2_kqw", "p2_vw"):
                gps.wait_ge(prog.sems[nm], prog.counts[nm])
        blk2.gpsimd(f_end)
    prog.close()
    return prog



def declare_l2(prog):
    nc = prog.nc
    sb = prog.sb
    io = {}
    cdefs = [
        ("c_eye", [128, 128], F32R, "eye128", "eyeb"),
        ("c_triin", [128, 128], F32R, "tri_incl", "triin"),
        ("c_triex", [128, 128], F32R, "tri_excl", "triex"),
        ("c_ones", [128, 1], F32R, "ones128", "onesb"),
        ("c_onesrow", [1, 128], F32R, "onesrow", "onesrow"),
        ("c_iota8", [128, 8], F32, "iota8", "iota8b"),
        ("c_iota8p", [128, 8], F32, "iota8p", "iota8pb"),
        ("c_tok0", [128, 32], F32, "tok0", "tok0b"),
        ("c_eyef", [128, 128], F32, "eye128", "eyef"),
    ]
    io["_cdefs"] = cdefs
    for nm, shape, dt_, ckey, sbkey in cdefs:
        io[nm] = nc.dram_tensor(nm, shape, dt_, kind="ExternalInput").ap()
        io[sbkey] = sb("cb2_" + sbkey, shape, dt_)
    io["pp"] = prog.ps("pp", [128, 6, 512], F32)
    io["ppr"] = prog.ps("ppr", [128, 2, 512], F32R)
    io["lgT_all"] = nc.dram_tensor("lgT_all", [4 * 8, T], F32R, kind="ExternalInput").ap()
    io["h2tm_all"] = nc.dram_tensor("h2tm_all", [N_TOK, C], F32R, kind="ExternalInput").ap()
    io["w1"] = nc.dram_tensor("w1", [C, 4 * C], F32R, kind="ExternalInput").ap()
    io["w2"] = nc.dram_tensor("w2", [4 * C, C], F32R, kind="ExternalInput").ap()
    io["b1_in"] = nc.dram_tensor("b1_in", [128, 32], F32, kind="ExternalInput").ap()
    io["b2row"] = nc.dram_tensor("b2row", [1, C], F32R, kind="ExternalInput").ap()
    io["emeta"] = nc.dram_tensor("emeta", [128, 1], F32, kind="ExternalInput").ap()
    io["exp_out"] = nc.dram_tensor("exp_out", [CAP, C], F32, kind="ExternalOutput").ap()
    io["trow_hbm"] = nc.dram_tensor("trow_hbm", [4, 128, 64], F32, kind="ExternalOutput").ap()
    io["tgate_hbm"] = nc.dram_tensor("tgate_hbm", [4, 128, 64], F32, kind="ExternalOutput").ap()
    io["gathtab"] = nc.dram_tensor("gathtab", [4097, 1], F32, kind="Internal").ap()
    io["tsd_d"] = nc.dram_tensor("tsd_d", [4, 256], F32R, kind="Internal").ap()
    io["toff_d"] = nc.dram_tensor("toff_d", [128, 8], F32R, kind="Internal").ap()
    io["ein_hbm"] = nc.dram_tensor("ein_hbm", [C, CAP], F32R, kind="Internal").ap()
    return io


def emit_l2_consts(prog, blk, io):
    def f_sync(sync):
        for nm, shape, dt_, ckey, sbkey in io["_cdefs"]:
            prog.gdma(sync, io[sbkey][:], io[nm], "l2_c")
        sync.wait_ge(prog.sem("l2_c"), prog.counts["l2_c"])
    blk.sync(f_sync)


def l2_const_in_map(consts):
    return {
        "c_eye": consts["eye128"], "c_triin": consts["tri_incl"],
        "c_triex": consts["tri_excl"], "c_ones": consts["ones128"],
        "c_onesrow": consts["onesrow"][:, 0:128],
        "c_iota8": consts["iota8"], "c_iota8p": consts["iota8p"],
        "c_tok0": consts["tok0"], "c_eyef": consts["eye128"],
    }


def emit_l2_routing(prog, io):
    sb = prog.sb
    lgt = sb("r_lgt", [8, 4, T], F32R)
    lgm = sb("r_lgm", [128, 4, 32, 8], F32)
    m_all = sb("r_m", [128, 4, 32, 8], F32R)
    m1 = sb("r_m1", [128, 4, 32, 8], F32)
    m2 = sb("r_m2", [128, 4, 32, 8], F32)
    prb = sb("r_p", [128, 4, 32, 8], F32)
    dbb = sb("r_d", [128, 4, 32, 8], F32)
    tmp8 = sb("r_t8", [128, 32, 8], F32)
    mx1 = sb("r_mx1", [128, 4, 32], F32)
    idx1 = sb("r_i1", [128, 4, 32], F32)
    idx2 = sb("r_i2", [128, 4, 32], F32)
    gate1 = sb("r_g1", [128, 4, 32], F32)
    gate2 = sb("r_g2", [128, 4, 32], F32)
    posge = sb("r_pos", [128, 4, 256], F32)
    sumB = sb("r_sum", [128, 4, 32], F32)
    g1B = sb("r_g1b", [128, 4, 32], F32)
    g2B = sb("r_g2b", [128, 4, 32], F32)
    tmpB = sb("r_tmpB", [128, 4, 32, 8], F32)
    kpB = sb("r_kpB", [128, 4, 256], F32)
    pos1B = sb("r_p1", [128, 4, 32], F32)
    pos2B = sb("r_p2", [128, 4, 32], F32)
    kp1B = sb("r_kp1", [128, 4, 32], F32)
    kp2B = sb("r_kp2", [128, 4, 32], F32)
    t32B = sb("r_t32", [128, 4, 32], F32)
    u32B = sb("r_u32", [128, 4, 32], F32)
    w32B = sb("r_w32", [128, 4, 32], F32)
    trow = sb("r_trow", [128, 4, 32, 2], F32)
    tgate = sb("r_tg", [128, 4, 32, 2], F32)
    offs = sb("r_off", [128, 4, 32], F32)
    offi = sb("r_offi", [128, 4, 32], I32)
    tokf = sb("r_tokf", [128, 4, 32], F32)
    tsall = sb("r_ts", [1, 4, 256], F32R)
    tspos = sb("r_tsp", [128, 256], F32R)
    toffx = sb("r_toffx", [128, 8], F32)
    toffrow = sb("r_toffr", [1, 4, 256], F32R)
    zbuf = sb("r_z", [128, 33], F32)
    emetab = sb("r_em", [128, 1], F32)

    eyeb, triin, triex = io["eyeb"], io["triin"], io["triex"]
    onesb, onesrow = io["onesb"], io["onesrow"]
    iota8b, iota8pb, tok0b = io["iota8b"], io["iota8pb"], io["tok0b"]
    pp, ppr = io["pp"], io["ppr"]

    names = ["ld", "tp", "tpe", "top", "ex", "ex2", "msum", "msf", "tse",
             "tsd", "tri", "trif", "toe", "tod", "pos", "posf", "pose",
             "sl", "zf", "sc", "tw", "em"]
    s = {n: prog.sem("l2r_" + n) for n in names}

    def f_sync(sync):
        prog.gdma(sync, emetab[:], io["emeta"], "l2r_em")
        prog.gdma(sync, lgt[:], io["lgT_all"].rearrange("(s e) t -> e s t", e=8), "l2r_ld")

    def f_pe(pe):
        pe.wait_ge(s["ld"], 16)
        for n in range(128):
            sq, tt = divmod(n, 32)
            if n >= 2:
                pe.wait_ge(s["tpe"], n - 1)
            pe.transpose(ppr[:, n % 2, 0:8], lgt[0:8, sq, tt * 128:(tt + 1) * 128],
                         eyeb[0:8, 0:8]).then_inc(s["tp"], 1)
        for sq in range(4):
            pe.wait_ge(s["top"], sq + 1)
            pe.matmul(pp[0:1, 0, sq * 128:sq * 128 + 256] if False else
                      pp[0:1, 0, 0:256] if sq == 0 else
                      pp[0:1, 1, 0:256] if sq == 1 else
                      pp[0:1, 2, 0:256] if sq == 2 else pp[0:1, 3, 0:256],
                      onesb[:, 0:1], m_all[:, sq], start=True, stop=True)
        pe.matmul(pp[:, 4, :], eyeb[:], m_all[:, 0:2].rearrange("p s a e -> p (s a e)"),
                  start=True, stop=True).then_inc(s["msf"], 1)
        pe.wait_ge(prog.sem("l2r_tsd"), 32)
        pe.matmul(pp[:, 0, 0:256], triex[:], tspos[:], start=True, stop=True)
        pe.matmul(pp[:, 4, :], eyeb[:], m_all[:, 0:2].rearrange("p s a e -> p (s a e)"),
                  start=True, stop=True).then_inc(s["trif"], 1)
        pe.wait_ge(prog.sem("l2r_tod"), 80)
        for sq in range(4):
            pe.matmul(pp[:, sq, 0:256], triin[:], m_all[:, sq], start=True, stop=False)
            pe.matmul(pp[:, sq, 0:256], onesrow[0:1, :], toffrow[:, sq],
                      start=False, stop=True)
        pe.matmul(pp[:, 4, :], eyeb[:], m_all[:, 0:2].rearrange("p s a e -> p (s a e)"),
                  start=True, stop=True).then_inc(s["posf"], 1)

    def f_dve(dve):
        for n in range(128):
            sq, tt = divmod(n, 32)
            dve.wait_ge(s["tp"], n + 1)
            dve.tensor_copy(lgm[:, sq, tt], ppr[:, n % 2, 0:8]).then_inc(s["tpe"], 1)
        for step in range(7):
            for sq in range(4):
                L = lgm[:, sq]
                if step == 0:
                    dve.tensor_reduce(mx1[:, sq, :], L, axis=mybir.AxisListType.X,
                                      op=ALU.max)
                elif step == 1:
                    mb = mx1[:, sq].unsqueeze(2).broadcast_to([128, 32, 8])
                    dve.tensor_tensor(m1[:, sq], L, mb, ALU.is_equal)
                elif step == 2:
                    ib = iota8pb[:].unsqueeze(1).broadcast_to([128, 32, 8])
                    dve.scalar_tensor_tensor(tmp8[:], m1[:, sq], -1000.0, ib,
                                             op0=ALU.mult, op1=ALU.add)
                    dve.tensor_reduce(idx1[:, sq, :], tmp8[:], axis=mybir.AxisListType.X,
                                      op=ALU.min)
                elif step == 3:
                    ixb = idx1[:, sq].unsqueeze(2).broadcast_to([128, 32, 8])
                    iob = iota8b[:].unsqueeze(1).broadcast_to([128, 32, 8])
                    dve.tensor_tensor(m1[:, sq], iob, ixb, ALU.is_equal)
                    dve.scalar_tensor_tensor(tmp8[:], m1[:, sq], -1e30, L,
                                             op0=ALU.mult, op1=ALU.add)
                    dve.tensor_reduce(mx1[:, sq, :], tmp8[:], axis=mybir.AxisListType.X,
                                      op=ALU.max)
                    mb = mx1[:, sq].unsqueeze(2).broadcast_to([128, 32, 8])
                    dve.tensor_tensor(m2[:, sq], tmp8[:], mb, ALU.is_equal)
                elif step == 4:
                    ib = iota8pb[:].unsqueeze(1).broadcast_to([128, 32, 8])
                    dve.scalar_tensor_tensor(tmp8[:], m2[:, sq], -1000.0, ib,
                                             op0=ALU.mult, op1=ALU.add)
                    dve.tensor_reduce(idx2[:, sq, :], tmp8[:], axis=mybir.AxisListType.X,
                                      op=ALU.min)
                elif step == 5:
                    ixb = idx2[:, sq].unsqueeze(2).broadcast_to([128, 32, 8])
                    iob = iota8b[:].unsqueeze(1).broadcast_to([128, 32, 8])
                    dve.tensor_tensor(m2[:, sq], iob, ixb, ALU.is_equal)
                    dve.tensor_tensor(m_all[:, sq], m1[:, sq], m2[:, sq], ALU.add)
                elif step == 6:
                    dve.tensor_reduce(mx1[:, sq, :], L, axis=mybir.AxisListType.X,
                                      op=ALU.max)
                    mb = mx1[:, sq].unsqueeze(2).broadcast_to([128, 32, 8])
                    ins = dve.tensor_tensor(dbb[:, sq], L, mb, ALU.subtract)
                    if sq == 3:
                        ins.then_inc(s["ex"], 1)
        dve.engine_nop().then_inc(s["top"], 4)
        padw = tmp8
        def pad():
            dve.tensor_copy(padw[:], prb[:, 0])
        dve.wait_ge(s["ex2"], 1)
        dve.tensor_reduce(sumB[:], prb[:], axis=mybir.AxisListType.X, op=ALU.add)
        pad()
        dve.reciprocal(sumB[:], sumB[:])
        dve.tensor_tensor(tmpB[:], prb[:], m1[:], ALU.mult)
        pad()
        dve.tensor_reduce(g1B[:], tmpB[:], axis=mybir.AxisListType.X, op=ALU.add)
        dve.tensor_tensor(tmpB[:], prb[:], m2[:], ALU.mult)
        dve.tensor_tensor(gate1[:], g1B[:], sumB[:], ALU.mult)
        dve.tensor_reduce(g2B[:], tmpB[:], axis=mybir.AxisListType.X, op=ALU.add)
        pad()
        dve.tensor_tensor(gate2[:], g2B[:], sumB[:], ALU.mult)
        dve.wait_ge(s["msf"], 1)
        for sq in range(4):
            src = (pp[0:1, 0, 0:256] if sq == 0 else pp[0:1, 1, 0:256]
                   if sq == 1 else pp[0:1, 2, 0:256] if sq == 2 else pp[0:1, 3, 0:256])
            dve.tensor_copy(tsall[:, sq], src)
        dve.memset(zbuf[:], 0.0)
        dve.tensor_scalar_mul(tspos[:], io["onesb"][:, 0:1].broadcast_to([128, 256]), 0.0)
        dve.engine_nop().then_inc(s["tse"], 1)
        dve.wait_ge(s["trif"], 1)
        dve.tensor_copy(toffx[:], pp[0:128, 0, 0:8]).then_inc(s["toe"], 1)
        dve.wait_ge(s["posf"], 1)
        for sq in range(4):
            dve.tensor_copy(posge[:, sq], pp[:, sq, 0:256])
        dve.engine_nop().then_inc(s["pose"], 4)
        PB = posge[:].rearrange("p s (a e) -> p s a e", e=8)
        dve.tensor_scalar_min(kpB[:], posge[:], float(CAP))
        pad()
        dve.tensor_tensor(kpB[:], kpB[:], posge[:], ALU.is_equal)
        pad()
        dve.tensor_tensor(kpB[:], kpB[:], m_all[:].bitcast(F32).rearrange(
            "p s a e -> p s (a e)"), ALU.mult)
        pad()
        KB = kpB[:].rearrange("p s (a e) -> p s a e", e=8)
        for j, (mj, posj, kpj, idxj, gatej) in enumerate(
                ((m1, pos1B, kp1B, idx1, gate1), (m2, pos2B, kp2B, idx2, gate2))):
            dve.tensor_tensor(tmpB[:], PB, mj[:], ALU.mult)
            pad()
            dve.tensor_reduce(posj[:], tmpB[:], axis=mybir.AxisListType.X, op=ALU.add)
            dve.tensor_tensor(tmpB[:], KB, mj[:], ALU.mult)
            pad()
            dve.tensor_reduce(kpj[:], tmpB[:], axis=mybir.AxisListType.X, op=ALU.add)
            pad()
            dve.tensor_tensor(t32B[:], kpj[:], posj[:], ALU.mult)
            pad()
            dve.scalar_tensor_tensor(t32B[:], kpj[:], -float(CAP), t32B[:],
                                     op0=ALU.mult, op1=ALU.add)
            pad()
            dve.tensor_scalar_add(t32B[:], t32B[:], float(CAP) - 1.0)
            pad()
            dve.scalar_tensor_tensor(trow[:, :, :, j], idxj[:], float(CAP),
                                     t32B[:], op0=ALU.mult, op1=ALU.add)
            dve.tensor_tensor(tgate[:, :, :, j], gatej[:], kpj[:], ALU.mult)
        emb = emetab[:].unsqueeze(2).broadcast_to([128, 4, 32])
        dve.tensor_tensor(t32B[:], idx1[:], emb, ALU.is_equal)
        pad()
        dve.tensor_tensor(t32B[:], t32B[:], kp1B[:], ALU.mult)
        dve.tensor_tensor(u32B[:], idx2[:], emb, ALU.is_equal)
        pad()
        dve.tensor_tensor(u32B[:], u32B[:], kp2B[:], ALU.mult)
        pad()
        dve.tensor_tensor(offs[:], t32B[:], pos1B[:], ALU.mult)
        dve.tensor_tensor(w32B[:], u32B[:], pos2B[:], ALU.mult)
        pad()
        dve.tensor_tensor(offs[:], offs[:], w32B[:], ALU.add)
        dve.tensor_tensor(t32B[:], t32B[:], u32B[:], ALU.add)
        pad()
        dve.tensor_tensor(offs[:], offs[:], t32B[:], ALU.subtract)
        pad()
        dve.scalar_tensor_tensor(offs[:], t32B[:], -float(CAP), offs[:],
                                 op0=ALU.mult, op1=ALU.add)
        pad()
        dve.tensor_scalar_add(offs[:], offs[:], float(CAP))
        pad()
        dve.tensor_copy(offi[:], offs[:])
        for sq in range(4):
            dve.tensor_scalar_add(tokf[:, sq], tok0b[:], float(sq * T))
        dve.engine_nop().then_inc(s["sl"], 4)

    def f_scalar(act):
        act.wait_ge(s["ex"], 1)
        act.activation(prb[:], dbb[:], AF.Exp).then_inc(s["ex2"], 1)

    def f_gps(gps):
        gps.wait_ge(s["tse"], 1)
        prog.gdma(gps, io["gathtab"][0:4096].rearrange("(p a) x -> p a x", p=128),
                  zbuf[:, 0:32].unsqueeze(2), "l2r_zf")
        prog.gdma(gps, io["gathtab"][4096:4097], zbuf[0:1, 32:33], "l2r_zf")
        prog.gdma(gps, io["tsd_d"], tsall[:, :, :], "l2r_tsd")
        prog.gdma(gps, tspos[:, 0:8],
                  io["tsd_d"].rearrange("s (q e) -> (s q) e", e=8), "l2r_tsd")
        gps.wait_ge(s["toe"], 1)
        prog.gdma(gps, io["toff_d"].bitcast(F32), toffx[:], "l2r_tod")
        for sq in range(4):
            src = io["toff_d"].rearrange("(s q) e -> s q e", s=4)[sq:sq + 1]
            prog.gdma(gps, toffrow[0:1, sq].rearrange("p (q e) -> p q e", e=8),
                      src, "l2r_tod")
        if "dbg_l2" in io:
            d = io["dbg_l2"]
            gps.wait_ge(s["sl"], 4)
            prog.gdma(gps, d["lgm"], lgm[:].rearrange("p a b c -> p (a b c)"), "l2r_dbg")
            prog.gdma(gps, d["mall"], m_all[:].rearrange("p a b c -> p (a b c)").bitcast(F32), "l2r_dbg")
            prog.gdma(gps, d["idx"], idx1[:].rearrange("p a b -> p (a b)"), "l2r_dbg")
            prog.gdma(gps, d["idx2"], idx2[:].rearrange("p a b -> p (a b)"), "l2r_dbg")
            prog.gdma(gps, d["posge"], posge[:].rearrange("p a b -> p (a b)"), "l2r_dbg")
            prog.gdma(gps, d["toffx"], toffx[:], "l2r_dbg")
            prog.gdma(gps, d["tsp"], tspos[:].bitcast(F32), "l2r_dbg")
            prog.gdma(gps, d["g1"], gate1[:].rearrange("p a b -> p (a b)"), "l2r_dbg")
            prog.gdma(gps, d["offs"], offs[:].rearrange("p a b -> p (a b)"), "l2r_dbg")
        gps.wait_ge(prog.sem("l2r_zf"), 32)
        for sq in range(4):
            gps.wait_ge(s["sl"], 4)
            for tt in range(32):
                ins = gps.indirect_dma_start(
                    out=io["gathtab"],
                    out_offset=IndirectOffsetOnAxis(offi[:, sq, tt:tt + 1], 0),
                    in_=tokf[:, sq, tt:tt + 1], in_offset=None)
                prog.inc(ins, "l2r_sc", dma=True)
            prog.gdma(gps, io["trow_hbm"][sq], trow[:, sq].rearrange("p a j -> p (a j)"),
                      "l2r_tw")
            prog.gdma(gps, io["tgate_hbm"][sq], tgate[:, sq].rearrange("p a j -> p (a j)"),
                      "l2r_tw")

    return dict(f_sync=f_sync, f_pe=f_pe, f_dve=f_dve, f_scalar=f_scalar, f_gps=f_gps)

def emit_l2_f0(prog, io):
    sb = prog.sb
    gtb2 = sb("f0_gtb2", [32, 128], F32)
    gtb = sb("f0_gtb", [128, 32], F32)
    gti = sb("f0_gti", [128, 32], I32)
    padf0 = sb("f0_pad", [128, 512], F32)
    etm = sb("f0_etm", [128, 2, 1024], F32R)
    efm = sb("f0_efm", [128, 8, 512], F32R)
    eyeb, eyef = io["eyeb"], io["eyef"]
    pp, ppr = io["pp"], io["ppr"]
    names = ["gt", "gt2", "gi", "g", "tp", "tpe", "ew"]
    s = {n: prog.sem("l2f0_" + n) for n in names}

    def f_sync(sync):
        sync.wait_ge(prog.sem("l2r_sc"), prog.counts["l2r_sc"])
        sync.wait_ge(prog.sem("l2r_zf"), prog.counts["l2r_zf"])
        prog.gdma(sync, gtb2[:].unsqueeze(2),
                  io["gathtab"][0:4096].rearrange("(a p) x -> a p x", p=128), "l2f0_gt")

    def f_gps(gps):
        gps.wait_ge(s["gi"], 1)
        for n in range(32):
            if n >= 2:
                gps.wait_ge(s["tp"], 8 * (n - 1))
            ins = gps.indirect_dma_start(
                out=etm[:, n % 2], out_offset=None,
                in_=io["h2tm_all"], in_offset=IndirectOffsetOnAxis(gti[:, n:n + 1], 0))
            prog.inc(ins, "l2f0_g", dma=True)

    def f_pe(pe):
        pe.wait_ge(prog.sem("l2f0_gt"), 16)
        pe.transpose(pp[:, 3, 0:32], gtb2[:], eyef[0:32, 0:32]).then_inc(s["gt2"], 1)
        for n in range(32):
            pe.wait_ge(s["g"], 16 * (n + 1))
            for k in range(8):
                ntp = n * 8 + k + 1
                if ntp >= 3:
                    pe.wait_ge(s["tpe"], ntp - 2)
                pe.transpose(ppr[:, (ntp - 1) % 2, 0:128],
                             etm[:, n % 2, k * 128:(k + 1) * 128], eyeb[:]
                             ).then_inc(s["tp"], 1)

    def f_dve(dve):
        dve.wait_ge(s["gt2"], 1)
        dve.tensor_copy(gtb[:], pp[:, 3, 0:32])
        dve.tensor_copy(padf0[:], pp[:, 3, 0:512])
        dve.tensor_copy(gti[:], gtb[:]).then_inc(s["gi"], 1)
        for n in range(32):
            TT, q = divmod(n, 4)
            if q == 0 and TT >= 1:
                dve.wait_ge(s["ew"], 16 * TT)
            for k in range(8):
                ntp = n * 8 + k + 1
                dve.wait_ge(s["tp"], ntp)
                dve.tensor_copy(efm[:, k, q * 128:(q + 1) * 128],
                                ppr[:, (ntp - 1) % 2, 0:128]).then_inc(s["tpe"], 1)

    def f_scalar(act):
        for TT in range(8):
            act.wait_ge(s["tpe"], (TT + 1) * 32)
            dst = io["ein_hbm"].rearrange("(k p) c -> p k c", p=128)[:, :, TT * 512:(TT + 1) * 512]
            prog.gdma(act, dst, efm[:], "l2f0_ew")

    return dict(f_sync=f_sync, f_gps=f_gps, f_pe=f_pe, f_dve=f_dve, f_scalar=f_scalar)


def emit_l2_f1(prog, io):
    sb = prog.sb
    w1q = sb("f1_w1", [128, 8, 1024], F32R)
    w2q = sb("f1_w2", [128, 8, 1024], F32R)
    einb = sb("f1_ein", [128, 2, 8, 512], F32R)
    hmq = sb("f1_hm", [128, 8, 512], F32R)
    tbuf = sb("f1_tb", [128, 2, 512], F32)
    ostg = sb("f1_os", [128, 4, 512], F32)
    b1b = sb("f1_b1", [128, 32], F32)
    b2bc = sb("f1_b2", [128, 2, 512], F32)
    b2r = sb("f1_b2r", [1, 1024], F32R)
    eyeb, onesrow = io["eyeb"], io["onesrow"]
    pp, ppr = io["pp"], io["ppr"]
    names = ["w", "eld", "h1", "b1a", "gel", "w2", "oev", "ow", "b2e", "b2f"]
    s = {n: prog.sem("l2f1_" + n) for n in names}

    def f_sync(sync):
        for p in range(4):
            for TT in range(8):
                n = p * 8 + TT
                if n >= 2:
                    sync.wait_ge(s["h1"], (n - 1) * 8)
                src = io["ein_hbm"].rearrange("(k p) c -> p k c", p=128)[:, :, TT * 512:(TT + 1) * 512]
                prog.gdma(sync, einb[:, n % 2], src, "l2f1_eld")

    def f_pe(pe):
        pe.wait_ge(s["w"], 32)
        pe.matmul(pp[:, 5, :], onesrow[0:1, :], b2r[:, 0:512], start=True, stop=True)
        pe.matmul(pp[:, 4, :], onesrow[0:1, :], b2r[:, 512:1024], start=True, stop=True)
        pe.matmul(pp[0:64, 3, :], eyeb[0:64, 0:64], w1q[0:64, 0, 0:512],
                  start=True, stop=True).then_inc(s["b2f"], 1)
        for p in range(4):
            pe.wait_ge(s["w"], 32 + 32 * (p + 1))
            if p == 0:
                pe.wait_ge(s["b2e"], 1)
            for TT in range(8):
                n = p * 8 + TT
                pe.wait_ge(s["eld"], 16 * (n + 1))
                for fc in range(8):
                    nh = n * 8 + fc
                    if nh >= 2:
                        pe.wait_ge(s["b1a"], nh - 1)
                    for kc in range(8):
                        ins = pe.matmul(pp[:, 4 + fc % 2, :],
                                        w1q[:, kc, fc * 128:(fc + 1) * 128],
                                        einb[:, n % 2, kc], start=(kc == 0), stop=(kc == 7))
                    ins.then_inc(s["h1"], 1)
                for tq in range(4):
                    for dh in range(2):
                        no = (n * 4 + tq) * 2 + dh
                        bank = no % 4
                        if no >= 4:
                            pe.wait_ge(s["oev"], no - 3)
                        for kc in range(8):
                            pe.wait_ge(s["gel"], n * 8 + kc + 1)
                            ins = pe.matmul(pp[:, bank, :],
                                            hmq[:, kc, tq * 128:(tq + 1) * 128],
                                            w2q[:, kc, dh * 512:(dh + 1) * 512],
                                            start=(kc == 0), stop=(kc == 7))
                        ins.then_inc(s["w2"], 1)

    def f_dve(dve):
        dve.wait_ge(s["b2f"], 1)
        dve.tensor_copy(b2bc[:, 0], pp[:, 5, :])
        dve.tensor_copy(b2bc[:, 1], pp[:, 4, :]).then_inc(s["b2e"], 1)
        for p in range(4):
            for TT in range(8):
                n = p * 8 + TT
                for fc in range(8):
                    nh = n * 8 + fc
                    dve.wait_ge(s["h1"], nh + 1)
                    if nh >= 2:
                        dve.wait_ge(s["gel"], nh - 1)
                    bb = b1b[:, p * 8 + fc:p * 8 + fc + 1].broadcast_to([128, 512])
                    dve.tensor_tensor(tbuf[:, nh % 2], pp[:, 4 + fc % 2, :], bb, ALU.add
                                      ).then_inc(s["b1a"], 1)
                for tq in range(4):
                    for dh in range(2):
                        no = (n * 4 + tq) * 2 + dh
                        bank = no % 4
                        dve.wait_ge(s["w2"], no + 1)
                        if no >= 4:
                            dve.wait_ge(s["ow"], 16 * (no - 3))
                        if p == 0:
                            dve.tensor_tensor(ostg[:, no % 4], pp[:, bank, :],
                                              b2bc[:, dh], ALU.add).then_inc(s["oev"], 1)
                        else:
                            dve.tensor_copy(ostg[:, no % 4], pp[:, bank, :]
                                            ).then_inc(s["oev"], 1)

    def f_scalar(act):
        prog.gdma(act, b1b[:], io["b1_in"], "l2f1_w")
        prog.gdma(act, b2r[:], io["b2row"], "l2f1_w")
        for p in range(4):
            if p > 0:
                act.wait_ge(s["w2"], p * 64)
            prog.gdma(act, w1q[:],
                      io["w1"][:, p * 1024:(p + 1) * 1024].rearrange("(k p) f -> p k f", p=128),
                      "l2f1_w")
            prog.gdma(act, w2q[:],
                      io["w2"][p * 1024:(p + 1) * 1024, :].rearrange("(k p) f -> p k f", p=128),
                      "l2f1_w")
            for TT in range(8):
                n = p * 8 + TT
                for fc in range(8):
                    nh = n * 8 + fc
                    act.wait_ge(s["b1a"], nh + 1)
                    if n >= 1 and fc == 0:
                        act.wait_ge(s["w2"], n * 8)
                    act.activation(hmq[:, fc], tbuf[:, nh % 2], AF.Gelu
                                   ).then_inc(s["gel"], 1)

    def f_gps(gps):
        for p in range(4):
            for TT in range(8):
                n = p * 8 + TT
                for tq in range(4):
                    for dh in range(2):
                        no = (n * 4 + tq) * 2 + dh
                        gps.wait_ge(s["oev"], no + 1)
                        t0 = TT * 512 + tq * 128
                        dst = io["exp_out"][t0:t0 + 128, dh * 512:(dh + 1) * 512]
                        kw = {} if p == 0 else {"accum_op": ALU.add}
                        prog.gdma(gps, dst, ostg[:, no % 4], "l2f1_ow", **kw)

    return dict(f_sync=f_sync, f_pe=f_pe, f_dve=f_dve, f_scalar=f_scalar, f_gps=f_gps)


def build_l2_test(consts):
    prog = Prog()
    io = declare_l2(prog)
    with prog.nc.Block() as blk0:
        emit_l2_consts(prog, blk0, io)
    emit_phase_seq(prog, io, [("l2r", emit_l2_routing), ("l2f0", emit_l2_f0),
                              ("l2f1", emit_l2_f1)])
    with prog.nc.Block() as blk2:
        def f_end(gps):
            for nm in ("l2f1_ow", "l2r_tw"):
                gps.wait_ge(prog.sems[nm], prog.counts[nm])
        blk2.gpsimd(f_end)
    prog.close()
    return prog



def build_l3(consts):
    prog = Prog()
    nc = prog.nc
    sb = prog.sb
    eyef_d = nc.dram_tensor("c_eyef", [128, 128], F32, kind="ExternalInput").ap()
    x1_d = nc.dram_tensor("x1", [C, T], F32R, kind="ExternalInput").ap()
    xm_d = nc.dram_tensor("xm", [C, T], F32R, kind="ExternalInput").ap()
    trow_d = nc.dram_tensor("trow", [128, 64], F32, kind="ExternalInput").ap()
    tgate_d = nc.dram_tensor("tgate", [128, 64], F32, kind="ExternalInput").ap()
    eo_d = nc.dram_tensor("eo_all", [8 * CAP, C], F32, kind="ExternalInput").ap()
    out_d = nc.dram_tensor("out_hbm", [T, C], F32, kind="ExternalOutput").ap()

    eyef = sb("l3_eyef", [128, 128], F32)
    trowb = sb("l3_trow", [128, 64], F32)
    tgateb = sb("l3_tg", [128, 64], F32)
    trowi = sb("l3_ti", [128, 64], I32)
    gbuf = sb("l3_g", [128, 2, 2, 1024], F32)
    mbuf = sb("l3_m", [128, 2, 1024], F32)
    x1t = sb("l3_x1", [128, 2, 8, 128], F32R)
    xmt = sb("l3_xm", [128, 2, 8, 128], F32R)
    s12 = sb("l3_s12", [128, 2, 8, 128], F32)
    outt = sb("l3_o", [128, 2, 1024], F32)
    pp = prog.ps("pp", [128, 6, 512], F32)

    names = ["c", "t", "ti", "g", "mo", "ld", "sm", "tp", "ad", "ow"]
    s = {n: prog.sem("l3_" + n) for n in names}

    with prog.nc.Block() as blk:
        def f_sync(sync):
            prog.gdma(sync, eyef[:], eyef_d, "l3_c")
            prog.gdma(sync, trowb[:], trow_d, "l3_c")
            prog.gdma(sync, tgateb[:], tgate_d, "l3_c")
            for tt in range(32):
                ts = slice(tt * 128, (tt + 1) * 128)
                if tt >= 2:
                    sync.wait_ge(s["sm"], tt - 1)
                prog.gdma(sync, x1t[:, tt % 2],
                          x1_d.rearrange("(k p) t -> p k t", p=128)[:, :, ts], "l3_ld")
                prog.gdma(sync, xmt[:, tt % 2],
                          xm_d.rearrange("(k p) t -> p k t", p=128)[:, :, ts], "l3_ld")

        def f_gps(gps):
            gps.wait_ge(s["ti"], 1)
            for tt in range(32):
                for j in range(2):
                    if tt >= 2 and j == 0:
                        gps.wait_ge(s["mo"], tt - 1)
                    ins = gps.indirect_dma_start(
                        out=gbuf[:, tt % 2, j], out_offset=None,
                        in_=eo_d,
                        in_offset=IndirectOffsetOnAxis(trowi[:, tt * 2 + j:tt * 2 + j + 1], 0))
                    prog.inc(ins, "l3_g", dma=True)
                gps.wait_ge(s["ad"], tt + 1)
                prog.gdma(gps, out_d[tt * 128:(tt + 1) * 128, :], outt[:, tt % 2], "l3_ow")

        def f_pe(pe):
            for tt in range(32):
                pe.wait_ge(s["sm"], tt + 1)
                if tt >= 2:
                    pe.wait_ge(s["ad"], tt - 1)
                for k in range(8):
                    b = (tt % 2) * 2 + k // 4
                    pe.transpose(pp[:, b, (k % 4) * 128:(k % 4 + 1) * 128],
                                 s12[:, tt % 2, k], eyef[:]).then_inc(s["tp"], 1)

        def f_dve(dve):
            dve.wait_ge(prog.sem("l3_c"), 48)
            dve.tensor_copy(trowi[:], trowb[:]).then_inc(s["ti"], 1)
            for tt in range(32):
                dve.wait_ge(s["ld"], 32 * (tt + 1))
                dve.tensor_tensor(s12[:, tt % 2].bitcast(F32), x1t[:, tt % 2].bitcast(F32),
                                  xmt[:, tt % 2].bitcast(F32), ALU.add).then_inc(s["sm"], 1)
                dve.wait_ge(s["g"], 32 * (tt + 1))
                g0 = tgateb[:, tt * 2:tt * 2 + 1].broadcast_to([128, 1024])
                g1 = tgateb[:, tt * 2 + 1:tt * 2 + 2].broadcast_to([128, 1024])
                dve.tensor_tensor(mbuf[:, tt % 2], gbuf[:, tt % 2, 0], g0, ALU.mult)
                dve.tensor_tensor(gbuf[:, tt % 2, 1], gbuf[:, tt % 2, 1], g1, ALU.mult)
                dve.tensor_tensor(mbuf[:, tt % 2], mbuf[:, tt % 2], gbuf[:, tt % 2, 1],
                                  ALU.add).then_inc(s["mo"], 1)
                dve.wait_ge(s["tp"], 8 * (tt + 1))
                if tt >= 2:
                    dve.wait_ge(s["ow"], 16 * (tt - 1))
                pb = (tt % 2) * 2
                dve.tensor_tensor(outt[:, tt % 2],
                                  pp[:, pb:pb + 2, :].rearrange("p a b -> p (a b)"),
                                  mbuf[:, tt % 2], ALU.add).then_inc(s["ad"], 1)

        blk.sync(f_sync)
        blk.gpsimd(f_gps)
        blk.tensor(f_pe)
        blk.vector(f_dve)
    with prog.nc.Block() as blk2:
        def f_end(gps):
            gps.wait_ge(prog.sems["l3_ow"], prog.counts["l3_ow"])
        blk2.gpsimd(f_end)
    prog.close()
    return prog


def build_l1(consts):
    prog = Prog()
    nc = prog.nc
    io = declare_common(prog)
    for nm in ("wqT", "wkT", "wvT", "woT"):
        io[nm] = nc.dram_tensor(nm, [C, C], F32R, kind="ExternalInput").ap()
    io["xT"] = nc.dram_tensor("xT", [C, T], F32R, kind="ExternalInput").ap()
    io["gnw_in"] = nc.dram_tensor("gnw_in", [64, 16], F32, kind="ExternalInput").ap()
    io["gnb_in"] = nc.dram_tensor("gnb_in", [64, 16], F32, kind="ExternalInput").ap()
    io["wqkv"] = nc.dram_tensor("wqkv", [C, 3 * C], F32R, kind="ExternalInput").ap()
    io["lngb_in"] = nc.dram_tensor("lngb_in", [128, 16], F32R, kind="ExternalInput").ap()
    io["wproj"] = nc.dram_tensor("wproj", [C, C], F32R, kind="ExternalInput").ap()
    io["wgate"] = nc.dram_tensor("wgate", [C, 8], F32R, kind="ExternalInput").ap()
    io["lngb2_in"] = nc.dram_tensor("lngb2_in", [128, 16], F32, kind="ExternalInput").ap()
    io["ret_hbm"] = nc.dram_tensor("ret_hbm", [C, T], F32R, kind="Internal").ap()
    io["x1_hbm"] = nc.dram_tensor("x1_hbm", [C, T], F32R, kind="ExternalOutput").ap()
    io["xm_hbm"] = nc.dram_tensor("xm_hbm", [C, T], F32R, kind="ExternalOutput").ap()
    io["k_hbm"] = nc.dram_tensor("k_hbm", [H, 64, T], F32R, kind="Internal").ap()
    io["q_hbm"] = nc.dram_tensor("q_hbm", [H, 64, T], F32R, kind="Internal").ap()
    io["v_hbm"] = nc.dram_tensor("v_hbm", [T, C], F32R, kind="Internal").ap()
    io["o_hbm"] = nc.dram_tensor("o_hbm", [H, 64, T], F32R, kind="Internal").ap()
    io["lgT_hbm"] = nc.dram_tensor("lgT_hbm", [8, T], F32, kind="ExternalOutput").ap()
    io["h2tm_hbm"] = nc.dram_tensor("h2tm_hbm", [T, C], F32R, kind="ExternalOutput").ap()
    io["gs_f"] = consts["ret_gs_f"]

    with prog.nc.Block() as blk0:
        emit_const_loads(prog, blk0, io)
    emit_phase_seq(prog, io, [("p1", emit_p1_retention), ("p2", emit_p2_qkv),
                              ("p3", emit_p3_attention), ("p4", emit_p4_gate)])
    with prog.nc.Block() as blk2:
        def f_end(gps):
            for nm in ("p2_x1w", "p3_wod", "p4_lgw", "p4_h2w"):
                gps.wait_ge(prog.sems[nm], prog.counts[nm])
        blk2.gpsimd(f_end)
    prog.close()
    return prog


def _l1_in_map(consts, inputs, s):
    qc = [h * 192 + d for h in range(16) for d in range(64)]
    kc = [h * 192 + 64 + d for h in range(16) for d in range(64)]
    vc = [h * 192 + 128 + d for h in range(16) for d in range(64)]
    Wp = np.ascontiguousarray(inputs["L0_ffn_Wqkv"][:, qc + kc + vc])
    g1 = inputs["L0_ffn_ln1_g"].reshape(8, 128).T
    b1 = inputs["L0_ffn_ln1_b"].reshape(8, 128).T
    g2 = inputs["L0_ffn_ln2_g"].reshape(8, 128).T
    b2 = inputs["L0_ffn_ln2_b"].reshape(8, 128).T
    im = const_in_map(consts)
    im.update({
        "xT": np.ascontiguousarray(inputs["L0_attn_x"][s].T),
        "wqT": np.ascontiguousarray(inputs["L0_attn_wq"].T),
        "wkT": np.ascontiguousarray(inputs["L0_attn_wk"].T),
        "wvT": np.ascontiguousarray(inputs["L0_attn_wv"].T),
        "woT": np.ascontiguousarray(inputs["L0_attn_wo"].T),
        "gnw_in": np.ascontiguousarray(inputs["L0_attn_gn_w"].reshape(16, 64).T),
        "gnb_in": np.ascontiguousarray(inputs["L0_attn_gn_b"].reshape(16, 64).T),
        "wqkv": Wp,
        "lngb_in": np.ascontiguousarray(np.concatenate([g1, b1], axis=1)),
        "wproj": np.ascontiguousarray(inputs["L0_ffn_Wproj"]),
        "wgate": np.ascontiguousarray(inputs["L0_ffn_Wgate"]),
        "lngb2_in": np.ascontiguousarray(np.concatenate([g2, b2], axis=1)),
    })
    return im


def kernel(**inputs):
    inputs = {k: np.asarray(v) for k, v in inputs.items()}
    consts = host_constants()

    prog1 = build_l1(consts)
    in_maps1 = [_l1_in_map(consts, inputs, c >> 1) for c in range(NCORES)]
    res1 = run_bass_kernel_spmd(prog1.nc, in_maps1, core_ids=list(range(NCORES)))
    r1 = [res1.results[2 * s] for s in range(4)]

    prog2 = build_l2_test(consts)
    lgT_all = np.ascontiguousarray(np.stack([r1[s]["lgT_hbm"] for s in range(4)]
                                            ).reshape(32, T).astype(np.float32))
    h2tm_all = np.ascontiguousarray(np.concatenate(
        [r1[s]["h2tm_hbm"] for s in range(4)], axis=0).astype(np.float32))
    cim2 = l2_const_in_map(consts)
    in_maps2 = []
    for e in range(NCORES):
        im = dict(cim2)
        im.update({
            "lgT_all": lgT_all, "h2tm_all": h2tm_all,
            "w1": np.ascontiguousarray(inputs["L0_ffn_W1"][e]),
            "w2": np.ascontiguousarray(inputs["L0_ffn_W2"][e]),
            "b1_in": np.ascontiguousarray(inputs["L0_ffn_b1"][e].reshape(32, 128).T),
            "b2row": np.ascontiguousarray(inputs["L0_ffn_b2"][e].reshape(1, C)),
            "emeta": np.full((128, 1), float(e), dtype=np.float32),
        })
        in_maps2.append(im)
    res2 = run_bass_kernel_spmd(prog2.nc, in_maps2, core_ids=list(range(NCORES)))
    eo_all = np.ascontiguousarray(np.concatenate(
        [res2.results[e]["exp_out"] for e in range(NCORES)], axis=0).astype(np.float32))
    trow = res2.results[0]["trow_hbm"]
    tgate = res2.results[0]["tgate_hbm"]

    prog3 = build_l3(consts)
    in_maps3 = []
    for c in range(NCORES):
        s = c >> 1
        in_maps3.append({
            "c_eyef": consts["eye128"],
            "x1": np.ascontiguousarray(r1[s]["x1_hbm"].astype(np.float32)),
            "xm": np.ascontiguousarray(r1[s]["xm_hbm"].astype(np.float32)),
            "trow": np.ascontiguousarray(trow[s].astype(np.float32)),
            "tgate": np.ascontiguousarray(tgate[s].astype(np.float32)),
            "eo_all": eo_all,
        })
    res3 = run_bass_kernel_spmd(prog3.nc, in_maps3, core_ids=list(range(NCORES)))
    out = np.stack([res3.results[2 * s]["out_hbm"] for s in range(4)], axis=0)
    return np.ascontiguousarray(out.astype(np.float32))
